# version 17
# baseline (speedup 1.0000x reference)
import numpy as np
import ml_dtypes
from contextlib import ExitStack
import concourse.bass as bass
import concourse.mybir as mybir
from concourse.bass_utils import run_bass_kernel_spmd
from concourse.bass import IndirectOffsetOnAxis as IOA

F32 = mybir.dt.float32
BF16 = mybir.dt.bfloat16
I32 = mybir.dt.int32
AF = mybir.ActivationFunctionType
ALU = mybir.AluOpType
AX = mybir.AxisListType.X
BFNP = ml_dtypes.bfloat16

D = 1024
TC = 4096
NBLK = 8
TB = 512
KC = 8
NEXP = 32
EGRP = 2
NGRP = NEXP // EGRP
SCALE = 2.0 / 4096.0
ERR_CLIP = 5.0
GRAD_CLIP = 1.0
LN_EPS = 1e-5

DEBUG_TAPS = []

TAP_SHAPES = {
    "ln1": ([128, 4096], BF16), "kfm": ([128, 4096], BF16), "vtm": ([128, 4096], BF16),
    "qfm": ([128, 4096], BF16), "a1tm": ([128, 4096], BF16), "etm": ([128, 4096], BF16),
    "d1tm": ([128, 4096], BF16), "g0": ([128, 8192], F32), "g1": ([128, 8192], F32),
    "w0n": ([128, 8192], BF16), "resid": ([128, 4096], F32), "hit": ([128, 4096], BF16),
    "logits": ([128, 1024], F32), "wma": ([128, 1024], F32), "gacc": ([1, 4], F32),
    "gsig": ([1, 4], F32), "u2": ([128, 4096], BF16), "h2": ([128, 4096], BF16),
}


class Chan:
    def __init__(self, ctx, nc, name):
        self.h = ctx.enter_context(nc.semaphore(name))
        self.n = 0

    def inc(self, inst, k=1):
        inst.then_inc(self.h, k)
        self.n += k
        return (self.h, self.n)

    def tok(self):
        return (self.h, self.n)


def wait(eng, tok):
    if tok is None:
        return
    if isinstance(tok, _Multi):
        for t in tok.toks:
            wait(eng, t)
        return
    eng.wait_ge(tok[0], tok[1])


class _Multi:
    def __init__(self, toks):
        self.toks = [t for t in toks if t is not None]


def _multi(toks):
    return _Multi(toks)


class DmaQ:

    def __init__(self, ctx, nc, name):
        self.h = ctx.enter_context(nc.semaphore(name))
        self.n = 0

    def dma(self, eng, out, in_, **kw):
        if self.n:
            eng.wait_ge(self.h, self.n)
        inst = eng.dma_start(out, in_, **kw)
        inst.then_inc(self.h, 16)
        self.n += 16
        return (self.h, self.n)

    def dma_indirect_gather(self, eng, out, in_dram, idx_ap, bound):
        if self.n:
            eng.wait_ge(self.h, self.n)
        inst = eng.indirect_dma_start(
            out=out, out_offset=None, in_=in_dram,
            in_offset=IOA(axis=0, ap=idx_ap),
            bounds_check=bound, oob_is_err=False)
        inst.then_inc(self.h, 16)
        self.n += 16
        return (self.h, self.n)

    def dma_indirect_scatter(self, eng, out_dram, idx_ap, in_sb, bound):
        if self.n:
            eng.wait_ge(self.h, self.n)
        inst = eng.indirect_dma_start(
            out=out_dram, out_offset=IOA(axis=0, ap=idx_ap),
            in_=in_sb, in_offset=None,
            bounds_check=bound, oob_is_err=False)
        inst.then_inc(self.h, 16)
        self.n += 16
        return (self.h, self.n)

    def dma_tr(self, eng, out, in_):
        if self.n:
            eng.wait_ge(self.h, self.n)
        inst = eng.dma_start_transpose(out, in_)
        inst.then_inc(self.h, 16)
        self.n += 16
        return (self.h, self.n)

    def tok(self):
        return (self.h, self.n) if self.n else None


def build(taps=(), phases=7):
    nc = bass.Bass("TRN2", target_bir_lowering=False, debug=False)

    x_d = nc.dram_tensor("x", [TC, D], F32, kind="ExternalInput").ap()
    w_d = {w: nc.dram_tensor(w, [D, D], F32, kind="ExternalInput").ap()
           for w in ["w1", "wkt", "wvt", "wqt", "w0t", "w1t", "woutt"]}
    wg3_d = nc.dram_tensor("wg3", [3, D], F32, kind="ExternalInput").ap()
    bg3_d = nc.dram_tensor("bg3", [1, 4], F32, kind="ExternalInput").ap()
    router_d = nc.dram_tensor("router", [NEXP, D], F32, kind="ExternalInput").ap()
    experts_d = nc.dram_tensor("experts", [NEXP * D, D], F32, kind="ExternalInput").ap()
    ident_d = nc.dram_tensor("identin", [128, 128], BF16, kind="ExternalInput").ap()
    oh_d = nc.dram_tensor("oh", [NEXP, NEXP * 128], BF16, kind="ExternalInput").ap()
    lt_d = nc.dram_tensor("lt128d", [128, 128], BF16, kind="ExternalInput").ap()
    iota32_d = nc.dram_tensor("iota32d", [128, NEXP], F32, kind="ExternalInput").ap()
    iotac_d = nc.dram_tensor("iotacd", [128, 1], F32, kind="ExternalInput").ap()
    tokc4_d = nc.dram_tensor("tokc4d", [128, 4], F32, kind="ExternalInput").ap()
    out_d = nc.dram_tensor("out", [TC, D], F32, kind="ExternalOutput").ap()

    ksp = nc.dram_tensor("ksp", [128, KC * TC], BF16, kind="Internal").ap()
    qsp = nc.dram_tensor("qsp", [128, KC * TC], BF16, kind="Internal").ap()
    vsp = nc.dram_tensor("vsp", [128, (TC // 128) * D], BF16, kind="Internal").ap()
    ln2sp = nc.dram_tensor("ln2sp", [128, KC * TC], BF16, kind="Internal").ap()
    residsp = nc.dram_tensor("residsp", [128, (TC // 128) * D], F32, kind="Internal").ap()
    moeacc = nc.dram_tensor("moeacc", [TC, D], F32, kind="Internal").ap()
    ln2tmsp = nc.dram_tensor("ln2tmsp", [TC, D], BF16, kind="Internal").ap()
    idxstage = nc.dram_tensor("idxstage", [NEXP * 512, 2], F32, kind="Internal").ap()
    bufA = nc.dram_tensor("bufA", [TC, D], BF16, kind="Internal").ap()
    bufB = nc.dram_tensor("bufB", [TC, D], BF16, kind="Internal").ap()
    ccin = nc.dram_tensor("ccin", [128, 8192], F32, kind="Internal").ap()
    ccout = nc.dram_tensor("ccout", [128, 8192], F32, kind="Internal").ap()
    ccg_in = nc.dram_tensor("ccg_in", [1, 4], F32, kind="Internal").ap()
    ccg_out = nc.dram_tensor("ccg_out", [1, 4], F32, kind="Internal").ap()
    ccin2 = nc.dram_tensor("ccin2", [128, 8192], F32, kind="Internal").ap()
    ccout2 = nc.dram_tensor("ccout2", [128, 8192], F32, kind="Internal").ap()
    gbd = nc.dram_tensor("gbd", [1, 4], F32, kind="Internal").ap()

    tap_d = {t: nc.dram_tensor("tap_" + t, TAP_SHAPES[t][0], TAP_SHAPES[t][1],
                               kind="ExternalOutput").ap() for t in taps}

    with ExitStack() as ctx:
        en = ctx.enter_context
        ART = en(nc.sbuf_tensor("arena", [128, 102400], BF16))
        ARF = ART.bitcast(F32)
        ident = en(nc.sbuf_tensor("ident", [128, 128], BF16))
        rthi = en(nc.sbuf_tensor("rthi", [128, KC * NEXP], BF16))
        rtlo = en(nc.sbuf_tensor("rtlo", [128, KC * NEXP], BF16))
        wgt = en(nc.sbuf_tensor("wgt", [128, KC * 4], BF16))
        small = en(nc.sbuf_tensor("small", [128, 64], F32))
        stats = en(nc.sbuf_tensor("stats", [128, 32], F32))
        g3 = en(nc.sbuf_tensor("g3", [1, 16], F32))
        ones_bf = en(nc.sbuf_tensor("ones_bf", [128, 1], BF16))
        colsum_bf = en(nc.sbuf_tensor("colsum_bf", [128, 8], BF16))
        ones1 = en(nc.sbuf_tensor("ones1", [1, 128], F32))
        ones1b = en(nc.sbuf_tensor("ones1b", [1, 128], BF16))
        lt128 = en(nc.sbuf_tensor("lt128", [128, 128], BF16))
        iota32 = en(nc.sbuf_tensor("iota32", [128, NEXP], F32))
        iotac = en(nc.sbuf_tensor("iotac", [128, 1], F32))
        tokc4 = en(nc.sbuf_tensor("tokc4", [128, 4], F32))
        baserow = en(nc.sbuf_tensor("baserow", [1, 2 * NEXP], F32))
        baserow_bf = en(nc.sbuf_tensor("baserow_bf", [1, 2 * NEXP], BF16))
        br4b = en(nc.sbuf_tensor("br4b", [1, 256], BF16))
        wgsb = en(nc.sbuf_tensor("wgsb", [4, D], BF16))

        PS = [en(nc.psum_tensor(f"ps{i}", [128, 512], F32)) for i in range(6)]
        PT = [en(nc.psum_tensor(f"pt{i}", [128, 1024], BF16)) for i in range(2)]
        PTF = [p.bitcast(F32) for p in PT]
        PSF = [None] * 8

        def bwait(eng, bi):
            wait(eng, PSF[bi])

        def bset(bi, tok):
            PSF[bi] = tok

        def bfa(off_kb, cols):
            off = int(off_kb * 1024) // 2
            return ART[:, off:off + cols]

        def f32a(off_kb, cols):
            off = int(off_kb * 1024) // 4
            return ARF[:, off:off + cols]

        W_OFF = {"wk": 0, "wv": 16, "wq": 32, "w0": 48, "w1": 64, "w1p": 80}
        WT = {k: bfa(v, KC * D) for k, v in W_OFF.items()}
        X1 = f32a(96, 4096)
        LN1T = bfa(112, 4096)
        LN1F = bfa(120, 4096)
        QSB = bfa(128, 4096)
        KSB = bfa(136, 4096)
        VSB = bfa(144, 4096)
        RSBF = f32a(168, 1024)
        RHI = bfa(172, 1024)
        RLO = bfa(174, 1024)
        JNK1A = bfa(184, 4096)
        G0 = f32a(0, 8192)
        WOUTT = bfa(32, KC * D)
        G1 = f32a(96, 8192)
        KLD = [bfa(128, 4096), bfa(136, 4096)]
        VLD = [bfa(144, 4096), bfa(152, 4096)]
        SGT = bfa(160, 4096)
        PRC = bfa(168, 4096)
        A1T = bfa(176, 4096)
        AF1 = bfa(184, 4096)
        ETM = bfa(192, 4096)
        W0N = bfa(128, KC * D)
        W1N = bfa(144, KC * D)
        X2 = f32a(0, 4096)
        JNK2 = bfa(16, 4096)
        JNKF2 = f32a(18, 512)
        RES = f32a(48, 4096)
        LN2F32 = f32a(64, 4096)
        HIT = bfa(80, 4096)
        LOT = bfa(88, 4096)
        HIF = bfa(96, 4096)
        LOF = bfa(104, 4096)
        WMT = bfa(112, 4096)
        LOGI = f32a(120, 1024)
        WMA = f32a(124, 1024)
        Q2 = [bfa(160, 4096), bfa(168, 4096)]
        SG2 = bfa(176, 4096)
        U2 = bfa(184, 4096)
        H2 = bfa(192, 4096)
        EX = [bfa(0, KC * D), bfa(16, KC * D), bfa(32, KC * D), bfa(48, KC * D)]
        OHSB = bfa(64, NEXP * 128)
        WBC = bfa(72, 1024)
        LSTR = [bfa(74, 4096), bfa(82, 4096)]
        ZSB = [[bfa(90, 4096), bfa(106, 4096)], [bfa(98, 4096), bfa(136, 4096)]]
        STG = [bfa(144, 4096), bfa(152, 4096)]
        R4 = [f32a(0, 4096), f32a(16, 4096)]
        M4 = [f32a(32, 4096), f32a(48, 4096)]
        O4 = [f32a(64, 4096), f32a(80, 4096)]

        ch_t = Chan(ctx, nc, "ch_t")
        ch_a = Chan(ctx, nc, "ch_a")
        ch_v = Chan(ctx, nc, "ch_v")
        ch_cc = Chan(ctx, nc, "ch_cc")
        WQ = DmaQ(ctx, nc, "WQ")
        TRQ = DmaQ(ctx, nc, "TRQ")
        XQ = DmaQ(ctx, nc, "XQ")
        SPQ = DmaQ(ctx, nc, "SPQ")
        SPK = DmaQ(ctx, nc, "SPK")
        SPV = DmaQ(ctx, nc, "SPV")
        KVQ = [DmaQ(ctx, nc, "KVQ0"), DmaQ(ctx, nc, "KVQ1")]
        Q2Q = DmaQ(ctx, nc, "Q2Q")
        RSQ = DmaQ(ctx, nc, "RSQ")
        LNQ = DmaQ(ctx, nc, "LNQ")
        CCS = DmaQ(ctx, nc, "CCS")
        CCL = DmaQ(ctx, nc, "CCL")
        GBQ = DmaQ(ctx, nc, "GBQ")
        EXQ = [DmaQ(ctx, nc, "EXQ0"), DmaQ(ctx, nc, "EXQ1")]
        LSQ = [DmaQ(ctx, nc, "LSQ0"), DmaQ(ctx, nc, "LSQ1")]
        ACQ = [DmaQ(ctx, nc, "ACQ0"), DmaQ(ctx, nc, "ACQ1")]
        P4Q = [DmaQ(ctx, nc, "P4Q0"), DmaQ(ctx, nc, "P4Q1")]
        STQ = [DmaQ(ctx, nc, "STQ0"), DmaQ(ctx, nc, "STQ1")]
        TAPQ = DmaQ(ctx, nc, "TAPQ")

        block = en(nc.Block())
        T, A, V, SY, G = block.tensor, block.scalar, block.vector, block.sync, block.gpsimd

        def tap(name, src, tok):
            if name in tap_d:
                def f(g):
                    wait(g, tok)
                    TAPQ.dma(g, tap_d[name], src)
                G(f)

        cast_toks = {}

        def stw_g(g):
            for w, q in [("wq", WQ), ("wk", EXQ[0]), ("wv", EXQ[1]),
                         ("w0", ACQ[0]), ("w1", ACQ[1])]:
                cast_toks[w + "T"] = q.dma(
                    g, WT[w][:, :].rearrange("p (c e) -> p c e", c=KC),
                    w_d[w + "t"].rearrange("(c p) e -> p c e", p=128))
            cast_toks["router"] = WQ.dma(g, RSBF[0:32, 0:D], router_d)
            cast_toks["bg3"] = WQ.dma(g, g3[0:1, 4:8], bg3_d)
            cast_toks["wg3"] = WQ.dma(g, wgsb[0:3, :], wg3_d)
            cast_toks["lt"] = WQ.dma(g, lt128[:, :], lt_d)
            cast_toks["iota32"] = WQ.dma(g, iota32[:, :], iota32_d)
            cast_toks["iotac"] = WQ.dma(g, iotac[:, :], iotac_d)
            cast_toks["tokc4"] = WQ.dma(g, tokc4[:, :], tokc4_d)
            cast_toks["w1p"] = WQ.dma(
                g, WT["w1p"].rearrange("p (c e) -> p c e", c=KC),
                w_d["w1"].rearrange("(c p) e -> p c e", p=128))
        G(stw_g)

        def stw_sy(sy):
            cast_toks["ident"] = TRQ.dma(sy, ident[:, :], ident_d)
        SY(stw_sy)

        def stw_wg_t(t):
            wait(t, cast_toks["wg3"])
            wait(t, cast_toks["ident"])
            bwait(t, 6)
            for kc in range(KC):
                li = t.transpose(PT[0][0:128, kc * 128:kc * 128 + 3],
                                 wgsb[0:3, kc * 128:(kc + 1) * 128], ident[0:3, 0:3])
            ch_t.inc(li)
        T(stw_wg_t)
        wg_ps_tok = ch_t.tok()

        def stw_wg_v(v):
            wait(v, wg_ps_tok)
            for kc in range(KC):
                v.tensor_copy(wgt[:, kc * 4:kc * 4 + 3], PT[0][:, kc * 128:kc * 128 + 3])
            ch_v.inc(v.drain(fusable=False))
        V(stw_wg_v)
        wgt_tok = ch_v.tok()
        bset(6, wgt_tok)

        def stw_r_v(v):
            wait(v, cast_toks["router"])
            v.tensor_copy(RHI[0:32, 0:D], RSBF[0:32, 0:D])
            t1 = ch_v.inc(v.drain(fusable=False))
            wait(v, t1)
            v.tensor_tensor(RLO[0:32, 0:D], RSBF[0:32, 0:D], RHI[0:32, 0:D], ALU.subtract)
            ch_v.inc(v.drain(fusable=False))
        V(stw_r_v)
        rhilo_tok = ch_v.tok()

        rt_ev = [None] * KC
        rt_pe = [None] * KC
        for kc in range(KC):
            def rt_t(t, kc=kc):
                if kc == 0:
                    wait(t, rhilo_tok)
                    bwait(t, 7)
                if kc >= 2:
                    wait(t, rt_ev[kc - 2])
                half = (kc % 2) * 512
                t.transpose(PT[1][0:128, half:half + 32],
                            RHI[0:32, kc * 128:(kc + 1) * 128], ident[0:32, 0:32])
                li = t.transpose(PT[1][0:128, half + 128:half + 160],
                                 RLO[0:32, kc * 128:(kc + 1) * 128], ident[0:32, 0:32])
                rt_pe[kc] = ch_t.inc(li)
            T(rt_t)

            def rt_e(s, kc=kc):
                wait(s, rt_pe[kc])
                half = (kc % 2) * 512
                s.copy(rthi[:, kc * NEXP:(kc + 1) * NEXP], PT[1][:, half:half + 32])
                s.copy(rtlo[:, kc * NEXP:(kc + 1) * NEXP], PT[1][:, half + 128:half + 160])
                rt_ev[kc] = ch_a.inc(s.drain(fusable=False))
            A(rt_e)
        rt_tok = rt_ev[-1]
        bset(7, rt_tok)

        def emit_ln(x_ap, outs, junk, xin_tok):
            sums = stats[:, 0:4]
            sq = stats[:, 4:8]
            rstd = stats[:, 16:20]
            nb = stats[:, 20:24]
            t1 = small[:, 32:36]
            t2 = small[:, 36:40]

            def on_sq(s):
                wait(s, xin_tok)
                for tc in range(4):
                    s.activation(junk[:, tc * 1024:(tc + 1) * 1024],
                                 x_ap[:, tc * 1024:(tc + 1) * 1024],
                                 AF.Square, accum_out=sq[:, tc:tc + 1])
                ch_a.inc(s.drain(fusable=False))
            A(on_sq)
            sq_tok = ch_a.tok()

            def on_v(v):
                wait(v, xin_tok)
                for tc in range(4):
                    v.reduce_sum(sums[:, tc:tc + 1], x_ap[:, tc * 1024:(tc + 1) * 1024], axis=AX)
                tk = ch_v.inc(v.drain(fusable=False)); wait(v, tk)
                wait(v, sq_tok)
                v.tensor_scalar_mul(nb[:, :], sums[:, :], 1.0 / D)
                tk = ch_v.inc(v.drain(fusable=False)); wait(v, tk)
                v.tensor_tensor(t1[:, :], nb[:, :], nb[:, :], ALU.mult)
                v.tensor_scalar(t2[:, :], sq[:, :], 1.0 / D, LN_EPS, ALU.mult, ALU.add)
                tk = ch_v.inc(v.drain(fusable=False)); wait(v, tk)
                v.tensor_tensor(t2[:, :], t2[:, :], t1[:, :], ALU.subtract)
                tk = ch_v.inc(v.drain(fusable=False)); wait(v, tk)
                ti = t1[:, :].bitcast(I32)
                ri = rstd[:, :].bitcast(I32)
                v.tensor_single_scalar(ti, t2[:, :].bitcast(I32), 1, ALU.logical_shift_right)
                tk = ch_v.inc(v.drain(fusable=False)); wait(v, tk)
                v.tensor_scalar(ri, ti, -1, 0x5F3759DF, ALU.mult, ALU.add)
                tk = ch_v.inc(v.drain(fusable=False)); wait(v, tk)
                for _ in range(3):
                    v.tensor_tensor(t1[:, :], rstd[:, :], rstd[:, :], ALU.mult)
                    tk = ch_v.inc(v.drain(fusable=False)); wait(v, tk)
                    v.tensor_tensor(t1[:, :], t1[:, :], t2[:, :], ALU.mult)
                    tk = ch_v.inc(v.drain(fusable=False)); wait(v, tk)
                    v.tensor_scalar(t1[:, :], t1[:, :], -0.5, 1.5, ALU.mult, ALU.add)
                    tk = ch_v.inc(v.drain(fusable=False)); wait(v, tk)
                    v.tensor_tensor(rstd[:, :], rstd[:, :], t1[:, :], ALU.mult)
                    tk = ch_v.inc(v.drain(fusable=False)); wait(v, tk)
                v.tensor_tensor(nb[:, :], nb[:, :], rstd[:, :], ALU.mult)
                tk = ch_v.inc(v.drain(fusable=False)); wait(v, tk)
                v.tensor_scalar_mul(nb[:, :], nb[:, :], -1.0)
                ch_v.inc(v.drain(fusable=False))
            V(on_v)
            stat_tok = ch_v.tok()

            def on_s(s):
                wait(s, stat_tok)
                for ap in outs:
                    for tc in range(4):
                        s.activation(ap[:, tc * 1024:(tc + 1) * 1024],
                                     x_ap[:, tc * 1024:(tc + 1) * 1024], AF.Identity,
                                     bias=nb[:, tc:tc + 1], scale=rstd[:, tc:tc + 1])
                ch_a.inc(s.drain(fusable=False))
            A(on_s)
            return ch_a.tok()

        def transpose_block(src, dst, to_fm, pe_wait):
            n_out = KC if to_fm else 4
            pe_toks = [None] * n_out
            ev_toks = [None] * n_out
            for i in range(n_out):
                bank = PT[i % 2]

                def on_t(t, i=i, bank=bank):
                    if i == 0 and pe_wait is not None:
                        wait(t, pe_wait)
                    bwait(t, 6 + i % 2)
                    if to_fm:
                        for tc in range(4):
                            li = t.transpose(bank[:, tc * 128:tc * 128 + 128],
                                             src[:, tc * 1024 + i * 128:
                                                 tc * 1024 + i * 128 + 128],
                                             ident[:, :])
                    else:
                        for kc in range(KC):
                            li = t.transpose(bank[:, kc * 128:kc * 128 + 128],
                                             src[:, kc * 512 + i * 128:
                                                 kc * 512 + i * 128 + 128],
                                             ident[:, :])
                    pe_toks[i] = ch_t.inc(li)
                T(on_t)

                def on_e(v, i=i, bank=bank):
                    wait(v, pe_toks[i])
                    if to_fm:
                        inst = v.tensor_copy(dst[:, i * 512:(i + 1) * 512], bank[:, 0:512])
                    else:
                        inst = v.tensor_copy(dst[:, i * 1024:(i + 1) * 1024], bank[:, :])
                    ev_toks[i] = ch_v.inc(inst)
                V(on_e)
                bset(6 + i % 2, ev_toks[i])
            return ev_toks[-1]

        def proj_fm(wt_ap, rhs_fm, dst, rhs_tok, w_tok, dst2=None, fn=AF.Identity,
                    fn2=AF.Identity):
            mm = [None] * KC
            ev = [None] * KC
            for oc in range(KC):
                bank = PS[oc % 2]

                def on_t(t, oc=oc, bank=bank):
                    if oc == 0:
                        wait(t, rhs_tok)
                        wait(t, w_tok)
                    bwait(t, oc % 2)
                    for kc in range(KC):
                        li = t.matmul(bank[:, :],
                                      wt_ap[:, kc * 1024 + oc * 128:
                                            kc * 1024 + oc * 128 + 128],
                                      rhs_fm[:, kc * 512:(kc + 1) * 512],
                                      start=(kc == 0), stop=(kc == KC - 1))
                    mm[oc] = ch_t.inc(li)
                T(on_t)

                def on_s(v, oc=oc, bank=bank):
                    wait(v, mm[oc])
                    li = v.tensor_copy(dst[:, oc * 512:(oc + 1) * 512], bank[:, :])
                    if dst2 is not None:
                        li = v.tensor_copy(dst2[:, oc * 512:(oc + 1) * 512], bank[:, :])
                    ev[oc] = ch_v.inc(li)
                V(on_s)
                bset(oc % 2, ev[oc])
            return ev[-1], mm[-1]

        def x_view(b):
            return x_d[b * TB:(b + 1) * TB, :].rearrange("(tc p) d -> p tc d", p=128)

        def sp_view(sp, b):
            return sp.rearrange("p (kc t) -> p kc t", kc=KC)[:, :, b * TB:(b + 1) * TB]

        def sb_kc(sb):
            return sb.rearrange("p (kc t) -> p kc t", kc=KC)

        gate_red = [None] * NBLK
        gate_mm = [None] * NBLK
        spq = [None] * NBLK
        spk = [None] * NBLK
        spv = [None] * NBLK
        xtok = [None] * NBLK

        def z_gates(v):
            v.memset(ones1[:, :], 1.0)
            v.memset(ones_bf[:, :], 1.0)
            v.memset(ones1b[:, :], 1.0)
            v.memset(g3[0:1, 3:4], 0.0)
            v.memset(baserow[:, :], 0.0)
            v.memset(baserow_bf[:, :], 0.0)
            ch_v.inc(v.drain(fusable=False))
        V(z_gates)
        gz_tok = ch_v.tok()

        ZTILE = VLD[1]
        PADP = f32a(160, 256)
        INITQ = GBQ

        def init_v(v):
            v.memset(ZTILE[:, :], 0.0)
            pv = PADP.rearrange("p (c two) -> p c two", two=2)
            v.memset(pv[:, :, 0:1], 100000.0)
            tk = ch_v.inc(v.drain(fusable=False)); wait(v, tk)
            v.memset(pv[:, :, 1:2], 0.0)
            ch_v.inc(v.drain(fusable=False))
        V(init_v)
        initv_tok = ch_v.tok()

        def init_dma(sa):
            wait(sa, initv_tok)
            for i in range(NBLK):
                INITQ.dma(sa, bufA[i * TB:(i + 1) * TB, :].rearrange(
                    "(tc p) d -> p tc d", p=128),
                    ZTILE[:, :].rearrange("p (tc d) -> p tc d", tc=4))
            for i in range(NBLK):
                INITQ.dma(sa, bufB[i * TB:(i + 1) * TB, :].rearrange(
                    "(tc p) d -> p tc d", p=128),
                    ZTILE[:, :].rearrange("p (tc d) -> p tc d", tc=4))
            INITQ.dma(sa, idxstage.rearrange("(c p) two -> p c two", p=128),
                      PADP.rearrange("p (c two) -> p c two", two=2))
        A(init_dma)
        init_tok = INITQ.tok()


        prev_guards = [None]
        for b in range(NBLK):
            def on_x(sy, b=b, gl=prev_guards[0]):
                if gl is not None:
                    for g in gl:
                        wait(sy, g)
                wait(sy, TAPQ.tok())
                xtok[b] = XQ.dma(sy, X1[:, :].rearrange("p (tc d) -> p tc d", tc=4),
                                 x_view(b))
            SY(on_x)
            ln_tok = emit_ln(X1, [LN1T], JNK1A, xtok[b])
            ln1f_tok = transpose_block(LN1T, LN1F, True, ln_tok)

            q_tok, q_mm = proj_fm(WT["wq"], LN1F, QSB, ln1f_tok,
                                  cast_toks["wqT"] if b == 0 else None)
            k_tok, _ = proj_fm(WT["wk"], LN1F, KSB, ln1f_tok,
                               cast_toks["wkT"] if b == 0 else None)

            v_mm = [None] * 8
            v_ev = [None] * 8
            for i, (tc, h) in enumerate([(tc, h) for tc in range(4) for h in range(2)]):
                bank = PS[2 + i % 2]

                def on_t(t, i=i, tc=tc, h=h, bank=bank, b=b):
                    if i == 0:
                        wait(t, ln1f_tok)
                        if b == 0:
                            wait(t, cast_toks["wvT"])
                    bwait(t, 2 + i % 2)
                    for kc in range(KC):
                        li = t.matmul(bank[:, :],
                                      LN1F[:, kc * 512 + tc * 128:kc * 512 + tc * 128 + 128],
                                      WT["wv"][:, kc * 1024 + h * 512:
                                               kc * 1024 + (h + 1) * 512],
                                      start=(kc == 0), stop=(kc == KC - 1))
                    v_mm[i] = ch_t.inc(li)
                T(on_t)

                def on_s(v, i=i, tc=tc, h=h, bank=bank):
                    wait(v, v_mm[i])
                    v_ev[i] = ch_v.inc(v.tensor_copy(
                        VSB[:, tc * 1024 + h * 512:tc * 1024 + (h + 1) * 512], bank[:, :]))
                V(on_s)
                bset(2 + i % 2, v_ev[i])
            v_tok = v_ev[-1]

            def on_gt(t, b=b):
                wait(t, ln_tok)
                if b == 0:
                    wait(t, gz_tok)
                    bwait(t, 4)
                for kc in range(KC):
                    for tc in range(4):
                        li = t.matmul(PS[4][:, kc:kc + 1],
                                      LN1T[:, tc * 1024 + kc * 128:
                                           tc * 1024 + kc * 128 + 128],
                                      ones_bf[:, :],
                                      start=(b == 0 and kc == 0 and tc == 0),
                                      stop=(b == NBLK - 1 and kc == KC - 1
                                            and tc == 3))
                gate_mm[b] = ch_t.inc(li)
            T(on_gt)

            def on_sp(sy, b=b, q_tok=q_tok, k_tok=k_tok, v_tok=v_tok):
                wait(sy, q_tok)
                spq[b] = SPQ.dma(sy, sp_view(qsp, b), sb_kc(QSB))
                wait(sy, k_tok)
                spk[b] = SPK.dma(sy, sp_view(ksp, b), sb_kc(KSB))
                wait(sy, v_tok)
                spv[b] = SPV.dma(sy, vsp[:, b * 4096:(b + 1) * 4096], VSB[:, :])
            SY(on_sp)
            prev_guards[0] = [ln1f_tok, spq[b], spk[b], spv[b], gate_mm[b]]

            if b == 0:
                tap("ln1", LN1T[:, :], ln_tok)
                tap("kfm", KSB[:, :], k_tok)
                tap("vtm", VSB[:, :], v_tok)
                tap("qfm", QSB[:, :], q_tok)

        last_1a_pe = ch_t.tok()

        def on_gf1(s):
            wait(s, gate_mm[NBLK - 1])
            s.activation(colsum_bf[:, :], PS[4][:, 0:8], AF.Identity)
            ch_a.inc(s.drain(fusable=False))
        A(on_gf1)
        colsum_tok = ch_a.tok()
        bset(4, colsum_tok)

        def on_gf2(t):
            wait(t, colsum_tok)
            wait(t, wgt_tok)
            bwait(t, 5)
            for kc in range(KC):
                li = t.matmul(PS[5][0:1, 0:3], colsum_bf[:, kc:kc + 1],
                              wgt[:, kc * 4:kc * 4 + 3],
                              start=(kc == 0), stop=(kc == KC - 1))
            ch_t.inc(li)
        T(on_gf2)
        gmv_tok = ch_t.tok()

        def on_gf3(v):
            wait(v, gmv_tok)
            v.tensor_copy(g3[0:1, 0:3], PS[5][0:1, 0:3])
            ch_v.inc(v.drain(fusable=False))
        V(on_gf3)
        gate_red[NBLK - 1] = ch_v.tok()
        bset(5, gate_red[NBLK - 1])

        def finalize_partial():
            def fin_p(g):
                for q in [WQ, TRQ, XQ, SPQ, SPK, SPV, KVQ[0], KVQ[1], Q2Q, RSQ, LNQ,
                          CCS, CCL, GBQ, EXQ[0], EXQ[1], LSQ[0], LSQ[1], ACQ[0],
                          ACQ[1], P4Q[0], P4Q[1], STQ[0], STQ[1], TAPQ]:
                    if q.n:
                        g.wait_ge(q.h, q.n)
            G(fin_p)

        if phases < 2:
            finalize_partial()
            return nc

        def gz(v):
            v.memset(G0[:, :], 0.0)
            v.memset(G1[:, :], 0.0)
            ch_v.inc(v.drain(fusable=False))
        V(gz)
        gzero_tok = ch_v.tok()

        def on_woutt(g):
            wait(g, last_1a_pe)
            cast_toks["woutT"] = CCL.dma(
                g, WOUTT[:, :].rearrange("p (c e) -> p c e", c=KC),
                w_d["woutt"].rearrange("(c p) e -> p c e", p=128))
        G(on_woutt)

        kld_t = [None] * NBLK
        vld_t = [None] * NBLK

        def kv_load(b, guard):
            def f(sy):
                if guard is not None:
                    wait(sy, guard)
                wait(sy, TAPQ.tok())
                wait(sy, init_tok)
                wait(sy, spq[NBLK - 1])
                wait(sy, spk[NBLK - 1])
                wait(sy, spv[NBLK - 1])
                kld_t[b] = KVQ[b % 2].dma(sy, KLD[b % 2][:, :], sp_view(ksp, b))
                vld_t[b] = KVQ[b % 2].dma(sy, VLD[b % 2][:, :],
                                          vsp[:, b * 4096:(b + 1) * 4096])
            SY(f)

        kv_load(0, None)
        blk_done = [None] * NBLK

        def emit_g(lhs_tm, rhs_tm, acc, lhs_tok, rhs_tok, first_guard):
            mm = {}
            ev = {}
            idx = 0
            for ec in range(KC):
                for on in range(2):
                    bank = PS[2 + idx % 2]

                    def on_t(t, ec=ec, on=on, bank=bank, idx=idx):
                        if idx == 0:
                            wait(t, lhs_tok)
                            wait(t, rhs_tok)
                        bwait(t, 2 + idx % 2)
                        for tc in range(4):
                            li = t.matmul(bank[:, :],
                                          lhs_tm[:, tc * 1024 + ec * 128:
                                                 tc * 1024 + ec * 128 + 128],
                                          rhs_tm[:, tc * 1024 + on * 512:
                                                 tc * 1024 + (on + 1) * 512],
                                          start=(tc == 0), stop=(tc == 3))
                        mm[idx] = ch_t.inc(li)
                    T(on_t)

                    def on_v(v, ec=ec, on=on, bank=bank, idx=idx):
                        wait(v, mm[idx])
                        if idx == 0 and first_guard is not None:
                            wait(v, first_guard)
                        sl = slice(ec * 1024 + on * 512, ec * 1024 + (on + 1) * 512)
                        v.scalar_tensor_tensor(acc[:, sl], bank[:, :], SCALE, acc[:, sl],
                                               ALU.mult, ALU.add)
                        ev[idx] = ch_v.inc(v.drain(fusable=False))
                    V(on_v)
                    bset(2 + idx % 2, ev[idx])
                    idx += 1
            return {"mm_done": mm[idx - 1], "acc_done": ev[idx - 1]}

        for b in range(NBLK):
            kf = KLD[b % 2]
            vt = VLD[b % 2]
            if b + 1 < NBLK:
                kv_load(b + 1, blk_done[b - 1] if b >= 1 else None)

            pre_mm = [None] * 8
            pre_ev = [None] * 8
            for i, (tc, h) in enumerate([(tc, h) for tc in range(4) for h in range(2)]):
                bank = PS[i % 2]

                def on_t(t, i=i, tc=tc, h=h, bank=bank, b=b):
                    if i == 0:
                        wait(t, kld_t[b])
                        if b == 0:
                            wait(t, cast_toks["w0T"])
                    bwait(t, i % 2)
                    for kc in range(KC):
                        li = t.matmul(bank[:, :],
                                      kf[:, kc * 512 + tc * 128:kc * 512 + tc * 128 + 128],
                                      WT["w0"][:, kc * 1024 + h * 512:
                                               kc * 1024 + (h + 1) * 512],
                                      start=(kc == 0), stop=(kc == KC - 1))
                    pre_mm[i] = ch_t.inc(li)
                T(on_t)

                def on_s(v, i=i, tc=tc, h=h, bank=bank):
                    wait(v, pre_mm[i])
                    sl = slice(tc * 1024 + h * 512, tc * 1024 + (h + 1) * 512)
                    pre_ev[i] = ch_v.inc(v.tensor_copy(PRC[:, sl], bank[:, :]))
                V(on_s)
                bset(i % 2, pre_ev[i])
            pre_tok = pre_ev[-1]

            def on_sg(sa):
                wait(sa, pre_tok)
                ch_a.inc(sa.activation(SGT[:, :], PRC[:, :], AF.Sigmoid))
            A(on_sg)
            sg_tok = ch_a.tok()

            def on_a1(v):
                wait(v, sg_tok)
                v.tensor_tensor(A1T[:, :], PRC[:, :], SGT[:, :], ALU.mult)
                ch_v.inc(v.drain(fusable=False))
            V(on_a1)
            a1_tok = ch_v.tok()
            if b == 0:
                tap("a1tm", A1T[:, :], a1_tok)

            a1f_tok = transpose_block(A1T, AF1, True, a1_tok)

            prd_mm = [None] * 8
            prd_ev = [None] * 8
            for i, (tc, h) in enumerate([(tc, h) for tc in range(4) for h in range(2)]):
                bank = PS[i % 2]

                def on_t(t, i=i, tc=tc, h=h, bank=bank, b=b):
                    if i == 0:
                        wait(t, a1f_tok)
                        if b == 0:
                            wait(t, cast_toks["w1T"])
                    bwait(t, i % 2)
                    for kc in range(KC):
                        li = t.matmul(bank[:, :],
                                      AF1[:, kc * 512 + tc * 128:kc * 512 + tc * 128 + 128],
                                      WT["w1"][:, kc * 1024 + h * 512:
                                               kc * 1024 + (h + 1) * 512],
                                      start=(kc == 0), stop=(kc == KC - 1))
                    prd_mm[i] = ch_t.inc(li)
                T(on_t)

                def on_v(v, i=i, tc=tc, h=h, bank=bank, b=b):
                    wait(v, prd_mm[i])
                    if i == 0:
                        wait(v, vld_t[b])
                    sl = slice(tc * 1024 + h * 512, tc * 1024 + (h + 1) * 512)
                    v.tensor_tensor(ETM[:, sl], bank[:, :], vt[:, sl], ALU.subtract)
                    tk = ch_v.inc(v.drain(fusable=False)); wait(v, tk)
                    v.tensor_scalar(ETM[:, sl], ETM[:, sl], ERR_CLIP, -ERR_CLIP,
                                    ALU.min, ALU.max)
                    prd_ev[i] = ch_v.inc(v.drain(fusable=False))
                V(on_v)
                bset(i % 2, prd_ev[i])
            e_tok = prd_ev[-1]
            if b == 0:
                tap("etm", ETM[:, :], e_tok)

            def on_sd(v):
                wait(v, a1_tok)
                v.tensor_scalar(PRC[:, :], SGT[:, :], -1.0, 1.0, ALU.mult, ALU.add)
                tk = ch_v.inc(v.drain(fusable=False)); wait(v, tk)
                v.tensor_tensor(PRC[:, :], A1T[:, :], PRC[:, :], ALU.mult)
                tk = ch_v.inc(v.drain(fusable=False)); wait(v, tk)
                v.tensor_tensor(SGT[:, :], SGT[:, :], PRC[:, :], ALU.add)
                ch_v.inc(v.drain(fusable=False))
            V(on_sd)
            sd_tok = ch_v.tok()

            ef_tok = transpose_block(ETM, AF1, True, e_tok)

            d1_mm = [None] * 8
            d1_ev = [None] * 8
            for i, (tc, h) in enumerate([(tc, h) for tc in range(4) for h in range(2)]):
                bank = PS[i % 2]

                def on_t(t, i=i, tc=tc, h=h, bank=bank, b=b):
                    if i == 0:
                        wait(t, ef_tok)
                        if b == 0:
                            wait(t, cast_toks["w1p"])
                    bwait(t, i % 2)
                    for kc in range(KC):
                        li = t.matmul(bank[:, :],
                                      AF1[:, kc * 512 + tc * 128:kc * 512 + tc * 128 + 128],
                                      WT["w1p"][:, kc * 1024 + h * 512:
                                               kc * 1024 + (h + 1) * 512],
                                      start=(kc == 0), stop=(kc == KC - 1))
                    d1_mm[i] = ch_t.inc(li)
                T(on_t)

                def on_v(v, i=i, tc=tc, h=h, bank=bank):
                    wait(v, d1_mm[i])
                    if i == 0:
                        wait(v, sd_tok)
                    sl = slice(tc * 1024 + h * 512, tc * 1024 + (h + 1) * 512)
                    v.tensor_tensor(PRC[:, sl], bank[:, :], SGT[:, sl], ALU.mult)
                    d1_ev[i] = ch_v.inc(v.drain(fusable=False))
                V(on_v)
                bset(i % 2, d1_ev[i])
            d1_tok = d1_ev[-1]
            if b == 0:
                tap("d1tm", PRC[:, :], d1_tok)

            def ktm_guard(t):
                wait(t, TAPQ.tok())
            T(ktm_guard)
            ktm_tok = transpose_block(kf, SGT, False, d1_tok)

            g0r = emit_g(SGT, PRC, G0, ktm_tok, d1_tok, gzero_tok if b == 0 else None)

            g1r = emit_g(A1T, ETM, G1, a1_tok, e_tok, None)
            blk_done[b] = g1r["acc_done"]

        g0_fin = g0r["acc_done"]
        g1_fin = g1r["acc_done"]
        gates_fin = gate_red[NBLK - 1]
        cc_tok = {}

        def r_stage0(sy):
            wait(sy, gates_fin)
            cc_tok["ga"] = CCS.dma(sy, ccg_in[0:1, 0:4], g3[0:1, 0:4])
        SY(r_stage0)

        def r_cc0(g):
            wait(g, cc_tok["ga"])
            g.collective_compute(
                "AllReduce", ALU.add,
                replica_groups=[[0, 1, 2, 3, 4, 5, 6, 7]],
                ins=[ccg_in[0:1, :]], outs=[ccg_out[0:1, :]],
            ).then_inc(ch_cc.h, 1)
            ch_cc.n += 1
            g.wait_ge(ch_cc.h, ch_cc.n)
            cc_tok["gar"] = CCL.dma(g, g3[0:1, 0:3], ccg_out[0:1, 0:3])
        G(r_cc0)
        tap("g0", G0[:, :], g0_fin)
        tap("g1", G1[:, :], g1_fin)
        tap("gacc", g3[0:1, 0:4], gates_fin)

        if phases < 3:
            finalize_partial()
            return nc

        def r_stage(sy):
            wait(sy, g0_fin)
            cc_tok["g0"] = CCS.dma(sy, ccin[:, :], G0[:, :])
        SY(r_stage)

        def r_stage2(sy):
            wait(sy, g1_fin)
            cc_tok["g1"] = LNQ.dma(sy, ccin2[:, :], G1[:, :])
        SY(r_stage2)

        def r_cc(g):
            wait(g, cc_tok["g0"])
            g.collective_compute(
                "AllReduce", ALU.add,
                replica_groups=[[0, 1, 2, 3, 4, 5, 6, 7]],
                ins=[ccin[:, :]], outs=[ccout[:, :]],
            ).then_inc(ch_cc.h, 1)
            ch_cc.n += 1
            wait(g, cc_tok["g1"])
            g.collective_compute(
                "AllReduce", ALU.add,
                replica_groups=[[0, 1, 2, 3, 4, 5, 6, 7]],
                ins=[ccin2[:, :]], outs=[ccout2[:, :]],
            ).then_inc(ch_cc.h, 1)
            ch_cc.n += 1
            g.wait_ge(ch_cc.h, ch_cc.n - 1)
            for c in range(4):
                cc_tok["g0r%d" % c] = CCL.dma(
                    g, G0[:, c * 2048:(c + 1) * 2048], ccout[:, c * 2048:(c + 1) * 2048])
            g.wait_ge(ch_cc.h, ch_cc.n)
            for c in range(4):
                cc_tok["g1r%d" % c] = CCL.dma(
                    g, G1[:, c * 2048:(c + 1) * 2048], ccout2[:, c * 2048:(c + 1) * 2048])
        G(r_cc)

        def r_gv(v):
            wait(v, cc_tok["gar"])
            v.tensor_scalar(g3[0:1, 8:11], g3[0:1, 0:3], 1.0 / 32768.0, None, ALU.mult)
            tk = ch_v.inc(v.drain(fusable=False)); wait(v, tk)
            v.tensor_tensor(g3[0:1, 8:11], g3[0:1, 8:11], g3[0:1, 4:7], ALU.add)
            ch_v.inc(v.drain(fusable=False))
        V(r_gv)
        gate_m_tok = ch_v.tok()

        def r_gs(s):
            wait(s, gate_m_tok)
            s.activation(g3[0:1, 0:3], g3[0:1, 8:11], AF.Sigmoid)
            ch_a.inc(s.drain(fusable=False))
        A(r_gs)
        gate_sig_tok = ch_a.tok()
        tap("gsig", g3[0:1, 0:4], gate_sig_tok)

        def r_gv2(v):
            wait(v, gate_sig_tok)
            v.tensor_scalar(g3[0:1, 12:13], g3[0:1, 0:1], -1.0, 1.0, ALU.mult, ALU.add)
            tk = ch_v.inc(v.drain(fusable=False)); wait(v, tk)
            v.tensor_scalar(g3[0:1, 13:14], g3[0:1, 1:2], -1.0, None, ALU.mult)
            ch_v.inc(v.drain(fusable=False))
        V(r_gv2)
        gate_c_tok = ch_v.tok()

        def r_bc(t):
            wait(t, gate_c_tok)
            wait(t, gz_tok)
            bwait(t, 5)
            li = t.matmul(PS[5][:, 0:2], ones1[:, :], g3[0:1, 12:14],
                          start=True, stop=True)
            ch_t.inc(li)
        T(r_bc)
        bc_tok = ch_t.tok()

        def r_bce(v):
            wait(v, bc_tok)
            v.tensor_copy(small[:, 48:50], PS[5][:, 0:2])
            ch_v.inc(v.drain(fusable=False))
        V(r_bce)
        coef_tok = ch_v.tok()
        bset(5, coef_tok)
        C_OMA = small[:, 48:49]
        C_NLR = small[:, 49:50]

        upd_toks = []
        for gsrc, wsrc, wdst, grt, sfin in [
                (G0, WT["w0"], W0N, "g0r", [blk_done[NBLK - 1], spq[NBLK - 1],
                                            spk[NBLK - 1]]),
                (G1, WT["w1"], W1N, "g1r", [blk_done[NBLK - 1], spv[NBLK - 1]])]:
            def on_us(s, wsrc=wsrc, wdst=wdst, sfin=sfin):
                wait(s, coef_tok)
                for tk in sfin:
                    wait(s, tk)
                s.activation(wdst[:, 0:4096], wsrc[:, 0:4096], AF.Identity, scale=C_OMA)
                s.activation(wdst[:, 4096:8192], wsrc[:, 4096:8192], AF.Identity, scale=C_OMA)
                ch_a.inc(s.drain(fusable=False))
            A(on_us)
            sc_tok = ch_a.tok()

            def on_uv(v, gsrc=gsrc, wdst=wdst, sc_tok=sc_tok, grt=grt):
                wait(v, sc_tok)
                for c in range(4):
                    wait(v, cc_tok[grt + str(c)])
                    sl = slice(c * 2048, (c + 1) * 2048)
                    v.tensor_scalar(gsrc[:, sl], gsrc[:, sl], GRAD_CLIP, -GRAD_CLIP,
                                    ALU.min, ALU.max)
                    tk = ch_v.inc(v.drain(fusable=False)); wait(v, tk)
                    v.scalar_tensor_tensor(wdst[:, sl], gsrc[:, sl], C_NLR, wdst[:, sl],
                                           ALU.mult, ALU.add)
                    tk = ch_v.inc(v.drain(fusable=False)); wait(v, tk)
                upd_toks.append(tk)
            V(on_uv)
        w0n_tok, w1n_tok = upd_toks
        tap("w0n", W0N[:, :], w0n_tok)

        if phases < 4:
            finalize_partial()
            return nc

        q2t = [None] * NBLK
        x2t = [None] * NBLK
        res_sp = [None] * NBLK
        ln2_sp = [None] * NBLK
        logit_t = [None] * NBLK
        u_last = [None] * NBLK
        res_last = [None] * NBLK
        ln2_toks = [None] * NBLK
        ln2tm_sp = [None] * NBLK

        HUGE = 1.0e6
        SCQ = DmaQ(ctx, nc, "SCQ")
        bnd_regs = {}

        def setup_bnd(g):
            r1 = g.alloc_register("bnd_idx")
            g.reg_mov(r1, NEXP * 512 - 1)
            r2 = g.alloc_register("bnd_tok")
            g.reg_mov(r2, TC - 1)
            bnd_regs["idx"] = r1
            bnd_regs["tok"] = r2
        G(setup_bnd)

        JF2 = f32a(24, 1024)
        MKB = bfa(28, 256)
        routeA_fin = [None] * NBLK
        routeB_fin = [None]
        rk_base_tok = [None, None]
        sc_last = [None]
        scq_slot = [None] * 4
        PAIRJ = [640, 656, 672, 688]

        def v3(ap):
            return ap.rearrange("p (c e) -> p c e", e=32)

        def bc3(ap):
            return ap.unsqueeze(2).broadcast_to([128, 4, 32])

        def route_soft(b):
            sA = small[:, 40:44]
            sR = small[:, 44:48]
            sM2 = small[:, 52:56]
            E = JF2[:, 0:128]
            M1 = JF2[:, 128:256]
            T = JF2[:, 256:384]
            M2 = JF2[:, 384:512]
            W = WMA[:, b * 128:(b + 1) * 128]
            L = LOGI[:, b * 128:(b + 1) * 128]
            neg_tok = [None]
            exp_tok = [None]

            def on_a(v, b=b):
                wait(v, logit_t[b])
                if b == 0:
                    wait(v, cast_toks["iota32"])
                    wait(v, cast_toks["iotac"])
                    wait(v, cast_toks["tokc4"])
                else:
                    wait(v, routeB_fin[0])
                v.tensor_reduce(sA, v3(L), AX, ALU.max)
                tk = ch_v.inc(v.drain(fusable=False)); wait(v, tk)
                v.tensor_scalar_mul(sA, sA, -1.0)
                tk = ch_v.inc(v.drain(fusable=False)); wait(v, tk)
                v.tensor_scalar(small[:, 60:64], tokc4[:, :], 1.0, float(b * 512),
                                ALU.mult, ALU.add)
                neg_tok[0] = ch_v.inc(v.drain(fusable=False))
            V(on_a)

            def on_exp(sa, b=b):
                wait(sa, neg_tok[0])
                for c in range(4):
                    sa.activation(JF2[:, c * 32:(c + 1) * 32],
                                  LOGI[:, (b * 4 + c) * 32:(b * 4 + c + 1) * 32],
                                  AF.Exp, bias=small[:, 40 + c:41 + c], scale=1.0)
                exp_tok[0] = ch_a.inc(sa.drain(fusable=False))
            A(on_exp)

            def on_sm(v, b=b):
                wait(v, exp_tok[0])
                v.tensor_reduce(sR, v3(E), AX, ALU.add)
                tk = ch_v.inc(v.drain(fusable=False)); wait(v, tk)
                v.reciprocal(sR, sR)
                tk = ch_v.inc(v.drain(fusable=False)); wait(v, tk)
                v.tensor_tensor(v3(W), v3(E), bc3(sR), ALU.mult)
                tk = ch_v.inc(v.drain(fusable=False)); wait(v, tk)
                v.tensor_reduce(sA, v3(W), AX, ALU.max)
                tk = ch_v.inc(v.drain(fusable=False)); wait(v, tk)
                v.tensor_tensor(v3(M1), v3(W), bc3(sA), ALU.is_ge)
                tk = ch_v.inc(v.drain(fusable=False)); wait(v, tk)
                v.tensor_tensor(v3(T), v3(W), bc3(sA), ALU.is_lt)
                tk = ch_v.inc(v.drain(fusable=False)); wait(v, tk)
                v.tensor_tensor(T, T, W, ALU.mult)
                tk = ch_v.inc(v.drain(fusable=False)); wait(v, tk)
                v.tensor_reduce(sM2, v3(T), AX, ALU.max)
                tk = ch_v.inc(v.drain(fusable=False)); wait(v, tk)
                v.tensor_tensor(v3(T), v3(W), bc3(sM2), ALU.is_ge)
                tk = ch_v.inc(v.drain(fusable=False)); wait(v, tk)
                v.tensor_tensor(M2, T, M1, ALU.subtract)
                tk = ch_v.inc(v.drain(fusable=False)); wait(v, tk)
                v.tensor_tensor(W, W, T, ALU.mult)
                routeA_fin[b] = ch_v.inc(v.drain(fusable=False))
            V(on_sm)

        def route_ranks(b):
            for r in range(2):
                mcol = JF2[:, 128:256] if r == 0 else JF2[:, 384:512]
                mk = MKB[:, r * 128:(r + 1) * 128]
                mb_tok = [None]
                cn_tok = [None]
                cv_tok = [None]
                br_tok = [None]
                rk_tok = [None]
                sl_tok = [None]

                def on_mb(v, b=b, r=r, mcol=mcol, mk=mk):
                    wait(v, routeA_fin[b])
                    wait(v, rk_base_tok[r])
                    mb_tok[0] = ch_v.inc(v.tensor_copy(mk, mcol))
                V(on_mb)

                def on_mm1(t, b=b, r=r, mk=mk):
                    wait(t, mb_tok[0])
                    if b == 0 and r == 0:
                        wait(t, cast_toks["lt"])
                        wait(t, gz_tok)
                    bwait(t, 4 + r)
                    cn_tok[0] = ch_t.inc(t.matmul(
                        PS[4 + r][0:1, 0:128], ones_bf[:, :], mk,
                        start=True, stop=True))
                    bwait(t, 2 + r)
                    t.matmul(PS[2 + r][:, 0:128], lt128[:, :], mk,
                             start=True, stop=False)
                T(on_mm1)

                def on_cv(s, r=r):
                    wait(s, cn_tok[0])
                    s.copy(JF2[0:1, 712:840], PS[4 + r][0:1, 0:128])
                    cv_tok[0] = ch_a.inc(s.drain(fusable=False))
                A(on_cv)
                bset(4 + r, cv_tok[0])

                def on_br(v, r=r):
                    wait(v, cv_tok[0])
                    CNT = JF2[0:1, 712:840]
                    BR4 = JF2[0:1, 840:968]
                    bro = baserow[0:1, r * 32:(r + 1) * 32]
                    v.tensor_copy(BR4[0:1, 0:32], bro)
                    tk = ch_v.inc(v.drain(fusable=False)); wait(v, tk)
                    v.tensor_tensor(BR4[0:1, 32:64], BR4[0:1, 0:32],
                                    CNT[0:1, 0:32], ALU.add)
                    tk = ch_v.inc(v.drain(fusable=False)); wait(v, tk)
                    v.tensor_tensor(BR4[0:1, 64:96], BR4[0:1, 32:64],
                                    CNT[0:1, 32:64], ALU.add)
                    tk = ch_v.inc(v.drain(fusable=False)); wait(v, tk)
                    v.tensor_tensor(BR4[0:1, 96:128], BR4[0:1, 64:96],
                                    CNT[0:1, 64:96], ALU.add)
                    tk = ch_v.inc(v.drain(fusable=False)); wait(v, tk)
                    v.tensor_tensor(bro, BR4[0:1, 96:128],
                                    CNT[0:1, 96:128], ALU.add)
                    tk = ch_v.inc(v.drain(fusable=False)); wait(v, tk)
                    v.tensor_copy(br4b[0:1, r * 128:(r + 1) * 128], BR4[0:1, :])
                    br_tok[0] = ch_v.inc(v.drain(fusable=False))
                V(on_br)

                def on_mm2(t, r=r):
                    wait(t, br_tok[0])
                    rk_tok[0] = ch_t.inc(t.matmul(
                        PS[2 + r][:, 0:128], ones1b[:, :],
                        br4b[0:1, r * 128:(r + 1) * 128], start=False, stop=True))
                    rk_base_tok[r] = rk_tok[0]
                T(on_mm2)

                def on_sl(v, b=b, r=r, mcol=mcol):
                    wait(v, rk_tok[0])
                    SL = JF2[:, 512:640]
                    T3 = JF2[:, 256:384]
                    v.tensor_copy(SL, PS[2 + r][:, 0:128])
                    tk = ch_v.inc(v.drain(fusable=False)); wait(v, tk)
                    v.tensor_scalar(T3, SL, 255.5, None, ALU.is_gt)
                    tk = ch_v.inc(v.drain(fusable=False)); wait(v, tk)
                    v.scalar_tensor_tensor(SL, T3, HUGE, SL, ALU.mult, ALU.add)
                    tk = ch_v.inc(v.drain(fusable=False)); wait(v, tk)
                    v.tensor_scalar(T3, mcol, -1.0, 1.0, ALU.mult, ALU.add)
                    tk = ch_v.inc(v.drain(fusable=False)); wait(v, tk)
                    v.scalar_tensor_tensor(SL, T3, HUGE, SL, ALU.mult, ALU.add)
                    tk = ch_v.inc(v.drain(fusable=False)); wait(v, tk)
                    v.tensor_reduce(small[:, 52:56], v3(SL), AX, ALU.min)
                    tk = ch_v.inc(v.drain(fusable=False)); wait(v, tk)
                    v.tensor_tensor(v3(T3), v3(mcol),
                                    iota32[:, :].unsqueeze(1).broadcast_to(
                                        [128, 4, 32]), ALU.mult)
                    tk = ch_v.inc(v.drain(fusable=False)); wait(v, tk)
                    v.tensor_reduce(small[:, 56:60], v3(T3), AX, ALU.add)
                    tk = ch_v.inc(v.drain(fusable=False)); wait(v, tk)
                    v.tensor_tensor(T3, WMA[:, b * 128:(b + 1) * 128], mcol,
                                    ALU.mult)
                    tk = ch_v.inc(v.drain(fusable=False)); wait(v, tk)
                    v.tensor_reduce(small[:, 48:52], v3(T3), AX, ALU.add)
                    tk = ch_v.inc(v.drain(fusable=False)); wait(v, tk)
                    SLG = JF2[:, 968:972]
                    v.scalar_tensor_tensor(SLG, small[:, 56:60], 512.0,
                                           small[:, 52:56], ALU.mult, ALU.add)
                    tk = ch_v.inc(v.drain(fusable=False)); wait(v, tk)
                    if r:
                        v.tensor_scalar_add(SLG, SLG, 256.0)
                        tk = ch_v.inc(v.drain(fusable=False)); wait(v, tk)
                    sidx = (b * 2 + r) % 4
                    pc = PAIRJ[sidx]
                    wait(v, scq_slot[sidx])
                    P3 = JF2[:, pc:pc + 8].rearrange("p (c two) -> p c two", two=2)
                    v.tensor_copy(P3[:, :, 0:1], small[:, 60:64].unsqueeze(2))
                    tk = ch_v.inc(v.drain(fusable=False)); wait(v, tk)
                    v.tensor_copy(P3[:, :, 1:2], small[:, 48:52].unsqueeze(2))
                    tk = ch_v.inc(v.drain(fusable=False)); wait(v, tk)
                    v.tensor_copy(JF2[:, pc + 8:pc + 12].bitcast(mybir.dt.uint32),
                                  SLG)
                    sl_tok[0] = ch_v.inc(v.drain(fusable=False))
                    routeB_fin[0] = sl_tok[0]
                V(on_sl)
                bset(2 + r, sl_tok[0])

                def on_sc(g, b=b, r=r):
                    wait(g, sl_tok[0])
                    sidx = (b * 2 + r) % 4
                    pc = PAIRJ[sidx]
                    for c in range(4):
                        sc_last[0] = SCQ.dma_indirect_scatter(
                            g, idxstage,
                            JF2[:, pc + 8 + c:pc + 9 + c].bitcast(mybir.dt.uint32),
                            JF2[:, pc + c * 2:pc + c * 2 + 2], bnd_regs["idx"])
                    scq_slot[sidx] = sc_last[0]
                G(on_sc)

        for b in range(NBLK):
            def on_q2(sy, b=b):
                if b == 0:
                    wait(sy, blk_done[NBLK - 1])
                else:
                    wait(sy, u_last[b - 1])
                wait(sy, spq[b])
                wait(sy, TAPQ.tok())
                q2t[b] = Q2Q.dma(sy, Q2[b % 2][:, :], sp_view(qsp, b))
            SY(on_q2)

            u_mm = [None] * KC
            u_ev = [None] * KC
            for oc in range(KC):
                bank = PS[oc % 2]

                def on_t(t, oc=oc, bank=bank, b=b):
                    if oc == 0:
                        wait(t, q2t[b])
                        if b == 0:
                            wait(t, w0n_tok)
                    bwait(t, oc % 2)
                    for kc in range(KC):
                        li = t.matmul(bank[:, :],
                                      W0N[:, kc * 1024 + oc * 128:kc * 1024 + oc * 128 + 128],
                                      Q2[b % 2][:, kc * 512:(kc + 1) * 512],
                                      start=(kc == 0), stop=(kc == KC - 1))
                    u_mm[oc] = ch_t.inc(li)
                T(on_t)

                def on_s(v, oc=oc, bank=bank):
                    wait(v, u_mm[oc])
                    u_ev[oc] = ch_v.inc(v.tensor_copy(
                        U2[:, oc * 512:(oc + 1) * 512], bank[:, :]))
                V(on_s)
                bset(oc % 2, u_ev[oc])
            u_last[b] = u_mm[-1]

            def on_sg2(sa, dep=u_ev[-1]):
                wait(sa, dep)
                ch_a.inc(sa.activation(SG2[:, :], U2[:, :], AF.Sigmoid))
            A(on_sg2)
            sg2_tok = ch_a.tok()

            def on_u(v):
                wait(v, sg2_tok)
                v.tensor_tensor(U2[:, :], U2[:, :], SG2[:, :], ALU.mult)
                ch_v.inc(v.drain(fusable=False))
            V(on_u)
            u_tok = ch_v.tok()

            if phases >= 5 and b > 0:
                route_ranks(b - 1)

            def on_x2(sa, b=b):
                if b == 0:
                    wait(sa, w0n_tok)
                else:
                    wait(sa, res_last[b - 1])
                x2t[b] = GBQ.dma(sa, X2[:, :].rearrange("p (tc d) -> p tc d", tc=4),
                                 x_view(b))
            A(on_x2)

            h2_tok, _ = proj_fm(W1N, U2, H2, u_tok, w1n_tok if b == 0 else None)

            at_mm = [None] * 8
            at_ev = [None] * 8
            for i, (tc, dn) in enumerate([(tc, dn) for tc in range(4) for dn in range(2)]):
                bank = PS[2 + i % 2]

                def on_t(t, i=i, tc=tc, dn=dn, bank=bank, b=b):
                    if i == 0:
                        wait(t, h2_tok)
                        if b == 0:
                            wait(t, cast_toks["woutT"])
                    bwait(t, 2 + i % 2)
                    for kc in range(KC):
                        li = t.matmul(bank[:, :],
                                      H2[:, kc * 512 + tc * 128:kc * 512 + tc * 128 + 128],
                                      WOUTT[:, kc * 1024 + dn * 512:
                                            kc * 1024 + (dn + 1) * 512],
                                      start=(kc == 0), stop=(kc == KC - 1))
                    at_mm[i] = ch_t.inc(li)
                T(on_t)

                def on_v(v, i=i, tc=tc, dn=dn, bank=bank, b=b):
                    wait(v, at_mm[i])
                    if i == 0:
                        wait(v, x2t[b])
                        if b > 0:
                            wait(v, ln2_toks[b - 1])
                            wait(v, res_sp[b - 1])
                    sl = slice(tc * 1024 + dn * 512, tc * 1024 + (dn + 1) * 512)
                    v.tensor_tensor(RES[:, sl], bank[:, :], X2[:, sl], ALU.add)
                    at_ev[i] = ch_v.inc(v.drain(fusable=False))
                V(on_v)
                bset(2 + i % 2, at_ev[i])
            res_tok = at_ev[-1]
            res_last[b] = res_tok

            def on_rsp(sy, b=b, res_tok=res_tok):
                wait(sy, res_tok)
                res_sp[b] = RSQ.dma(sy, residsp[:, b * 4096:(b + 1) * 4096], RES[:, :])
            SY(on_rsp)

            ln2_guard = [res_tok]
            if b > 0:
                ln2_guard.append(logit_t[b - 1])
                ln2_guard.append(ln2_sp[b - 1])
            ln2_tok = emit_ln(RES, [HIT, LN2F32], JNK2, _multi(ln2_guard))
            ln2_toks[b] = ln2_tok

            def on_lo(v, ln2_tok=ln2_tok):
                wait(v, ln2_tok)
                v.tensor_tensor(LOT[:, :], LN2F32[:, :], HIT[:, :], ALU.subtract)
                ch_v.inc(v.drain(fusable=False))
            V(on_lo)
            lo_tok = ch_v.tok()

            hif_tok = transpose_block(HIT, HIF, True, ln2_tok)
            lof_tok = transpose_block(LOT, LOF, True, lo_tok)

            def on_lsp(sy, b=b, hif_tok=hif_tok, ln2_tok=ln2_tok):
                wait(sy, ln2_tok)
                ln2tm_sp[b] = LNQ.dma(
                    sy, ln2tmsp[b * TB:(b + 1) * TB, :].rearrange("(tc p) d -> p tc d", p=128),
                    HIT[:, :].rearrange("p (tc d) -> p tc d", tc=4))
                ln2_sp[b] = ln2tm_sp[b]
            SY(on_lsp)

            lg_mm = [None] * 4
            lg_ev = [None] * 4
            for tc in range(4):
                bank = PS[4 + tc % 2]

                def on_t(t, tc=tc, bank=bank, b=b):
                    if tc == 0:
                        wait(t, hif_tok)
                        wait(t, lof_tok)
                        if b == 0:
                            wait(t, rt_tok)
                    bwait(t, 4 + tc % 2)
                    for kc in range(KC):
                        sl = slice(kc * 512 + tc * 128, kc * 512 + tc * 128 + 128)
                        t.matmul(bank[:, 0:32], HIF[:, sl], rthi[:, kc * 32:(kc + 1) * 32],
                                 start=(kc == 0), stop=False)
                        t.matmul(bank[:, 32:64], HIF[:, sl], rtlo[:, kc * 32:(kc + 1) * 32],
                                 start=False, stop=False)
                        li = t.matmul(bank[:, 0:32], LOF[:, sl],
                                      rthi[:, kc * 32:(kc + 1) * 32],
                                      start=False, stop=(kc == KC - 1))
                    lg_mm[tc] = ch_t.inc(li)
                T(on_t)

                lg_cp = [None]

                def on_sc(s, tc=tc, bank=bank, b=b):
                    wait(s, lg_mm[tc])
                    cc = b * 4 + tc
                    s.copy(LOGI[:, cc * 32:(cc + 1) * 32], bank[:, 0:32])
                    lg_cp[0] = ch_a.inc(s.drain(fusable=False))
                A(on_sc)

                def on_v(v, tc=tc, bank=bank, b=b):
                    wait(v, lg_cp[0])
                    cc = b * 4 + tc
                    v.tensor_tensor(LOGI[:, cc * 32:(cc + 1) * 32],
                                    LOGI[:, cc * 32:(cc + 1) * 32],
                                    bank[:, 32:64], ALU.add)
                    lg_ev[tc] = ch_v.inc(v.drain(fusable=False))
                V(on_v)
                bset(4 + tc % 2, lg_ev[tc])
            logit_t[b] = lg_ev[-1]

            if phases >= 5:
                route_soft(b)

            if b == 0:
                tap("u2", U2[:, :], u_tok)
                tap("h2", H2[:, :], h2_tok)
                tap("resid", RES[:, :], res_tok)
                tap("hit", HIT[:, :], ln2_tok)

        tap("logits", LOGI[:, :], logit_t[NBLK - 1])

        if phases < 5:
            finalize_partial()
            return nc

        route_ranks(NBLK - 1)
        tap("wma", WMA[:, :], routeA_fin[NBLK - 1])
        bookkeeping_fin = sc_last[0]

        if phases < 6:
            finalize_partial()
            return nc

        EXS = [bfa(0, KC * D), bfa(16, KC * D), bfa(32, KC * D)]
        ex_t = [None] * NEXP
        IQ = [LSQ[0], LSQ[1]]
        GQ = [ACQ[0], ACQ[1]]

        def exs_load(g, ei, guard=None):
            if guard is not None:
                wait(g, guard)
            ex_t[ei] = EXQ[ei % 2].dma(
                g, EXS[ei % 3][:, :].rearrange("p (c o) -> p c o", c=KC),
                experts_d[ei * D:(ei + 1) * D, :].rearrange("(c p) o -> p c o", p=128))

        ROT = 4
        IDXS4 = [f32a(96, 256), f32a(97, 256), f32a(98, 256), f32a(99, 256)]
        IDXU4 = [f32a(100, 256), f32a(101, 256), f32a(102, 256), f32a(103, 256)]
        GXT4 = [bfa(104, 4096), bfa(112, 4096), bfa(120, 4096), bfa(128, 4096)]
        GXF2 = [bfa(64, 4096), bfa(72, 4096)]
        YSB2 = [bfa(80, 4096), bfa(88, 4096)]
        SQC = [DmaQ(ctx, nc, "SQC0"), DmaQ(ctx, nc, "SQC1")]
        idx_t = [None] * ROT
        idxu_t = [None] * ROT
        gxt_t = [None] * ROT
        tr_fin = {}
        mm_fin = {}
        ev_fin = {}
        sc_fin = {}

        def S0(e):
            r4 = e % ROT

            def on_idx(sy, e=e, r4=r4):
                if e < 2:
                    wait(sy, bookkeeping_fin)
                    wait(sy, SCQ.tok())
                    for b in range(NBLK):
                        wait(sy, ln2tm_sp[b])
                    wait(sy, res_sp[NBLK - 1])
                if e - ROT >= 0:
                    wait(sy, sc_fin[e - ROT])
                idx_t[r4] = IQ[e % 2].dma(
                    sy, IDXS4[r4][:, 0:8].rearrange("p (c two) -> p c two", two=2),
                    idxstage[e * 512:(e + 1) * 512, :].rearrange(
                        "(c p) two -> p c two", p=128))
            SY(on_idx)

            def on_cast(v, e=e, r4=r4):
                wait(v, idx_t[r4])
                idxu_t[r4] = ch_v.inc(v.tensor_copy(
                    IDXU4[r4][:, 0:8].bitcast(mybir.dt.uint32), IDXS4[r4][:, 0:8]))
            V(on_cast)

        def S2(e):
            r4 = e % ROT

            def on_gather(g, e=e, r4=r4):
                wait(g, idxu_t[r4])
                if e - ROT + 1 > 0 and (e - ROT) in tr_fin:
                    wait(g, tr_fin[e - ROT])
                for c in range(4):
                    gxt_t[r4] = GQ[e % 2].dma_indirect_gather(
                        g, GXT4[r4][:, c * 1024:(c + 1) * 1024], ln2tmsp,
                        IDXU4[r4][:, 0:8].bitcast(mybir.dt.uint32).rearrange(
                            "p (c two) -> p c two", two=2)[:, c:c + 1, 0:1],
                        bnd_regs["tok"])
            G(on_gather)

        def S345(e):
            r4 = e % ROT
            s2 = e % 2
            tr_toks = [None] * 8
            for i in range(8):
                bank = PT[i % 2]

                def on_tr(t, i=i, r4=r4, s2=s2, e=e, bank=bank):
                    if i == 0:
                        wait(t, gxt_t[r4])
                        if (e - 2) in mm_fin:
                            wait(t, mm_fin[e - 2])
                    bwait(t, 6 + i % 2)
                    for tc in range(4):
                        li = t.transpose(bank[:, tc * 128:tc * 128 + 128],
                                         GXT4[r4][:, tc * 1024 + i * 128:
                                                  tc * 1024 + i * 128 + 128],
                                         ident[:, :])
                    tr_toks[i] = ch_t.inc(li)
                T(on_tr)

                def on_tre(v, i=i, s2=s2, e=e, bank=bank):
                    wait(v, tr_toks[i])
                    tr_fin[e] = ch_v.inc(v.tensor_copy(
                        GXF2[s2][:, i * 512:(i + 1) * 512], bank[:, 0:512]))
                V(on_tre)
                bset(6 + i % 2, tr_fin[e])

            mm_toks = [None] * 8
            ev_toks = [None] * 8
            for gi, (ts, on) in enumerate([(ts, on) for ts in range(4) for on in range(2)]):
                bank = PS[gi % 6]

                def on_t(t, gi=gi, ts=ts, on=on, bank=bank, s2=s2, e=e):
                    if gi == 0:
                        wait(t, tr_fin[e])
                        wait(t, ex_t[e])
                        if (e - 2) in ev_fin:
                            wait(t, ev_fin[e - 2])
                    bwait(t, gi % 6)
                    for kc in range(KC):
                        li = t.matmul(bank[:, :],
                                      GXF2[s2][:, kc * 512 + ts * 128:
                                               kc * 512 + ts * 128 + 128],
                                      EXS[e % 3][:, kc * 1024 + on * 512:
                                                 kc * 1024 + (on + 1) * 512],
                                      start=(kc == 0), stop=(kc == KC - 1))
                    mm_toks[gi] = ch_t.inc(li)
                T(on_t)

                def on_ev(v, gi=gi, ts=ts, on=on, bank=bank, s2=s2, e=e, r4=r4):
                    wait(v, mm_toks[gi])
                    if gi == 0 and (e - 2) in sc_fin:
                        wait(v, sc_fin[e - 2])
                    wcol = IDXS4[r4][:, 0:8].rearrange("p (c two) -> p c two", two=2)[
                        :, ts:ts + 1, 1:2]
                    ev_toks[gi] = ch_v.inc(v.tensor_scalar(
                        YSB2[s2][:, ts * 1024 + on * 512:ts * 1024 + (on + 1) * 512],
                        bank[:, :], wcol, None, ALU.mult))
                V(on_ev)
                bset(gi % 6, ev_toks[gi])
            mm_fin[e] = mm_toks[-1]
            ev_fin[e] = ev_toks[-1]

            def on_scat(g, e=e, s2=s2, r4=r4):
                wait(g, ev_fin[e])
                for c in range(4):
                    dstbuf = bufA if c < 2 else bufB
                    sc_fin[e] = SQC[s2].dma_indirect_scatter(
                        g, dstbuf,
                        IDXU4[r4][:, 0:8].bitcast(mybir.dt.uint32).rearrange(
                            "p (c two) -> p c two", two=2)[:, c:c + 1, 0:1],
                        YSB2[s2][:, c * 1024:(c + 1) * 1024], bnd_regs["tok"])
            G(on_scat)

        def ph3_pre(g):
            wait(g, logit_t[NBLK - 1])
            wait(g, bookkeeping_fin)
            wait(g, SCQ.tok())
            exs_load(g, 0)
            exs_load(g, 1)
        G(ph3_pre)

        S0(0)
        S0(1)
        S2(0)
        for e in range(NEXP):
            if e + 2 < NEXP:
                S0(e + 2)
            if e + 1 < NEXP:
                S2(e + 1)
            S345(e)
            if e + 2 < NEXP:
                def on_pf2(g, e=e):
                    guard = mm_fin.get(e - 1)
                    if guard is not None:
                        wait(g, guard)
                    exs_load(g, e + 2)
                G(on_pf2)

        moe_fin = [sc_fin[NEXP - 2], sc_fin[NEXP - 1]]

        if phases < 7:
            finalize_partial()
            return nc

        st_toks = [None] * NBLK
        p4_guard = [None, None]
        AB4 = [bfa(100, 4096), bfa(108, 4096), bfa(116, 4096), bfa(124, 4096)]
        for b in range(NBLK):
            sl = b % 2
            ld_tok = [None]

            def on_ld(sy, b=b, sl=sl):
                if b == 0:
                    wait(sy, moe_fin[0])
                    wait(sy, moe_fin[1])
                    wait(sy, TAPQ.tok())
                if p4_guard[sl] is not None:
                    wait(sy, p4_guard[sl])
                P4Q[sl].dma(sy, R4[sl][:, :], residsp[:, b * 4096:(b + 1) * 4096])
                P4Q[sl].dma(sy, AB4[sl][:, :].rearrange("p (tc d) -> p tc d", tc=4),
                            bufA[b * TB:(b + 1) * TB, :].rearrange(
                                "(tc p) d -> p tc d", p=128))
                ld_tok[0] = P4Q[sl].dma(
                    sy, AB4[2 + sl][:, :].rearrange("p (tc d) -> p tc d", tc=4),
                    bufB[b * TB:(b + 1) * TB, :].rearrange("(tc p) d -> p tc d", p=128))
            SY(on_ld)

            add_tok = [None]

            def on_add(v, b=b, sl=sl):
                wait(v, ld_tok[0])
                wait(v, STQ[sl].tok())
                v.tensor_tensor(O4[sl][:, :], R4[sl][:, :], AB4[sl][:, :], ALU.add)
                tk = ch_v.inc(v.drain(fusable=False)); wait(v, tk)
                v.tensor_tensor(O4[sl][:, :], O4[sl][:, :], AB4[2 + sl][:, :], ALU.add)
                add_tok[0] = ch_v.inc(v.drain(fusable=False))
                p4_guard[sl] = add_tok[0]
            V(on_add)

            def on_st(sa, b=b, sl=sl):
                wait(sa, add_tok[0])
                st_toks[b] = STQ[sl].dma(
                    sa, out_d[b * TB:(b + 1) * TB, :].rearrange("(tc p) d -> p tc d", p=128),
                    O4[sl][:, :].rearrange("p (tc d) -> p tc d", tc=4))
            A(on_st)

        def fin(s):
            wait(s, STQ[0].tok())
            wait(s, STQ[1].tok())
            wait(s, TAPQ.tok())
        A(fin)

    return nc


def kernel(L0_attn_x, L0_attn_Wk, L0_attn_Wv, L0_attn_Wq, L0_attn_Wout,
           L0_attn_W0, L0_attn_W1, L0_attn_Wg_decay, L0_attn_bg_decay,
           L0_attn_Wg_lr, L0_attn_bg_lr, L0_attn_Wg_mom, L0_attn_bg_mom,
           L0_ffn_router_w, L0_ffn_expert_weights):
    B, S, DD = L0_attn_x.shape
    xf = np.ascontiguousarray(np.asarray(L0_attn_x, dtype=np.float32).reshape(B * S, DD))
    wg3 = np.ascontiguousarray(np.stack([
        np.asarray(L0_attn_Wg_decay)[0], np.asarray(L0_attn_Wg_lr)[0],
        np.asarray(L0_attn_Wg_mom)[0]], axis=0).astype(np.float32))
    bg3 = np.zeros((1, 4), dtype=np.float32)
    bg3[0, 0] = np.asarray(L0_attn_bg_decay).reshape(())
    bg3[0, 1] = np.asarray(L0_attn_bg_lr).reshape(())
    bg3[0, 2] = np.asarray(L0_attn_bg_mom).reshape(())
    ident = np.eye(128, dtype=BFNP)
    oh = np.zeros((NEXP, NEXP, 128), dtype=BFNP)
    for e_ in range(NEXP):
        oh[e_, e_, :] = 1
    oh = oh.reshape(NEXP, NEXP * 128)
    lt = np.triu(np.ones((128, 128), dtype=np.float32), k=1).astype(BFNP)
    iota32 = np.broadcast_to(np.arange(NEXP, dtype=np.float32), (128, NEXP)).copy()
    iotac = np.arange(128, dtype=np.float32).reshape(128, 1)
    tokc4 = (np.arange(128, dtype=np.float32).reshape(128, 1)
             + 128.0 * np.arange(4, dtype=np.float32).reshape(1, 4))
    experts = np.ascontiguousarray(
        np.asarray(L0_ffn_expert_weights, dtype=np.float32).reshape(NEXP * D, D))

    nc = build(taps=tuple(DEBUG_TAPS))
    base = {
        "wkt": np.ascontiguousarray(np.asarray(L0_attn_Wk, dtype=np.float32).T),
        "wvt": np.ascontiguousarray(np.asarray(L0_attn_Wv, dtype=np.float32).T),
        "wqt": np.ascontiguousarray(np.asarray(L0_attn_Wq, dtype=np.float32).T),
        "woutt": np.ascontiguousarray(np.asarray(L0_attn_Wout, dtype=np.float32).T),
        "w0t": np.ascontiguousarray(np.asarray(L0_attn_W0, dtype=np.float32).T),
        "w1t": np.ascontiguousarray(np.asarray(L0_attn_W1, dtype=np.float32).T),
        "w1": np.ascontiguousarray(np.asarray(L0_attn_W1, dtype=np.float32)),
        "wg3": wg3, "bg3": bg3,
        "router": np.ascontiguousarray(np.asarray(L0_ffn_router_w, dtype=np.float32)),
        "experts": experts, "identin": ident, "oh": oh,
        "lt128d": lt, "iota32d": iota32, "iotacd": iotac, "tokc4d": tokc4,
    }
    in_maps = [dict(base, x=xf[c * TC:(c + 1) * TC]) for c in range(8)]
    res = run_bass_kernel_spmd(nc, in_maps, core_ids=list(range(8)))
    out = np.concatenate([res.results[c]["out"] for c in range(8)], axis=0)
    kernel.last_results = res
    return out.reshape(B, S, DD)


# version 18
# speedup vs baseline: 1.0243x; 1.0243x over previous
import numpy as np
import ml_dtypes
from contextlib import ExitStack
import concourse.bass as bass
import concourse.mybir as mybir
from concourse.bass_utils import run_bass_kernel_spmd
from concourse.bass import IndirectOffsetOnAxis as IOA

F32 = mybir.dt.float32
BF16 = mybir.dt.bfloat16
I32 = mybir.dt.int32
AF = mybir.ActivationFunctionType
ALU = mybir.AluOpType
AX = mybir.AxisListType.X
BFNP = ml_dtypes.bfloat16

D = 1024
TC = 4096
NBLK = 8
TB = 512
KC = 8
NEXP = 32
EGRP = 2
NGRP = NEXP // EGRP
SCALE = 2.0 / 4096.0
ERR_CLIP = 5.0
GRAD_CLIP = 1.0
LN_EPS = 1e-5

DEBUG_TAPS = []

TAP_SHAPES = {
    "ln1": ([128, 4096], BF16), "kfm": ([128, 4096], BF16), "vtm": ([128, 4096], BF16),
    "qfm": ([128, 4096], BF16), "a1tm": ([128, 4096], BF16), "etm": ([128, 4096], BF16),
    "d1tm": ([128, 4096], BF16), "g0": ([128, 8192], F32), "g1": ([128, 8192], F32),
    "w0n": ([128, 8192], BF16), "resid": ([128, 4096], F32), "hit": ([128, 4096], BF16),
    "logits": ([128, 1024], F32), "wma": ([128, 1024], F32), "gacc": ([1, 4], F32),
    "gsig": ([1, 4], F32), "u2": ([128, 4096], BF16), "h2": ([128, 4096], BF16),
}


class Chan:
    def __init__(self, ctx, nc, name):
        self.h = ctx.enter_context(nc.semaphore(name))
        self.n = 0

    def inc(self, inst, k=1):
        inst.then_inc(self.h, k)
        self.n += k
        return (self.h, self.n)

    def tok(self):
        return (self.h, self.n)


def wait(eng, tok):
    if tok is None:
        return
    if isinstance(tok, _Multi):
        for t in tok.toks:
            wait(eng, t)
        return
    eng.wait_ge(tok[0], tok[1])


class _Multi:
    def __init__(self, toks):
        self.toks = [t for t in toks if t is not None]


def _multi(toks):
    return _Multi(toks)


class DmaQ:

    def __init__(self, ctx, nc, name):
        self.h = ctx.enter_context(nc.semaphore(name))
        self.n = 0

    def dma(self, eng, out, in_, **kw):
        if self.n:
            eng.wait_ge(self.h, self.n)
        inst = eng.dma_start(out, in_, **kw)
        inst.then_inc(self.h, 16)
        self.n += 16
        return (self.h, self.n)

    def dma_indirect_gather(self, eng, out, in_dram, idx_ap, bound):
        if self.n:
            eng.wait_ge(self.h, self.n)
        inst = eng.indirect_dma_start(
            out=out, out_offset=None, in_=in_dram,
            in_offset=IOA(axis=0, ap=idx_ap),
            bounds_check=bound, oob_is_err=False)
        inst.then_inc(self.h, 16)
        self.n += 16
        return (self.h, self.n)

    def dma_indirect_scatter(self, eng, out_dram, idx_ap, in_sb, bound):
        if self.n:
            eng.wait_ge(self.h, self.n)
        inst = eng.indirect_dma_start(
            out=out_dram, out_offset=IOA(axis=0, ap=idx_ap),
            in_=in_sb, in_offset=None,
            bounds_check=bound, oob_is_err=False)
        inst.then_inc(self.h, 16)
        self.n += 16
        return (self.h, self.n)

    def dma_tr(self, eng, out, in_):
        if self.n:
            eng.wait_ge(self.h, self.n)
        inst = eng.dma_start_transpose(out, in_)
        inst.then_inc(self.h, 16)
        self.n += 16
        return (self.h, self.n)

    def tok(self):
        return (self.h, self.n) if self.n else None


def build(taps=(), phases=7):
    nc = bass.Bass("TRN2", target_bir_lowering=False, debug=False)

    x_d = nc.dram_tensor("x", [TC, D], F32, kind="ExternalInput").ap()
    w_d = {w: nc.dram_tensor(w, [D, D], F32, kind="ExternalInput").ap()
           for w in ["w1", "wkt", "wvt", "wqt", "w0t", "w1t", "woutt"]}
    wg3_d = nc.dram_tensor("wg3", [3, D], F32, kind="ExternalInput").ap()
    bg3_d = nc.dram_tensor("bg3", [1, 4], F32, kind="ExternalInput").ap()
    router_d = nc.dram_tensor("router", [NEXP, D], F32, kind="ExternalInput").ap()
    experts_d = nc.dram_tensor("experts", [NEXP * D, D], F32, kind="ExternalInput").ap()
    ident_d = nc.dram_tensor("identin", [128, 128], BF16, kind="ExternalInput").ap()
    oh_d = nc.dram_tensor("oh", [NEXP, NEXP * 128], BF16, kind="ExternalInput").ap()
    lt_d = nc.dram_tensor("lt128d", [128, 128], BF16, kind="ExternalInput").ap()
    iota32_d = nc.dram_tensor("iota32d", [128, NEXP], F32, kind="ExternalInput").ap()
    iotac_d = nc.dram_tensor("iotacd", [128, 1], F32, kind="ExternalInput").ap()
    tokc4_d = nc.dram_tensor("tokc4d", [128, 4], F32, kind="ExternalInput").ap()
    out_d = nc.dram_tensor("out", [TC, D], F32, kind="ExternalOutput").ap()

    ksp = nc.dram_tensor("ksp", [128, KC * TC], BF16, kind="Internal").ap()
    qsp = nc.dram_tensor("qsp", [128, KC * TC], BF16, kind="Internal").ap()
    vsp = nc.dram_tensor("vsp", [128, (TC // 128) * D], BF16, kind="Internal").ap()
    ln2sp = nc.dram_tensor("ln2sp", [128, KC * TC], BF16, kind="Internal").ap()
    residsp = nc.dram_tensor("residsp", [128, (TC // 128) * D], F32, kind="Internal").ap()
    moeacc = nc.dram_tensor("moeacc", [TC, D], F32, kind="Internal").ap()
    ln2tmsp = nc.dram_tensor("ln2tmsp", [TC, D], BF16, kind="Internal").ap()
    idxstage = nc.dram_tensor("idxstage", [NEXP * 512, 2], F32, kind="Internal").ap()
    bufA = nc.dram_tensor("bufA", [TC, D], BF16, kind="Internal").ap()
    bufB = nc.dram_tensor("bufB", [TC, D], BF16, kind="Internal").ap()
    ccin = nc.dram_tensor("ccin", [128, 8192], F32, kind="Internal").ap()
    ccout = nc.dram_tensor("ccout", [128, 8192], F32, kind="Internal").ap()
    ccg_in = nc.dram_tensor("ccg_in", [1, 4], F32, kind="Internal").ap()
    ccg_out = nc.dram_tensor("ccg_out", [1, 4], F32, kind="Internal").ap()
    ccin2 = nc.dram_tensor("ccin2", [128, 8192], F32, kind="Internal").ap()
    ccout2 = nc.dram_tensor("ccout2", [128, 8192], F32, kind="Internal").ap()
    gbd = nc.dram_tensor("gbd", [1, 4], F32, kind="Internal").ap()

    tap_d = {t: nc.dram_tensor("tap_" + t, TAP_SHAPES[t][0], TAP_SHAPES[t][1],
                               kind="ExternalOutput").ap() for t in taps}

    with ExitStack() as ctx:
        en = ctx.enter_context
        ART = en(nc.sbuf_tensor("arena", [128, 102400], BF16))
        ARF = ART.bitcast(F32)
        ident = en(nc.sbuf_tensor("ident", [128, 128], BF16))
        rthi = en(nc.sbuf_tensor("rthi", [128, KC * NEXP], BF16))
        rtlo = en(nc.sbuf_tensor("rtlo", [128, KC * NEXP], BF16))
        wgt = en(nc.sbuf_tensor("wgt", [128, KC * 4], BF16))
        small = en(nc.sbuf_tensor("small", [128, 64], F32))
        stats = en(nc.sbuf_tensor("stats", [128, 32], F32))
        g3 = en(nc.sbuf_tensor("g3", [1, 16], F32))
        ones_bf = en(nc.sbuf_tensor("ones_bf", [128, 1], BF16))
        colsum_bf = en(nc.sbuf_tensor("colsum_bf", [128, 8], BF16))
        ones1 = en(nc.sbuf_tensor("ones1", [1, 128], F32))
        ones1b = en(nc.sbuf_tensor("ones1b", [1, 128], BF16))
        lt128 = en(nc.sbuf_tensor("lt128", [128, 128], BF16))
        iota32 = en(nc.sbuf_tensor("iota32", [128, NEXP], F32))
        iotac = en(nc.sbuf_tensor("iotac", [128, 1], F32))
        tokc4 = en(nc.sbuf_tensor("tokc4", [128, 4], F32))
        baserow = en(nc.sbuf_tensor("baserow", [1, 2 * NEXP], F32))
        baserow_bf = en(nc.sbuf_tensor("baserow_bf", [1, 2 * NEXP], BF16))
        br4b = en(nc.sbuf_tensor("br4b", [1, 256], BF16))
        wgsb = en(nc.sbuf_tensor("wgsb", [4, D], BF16))

        PS = [en(nc.psum_tensor(f"ps{i}", [128, 512], F32)) for i in range(6)]
        PT = [en(nc.psum_tensor(f"pt{i}", [128, 1024], BF16)) for i in range(2)]
        PTF = [p.bitcast(F32) for p in PT]
        PSF = [None] * 8

        def bwait(eng, bi):
            wait(eng, PSF[bi])

        def bset(bi, tok):
            PSF[bi] = tok

        def bfa(off_kb, cols):
            off = int(off_kb * 1024) // 2
            return ART[:, off:off + cols]

        def f32a(off_kb, cols):
            off = int(off_kb * 1024) // 4
            return ARF[:, off:off + cols]

        W_OFF = {"wk": 0, "wv": 16, "wq": 32, "w0": 48, "w1": 64, "w1p": 80}
        WT = {k: bfa(v, KC * D) for k, v in W_OFF.items()}
        X1 = f32a(96, 4096)
        LN1T = bfa(112, 4096)
        LN1F = bfa(120, 4096)
        QSB = bfa(128, 4096)
        KSB = bfa(136, 4096)
        VSB = bfa(144, 4096)
        RSBF = f32a(168, 1024)
        RHI = bfa(172, 1024)
        RLO = bfa(174, 1024)
        JNK1A = bfa(184, 4096)
        G0 = f32a(0, 8192)
        WOUTT = bfa(32, KC * D)
        G1 = f32a(96, 8192)
        KLD = [bfa(128, 4096), bfa(136, 4096)]
        VLD = [bfa(144, 4096), bfa(152, 4096)]
        SGT = bfa(160, 4096)
        PRC = bfa(168, 4096)
        A1T = bfa(176, 4096)
        AF1 = bfa(184, 4096)
        ETM = bfa(192, 4096)
        W0N = bfa(128, KC * D)
        W1N = bfa(144, KC * D)
        X2 = f32a(0, 4096)
        JNK2 = bfa(16, 4096)
        JNKF2 = f32a(18, 512)
        RES = f32a(48, 4096)
        LN2F32 = f32a(64, 4096)
        HIT = bfa(80, 4096)
        LOT = bfa(88, 4096)
        HIF = bfa(96, 4096)
        LOF = bfa(104, 4096)
        WMT = bfa(112, 4096)
        LOGI = f32a(120, 1024)
        WMA = f32a(124, 1024)
        Q2 = [bfa(160, 4096), bfa(168, 4096)]
        SG2 = bfa(176, 4096)
        U2 = bfa(184, 4096)
        H2 = bfa(192, 4096)
        EX = [bfa(0, KC * D), bfa(16, KC * D), bfa(32, KC * D), bfa(48, KC * D)]
        OHSB = bfa(64, NEXP * 128)
        WBC = bfa(72, 1024)
        LSTR = [bfa(74, 4096), bfa(82, 4096)]
        ZSB = [[bfa(90, 4096), bfa(106, 4096)], [bfa(98, 4096), bfa(136, 4096)]]
        STG = [bfa(144, 4096), bfa(152, 4096)]
        R4 = [f32a(0, 4096), f32a(16, 4096)]
        M4 = [f32a(32, 4096), f32a(48, 4096)]
        O4 = [f32a(64, 4096), f32a(80, 4096)]

        ch_t = Chan(ctx, nc, "ch_t")
        ch_a = Chan(ctx, nc, "ch_a")
        ch_v = Chan(ctx, nc, "ch_v")
        ch_cc = Chan(ctx, nc, "ch_cc")
        WQ = DmaQ(ctx, nc, "WQ")
        TRQ = DmaQ(ctx, nc, "TRQ")
        XQ = DmaQ(ctx, nc, "XQ")
        SPQ = DmaQ(ctx, nc, "SPQ")
        SPK = DmaQ(ctx, nc, "SPK")
        SPV = DmaQ(ctx, nc, "SPV")
        KVQ = [DmaQ(ctx, nc, "KVQ0"), DmaQ(ctx, nc, "KVQ1")]
        Q2Q = DmaQ(ctx, nc, "Q2Q")
        RSQ = DmaQ(ctx, nc, "RSQ")
        LNQ = DmaQ(ctx, nc, "LNQ")
        CCS = DmaQ(ctx, nc, "CCS")
        CCL = DmaQ(ctx, nc, "CCL")
        GBQ = DmaQ(ctx, nc, "GBQ")
        EXQ = [DmaQ(ctx, nc, "EXQ0"), DmaQ(ctx, nc, "EXQ1")]
        LSQ = [DmaQ(ctx, nc, "LSQ0"), DmaQ(ctx, nc, "LSQ1")]
        ACQ = [DmaQ(ctx, nc, "ACQ0"), DmaQ(ctx, nc, "ACQ1")]
        P4Q = [DmaQ(ctx, nc, "P4Q0"), DmaQ(ctx, nc, "P4Q1")]
        STQ = [DmaQ(ctx, nc, "STQ0"), DmaQ(ctx, nc, "STQ1")]
        TAPQ = DmaQ(ctx, nc, "TAPQ")

        block = en(nc.Block())
        T, A, V, SY, G = block.tensor, block.scalar, block.vector, block.sync, block.gpsimd

        def tap(name, src, tok):
            if name in tap_d:
                def f(g):
                    wait(g, tok)
                    TAPQ.dma(g, tap_d[name], src)
                G(f)

        cast_toks = {}

        def stw_g(g):
            for w, q in [("wq", WQ), ("wk", EXQ[0]), ("wv", EXQ[1]),
                         ("w0", ACQ[0]), ("w1", ACQ[1])]:
                cast_toks[w + "T"] = q.dma(
                    g, WT[w][:, :].rearrange("p (c e) -> p c e", c=KC),
                    w_d[w + "t"].rearrange("(c p) e -> p c e", p=128))
            cast_toks["router"] = WQ.dma(g, RSBF[0:32, 0:D], router_d)
            cast_toks["bg3"] = WQ.dma(g, g3[0:1, 4:8], bg3_d)
            cast_toks["wg3"] = WQ.dma(g, wgsb[0:3, :], wg3_d)
            cast_toks["lt"] = WQ.dma(g, lt128[:, :], lt_d)
            cast_toks["iota32"] = WQ.dma(g, iota32[:, :], iota32_d)
            cast_toks["iotac"] = WQ.dma(g, iotac[:, :], iotac_d)
            cast_toks["tokc4"] = WQ.dma(g, tokc4[:, :], tokc4_d)
            cast_toks["w1p"] = WQ.dma(
                g, WT["w1p"].rearrange("p (c e) -> p c e", c=KC),
                w_d["w1"].rearrange("(c p) e -> p c e", p=128))
        G(stw_g)

        def stw_sy(sy):
            cast_toks["ident"] = TRQ.dma(sy, ident[:, :], ident_d)
        SY(stw_sy)

        def stw_wg_t(t):
            wait(t, cast_toks["wg3"])
            wait(t, cast_toks["ident"])
            bwait(t, 6)
            for kc in range(KC):
                li = t.transpose(PT[0][0:128, kc * 128:kc * 128 + 3],
                                 wgsb[0:3, kc * 128:(kc + 1) * 128], ident[0:3, 0:3])
            ch_t.inc(li)
        T(stw_wg_t)
        wg_ps_tok = ch_t.tok()

        def stw_wg_v(v):
            wait(v, wg_ps_tok)
            for kc in range(KC):
                v.tensor_copy(wgt[:, kc * 4:kc * 4 + 3], PT[0][:, kc * 128:kc * 128 + 3])
            ch_v.inc(v.drain(fusable=False))
        V(stw_wg_v)
        wgt_tok = ch_v.tok()
        bset(6, wgt_tok)

        def stw_r_v(v):
            wait(v, cast_toks["router"])
            v.tensor_copy(RHI[0:32, 0:D], RSBF[0:32, 0:D])
            t1 = ch_v.inc(v.drain(fusable=False))
            wait(v, t1)
            v.tensor_tensor(RLO[0:32, 0:D], RSBF[0:32, 0:D], RHI[0:32, 0:D], ALU.subtract)
            ch_v.inc(v.drain(fusable=False))
        V(stw_r_v)
        rhilo_tok = ch_v.tok()

        rt_ev = [None] * KC
        rt_pe = [None] * KC
        for kc in range(KC):
            def rt_t(t, kc=kc):
                if kc == 0:
                    wait(t, rhilo_tok)
                    bwait(t, 7)
                if kc >= 2:
                    wait(t, rt_ev[kc - 2])
                half = (kc % 2) * 512
                t.transpose(PT[1][0:128, half:half + 32],
                            RHI[0:32, kc * 128:(kc + 1) * 128], ident[0:32, 0:32])
                li = t.transpose(PT[1][0:128, half + 128:half + 160],
                                 RLO[0:32, kc * 128:(kc + 1) * 128], ident[0:32, 0:32])
                rt_pe[kc] = ch_t.inc(li)
            T(rt_t)

            def rt_e(s, kc=kc):
                wait(s, rt_pe[kc])
                half = (kc % 2) * 512
                s.copy(rthi[:, kc * NEXP:(kc + 1) * NEXP], PT[1][:, half:half + 32])
                s.copy(rtlo[:, kc * NEXP:(kc + 1) * NEXP], PT[1][:, half + 128:half + 160])
                rt_ev[kc] = ch_a.inc(s.drain(fusable=False))
            A(rt_e)
        rt_tok = rt_ev[-1]
        bset(7, rt_tok)

        def emit_ln(x_ap, outs, junk, xin_tok):
            sums = stats[:, 0:4]
            sq = stats[:, 4:8]
            rstd = stats[:, 16:20]
            nb = stats[:, 20:24]
            t1 = small[:, 32:36]
            t2 = small[:, 36:40]

            def on_sq(s):
                wait(s, xin_tok)
                for tc in range(4):
                    s.activation(junk[:, tc * 1024:(tc + 1) * 1024],
                                 x_ap[:, tc * 1024:(tc + 1) * 1024],
                                 AF.Square, accum_out=sq[:, tc:tc + 1])
                ch_a.inc(s.drain(fusable=False))
            A(on_sq)
            sq_tok = ch_a.tok()

            def on_v(v):
                wait(v, xin_tok)
                for tc in range(4):
                    v.reduce_sum(sums[:, tc:tc + 1], x_ap[:, tc * 1024:(tc + 1) * 1024], axis=AX)
                tk = ch_v.inc(v.drain(fusable=False)); wait(v, tk)
                wait(v, sq_tok)
                v.tensor_scalar_mul(nb[:, :], sums[:, :], 1.0 / D)
                tk = ch_v.inc(v.drain(fusable=False)); wait(v, tk)
                v.tensor_tensor(t1[:, :], nb[:, :], nb[:, :], ALU.mult)
                v.tensor_scalar(t2[:, :], sq[:, :], 1.0 / D, LN_EPS, ALU.mult, ALU.add)
                tk = ch_v.inc(v.drain(fusable=False)); wait(v, tk)
                v.tensor_tensor(t2[:, :], t2[:, :], t1[:, :], ALU.subtract)
                tk = ch_v.inc(v.drain(fusable=False)); wait(v, tk)
                ti = t1[:, :].bitcast(I32)
                ri = rstd[:, :].bitcast(I32)
                v.tensor_single_scalar(ti, t2[:, :].bitcast(I32), 1, ALU.logical_shift_right)
                tk = ch_v.inc(v.drain(fusable=False)); wait(v, tk)
                v.tensor_scalar(ri, ti, -1, 0x5F3759DF, ALU.mult, ALU.add)
                tk = ch_v.inc(v.drain(fusable=False)); wait(v, tk)
                for _ in range(3):
                    v.tensor_tensor(t1[:, :], rstd[:, :], rstd[:, :], ALU.mult)
                    tk = ch_v.inc(v.drain(fusable=False)); wait(v, tk)
                    v.tensor_tensor(t1[:, :], t1[:, :], t2[:, :], ALU.mult)
                    tk = ch_v.inc(v.drain(fusable=False)); wait(v, tk)
                    v.tensor_scalar(t1[:, :], t1[:, :], -0.5, 1.5, ALU.mult, ALU.add)
                    tk = ch_v.inc(v.drain(fusable=False)); wait(v, tk)
                    v.tensor_tensor(rstd[:, :], rstd[:, :], t1[:, :], ALU.mult)
                    tk = ch_v.inc(v.drain(fusable=False)); wait(v, tk)
                v.tensor_tensor(nb[:, :], nb[:, :], rstd[:, :], ALU.mult)
                tk = ch_v.inc(v.drain(fusable=False)); wait(v, tk)
                v.tensor_scalar_mul(nb[:, :], nb[:, :], -1.0)
                ch_v.inc(v.drain(fusable=False))
            V(on_v)
            stat_tok = ch_v.tok()

            def on_s(s):
                wait(s, stat_tok)
                for ap in outs:
                    for tc in range(4):
                        s.activation(ap[:, tc * 1024:(tc + 1) * 1024],
                                     x_ap[:, tc * 1024:(tc + 1) * 1024], AF.Identity,
                                     bias=nb[:, tc:tc + 1], scale=rstd[:, tc:tc + 1])
                ch_a.inc(s.drain(fusable=False))
            A(on_s)
            return ch_a.tok()

        def transpose_block(src, dst, to_fm, pe_wait):
            n_out = KC if to_fm else 4
            pe_toks = [None] * n_out
            ev_toks = [None] * n_out
            for i in range(n_out):
                bank = PT[i % 2]

                def on_t(t, i=i, bank=bank):
                    if i == 0 and pe_wait is not None:
                        wait(t, pe_wait)
                    bwait(t, 6 + i % 2)
                    if to_fm:
                        for tc in range(4):
                            li = t.transpose(bank[:, tc * 128:tc * 128 + 128],
                                             src[:, tc * 1024 + i * 128:
                                                 tc * 1024 + i * 128 + 128],
                                             ident[:, :])
                    else:
                        for kc in range(KC):
                            li = t.transpose(bank[:, kc * 128:kc * 128 + 128],
                                             src[:, kc * 512 + i * 128:
                                                 kc * 512 + i * 128 + 128],
                                             ident[:, :])
                    pe_toks[i] = ch_t.inc(li)
                T(on_t)

                def on_e(v, i=i, bank=bank):
                    wait(v, pe_toks[i])
                    if to_fm:
                        inst = v.tensor_copy(dst[:, i * 512:(i + 1) * 512], bank[:, 0:512])
                    else:
                        inst = v.tensor_copy(dst[:, i * 1024:(i + 1) * 1024], bank[:, :])
                    ev_toks[i] = ch_v.inc(inst)
                V(on_e)
                bset(6 + i % 2, ev_toks[i])
            return ev_toks[-1]

        def proj_fm(wt_ap, rhs_fm, dst, rhs_tok, w_tok, dst2=None, fn=AF.Identity,
                    fn2=AF.Identity):
            mm = [None] * KC
            ev = [None] * KC
            for oc in range(KC):
                bank = PS[oc % 2]

                def on_t(t, oc=oc, bank=bank):
                    if oc == 0:
                        wait(t, rhs_tok)
                        wait(t, w_tok)
                    bwait(t, oc % 2)
                    for kc in range(KC):
                        li = t.matmul(bank[:, :],
                                      wt_ap[:, kc * 1024 + oc * 128:
                                            kc * 1024 + oc * 128 + 128],
                                      rhs_fm[:, kc * 512:(kc + 1) * 512],
                                      start=(kc == 0), stop=(kc == KC - 1))
                    mm[oc] = ch_t.inc(li)
                T(on_t)

                def on_s(v, oc=oc, bank=bank):
                    wait(v, mm[oc])
                    li = v.tensor_copy(dst[:, oc * 512:(oc + 1) * 512], bank[:, :])
                    if dst2 is not None:
                        li = v.tensor_copy(dst2[:, oc * 512:(oc + 1) * 512], bank[:, :])
                    ev[oc] = ch_v.inc(li)
                V(on_s)
                bset(oc % 2, ev[oc])
            return ev[-1], mm[-1]

        def x_view(b):
            return x_d[b * TB:(b + 1) * TB, :].rearrange("(tc p) d -> p tc d", p=128)

        def sp_view(sp, b):
            return sp.rearrange("p (kc t) -> p kc t", kc=KC)[:, :, b * TB:(b + 1) * TB]

        def sb_kc(sb):
            return sb.rearrange("p (kc t) -> p kc t", kc=KC)

        gate_red = [None] * NBLK
        gate_mm = [None] * NBLK
        spq = [None] * NBLK
        spk = [None] * NBLK
        spv = [None] * NBLK
        xtok = [None] * NBLK

        def z_gates(v):
            v.memset(ones1[:, :], 1.0)
            v.memset(ones_bf[:, :], 1.0)
            v.memset(ones1b[:, :], 1.0)
            v.memset(g3[0:1, 3:4], 0.0)
            v.memset(baserow[:, :], 0.0)
            v.memset(baserow_bf[:, :], 0.0)
            ch_v.inc(v.drain(fusable=False))
        V(z_gates)
        gz_tok = ch_v.tok()

        ZTILE = VLD[1]
        PADP = f32a(160, 256)
        INITQ = GBQ

        def init_v(v):
            v.memset(ZTILE[:, :], 0.0)
            pv = PADP.rearrange("p (c two) -> p c two", two=2)
            v.memset(pv[:, :, 0:1], 100000.0)
            tk = ch_v.inc(v.drain(fusable=False)); wait(v, tk)
            v.memset(pv[:, :, 1:2], 0.0)
            ch_v.inc(v.drain(fusable=False))
        V(init_v)
        initv_tok = ch_v.tok()

        def init_dma(sa):
            wait(sa, initv_tok)
            for i in range(NBLK):
                INITQ.dma(sa, bufA[i * TB:(i + 1) * TB, :].rearrange(
                    "(tc p) d -> p tc d", p=128),
                    ZTILE[:, :].rearrange("p (tc d) -> p tc d", tc=4))
            for i in range(NBLK):
                INITQ.dma(sa, bufB[i * TB:(i + 1) * TB, :].rearrange(
                    "(tc p) d -> p tc d", p=128),
                    ZTILE[:, :].rearrange("p (tc d) -> p tc d", tc=4))
            INITQ.dma(sa, idxstage.rearrange("(c p) two -> p c two", p=128),
                      PADP.rearrange("p (c two) -> p c two", two=2))
        A(init_dma)
        init_tok = INITQ.tok()


        XLQ = [XQ, TRQ, CCS, RSQ]
        prev_guards = [None]
        for b in range(NBLK):
            def on_x(sy, b=b, gl=prev_guards[0]):
                if gl is not None:
                    for g in gl:
                        wait(sy, g)
                wait(sy, TAPQ.tok())
                tks = []
                for tc in range(4):
                    tks.append(XLQ[tc].dma(
                        sy, X1[:, tc * 1024:(tc + 1) * 1024],
                        x_d[b * TB + tc * 128:b * TB + (tc + 1) * 128, :]))
                xtok[b] = _multi(tks)
            SY(on_x)
            ln_tok = emit_ln(X1, [LN1T], JNK1A, xtok[b])
            ln1f_tok = transpose_block(LN1T, LN1F, True, ln_tok)

            q_tok, q_mm = proj_fm(WT["wq"], LN1F, QSB, ln1f_tok,
                                  cast_toks["wqT"] if b == 0 else None)
            k_tok, _ = proj_fm(WT["wk"], LN1F, KSB, ln1f_tok,
                               cast_toks["wkT"] if b == 0 else None)

            v_mm = [None] * 8
            v_ev = [None] * 8
            for i, (tc, h) in enumerate([(tc, h) for tc in range(4) for h in range(2)]):
                bank = PS[2 + i % 2]

                def on_t(t, i=i, tc=tc, h=h, bank=bank, b=b):
                    if i == 0:
                        wait(t, ln1f_tok)
                        if b == 0:
                            wait(t, cast_toks["wvT"])
                    bwait(t, 2 + i % 2)
                    for kc in range(KC):
                        li = t.matmul(bank[:, :],
                                      LN1F[:, kc * 512 + tc * 128:kc * 512 + tc * 128 + 128],
                                      WT["wv"][:, kc * 1024 + h * 512:
                                               kc * 1024 + (h + 1) * 512],
                                      start=(kc == 0), stop=(kc == KC - 1))
                    v_mm[i] = ch_t.inc(li)
                T(on_t)

                def on_s(v, i=i, tc=tc, h=h, bank=bank):
                    wait(v, v_mm[i])
                    v_ev[i] = ch_v.inc(v.tensor_copy(
                        VSB[:, tc * 1024 + h * 512:tc * 1024 + (h + 1) * 512], bank[:, :]))
                V(on_s)
                bset(2 + i % 2, v_ev[i])
            v_tok = v_ev[-1]

            def on_gt(t, b=b):
                wait(t, ln_tok)
                if b == 0:
                    wait(t, gz_tok)
                    bwait(t, 4)
                for kc in range(KC):
                    for tc in range(4):
                        li = t.matmul(PS[4][:, kc:kc + 1],
                                      LN1T[:, tc * 1024 + kc * 128:
                                           tc * 1024 + kc * 128 + 128],
                                      ones_bf[:, :],
                                      start=(b == 0 and kc == 0 and tc == 0),
                                      stop=(b == NBLK - 1 and kc == KC - 1
                                            and tc == 3))
                gate_mm[b] = ch_t.inc(li)
            T(on_gt)

            def on_sp(sy, b=b, q_tok=q_tok, k_tok=k_tok, v_tok=v_tok):
                wait(sy, q_tok)
                spq[b] = SPQ.dma(sy, sp_view(qsp, b), sb_kc(QSB))
                wait(sy, k_tok)
                spk[b] = SPK.dma(sy, sp_view(ksp, b), sb_kc(KSB))
                wait(sy, v_tok)
                spv[b] = SPV.dma(sy, vsp[:, b * 4096:(b + 1) * 4096], VSB[:, :])
            SY(on_sp)
            prev_guards[0] = [ln1f_tok, spq[b], spk[b], spv[b], gate_mm[b]]

            if b == 0:
                tap("ln1", LN1T[:, :], ln_tok)
                tap("kfm", KSB[:, :], k_tok)
                tap("vtm", VSB[:, :], v_tok)
                tap("qfm", QSB[:, :], q_tok)

        last_1a_pe = ch_t.tok()

        def on_gf1(s):
            wait(s, gate_mm[NBLK - 1])
            s.activation(colsum_bf[:, :], PS[4][:, 0:8], AF.Identity)
            ch_a.inc(s.drain(fusable=False))
        A(on_gf1)
        colsum_tok = ch_a.tok()
        bset(4, colsum_tok)

        def on_gf2(t):
            wait(t, colsum_tok)
            wait(t, wgt_tok)
            bwait(t, 5)
            for kc in range(KC):
                li = t.matmul(PS[5][0:1, 0:3], colsum_bf[:, kc:kc + 1],
                              wgt[:, kc * 4:kc * 4 + 3],
                              start=(kc == 0), stop=(kc == KC - 1))
            ch_t.inc(li)
        T(on_gf2)
        gmv_tok = ch_t.tok()

        def on_gf3(v):
            wait(v, gmv_tok)
            v.tensor_copy(g3[0:1, 0:3], PS[5][0:1, 0:3])
            ch_v.inc(v.drain(fusable=False))
        V(on_gf3)
        gate_red[NBLK - 1] = ch_v.tok()
        bset(5, gate_red[NBLK - 1])

        def finalize_partial():
            def fin_p(g):
                for q in [WQ, TRQ, XQ, SPQ, SPK, SPV, KVQ[0], KVQ[1], Q2Q, RSQ, LNQ,
                          CCS, CCL, GBQ, EXQ[0], EXQ[1], LSQ[0], LSQ[1], ACQ[0],
                          ACQ[1], P4Q[0], P4Q[1], STQ[0], STQ[1], TAPQ]:
                    if q.n:
                        g.wait_ge(q.h, q.n)
            G(fin_p)

        if phases < 2:
            finalize_partial()
            return nc

        def gz(v):
            v.memset(G0[:, :], 0.0)
            v.memset(G1[:, :], 0.0)
            ch_v.inc(v.drain(fusable=False))
        V(gz)
        gzero_tok = ch_v.tok()

        def on_woutt(g):
            wait(g, last_1a_pe)
            cast_toks["woutT"] = CCL.dma(
                g, WOUTT[:, :].rearrange("p (c e) -> p c e", c=KC),
                w_d["woutt"].rearrange("(c p) e -> p c e", p=128))
        G(on_woutt)

        kld_t = [None] * NBLK
        vld_t = [None] * NBLK

        def kv_load(b, guard):
            def f(sy):
                if guard is not None:
                    wait(sy, guard)
                wait(sy, TAPQ.tok())
                wait(sy, init_tok)
                wait(sy, spq[NBLK - 1])
                wait(sy, spk[NBLK - 1])
                wait(sy, spv[NBLK - 1])
                kld_t[b] = KVQ[b % 2].dma(sy, KLD[b % 2][:, :], sp_view(ksp, b))
                vld_t[b] = KVQ[b % 2].dma(sy, VLD[b % 2][:, :],
                                          vsp[:, b * 4096:(b + 1) * 4096])
            SY(f)

        kv_load(0, None)
        blk_done = [None] * NBLK

        def emit_g(lhs_tm, rhs_tm, acc, lhs_tok, rhs_tok, first_guard):
            mm = {}
            ev = {}
            idx = 0
            for ec in range(KC):
                for on in range(2):
                    bank = PS[2 + idx % 2]

                    def on_t(t, ec=ec, on=on, bank=bank, idx=idx):
                        if idx == 0:
                            wait(t, lhs_tok)
                            wait(t, rhs_tok)
                        bwait(t, 2 + idx % 2)
                        for tc in range(4):
                            li = t.matmul(bank[:, :],
                                          lhs_tm[:, tc * 1024 + ec * 128:
                                                 tc * 1024 + ec * 128 + 128],
                                          rhs_tm[:, tc * 1024 + on * 512:
                                                 tc * 1024 + (on + 1) * 512],
                                          start=(tc == 0), stop=(tc == 3))
                        mm[idx] = ch_t.inc(li)
                    T(on_t)

                    def on_v(v, ec=ec, on=on, bank=bank, idx=idx):
                        wait(v, mm[idx])
                        if idx == 0 and first_guard is not None:
                            wait(v, first_guard)
                        sl = slice(ec * 1024 + on * 512, ec * 1024 + (on + 1) * 512)
                        v.scalar_tensor_tensor(acc[:, sl], bank[:, :], SCALE, acc[:, sl],
                                               ALU.mult, ALU.add)
                        ev[idx] = ch_v.inc(v.drain(fusable=False))
                    V(on_v)
                    bset(2 + idx % 2, ev[idx])
                    idx += 1
            return {"mm_done": mm[idx - 1], "acc_done": ev[idx - 1]}

        for b in range(NBLK):
            kf = KLD[b % 2]
            vt = VLD[b % 2]
            if b + 1 < NBLK:
                kv_load(b + 1, blk_done[b - 1] if b >= 1 else None)

            pre_mm = [None] * 8
            pre_ev = [None] * 8
            for i, (tc, h) in enumerate([(tc, h) for tc in range(4) for h in range(2)]):
                bank = PS[i % 2]

                def on_t(t, i=i, tc=tc, h=h, bank=bank, b=b):
                    if i == 0:
                        wait(t, kld_t[b])
                        if b == 0:
                            wait(t, cast_toks["w0T"])
                    bwait(t, i % 2)
                    for kc in range(KC):
                        li = t.matmul(bank[:, :],
                                      kf[:, kc * 512 + tc * 128:kc * 512 + tc * 128 + 128],
                                      WT["w0"][:, kc * 1024 + h * 512:
                                               kc * 1024 + (h + 1) * 512],
                                      start=(kc == 0), stop=(kc == KC - 1))
                    pre_mm[i] = ch_t.inc(li)
                T(on_t)

                def on_s(v, i=i, tc=tc, h=h, bank=bank):
                    wait(v, pre_mm[i])
                    sl = slice(tc * 1024 + h * 512, tc * 1024 + (h + 1) * 512)
                    pre_ev[i] = ch_v.inc(v.tensor_copy(PRC[:, sl], bank[:, :]))
                V(on_s)
                bset(i % 2, pre_ev[i])
            pre_tok = pre_ev[-1]

            def on_sg(sa):
                wait(sa, pre_tok)
                ch_a.inc(sa.activation(SGT[:, :], PRC[:, :], AF.Sigmoid))
            A(on_sg)
            sg_tok = ch_a.tok()

            def on_a1(v):
                wait(v, sg_tok)
                v.tensor_tensor(A1T[:, :], PRC[:, :], SGT[:, :], ALU.mult)
                ch_v.inc(v.drain(fusable=False))
            V(on_a1)
            a1_tok = ch_v.tok()
            if b == 0:
                tap("a1tm", A1T[:, :], a1_tok)

            a1f_tok = transpose_block(A1T, AF1, True, a1_tok)

            prd_mm = [None] * 8
            prd_ev = [None] * 8
            for i, (tc, h) in enumerate([(tc, h) for tc in range(4) for h in range(2)]):
                bank = PS[i % 2]

                def on_t(t, i=i, tc=tc, h=h, bank=bank, b=b):
                    if i == 0:
                        wait(t, a1f_tok)
                        if b == 0:
                            wait(t, cast_toks["w1T"])
                    bwait(t, i % 2)
                    for kc in range(KC):
                        li = t.matmul(bank[:, :],
                                      AF1[:, kc * 512 + tc * 128:kc * 512 + tc * 128 + 128],
                                      WT["w1"][:, kc * 1024 + h * 512:
                                               kc * 1024 + (h + 1) * 512],
                                      start=(kc == 0), stop=(kc == KC - 1))
                    prd_mm[i] = ch_t.inc(li)
                T(on_t)

                def on_v(v, i=i, tc=tc, h=h, bank=bank, b=b):
                    wait(v, prd_mm[i])
                    if i == 0:
                        wait(v, vld_t[b])
                    sl = slice(tc * 1024 + h * 512, tc * 1024 + (h + 1) * 512)
                    v.tensor_tensor(ETM[:, sl], bank[:, :], vt[:, sl], ALU.subtract)
                    tk = ch_v.inc(v.drain(fusable=False)); wait(v, tk)
                    v.tensor_scalar(ETM[:, sl], ETM[:, sl], ERR_CLIP, -ERR_CLIP,
                                    ALU.min, ALU.max)
                    prd_ev[i] = ch_v.inc(v.drain(fusable=False))
                V(on_v)
                bset(i % 2, prd_ev[i])
            e_tok = prd_ev[-1]
            if b == 0:
                tap("etm", ETM[:, :], e_tok)

            def on_sd(v):
                wait(v, a1_tok)
                v.tensor_scalar(PRC[:, :], SGT[:, :], -1.0, 1.0, ALU.mult, ALU.add)
                tk = ch_v.inc(v.drain(fusable=False)); wait(v, tk)
                v.tensor_tensor(PRC[:, :], A1T[:, :], PRC[:, :], ALU.mult)
                tk = ch_v.inc(v.drain(fusable=False)); wait(v, tk)
                v.tensor_tensor(SGT[:, :], SGT[:, :], PRC[:, :], ALU.add)
                ch_v.inc(v.drain(fusable=False))
            V(on_sd)
            sd_tok = ch_v.tok()

            ef_tok = transpose_block(ETM, AF1, True, e_tok)

            d1_mm = [None] * 8
            d1_ev = [None] * 8
            for i, (tc, h) in enumerate([(tc, h) for tc in range(4) for h in range(2)]):
                bank = PS[i % 2]

                def on_t(t, i=i, tc=tc, h=h, bank=bank, b=b):
                    if i == 0:
                        wait(t, ef_tok)
                        if b == 0:
                            wait(t, cast_toks["w1p"])
                    bwait(t, i % 2)
                    for kc in range(KC):
                        li = t.matmul(bank[:, :],
                                      AF1[:, kc * 512 + tc * 128:kc * 512 + tc * 128 + 128],
                                      WT["w1p"][:, kc * 1024 + h * 512:
                                               kc * 1024 + (h + 1) * 512],
                                      start=(kc == 0), stop=(kc == KC - 1))
                    d1_mm[i] = ch_t.inc(li)
                T(on_t)

                def on_v(v, i=i, tc=tc, h=h, bank=bank):
                    wait(v, d1_mm[i])
                    if i == 0:
                        wait(v, sd_tok)
                    sl = slice(tc * 1024 + h * 512, tc * 1024 + (h + 1) * 512)
                    v.tensor_tensor(PRC[:, sl], bank[:, :], SGT[:, sl], ALU.mult)
                    d1_ev[i] = ch_v.inc(v.drain(fusable=False))
                V(on_v)
                bset(i % 2, d1_ev[i])
            d1_tok = d1_ev[-1]
            if b == 0:
                tap("d1tm", PRC[:, :], d1_tok)

            def ktm_guard(t):
                wait(t, TAPQ.tok())
            T(ktm_guard)
            ktm_tok = transpose_block(kf, SGT, False, d1_tok)

            g0r = emit_g(SGT, PRC, G0, ktm_tok, d1_tok, gzero_tok if b == 0 else None)

            g1r = emit_g(A1T, ETM, G1, a1_tok, e_tok, None)
            blk_done[b] = g1r["acc_done"]

        g0_fin = g0r["acc_done"]
        g1_fin = g1r["acc_done"]
        gates_fin = gate_red[NBLK - 1]
        cc_tok = {}

        def r_stage0(sy):
            wait(sy, gates_fin)
            cc_tok["ga"] = CCS.dma(sy, ccg_in[0:1, 0:4], g3[0:1, 0:4])
        SY(r_stage0)

        def r_cc0(g):
            wait(g, cc_tok["ga"])
            g.collective_compute(
                "AllReduce", ALU.add,
                replica_groups=[[0, 1, 2, 3, 4, 5, 6, 7]],
                ins=[ccg_in[0:1, :]], outs=[ccg_out[0:1, :]],
            ).then_inc(ch_cc.h, 1)
            ch_cc.n += 1
            g.wait_ge(ch_cc.h, ch_cc.n)
            cc_tok["gar"] = CCL.dma(g, g3[0:1, 0:3], ccg_out[0:1, 0:3])
        G(r_cc0)
        tap("g0", G0[:, :], g0_fin)
        tap("g1", G1[:, :], g1_fin)
        tap("gacc", g3[0:1, 0:4], gates_fin)

        if phases < 3:
            finalize_partial()
            return nc

        def r_stage(sy):
            wait(sy, g0_fin)
            cc_tok["g0"] = CCS.dma(sy, ccin[:, :], G0[:, :])
        SY(r_stage)

        def r_stage2(sy):
            wait(sy, g1_fin)
            cc_tok["g1"] = LNQ.dma(sy, ccin2[:, :], G1[:, :])
        SY(r_stage2)

        def r_cc(g):
            wait(g, cc_tok["g0"])
            g.collective_compute(
                "AllReduce", ALU.add,
                replica_groups=[[0, 1, 2, 3, 4, 5, 6, 7]],
                ins=[ccin[:, :]], outs=[ccout[:, :]],
            ).then_inc(ch_cc.h, 1)
            ch_cc.n += 1
            wait(g, cc_tok["g1"])
            g.collective_compute(
                "AllReduce", ALU.add,
                replica_groups=[[0, 1, 2, 3, 4, 5, 6, 7]],
                ins=[ccin2[:, :]], outs=[ccout2[:, :]],
            ).then_inc(ch_cc.h, 1)
            ch_cc.n += 1
            g.wait_ge(ch_cc.h, ch_cc.n - 1)
            for c in range(4):
                cc_tok["g0r%d" % c] = CCL.dma(
                    g, G0[:, c * 2048:(c + 1) * 2048], ccout[:, c * 2048:(c + 1) * 2048])
            g.wait_ge(ch_cc.h, ch_cc.n)
            for c in range(4):
                cc_tok["g1r%d" % c] = CCL.dma(
                    g, G1[:, c * 2048:(c + 1) * 2048], ccout2[:, c * 2048:(c + 1) * 2048])
        G(r_cc)

        def r_gv(v):
            wait(v, cc_tok["gar"])
            v.tensor_scalar(g3[0:1, 8:11], g3[0:1, 0:3], 1.0 / 32768.0, None, ALU.mult)
            tk = ch_v.inc(v.drain(fusable=False)); wait(v, tk)
            v.tensor_tensor(g3[0:1, 8:11], g3[0:1, 8:11], g3[0:1, 4:7], ALU.add)
            ch_v.inc(v.drain(fusable=False))
        V(r_gv)
        gate_m_tok = ch_v.tok()

        def r_gs(s):
            wait(s, gate_m_tok)
            s.activation(g3[0:1, 0:3], g3[0:1, 8:11], AF.Sigmoid)
            ch_a.inc(s.drain(fusable=False))
        A(r_gs)
        gate_sig_tok = ch_a.tok()
        tap("gsig", g3[0:1, 0:4], gate_sig_tok)

        def r_gv2(v):
            wait(v, gate_sig_tok)
            v.tensor_scalar(g3[0:1, 12:13], g3[0:1, 0:1], -1.0, 1.0, ALU.mult, ALU.add)
            tk = ch_v.inc(v.drain(fusable=False)); wait(v, tk)
            v.tensor_scalar(g3[0:1, 13:14], g3[0:1, 1:2], -1.0, None, ALU.mult)
            ch_v.inc(v.drain(fusable=False))
        V(r_gv2)
        gate_c_tok = ch_v.tok()

        def r_bc(t):
            wait(t, gate_c_tok)
            wait(t, gz_tok)
            bwait(t, 5)
            li = t.matmul(PS[5][:, 0:2], ones1[:, :], g3[0:1, 12:14],
                          start=True, stop=True)
            ch_t.inc(li)
        T(r_bc)
        bc_tok = ch_t.tok()

        def r_bce(v):
            wait(v, bc_tok)
            v.tensor_copy(small[:, 48:50], PS[5][:, 0:2])
            ch_v.inc(v.drain(fusable=False))
        V(r_bce)
        coef_tok = ch_v.tok()
        bset(5, coef_tok)
        C_OMA = small[:, 48:49]
        C_NLR = small[:, 49:50]

        upd_toks = []
        for gsrc, wsrc, wdst, grt, sfin in [
                (G0, WT["w0"], W0N, "g0r", [blk_done[NBLK - 1], spq[NBLK - 1],
                                            spk[NBLK - 1]]),
                (G1, WT["w1"], W1N, "g1r", [blk_done[NBLK - 1], spv[NBLK - 1]])]:
            def on_us(s, wsrc=wsrc, wdst=wdst, sfin=sfin):
                wait(s, coef_tok)
                for tk in sfin:
                    wait(s, tk)
                s.activation(wdst[:, 0:4096], wsrc[:, 0:4096], AF.Identity, scale=C_OMA)
                s.activation(wdst[:, 4096:8192], wsrc[:, 4096:8192], AF.Identity, scale=C_OMA)
                ch_a.inc(s.drain(fusable=False))
            A(on_us)
            sc_tok = ch_a.tok()

            def on_uv(v, gsrc=gsrc, wdst=wdst, sc_tok=sc_tok, grt=grt):
                wait(v, sc_tok)
                for c in range(4):
                    wait(v, cc_tok[grt + str(c)])
                    sl = slice(c * 2048, (c + 1) * 2048)
                    v.tensor_scalar(gsrc[:, sl], gsrc[:, sl], GRAD_CLIP, -GRAD_CLIP,
                                    ALU.min, ALU.max)
                    tk = ch_v.inc(v.drain(fusable=False)); wait(v, tk)
                    v.scalar_tensor_tensor(wdst[:, sl], gsrc[:, sl], C_NLR, wdst[:, sl],
                                           ALU.mult, ALU.add)
                    tk = ch_v.inc(v.drain(fusable=False)); wait(v, tk)
                upd_toks.append(tk)
            V(on_uv)
        w0n_tok, w1n_tok = upd_toks
        tap("w0n", W0N[:, :], w0n_tok)

        if phases < 4:
            finalize_partial()
            return nc

        q2t = [None] * NBLK
        x2t = [None] * NBLK
        res_sp = [None] * NBLK
        ln2_sp = [None] * NBLK
        logit_t = [None] * NBLK
        u_last = [None] * NBLK
        res_last = [None] * NBLK
        ln2_toks = [None] * NBLK
        ln2tm_sp = [None] * NBLK

        HUGE = 1.0e6
        SCQ = DmaQ(ctx, nc, "SCQ")
        bnd_regs = {}

        def setup_bnd(g):
            r1 = g.alloc_register("bnd_idx")
            g.reg_mov(r1, NEXP * 512 - 1)
            r2 = g.alloc_register("bnd_tok")
            g.reg_mov(r2, TC - 1)
            bnd_regs["idx"] = r1
            bnd_regs["tok"] = r2
        G(setup_bnd)

        QLQ = [Q2Q, KVQ[0], KVQ[1], XQ]
        X2Q = [GBQ, DmaQ(ctx, nc, "X2Q1"), DmaQ(ctx, nc, "X2Q2"),
               DmaQ(ctx, nc, "X2Q3")]
        JF2 = f32a(24, 1024)
        MKB = bfa(28, 256)
        routeA_fin = [None] * NBLK
        routeB_fin = [None]
        rk_base_tok = [None, None]
        sc_last = [None]
        scq_slot = [None] * 4
        PAIRJ = [640, 656, 672, 688]

        def v3(ap):
            return ap.rearrange("p (c e) -> p c e", e=32)

        def bc3(ap):
            return ap.unsqueeze(2).broadcast_to([128, 4, 32])

        def route_soft(b):
            sA = small[:, 40:44]
            sR = small[:, 44:48]
            sM2 = small[:, 52:56]
            E = JF2[:, 0:128]
            M1 = JF2[:, 128:256]
            T = JF2[:, 256:384]
            M2 = JF2[:, 384:512]
            W = WMA[:, b * 128:(b + 1) * 128]
            L = LOGI[:, b * 128:(b + 1) * 128]
            neg_tok = [None]
            exp_tok = [None]

            def on_a(v, b=b):
                wait(v, logit_t[b])
                if b == 0:
                    wait(v, cast_toks["iota32"])
                    wait(v, cast_toks["iotac"])
                    wait(v, cast_toks["tokc4"])
                else:
                    wait(v, routeB_fin[0])
                v.tensor_reduce(sA, v3(L), AX, ALU.max)
                tk = ch_v.inc(v.drain(fusable=False)); wait(v, tk)
                v.tensor_scalar_mul(sA, sA, -1.0)
                tk = ch_v.inc(v.drain(fusable=False)); wait(v, tk)
                v.tensor_scalar(small[:, 60:64], tokc4[:, :], 1.0, float(b * 512),
                                ALU.mult, ALU.add)
                neg_tok[0] = ch_v.inc(v.drain(fusable=False))
            V(on_a)

            def on_exp(sa, b=b):
                wait(sa, neg_tok[0])
                for c in range(4):
                    sa.activation(JF2[:, c * 32:(c + 1) * 32],
                                  LOGI[:, (b * 4 + c) * 32:(b * 4 + c + 1) * 32],
                                  AF.Exp, bias=small[:, 40 + c:41 + c], scale=1.0)
                exp_tok[0] = ch_a.inc(sa.drain(fusable=False))
            A(on_exp)

            def on_sm(v, b=b):
                wait(v, exp_tok[0])
                v.tensor_reduce(sR, v3(E), AX, ALU.add)
                tk = ch_v.inc(v.drain(fusable=False)); wait(v, tk)
                v.reciprocal(sR, sR)
                tk = ch_v.inc(v.drain(fusable=False)); wait(v, tk)
                v.tensor_tensor(v3(W), v3(E), bc3(sR), ALU.mult)
                tk = ch_v.inc(v.drain(fusable=False)); wait(v, tk)
                v.tensor_reduce(sA, v3(W), AX, ALU.max)
                tk = ch_v.inc(v.drain(fusable=False)); wait(v, tk)
                v.tensor_tensor(v3(M1), v3(W), bc3(sA), ALU.is_ge)
                tk = ch_v.inc(v.drain(fusable=False)); wait(v, tk)
                v.tensor_tensor(v3(T), v3(W), bc3(sA), ALU.is_lt)
                tk = ch_v.inc(v.drain(fusable=False)); wait(v, tk)
                v.tensor_tensor(T, T, W, ALU.mult)
                tk = ch_v.inc(v.drain(fusable=False)); wait(v, tk)
                v.tensor_reduce(sM2, v3(T), AX, ALU.max)
                tk = ch_v.inc(v.drain(fusable=False)); wait(v, tk)
                v.tensor_tensor(v3(T), v3(W), bc3(sM2), ALU.is_ge)
                tk = ch_v.inc(v.drain(fusable=False)); wait(v, tk)
                v.tensor_tensor(M2, T, M1, ALU.subtract)
                tk = ch_v.inc(v.drain(fusable=False)); wait(v, tk)
                v.tensor_tensor(W, W, T, ALU.mult)
                routeA_fin[b] = ch_v.inc(v.drain(fusable=False))
            V(on_sm)

        def route_ranks(b):
            for r in range(2):
                mcol = JF2[:, 128:256] if r == 0 else JF2[:, 384:512]
                mk = MKB[:, r * 128:(r + 1) * 128]
                mb_tok = [None]
                cn_tok = [None]
                cv_tok = [None]
                br_tok = [None]
                rk_tok = [None]
                sl_tok = [None]

                def on_mb(v, b=b, r=r, mcol=mcol, mk=mk):
                    wait(v, routeA_fin[b])
                    wait(v, rk_base_tok[r])
                    mb_tok[0] = ch_v.inc(v.tensor_copy(mk, mcol))
                V(on_mb)

                def on_mm1(t, b=b, r=r, mk=mk):
                    wait(t, mb_tok[0])
                    if b == 0 and r == 0:
                        wait(t, cast_toks["lt"])
                        wait(t, gz_tok)
                    bwait(t, 4 + r)
                    cn_tok[0] = ch_t.inc(t.matmul(
                        PS[4 + r][0:1, 0:128], ones_bf[:, :], mk,
                        start=True, stop=True))
                    bwait(t, 2 + r)
                    t.matmul(PS[2 + r][:, 0:128], lt128[:, :], mk,
                             start=True, stop=False)
                T(on_mm1)

                def on_cv(s, r=r):
                    wait(s, cn_tok[0])
                    s.copy(JF2[0:1, 712:840], PS[4 + r][0:1, 0:128])
                    cv_tok[0] = ch_a.inc(s.drain(fusable=False))
                A(on_cv)
                bset(4 + r, cv_tok[0])

                def on_br(v, r=r):
                    wait(v, cv_tok[0])
                    CNT = JF2[0:1, 712:840]
                    BR4 = JF2[0:1, 840:968]
                    bro = baserow[0:1, r * 32:(r + 1) * 32]
                    v.tensor_copy(BR4[0:1, 0:32], bro)
                    tk = ch_v.inc(v.drain(fusable=False)); wait(v, tk)
                    v.tensor_tensor(BR4[0:1, 32:64], BR4[0:1, 0:32],
                                    CNT[0:1, 0:32], ALU.add)
                    tk = ch_v.inc(v.drain(fusable=False)); wait(v, tk)
                    v.tensor_tensor(BR4[0:1, 64:96], BR4[0:1, 32:64],
                                    CNT[0:1, 32:64], ALU.add)
                    tk = ch_v.inc(v.drain(fusable=False)); wait(v, tk)
                    v.tensor_tensor(BR4[0:1, 96:128], BR4[0:1, 64:96],
                                    CNT[0:1, 64:96], ALU.add)
                    tk = ch_v.inc(v.drain(fusable=False)); wait(v, tk)
                    v.tensor_tensor(bro, BR4[0:1, 96:128],
                                    CNT[0:1, 96:128], ALU.add)
                    tk = ch_v.inc(v.drain(fusable=False)); wait(v, tk)
                    v.tensor_copy(br4b[0:1, r * 128:(r + 1) * 128], BR4[0:1, :])
                    br_tok[0] = ch_v.inc(v.drain(fusable=False))
                V(on_br)

                def on_mm2(t, r=r):
                    wait(t, br_tok[0])
                    rk_tok[0] = ch_t.inc(t.matmul(
                        PS[2 + r][:, 0:128], ones1b[:, :],
                        br4b[0:1, r * 128:(r + 1) * 128], start=False, stop=True))
                    rk_base_tok[r] = rk_tok[0]
                T(on_mm2)

                def on_sl(v, b=b, r=r, mcol=mcol):
                    wait(v, rk_tok[0])
                    SL = JF2[:, 512:640]
                    T3 = JF2[:, 256:384]
                    v.tensor_copy(SL, PS[2 + r][:, 0:128])
                    tk = ch_v.inc(v.drain(fusable=False)); wait(v, tk)
                    v.tensor_scalar(T3, SL, 255.5, None, ALU.is_gt)
                    tk = ch_v.inc(v.drain(fusable=False)); wait(v, tk)
                    v.scalar_tensor_tensor(SL, T3, HUGE, SL, ALU.mult, ALU.add)
                    tk = ch_v.inc(v.drain(fusable=False)); wait(v, tk)
                    v.tensor_scalar(T3, mcol, -1.0, 1.0, ALU.mult, ALU.add)
                    tk = ch_v.inc(v.drain(fusable=False)); wait(v, tk)
                    v.scalar_tensor_tensor(SL, T3, HUGE, SL, ALU.mult, ALU.add)
                    tk = ch_v.inc(v.drain(fusable=False)); wait(v, tk)
                    v.tensor_reduce(small[:, 52:56], v3(SL), AX, ALU.min)
                    tk = ch_v.inc(v.drain(fusable=False)); wait(v, tk)
                    v.tensor_tensor(v3(T3), v3(mcol),
                                    iota32[:, :].unsqueeze(1).broadcast_to(
                                        [128, 4, 32]), ALU.mult)
                    tk = ch_v.inc(v.drain(fusable=False)); wait(v, tk)
                    v.tensor_reduce(small[:, 56:60], v3(T3), AX, ALU.add)
                    tk = ch_v.inc(v.drain(fusable=False)); wait(v, tk)
                    v.tensor_tensor(T3, WMA[:, b * 128:(b + 1) * 128], mcol,
                                    ALU.mult)
                    tk = ch_v.inc(v.drain(fusable=False)); wait(v, tk)
                    v.tensor_reduce(small[:, 48:52], v3(T3), AX, ALU.add)
                    tk = ch_v.inc(v.drain(fusable=False)); wait(v, tk)
                    SLG = JF2[:, 968:972]
                    v.scalar_tensor_tensor(SLG, small[:, 56:60], 512.0,
                                           small[:, 52:56], ALU.mult, ALU.add)
                    tk = ch_v.inc(v.drain(fusable=False)); wait(v, tk)
                    if r:
                        v.tensor_scalar_add(SLG, SLG, 256.0)
                        tk = ch_v.inc(v.drain(fusable=False)); wait(v, tk)
                    sidx = (b * 2 + r) % 4
                    pc = PAIRJ[sidx]
                    wait(v, scq_slot[sidx])
                    P3 = JF2[:, pc:pc + 8].rearrange("p (c two) -> p c two", two=2)
                    v.tensor_copy(P3[:, :, 0:1], small[:, 60:64].unsqueeze(2))
                    tk = ch_v.inc(v.drain(fusable=False)); wait(v, tk)
                    v.tensor_copy(P3[:, :, 1:2], small[:, 48:52].unsqueeze(2))
                    tk = ch_v.inc(v.drain(fusable=False)); wait(v, tk)
                    v.tensor_copy(JF2[:, pc + 8:pc + 12].bitcast(mybir.dt.uint32),
                                  SLG)
                    sl_tok[0] = ch_v.inc(v.drain(fusable=False))
                    routeB_fin[0] = sl_tok[0]
                V(on_sl)
                bset(2 + r, sl_tok[0])

                def on_sc(g, b=b, r=r):
                    wait(g, sl_tok[0])
                    sidx = (b * 2 + r) % 4
                    pc = PAIRJ[sidx]
                    for c in range(4):
                        sc_last[0] = SCQ.dma_indirect_scatter(
                            g, idxstage,
                            JF2[:, pc + 8 + c:pc + 9 + c].bitcast(mybir.dt.uint32),
                            JF2[:, pc + c * 2:pc + c * 2 + 2], bnd_regs["idx"])
                    scq_slot[sidx] = sc_last[0]
                G(on_sc)

        for b in range(NBLK):
            def on_q2(sy, b=b):
                if b == 0:
                    wait(sy, blk_done[NBLK - 1])
                else:
                    wait(sy, u_last[b - 1])
                wait(sy, spq[b])
                wait(sy, TAPQ.tok())
                tks = []
                for j in range(4):
                    tks.append(QLQ[j].dma(
                        sy, Q2[b % 2][:, j * 1024:(j + 1) * 1024].rearrange(
                            "p (c t) -> p c t", c=2),
                        sp_view(qsp, b)[:, 2 * j:2 * j + 2, :]))
                q2t[b] = _multi(tks)
            SY(on_q2)

            u_mm = [None] * KC
            u_ev = [None] * KC
            for oc in range(KC):
                bank = PS[oc % 2]

                def on_t(t, oc=oc, bank=bank, b=b):
                    if oc == 0:
                        wait(t, q2t[b])
                        if b == 0:
                            wait(t, w0n_tok)
                    bwait(t, oc % 2)
                    for kc in range(KC):
                        li = t.matmul(bank[:, :],
                                      W0N[:, kc * 1024 + oc * 128:kc * 1024 + oc * 128 + 128],
                                      Q2[b % 2][:, kc * 512:(kc + 1) * 512],
                                      start=(kc == 0), stop=(kc == KC - 1))
                    u_mm[oc] = ch_t.inc(li)
                T(on_t)

                def on_s(v, oc=oc, bank=bank):
                    wait(v, u_mm[oc])
                    u_ev[oc] = ch_v.inc(v.tensor_copy(
                        U2[:, oc * 512:(oc + 1) * 512], bank[:, :]))
                V(on_s)
                bset(oc % 2, u_ev[oc])
            u_last[b] = u_mm[-1]

            def on_sg2(sa, dep=u_ev[-1]):
                wait(sa, dep)
                ch_a.inc(sa.activation(SG2[:, :], U2[:, :], AF.Sigmoid))
            A(on_sg2)
            sg2_tok = ch_a.tok()

            def on_u(v):
                wait(v, sg2_tok)
                v.tensor_tensor(U2[:, :], U2[:, :], SG2[:, :], ALU.mult)
                ch_v.inc(v.drain(fusable=False))
            V(on_u)
            u_tok = ch_v.tok()

            if phases >= 5 and b > 0:
                route_ranks(b - 1)

            def on_x2(sa, b=b):
                if b == 0:
                    wait(sa, w0n_tok)
                else:
                    wait(sa, res_last[b - 1])
                tks = []
                for tc in range(4):
                    tks.append(X2Q[tc].dma(
                        sa, X2[:, tc * 1024:(tc + 1) * 1024],
                        x_d[b * TB + tc * 128:b * TB + (tc + 1) * 128, :]))
                x2t[b] = _multi(tks)
            A(on_x2)

            h2_tok, _ = proj_fm(W1N, U2, H2, u_tok, w1n_tok if b == 0 else None)

            at_mm = [None] * 8
            at_ev = [None] * 8
            for i, (tc, dn) in enumerate([(tc, dn) for tc in range(4) for dn in range(2)]):
                bank = PS[2 + i % 2]

                def on_t(t, i=i, tc=tc, dn=dn, bank=bank, b=b):
                    if i == 0:
                        wait(t, h2_tok)
                        if b == 0:
                            wait(t, cast_toks["woutT"])
                    bwait(t, 2 + i % 2)
                    for kc in range(KC):
                        li = t.matmul(bank[:, :],
                                      H2[:, kc * 512 + tc * 128:kc * 512 + tc * 128 + 128],
                                      WOUTT[:, kc * 1024 + dn * 512:
                                            kc * 1024 + (dn + 1) * 512],
                                      start=(kc == 0), stop=(kc == KC - 1))
                    at_mm[i] = ch_t.inc(li)
                T(on_t)

                def on_v(v, i=i, tc=tc, dn=dn, bank=bank, b=b):
                    wait(v, at_mm[i])
                    if i == 0:
                        wait(v, x2t[b])
                        if b > 0:
                            wait(v, ln2_toks[b - 1])
                            wait(v, res_sp[b - 1])
                    sl = slice(tc * 1024 + dn * 512, tc * 1024 + (dn + 1) * 512)
                    v.tensor_tensor(RES[:, sl], bank[:, :], X2[:, sl], ALU.add)
                    at_ev[i] = ch_v.inc(v.drain(fusable=False))
                V(on_v)
                bset(2 + i % 2, at_ev[i])
            res_tok = at_ev[-1]
            res_last[b] = res_tok

            def on_rsp(sy, b=b, res_tok=res_tok):
                wait(sy, res_tok)
                res_sp[b] = RSQ.dma(sy, residsp[:, b * 4096:(b + 1) * 4096], RES[:, :])
            SY(on_rsp)

            ln2_guard = [res_tok]
            if b > 0:
                ln2_guard.append(logit_t[b - 1])
                ln2_guard.append(ln2_sp[b - 1])
            ln2_tok = emit_ln(RES, [HIT, LN2F32], JNK2, _multi(ln2_guard))
            ln2_toks[b] = ln2_tok

            def on_lo(v, ln2_tok=ln2_tok):
                wait(v, ln2_tok)
                v.tensor_tensor(LOT[:, :], LN2F32[:, :], HIT[:, :], ALU.subtract)
                ch_v.inc(v.drain(fusable=False))
            V(on_lo)
            lo_tok = ch_v.tok()

            hif_tok = transpose_block(HIT, HIF, True, ln2_tok)
            lof_tok = transpose_block(LOT, LOF, True, lo_tok)

            def on_lsp(sy, b=b, hif_tok=hif_tok, ln2_tok=ln2_tok):
                wait(sy, ln2_tok)
                ln2tm_sp[b] = LNQ.dma(
                    sy, ln2tmsp[b * TB:(b + 1) * TB, :].rearrange("(tc p) d -> p tc d", p=128),
                    HIT[:, :].rearrange("p (tc d) -> p tc d", tc=4))
                ln2_sp[b] = ln2tm_sp[b]
            SY(on_lsp)

            lg_mm = [None] * 4
            lg_ev = [None] * 4
            for tc in range(4):
                bank = PS[4 + tc % 2]

                def on_t(t, tc=tc, bank=bank, b=b):
                    if tc == 0:
                        wait(t, hif_tok)
                        wait(t, lof_tok)
                        if b == 0:
                            wait(t, rt_tok)
                    bwait(t, 4 + tc % 2)
                    for kc in range(KC):
                        sl = slice(kc * 512 + tc * 128, kc * 512 + tc * 128 + 128)
                        t.matmul(bank[:, 0:32], HIF[:, sl], rthi[:, kc * 32:(kc + 1) * 32],
                                 start=(kc == 0), stop=False)
                        t.matmul(bank[:, 32:64], HIF[:, sl], rtlo[:, kc * 32:(kc + 1) * 32],
                                 start=False, stop=False)
                        li = t.matmul(bank[:, 0:32], LOF[:, sl],
                                      rthi[:, kc * 32:(kc + 1) * 32],
                                      start=False, stop=(kc == KC - 1))
                    lg_mm[tc] = ch_t.inc(li)
                T(on_t)

                lg_cp = [None]

                def on_sc(s, tc=tc, bank=bank, b=b):
                    wait(s, lg_mm[tc])
                    cc = b * 4 + tc
                    s.copy(LOGI[:, cc * 32:(cc + 1) * 32], bank[:, 0:32])
                    lg_cp[0] = ch_a.inc(s.drain(fusable=False))
                A(on_sc)

                def on_v(v, tc=tc, bank=bank, b=b):
                    wait(v, lg_cp[0])
                    cc = b * 4 + tc
                    v.tensor_tensor(LOGI[:, cc * 32:(cc + 1) * 32],
                                    LOGI[:, cc * 32:(cc + 1) * 32],
                                    bank[:, 32:64], ALU.add)
                    lg_ev[tc] = ch_v.inc(v.drain(fusable=False))
                V(on_v)
                bset(4 + tc % 2, lg_ev[tc])
            logit_t[b] = lg_ev[-1]

            if phases >= 5:
                route_soft(b)

            if b == 0:
                tap("u2", U2[:, :], u_tok)
                tap("h2", H2[:, :], h2_tok)
                tap("resid", RES[:, :], res_tok)
                tap("hit", HIT[:, :], ln2_tok)

        tap("logits", LOGI[:, :], logit_t[NBLK - 1])

        if phases < 5:
            finalize_partial()
            return nc

        route_ranks(NBLK - 1)
        tap("wma", WMA[:, :], routeA_fin[NBLK - 1])
        bookkeeping_fin = sc_last[0]

        if phases < 6:
            finalize_partial()
            return nc

        EXS = [bfa(0, KC * D), bfa(16, KC * D), bfa(32, KC * D), bfa(144, KC * D)]
        EXC = [EXQ[0], EXQ[1], WQ, CCL]
        ex_t = [None] * NEXP
        IQ = [LSQ[0], LSQ[1]]
        GQ = [ACQ[0], ACQ[1]]

        def exs_load(g, ei, guard=None):
            if guard is not None:
                wait(g, guard)
            ex_t[ei] = EXC[ei % 4].dma(
                g, EXS[ei % 4][:, :].rearrange("p (c o) -> p c o", c=KC),
                experts_d[ei * D:(ei + 1) * D, :].rearrange("(c p) o -> p c o", p=128))

        ROT = 4
        IDXS4 = [f32a(96, 256), f32a(97, 256), f32a(98, 256), f32a(99, 256)]
        IDXU4 = [f32a(100, 256), f32a(101, 256), f32a(102, 256), f32a(103, 256)]
        GXT4 = [bfa(104, 4096), bfa(112, 4096), bfa(120, 4096), bfa(128, 4096)]
        GXF2 = [bfa(64, 4096), bfa(72, 4096)]
        YSB2 = [bfa(80, 4096), bfa(88, 4096)]
        SQC = [DmaQ(ctx, nc, "SQC0"), DmaQ(ctx, nc, "SQC1")]
        idx_t = [None] * ROT
        idxu_t = [None] * ROT
        gxt_t = [None] * ROT
        tr_fin = {}
        mm_fin = {}
        ev_fin = {}
        sc_fin = {}

        def S0(e):
            r4 = e % ROT

            def on_idx(sy, e=e, r4=r4):
                if e < 2:
                    wait(sy, bookkeeping_fin)
                    wait(sy, SCQ.tok())
                    for b in range(NBLK):
                        wait(sy, ln2tm_sp[b])
                    wait(sy, res_sp[NBLK - 1])
                if e - ROT >= 0:
                    wait(sy, sc_fin[e - ROT])
                idx_t[r4] = IQ[e % 2].dma(
                    sy, IDXS4[r4][:, 0:8].rearrange("p (c two) -> p c two", two=2),
                    idxstage[e * 512:(e + 1) * 512, :].rearrange(
                        "(c p) two -> p c two", p=128))
            SY(on_idx)

            def on_cast(v, e=e, r4=r4):
                wait(v, idx_t[r4])
                idxu_t[r4] = ch_v.inc(v.tensor_copy(
                    IDXU4[r4][:, 0:8].bitcast(mybir.dt.uint32), IDXS4[r4][:, 0:8]))
            V(on_cast)

        def S2(e):
            r4 = e % ROT

            def on_gather(g, e=e, r4=r4):
                wait(g, idxu_t[r4])
                if e - ROT + 1 > 0 and (e - ROT) in tr_fin:
                    wait(g, tr_fin[e - ROT])
                for c in range(4):
                    gxt_t[r4] = GQ[e % 2].dma_indirect_gather(
                        g, GXT4[r4][:, c * 1024:(c + 1) * 1024], ln2tmsp,
                        IDXU4[r4][:, 0:8].bitcast(mybir.dt.uint32).rearrange(
                            "p (c two) -> p c two", two=2)[:, c:c + 1, 0:1],
                        bnd_regs["tok"])
            G(on_gather)

        def S345(e):
            r4 = e % ROT
            s2 = e % 2
            tr_toks = [None] * 8
            for i in range(8):
                bank = PT[i % 2]

                def on_tr(t, i=i, r4=r4, s2=s2, e=e, bank=bank):
                    if i == 0:
                        wait(t, gxt_t[r4])
                        if (e - 2) in mm_fin:
                            wait(t, mm_fin[e - 2])
                    bwait(t, 6 + i % 2)
                    for tc in range(4):
                        li = t.transpose(bank[:, tc * 128:tc * 128 + 128],
                                         GXT4[r4][:, tc * 1024 + i * 128:
                                                  tc * 1024 + i * 128 + 128],
                                         ident[:, :])
                    tr_toks[i] = ch_t.inc(li)
                T(on_tr)

                def on_tre(v, i=i, s2=s2, e=e, bank=bank):
                    wait(v, tr_toks[i])
                    tr_fin[e] = ch_v.inc(v.tensor_copy(
                        GXF2[s2][:, i * 512:(i + 1) * 512], bank[:, 0:512]))
                V(on_tre)
                bset(6 + i % 2, tr_fin[e])

            mm_toks = [None] * 8
            ev_toks = [None] * 8
            for gi, (ts, on) in enumerate([(ts, on) for ts in range(4) for on in range(2)]):
                bank = PS[gi % 6]

                def on_t(t, gi=gi, ts=ts, on=on, bank=bank, s2=s2, e=e):
                    if gi == 0:
                        wait(t, tr_fin[e])
                        wait(t, ex_t[e])
                        if (e - 2) in ev_fin:
                            wait(t, ev_fin[e - 2])
                    bwait(t, gi % 6)
                    for kc in range(KC):
                        li = t.matmul(bank[:, :],
                                      GXF2[s2][:, kc * 512 + ts * 128:
                                               kc * 512 + ts * 128 + 128],
                                      EXS[e % 4][:, kc * 1024 + on * 512:
                                                 kc * 1024 + (on + 1) * 512],
                                      start=(kc == 0), stop=(kc == KC - 1))
                    mm_toks[gi] = ch_t.inc(li)
                T(on_t)

                def on_ev(v, gi=gi, ts=ts, on=on, bank=bank, s2=s2, e=e, r4=r4):
                    wait(v, mm_toks[gi])
                    if gi == 0 and (e - 2) in sc_fin:
                        wait(v, sc_fin[e - 2])
                    wcol = IDXS4[r4][:, 0:8].rearrange("p (c two) -> p c two", two=2)[
                        :, ts:ts + 1, 1:2]
                    ev_toks[gi] = ch_v.inc(v.tensor_scalar(
                        YSB2[s2][:, ts * 1024 + on * 512:ts * 1024 + (on + 1) * 512],
                        bank[:, :], wcol, None, ALU.mult))
                V(on_ev)
                bset(gi % 6, ev_toks[gi])
            mm_fin[e] = mm_toks[-1]
            ev_fin[e] = ev_toks[-1]

            def on_scat(g, e=e, s2=s2, r4=r4):
                wait(g, ev_fin[e])
                for c in range(4):
                    dstbuf = bufA if c < 2 else bufB
                    sc_fin[e] = SQC[s2].dma_indirect_scatter(
                        g, dstbuf,
                        IDXU4[r4][:, 0:8].bitcast(mybir.dt.uint32).rearrange(
                            "p (c two) -> p c two", two=2)[:, c:c + 1, 0:1],
                        YSB2[s2][:, c * 1024:(c + 1) * 1024], bnd_regs["tok"])
            G(on_scat)

        def ph3_pre(g):
            wait(g, logit_t[NBLK - 1])
            wait(g, bookkeeping_fin)
            wait(g, SCQ.tok())
            for ei in range(4):
                exs_load(g, ei)
        G(ph3_pre)

        S0(0)
        S0(1)
        S2(0)
        for e in range(NEXP):
            if e + 2 < NEXP:
                S0(e + 2)
            if e + 1 < NEXP:
                S2(e + 1)
            S345(e)
            if e + 4 < NEXP:
                def on_pf2(g, e=e):
                    wait(g, mm_fin[e])
                    exs_load(g, e + 4)
                G(on_pf2)

        moe_fin = [sc_fin[NEXP - 2], sc_fin[NEXP - 1]]

        if phases < 7:
            finalize_partial()
            return nc

        st_toks = [None] * NBLK
        p4_guard = [None, None]
        AB4 = [bfa(100, 4096), bfa(108, 4096), bfa(116, 4096), bfa(124, 4096)]
        for b in range(NBLK):
            sl = b % 2
            ld_tok = [None]

            def on_ld(sy, b=b, sl=sl):
                if b == 0:
                    wait(sy, moe_fin[0])
                    wait(sy, moe_fin[1])
                    wait(sy, TAPQ.tok())
                if p4_guard[sl] is not None:
                    wait(sy, p4_guard[sl])
                P4Q[sl].dma(sy, R4[sl][:, :], residsp[:, b * 4096:(b + 1) * 4096])
                P4Q[sl].dma(sy, AB4[sl][:, :].rearrange("p (tc d) -> p tc d", tc=4),
                            bufA[b * TB:(b + 1) * TB, :].rearrange(
                                "(tc p) d -> p tc d", p=128))
                ld_tok[0] = P4Q[sl].dma(
                    sy, AB4[2 + sl][:, :].rearrange("p (tc d) -> p tc d", tc=4),
                    bufB[b * TB:(b + 1) * TB, :].rearrange("(tc p) d -> p tc d", p=128))
            SY(on_ld)

            add_tok = [None]

            def on_add(v, b=b, sl=sl):
                wait(v, ld_tok[0])
                wait(v, STQ[sl].tok())
                v.tensor_tensor(O4[sl][:, :], R4[sl][:, :], AB4[sl][:, :], ALU.add)
                tk = ch_v.inc(v.drain(fusable=False)); wait(v, tk)
                v.tensor_tensor(O4[sl][:, :], O4[sl][:, :], AB4[2 + sl][:, :], ALU.add)
                add_tok[0] = ch_v.inc(v.drain(fusable=False))
                p4_guard[sl] = add_tok[0]
            V(on_add)

            def on_st(sa, b=b, sl=sl):
                wait(sa, add_tok[0])
                st_toks[b] = STQ[sl].dma(
                    sa, out_d[b * TB:(b + 1) * TB, :].rearrange("(tc p) d -> p tc d", p=128),
                    O4[sl][:, :].rearrange("p (tc d) -> p tc d", tc=4))
            A(on_st)

        def fin(s):
            wait(s, STQ[0].tok())
            wait(s, STQ[1].tok())
            wait(s, TAPQ.tok())
        A(fin)

    return nc


def kernel(L0_attn_x, L0_attn_Wk, L0_attn_Wv, L0_attn_Wq, L0_attn_Wout,
           L0_attn_W0, L0_attn_W1, L0_attn_Wg_decay, L0_attn_bg_decay,
           L0_attn_Wg_lr, L0_attn_bg_lr, L0_attn_Wg_mom, L0_attn_bg_mom,
           L0_ffn_router_w, L0_ffn_expert_weights):
    B, S, DD = L0_attn_x.shape
    xf = np.ascontiguousarray(np.asarray(L0_attn_x, dtype=np.float32).reshape(B * S, DD))
    wg3 = np.ascontiguousarray(np.stack([
        np.asarray(L0_attn_Wg_decay)[0], np.asarray(L0_attn_Wg_lr)[0],
        np.asarray(L0_attn_Wg_mom)[0]], axis=0).astype(np.float32))
    bg3 = np.zeros((1, 4), dtype=np.float32)
    bg3[0, 0] = np.asarray(L0_attn_bg_decay).reshape(())
    bg3[0, 1] = np.asarray(L0_attn_bg_lr).reshape(())
    bg3[0, 2] = np.asarray(L0_attn_bg_mom).reshape(())
    ident = np.eye(128, dtype=BFNP)
    oh = np.zeros((NEXP, NEXP, 128), dtype=BFNP)
    for e_ in range(NEXP):
        oh[e_, e_, :] = 1
    oh = oh.reshape(NEXP, NEXP * 128)
    lt = np.triu(np.ones((128, 128), dtype=np.float32), k=1).astype(BFNP)
    iota32 = np.broadcast_to(np.arange(NEXP, dtype=np.float32), (128, NEXP)).copy()
    iotac = np.arange(128, dtype=np.float32).reshape(128, 1)
    tokc4 = (np.arange(128, dtype=np.float32).reshape(128, 1)
             + 128.0 * np.arange(4, dtype=np.float32).reshape(1, 4))
    experts = np.ascontiguousarray(
        np.asarray(L0_ffn_expert_weights, dtype=np.float32).reshape(NEXP * D, D))

    nc = build(taps=tuple(DEBUG_TAPS))
    base = {
        "wkt": np.ascontiguousarray(np.asarray(L0_attn_Wk, dtype=np.float32).T),
        "wvt": np.ascontiguousarray(np.asarray(L0_attn_Wv, dtype=np.float32).T),
        "wqt": np.ascontiguousarray(np.asarray(L0_attn_Wq, dtype=np.float32).T),
        "woutt": np.ascontiguousarray(np.asarray(L0_attn_Wout, dtype=np.float32).T),
        "w0t": np.ascontiguousarray(np.asarray(L0_attn_W0, dtype=np.float32).T),
        "w1t": np.ascontiguousarray(np.asarray(L0_attn_W1, dtype=np.float32).T),
        "w1": np.ascontiguousarray(np.asarray(L0_attn_W1, dtype=np.float32)),
        "wg3": wg3, "bg3": bg3,
        "router": np.ascontiguousarray(np.asarray(L0_ffn_router_w, dtype=np.float32)),
        "experts": experts, "identin": ident, "oh": oh,
        "lt128d": lt, "iota32d": iota32, "iotacd": iotac, "tokc4d": tokc4,
    }
    in_maps = [dict(base, x=xf[c * TC:(c + 1) * TC]) for c in range(8)]
    res = run_bass_kernel_spmd(nc, in_maps, core_ids=list(range(8)))
    out = np.concatenate([res.results[c]["out"] for c in range(8)], axis=0)
    kernel.last_results = res
    return out.reshape(B, S, DD)


# version 19
# speedup vs baseline: 1.0858x; 1.0601x over previous
import numpy as np
import ml_dtypes
from contextlib import ExitStack
import concourse.bass as bass
import concourse.mybir as mybir
from concourse.bass_utils import run_bass_kernel_spmd
from concourse.bass import IndirectOffsetOnAxis as IOA

F32 = mybir.dt.float32
BF16 = mybir.dt.bfloat16
I32 = mybir.dt.int32
AF = mybir.ActivationFunctionType
ALU = mybir.AluOpType
AX = mybir.AxisListType.X
BFNP = ml_dtypes.bfloat16

D = 1024
TC = 4096
NBLK = 8
TB = 512
KC = 8
NEXP = 32
EGRP = 2
NGRP = NEXP // EGRP
SCALE = 2.0 / 4096.0
ERR_CLIP = 5.0
GRAD_CLIP = 1.0
LN_EPS = 1e-5

DEBUG_TAPS = []

TAP_SHAPES = {
    "ln1": ([128, 4096], BF16), "kfm": ([128, 4096], BF16), "vtm": ([128, 4096], BF16),
    "qfm": ([128, 4096], BF16), "a1tm": ([128, 4096], BF16), "etm": ([128, 4096], BF16),
    "d1tm": ([128, 4096], BF16), "g0": ([128, 8192], F32), "g1": ([128, 8192], F32),
    "w0n": ([128, 8192], BF16), "resid": ([128, 4096], F32), "hit": ([128, 4096], BF16),
    "logits": ([128, 1024], F32), "wma": ([128, 1024], F32), "gacc": ([1, 4], F32),
    "gsig": ([1, 4], F32), "u2": ([128, 4096], BF16), "h2": ([128, 4096], BF16),
}


class Chan:
    def __init__(self, ctx, nc, name):
        self.h = ctx.enter_context(nc.semaphore(name))
        self.n = 0

    def inc(self, inst, k=1):
        inst.then_inc(self.h, k)
        self.n += k
        return (self.h, self.n)

    def tok(self):
        return (self.h, self.n)


def wait(eng, tok):
    if tok is None:
        return
    if isinstance(tok, _Multi):
        for t in tok.toks:
            wait(eng, t)
        return
    eng.wait_ge(tok[0], tok[1])


class _Multi:
    def __init__(self, toks):
        self.toks = [t for t in toks if t is not None]


def _multi(toks):
    return _Multi(toks)


class DmaQ:

    def __init__(self, ctx, nc, name):
        self.h = ctx.enter_context(nc.semaphore(name))
        self.n = 0

    def dma(self, eng, out, in_, **kw):
        if self.n:
            eng.wait_ge(self.h, self.n)
        inst = eng.dma_start(out, in_, **kw)
        inst.then_inc(self.h, 16)
        self.n += 16
        return (self.h, self.n)

    def dma_indirect_gather(self, eng, out, in_dram, idx_ap, bound):
        if self.n:
            eng.wait_ge(self.h, self.n)
        inst = eng.indirect_dma_start(
            out=out, out_offset=None, in_=in_dram,
            in_offset=IOA(axis=0, ap=idx_ap),
            bounds_check=bound, oob_is_err=False)
        inst.then_inc(self.h, 16)
        self.n += 16
        return (self.h, self.n)

    def dma_indirect_scatter(self, eng, out_dram, idx_ap, in_sb, bound):
        if self.n:
            eng.wait_ge(self.h, self.n)
        inst = eng.indirect_dma_start(
            out=out_dram, out_offset=IOA(axis=0, ap=idx_ap),
            in_=in_sb, in_offset=None,
            bounds_check=bound, oob_is_err=False)
        inst.then_inc(self.h, 16)
        self.n += 16
        return (self.h, self.n)

    def dma_tr(self, eng, out, in_):
        if self.n:
            eng.wait_ge(self.h, self.n)
        inst = eng.dma_start_transpose(out, in_)
        inst.then_inc(self.h, 16)
        self.n += 16
        return (self.h, self.n)

    def tok(self):
        return (self.h, self.n) if self.n else None


def build(taps=(), phases=7):
    nc = bass.Bass("TRN2", target_bir_lowering=False, debug=False)

    x_d = nc.dram_tensor("x", [TC, D], F32, kind="ExternalInput").ap()
    w_d = {w: nc.dram_tensor(w, [D, D], F32, kind="ExternalInput").ap()
           for w in ["w1", "wkt", "wvt", "wqt", "w0t", "w1t", "woutt"]}
    wg3_d = nc.dram_tensor("wg3", [3, D], F32, kind="ExternalInput").ap()
    bg3_d = nc.dram_tensor("bg3", [1, 4], F32, kind="ExternalInput").ap()
    router_d = nc.dram_tensor("router", [NEXP, D], F32, kind="ExternalInput").ap()
    experts_d = nc.dram_tensor("experts", [NEXP * D, D], F32, kind="ExternalInput").ap()
    ident_d = nc.dram_tensor("identin", [128, 128], BF16, kind="ExternalInput").ap()
    oh_d = nc.dram_tensor("oh", [NEXP, NEXP * 128], BF16, kind="ExternalInput").ap()
    lt_d = nc.dram_tensor("lt128d", [128, 128], BF16, kind="ExternalInput").ap()
    iota32_d = nc.dram_tensor("iota32d", [128, NEXP], F32, kind="ExternalInput").ap()
    iotac_d = nc.dram_tensor("iotacd", [128, 1], F32, kind="ExternalInput").ap()
    tokc4_d = nc.dram_tensor("tokc4d", [128, 4], F32, kind="ExternalInput").ap()
    out_d = nc.dram_tensor("out", [TC, D], F32, kind="ExternalOutput").ap()

    ksp = nc.dram_tensor("ksp", [128, KC * TC], BF16, kind="Internal").ap()
    qsp = nc.dram_tensor("qsp", [128, KC * TC], BF16, kind="Internal").ap()
    vsp = nc.dram_tensor("vsp", [128, (TC // 128) * D], BF16, kind="Internal").ap()
    ln2sp = nc.dram_tensor("ln2sp", [128, KC * TC], BF16, kind="Internal").ap()
    residsp = nc.dram_tensor("residsp", [128, (TC // 128) * D], F32, kind="Internal").ap()
    moeacc = nc.dram_tensor("moeacc", [TC, D], F32, kind="Internal").ap()
    ln2tmsp = nc.dram_tensor("ln2tmsp", [TC, D], BF16, kind="Internal").ap()
    idxstage = nc.dram_tensor("idxstage", [NEXP * 512, 2], F32, kind="Internal").ap()
    bufA = nc.dram_tensor("bufA", [TC, D], BF16, kind="Internal").ap()
    bufB = nc.dram_tensor("bufB", [TC, D], BF16, kind="Internal").ap()
    ccin = nc.dram_tensor("ccin", [128, 8192], F32, kind="Internal").ap()
    ccout = nc.dram_tensor("ccout", [128, 8192], F32, kind="Internal").ap()
    ccg_in = nc.dram_tensor("ccg_in", [1, 4], F32, kind="Internal").ap()
    ccg_out = nc.dram_tensor("ccg_out", [1, 4], F32, kind="Internal").ap()
    ccin2 = nc.dram_tensor("ccin2", [128, 8192], F32, kind="Internal").ap()
    ccout2 = nc.dram_tensor("ccout2", [128, 8192], F32, kind="Internal").ap()
    gbd = nc.dram_tensor("gbd", [1, 4], F32, kind="Internal").ap()

    tap_d = {t: nc.dram_tensor("tap_" + t, TAP_SHAPES[t][0], TAP_SHAPES[t][1],
                               kind="ExternalOutput").ap() for t in taps}

    with ExitStack() as ctx:
        en = ctx.enter_context
        ART = en(nc.sbuf_tensor("arena", [128, 102400], BF16))
        ARF = ART.bitcast(F32)
        ident = en(nc.sbuf_tensor("ident", [128, 128], BF16))
        rthi = en(nc.sbuf_tensor("rthi", [128, KC * NEXP], BF16))
        rtlo = en(nc.sbuf_tensor("rtlo", [128, KC * NEXP], BF16))
        wgt = en(nc.sbuf_tensor("wgt", [128, KC * 4], BF16))
        small = en(nc.sbuf_tensor("small", [128, 64], F32))
        stats = en(nc.sbuf_tensor("stats", [128, 32], F32))
        g3 = en(nc.sbuf_tensor("g3", [1, 16], F32))
        ones_bf = en(nc.sbuf_tensor("ones_bf", [128, 1], BF16))
        colsum_bf = en(nc.sbuf_tensor("colsum_bf", [128, 8], BF16))
        ones1 = en(nc.sbuf_tensor("ones1", [1, 128], F32))
        ones1b = en(nc.sbuf_tensor("ones1b", [1, 128], BF16))
        lt128 = en(nc.sbuf_tensor("lt128", [128, 128], BF16))
        iota32 = en(nc.sbuf_tensor("iota32", [128, NEXP], F32))
        iotac = en(nc.sbuf_tensor("iotac", [128, 1], F32))
        tokc4 = en(nc.sbuf_tensor("tokc4", [128, 4], F32))
        baserow = en(nc.sbuf_tensor("baserow", [1, 2 * NEXP], F32))
        baserow_bf = en(nc.sbuf_tensor("baserow_bf", [1, 2 * NEXP], BF16))
        br4b = en(nc.sbuf_tensor("br4b", [1, 256], BF16))
        wgsb = en(nc.sbuf_tensor("wgsb", [4, D], BF16))

        PS = [en(nc.psum_tensor(f"ps{i}", [128, 512], F32)) for i in range(6)]
        PT = [en(nc.psum_tensor(f"pt{i}", [128, 1024], BF16)) for i in range(2)]
        PTF = [p.bitcast(F32) for p in PT]
        PSF = [None] * 8

        def bwait(eng, bi):
            wait(eng, PSF[bi])

        def bset(bi, tok):
            PSF[bi] = tok

        def bfa(off_kb, cols):
            off = int(off_kb * 1024) // 2
            return ART[:, off:off + cols]

        def f32a(off_kb, cols):
            off = int(off_kb * 1024) // 4
            return ARF[:, off:off + cols]

        W_OFF = {"wk": 0, "wv": 16, "wq": 32, "w0": 48, "w1": 64, "w1p": 80}
        WT = {k: bfa(v, KC * D) for k, v in W_OFF.items()}
        X1 = f32a(96, 4096)
        LN1T = bfa(112, 4096)
        LN1F = bfa(120, 4096)
        QSB = bfa(128, 4096)
        KSB = bfa(136, 4096)
        VSB = bfa(144, 4096)
        RSBF = f32a(168, 1024)
        RHI = bfa(172, 1024)
        RLO = bfa(174, 1024)
        JNK1A = bfa(184, 4096)
        G0 = f32a(0, 8192)
        WOUTT = bfa(32, KC * D)
        G1 = f32a(96, 8192)
        KLD = [bfa(128, 4096), bfa(136, 4096)]
        VLD = [bfa(144, 4096), bfa(152, 4096)]
        SGT = bfa(160, 4096)
        PRC = bfa(168, 4096)
        A1T = bfa(176, 4096)
        AF1 = bfa(184, 4096)
        ETM = bfa(192, 4096)
        W0N = bfa(128, KC * D)
        W1N = bfa(144, KC * D)
        X2 = f32a(0, 4096)
        JNK2 = bfa(16, 4096)
        JNKF2 = f32a(18, 512)
        RES = f32a(48, 4096)
        LN2F32 = f32a(64, 4096)
        HIT = bfa(80, 4096)
        LOT = bfa(88, 4096)
        HIF = bfa(96, 4096)
        LOF = bfa(104, 4096)
        WMT = bfa(112, 4096)
        LOGI = f32a(120, 1024)
        WMA = f32a(124, 1024)
        Q2 = [bfa(160, 4096), bfa(168, 4096)]
        SG2 = bfa(176, 4096)
        U2 = bfa(184, 4096)
        H2 = bfa(192, 4096)
        EX = [bfa(0, KC * D), bfa(16, KC * D), bfa(32, KC * D), bfa(48, KC * D)]
        OHSB = bfa(64, NEXP * 128)
        WBC = bfa(72, 1024)
        LSTR = [bfa(74, 4096), bfa(82, 4096)]
        ZSB = [[bfa(90, 4096), bfa(106, 4096)], [bfa(98, 4096), bfa(136, 4096)]]
        STG = [bfa(144, 4096), bfa(152, 4096)]
        R4 = [f32a(0, 4096), f32a(16, 4096)]
        M4 = [f32a(32, 4096), f32a(48, 4096)]
        O4 = [f32a(64, 4096), f32a(80, 4096)]

        ch_t = Chan(ctx, nc, "ch_t")
        ch_a = Chan(ctx, nc, "ch_a")
        ch_v = Chan(ctx, nc, "ch_v")
        ch_cc = Chan(ctx, nc, "ch_cc")
        WQ = DmaQ(ctx, nc, "WQ")
        TRQ = DmaQ(ctx, nc, "TRQ")
        XQ = DmaQ(ctx, nc, "XQ")
        SPQ = DmaQ(ctx, nc, "SPQ")
        SPK = DmaQ(ctx, nc, "SPK")
        SPV = DmaQ(ctx, nc, "SPV")
        KVQ = [DmaQ(ctx, nc, "KVQ0"), DmaQ(ctx, nc, "KVQ1")]
        Q2Q = DmaQ(ctx, nc, "Q2Q")
        RSQ = DmaQ(ctx, nc, "RSQ")
        LNQ = DmaQ(ctx, nc, "LNQ")
        CCS = DmaQ(ctx, nc, "CCS")
        CCL = DmaQ(ctx, nc, "CCL")
        GBQ = DmaQ(ctx, nc, "GBQ")
        EXQ = [DmaQ(ctx, nc, "EXQ0"), DmaQ(ctx, nc, "EXQ1")]
        LSQ = [DmaQ(ctx, nc, "LSQ0"), DmaQ(ctx, nc, "LSQ1")]
        ACQ = [DmaQ(ctx, nc, "ACQ0"), DmaQ(ctx, nc, "ACQ1")]
        P4Q = [DmaQ(ctx, nc, "P4Q0"), DmaQ(ctx, nc, "P4Q1")]
        STQ = [DmaQ(ctx, nc, "STQ0"), DmaQ(ctx, nc, "STQ1")]
        TAPQ = DmaQ(ctx, nc, "TAPQ")

        block = en(nc.Block())
        T, A, V, SY, G = block.tensor, block.scalar, block.vector, block.sync, block.gpsimd

        def tap(name, src, tok):
            if name in tap_d:
                def f(g):
                    wait(g, tok)
                    TAPQ.dma(g, tap_d[name], src)
                G(f)

        cast_toks = {}

        def stw_g(g):
            for w, q in [("wq", WQ), ("wk", EXQ[0]), ("wv", EXQ[1]),
                         ("w0", ACQ[0]), ("w1", ACQ[1])]:
                cast_toks[w + "T"] = q.dma(
                    g, WT[w][:, :].rearrange("p (c e) -> p c e", c=KC),
                    w_d[w + "t"].rearrange("(c p) e -> p c e", p=128))
            cast_toks["router"] = WQ.dma(g, RSBF[0:32, 0:D], router_d)
            cast_toks["bg3"] = WQ.dma(g, g3[0:1, 4:8], bg3_d)
            cast_toks["wg3"] = WQ.dma(g, wgsb[0:3, :], wg3_d)
            cast_toks["lt"] = WQ.dma(g, lt128[:, :], lt_d)
            cast_toks["iota32"] = WQ.dma(g, iota32[:, :], iota32_d)
            cast_toks["iotac"] = WQ.dma(g, iotac[:, :], iotac_d)
            cast_toks["tokc4"] = WQ.dma(g, tokc4[:, :], tokc4_d)
            cast_toks["w1p"] = WQ.dma(
                g, WT["w1p"].rearrange("p (c e) -> p c e", c=KC),
                w_d["w1"].rearrange("(c p) e -> p c e", p=128))
        G(stw_g)

        def stw_sy(sy):
            cast_toks["ident"] = TRQ.dma(sy, ident[:, :], ident_d)
        SY(stw_sy)

        def stw_wg_t(t):
            wait(t, cast_toks["wg3"])
            wait(t, cast_toks["ident"])
            bwait(t, 6)
            for kc in range(KC):
                li = t.transpose(PT[0][0:128, kc * 128:kc * 128 + 3],
                                 wgsb[0:3, kc * 128:(kc + 1) * 128], ident[0:3, 0:3])
            ch_t.inc(li)
        T(stw_wg_t)
        wg_ps_tok = ch_t.tok()

        def stw_wg_v(v):
            wait(v, wg_ps_tok)
            for kc in range(KC):
                v.tensor_copy(wgt[:, kc * 4:kc * 4 + 3], PT[0][:, kc * 128:kc * 128 + 3])
            ch_v.inc(v.drain(fusable=False))
        V(stw_wg_v)
        wgt_tok = ch_v.tok()
        bset(6, wgt_tok)

        def stw_r_v(v):
            wait(v, cast_toks["router"])
            v.tensor_copy(RHI[0:32, 0:D], RSBF[0:32, 0:D])
            t1 = ch_v.inc(v.drain(fusable=False))
            wait(v, t1)
            v.tensor_tensor(RLO[0:32, 0:D], RSBF[0:32, 0:D], RHI[0:32, 0:D], ALU.subtract)
            ch_v.inc(v.drain(fusable=False))
        V(stw_r_v)
        rhilo_tok = ch_v.tok()

        rt_ev = [None] * KC
        rt_pe = [None] * KC
        for kc in range(KC):
            def rt_t(t, kc=kc):
                if kc == 0:
                    wait(t, rhilo_tok)
                    bwait(t, 7)
                if kc >= 2:
                    wait(t, rt_ev[kc - 2])
                half = (kc % 2) * 512
                t.transpose(PT[1][0:128, half:half + 32],
                            RHI[0:32, kc * 128:(kc + 1) * 128], ident[0:32, 0:32])
                li = t.transpose(PT[1][0:128, half + 128:half + 160],
                                 RLO[0:32, kc * 128:(kc + 1) * 128], ident[0:32, 0:32])
                rt_pe[kc] = ch_t.inc(li)
            T(rt_t)

            def rt_e(s, kc=kc):
                wait(s, rt_pe[kc])
                half = (kc % 2) * 512
                s.copy(rthi[:, kc * NEXP:(kc + 1) * NEXP], PT[1][:, half:half + 32])
                s.copy(rtlo[:, kc * NEXP:(kc + 1) * NEXP], PT[1][:, half + 128:half + 160])
                rt_ev[kc] = ch_a.inc(s.drain(fusable=False))
            A(rt_e)
        rt_tok = rt_ev[-1]
        bset(7, rt_tok)

        def emit_ln(x_ap, outs, junk, xin_tok):
            sums = stats[:, 0:4]
            sq = stats[:, 4:8]
            rstd = stats[:, 16:20]
            nb = stats[:, 20:24]
            t1 = small[:, 32:36]
            t2 = small[:, 36:40]

            def on_sq(s):
                wait(s, xin_tok)
                for tc in range(4):
                    s.activation(junk[:, tc * 1024:(tc + 1) * 1024],
                                 x_ap[:, tc * 1024:(tc + 1) * 1024],
                                 AF.Square, accum_out=sq[:, tc:tc + 1])
                ch_a.inc(s.drain(fusable=False))
            A(on_sq)
            sq_tok = ch_a.tok()

            def on_v(v):
                wait(v, xin_tok)
                for tc in range(4):
                    v.reduce_sum(sums[:, tc:tc + 1], x_ap[:, tc * 1024:(tc + 1) * 1024], axis=AX)
                tk = ch_v.inc(v.drain(fusable=False)); wait(v, tk)
                wait(v, sq_tok)
                v.tensor_scalar_mul(nb[:, :], sums[:, :], 1.0 / D)
                tk = ch_v.inc(v.drain(fusable=False)); wait(v, tk)
                v.tensor_tensor(t1[:, :], nb[:, :], nb[:, :], ALU.mult)
                v.tensor_scalar(t2[:, :], sq[:, :], 1.0 / D, LN_EPS, ALU.mult, ALU.add)
                tk = ch_v.inc(v.drain(fusable=False)); wait(v, tk)
                v.tensor_tensor(t2[:, :], t2[:, :], t1[:, :], ALU.subtract)
                tk = ch_v.inc(v.drain(fusable=False)); wait(v, tk)
                ti = t1[:, :].bitcast(I32)
                ri = rstd[:, :].bitcast(I32)
                v.tensor_single_scalar(ti, t2[:, :].bitcast(I32), 1, ALU.logical_shift_right)
                tk = ch_v.inc(v.drain(fusable=False)); wait(v, tk)
                v.tensor_scalar(ri, ti, -1, 0x5F3759DF, ALU.mult, ALU.add)
                tk = ch_v.inc(v.drain(fusable=False)); wait(v, tk)
                for _ in range(3):
                    v.tensor_tensor(t1[:, :], rstd[:, :], rstd[:, :], ALU.mult)
                    tk = ch_v.inc(v.drain(fusable=False)); wait(v, tk)
                    v.tensor_tensor(t1[:, :], t1[:, :], t2[:, :], ALU.mult)
                    tk = ch_v.inc(v.drain(fusable=False)); wait(v, tk)
                    v.tensor_scalar(t1[:, :], t1[:, :], -0.5, 1.5, ALU.mult, ALU.add)
                    tk = ch_v.inc(v.drain(fusable=False)); wait(v, tk)
                    v.tensor_tensor(rstd[:, :], rstd[:, :], t1[:, :], ALU.mult)
                    tk = ch_v.inc(v.drain(fusable=False)); wait(v, tk)
                v.tensor_tensor(nb[:, :], nb[:, :], rstd[:, :], ALU.mult)
                tk = ch_v.inc(v.drain(fusable=False)); wait(v, tk)
                v.tensor_scalar_mul(nb[:, :], nb[:, :], -1.0)
                ch_v.inc(v.drain(fusable=False))
            V(on_v)
            stat_tok = ch_v.tok()

            def on_s(s):
                wait(s, stat_tok)
                for ap in outs:
                    for tc in range(4):
                        s.activation(ap[:, tc * 1024:(tc + 1) * 1024],
                                     x_ap[:, tc * 1024:(tc + 1) * 1024], AF.Identity,
                                     bias=nb[:, tc:tc + 1], scale=rstd[:, tc:tc + 1])
                ch_a.inc(s.drain(fusable=False))
            A(on_s)
            return ch_a.tok()

        def transpose_block(src, dst, to_fm, pe_wait):
            n_out = KC if to_fm else 4
            pe_toks = [None] * n_out
            ev_toks = [None] * n_out
            for i in range(n_out):
                bank = PT[i % 2]

                def on_t(t, i=i, bank=bank):
                    if i == 0 and pe_wait is not None:
                        wait(t, pe_wait)
                    bwait(t, 6 + i % 2)
                    if to_fm:
                        for tc in range(4):
                            li = t.transpose(bank[:, tc * 128:tc * 128 + 128],
                                             src[:, tc * 1024 + i * 128:
                                                 tc * 1024 + i * 128 + 128],
                                             ident[:, :])
                    else:
                        for kc in range(KC):
                            li = t.transpose(bank[:, kc * 128:kc * 128 + 128],
                                             src[:, kc * 512 + i * 128:
                                                 kc * 512 + i * 128 + 128],
                                             ident[:, :])
                    pe_toks[i] = ch_t.inc(li)
                T(on_t)

                def on_e(v, i=i, bank=bank):
                    wait(v, pe_toks[i])
                    if to_fm:
                        inst = v.tensor_copy(dst[:, i * 512:(i + 1) * 512], bank[:, 0:512])
                    else:
                        inst = v.tensor_copy(dst[:, i * 1024:(i + 1) * 1024], bank[:, :])
                    ev_toks[i] = ch_v.inc(inst)
                V(on_e)
                bset(6 + i % 2, ev_toks[i])
            return ev_toks[-1]

        def proj_fm(wt_ap, rhs_fm, dst, rhs_tok, w_tok, dst2=None, fn=AF.Identity,
                    fn2=AF.Identity):
            mm = [None] * KC
            ev = [None] * KC
            for oc in range(KC):
                bank = PS[oc % 2]

                def on_t(t, oc=oc, bank=bank):
                    if oc == 0:
                        wait(t, rhs_tok)
                        wait(t, w_tok)
                    bwait(t, oc % 2)
                    for kc in range(KC):
                        li = t.matmul(bank[:, :],
                                      wt_ap[:, kc * 1024 + oc * 128:
                                            kc * 1024 + oc * 128 + 128],
                                      rhs_fm[:, kc * 512:(kc + 1) * 512],
                                      start=(kc == 0), stop=(kc == KC - 1))
                    mm[oc] = ch_t.inc(li)
                T(on_t)

                def on_s(v, oc=oc, bank=bank):
                    wait(v, mm[oc])
                    li = v.tensor_copy(dst[:, oc * 512:(oc + 1) * 512], bank[:, :])
                    if dst2 is not None:
                        li = v.tensor_copy(dst2[:, oc * 512:(oc + 1) * 512], bank[:, :])
                    ev[oc] = ch_v.inc(li)
                V(on_s)
                bset(oc % 2, ev[oc])
            return ev[-1], mm[-1]

        def x_view(b):
            return x_d[b * TB:(b + 1) * TB, :].rearrange("(tc p) d -> p tc d", p=128)

        def sp_view(sp, b):
            return sp.rearrange("p (kc t) -> p kc t", kc=KC)[:, :, b * TB:(b + 1) * TB]

        def sb_kc(sb):
            return sb.rearrange("p (kc t) -> p kc t", kc=KC)

        gate_red = [None] * NBLK
        gate_mm = [None] * NBLK
        spq = [None] * NBLK
        spk = [None] * NBLK
        spv = [None] * NBLK
        xtok = [None] * NBLK

        def z_gates(v):
            v.memset(ones1[:, :], 1.0)
            v.memset(ones_bf[:, :], 1.0)
            v.memset(ones1b[:, :], 1.0)
            v.memset(g3[0:1, 3:4], 0.0)
            v.memset(baserow[:, :], 0.0)
            v.memset(baserow_bf[:, :], 0.0)
            ch_v.inc(v.drain(fusable=False))
        V(z_gates)
        gz_tok = ch_v.tok()

        ZTILE = VLD[1]
        PADP = f32a(160, 256)
        INITQ = GBQ

        def init_v(v):
            v.memset(ZTILE[:, :], 0.0)
            pv = PADP.rearrange("p (c two) -> p c two", two=2)
            v.memset(pv[:, :, 0:1], 100000.0)
            tk = ch_v.inc(v.drain(fusable=False)); wait(v, tk)
            v.memset(pv[:, :, 1:2], 0.0)
            ch_v.inc(v.drain(fusable=False))
        V(init_v)
        initv_tok = ch_v.tok()

        def init_dma(sa):
            wait(sa, initv_tok)
            for i in range(NBLK):
                INITQ.dma(sa, bufA[i * TB:(i + 1) * TB, :].rearrange(
                    "(tc p) d -> p tc d", p=128),
                    ZTILE[:, :].rearrange("p (tc d) -> p tc d", tc=4))
            for i in range(NBLK):
                INITQ.dma(sa, bufB[i * TB:(i + 1) * TB, :].rearrange(
                    "(tc p) d -> p tc d", p=128),
                    ZTILE[:, :].rearrange("p (tc d) -> p tc d", tc=4))
            INITQ.dma(sa, idxstage.rearrange("(c p) two -> p c two", p=128),
                      PADP.rearrange("p (c two) -> p c two", two=2))
        A(init_dma)
        init_tok = INITQ.tok()


        XLQ = [XQ, TRQ, CCS, RSQ]
        prev_guards = [None]
        for b in range(NBLK):
            def on_x(sy, b=b, gl=prev_guards[0]):
                if gl is not None:
                    for g in gl:
                        wait(sy, g)
                wait(sy, TAPQ.tok())
                tks = []
                for tc in range(4):
                    tks.append(XLQ[tc].dma(
                        sy, X1[:, tc * 1024:(tc + 1) * 1024],
                        x_d[b * TB + tc * 128:b * TB + (tc + 1) * 128, :]))
                xtok[b] = _multi(tks)
            SY(on_x)
            ln_tok = emit_ln(X1, [LN1T], JNK1A, xtok[b])
            ln1f_tok = transpose_block(LN1T, LN1F, True, ln_tok)

            q_tok, q_mm = proj_fm(WT["wq"], LN1F, QSB, ln1f_tok,
                                  cast_toks["wqT"] if b == 0 else None)
            k_tok, _ = proj_fm(WT["wk"], LN1F, KSB, ln1f_tok,
                               cast_toks["wkT"] if b == 0 else None)

            v_mm = [None] * 8
            v_ev = [None] * 8
            for i, (tc, h) in enumerate([(tc, h) for tc in range(4) for h in range(2)]):
                bank = PS[2 + i % 2]

                def on_t(t, i=i, tc=tc, h=h, bank=bank, b=b):
                    if i == 0:
                        wait(t, ln1f_tok)
                        if b == 0:
                            wait(t, cast_toks["wvT"])
                    bwait(t, 2 + i % 2)
                    for kc in range(KC):
                        li = t.matmul(bank[:, :],
                                      LN1F[:, kc * 512 + tc * 128:kc * 512 + tc * 128 + 128],
                                      WT["wv"][:, kc * 1024 + h * 512:
                                               kc * 1024 + (h + 1) * 512],
                                      start=(kc == 0), stop=(kc == KC - 1))
                    v_mm[i] = ch_t.inc(li)
                T(on_t)

                def on_s(v, i=i, tc=tc, h=h, bank=bank):
                    wait(v, v_mm[i])
                    v_ev[i] = ch_v.inc(v.tensor_copy(
                        VSB[:, tc * 1024 + h * 512:tc * 1024 + (h + 1) * 512], bank[:, :]))
                V(on_s)
                bset(2 + i % 2, v_ev[i])
            v_tok = v_ev[-1]

            def on_gt(t, b=b):
                wait(t, ln_tok)
                if b == 0:
                    wait(t, gz_tok)
                    bwait(t, 4)
                for kc in range(KC):
                    for tc in range(4):
                        li = t.matmul(PS[4][:, kc:kc + 1],
                                      LN1T[:, tc * 1024 + kc * 128:
                                           tc * 1024 + kc * 128 + 128],
                                      ones_bf[:, :],
                                      start=(b == 0 and kc == 0 and tc == 0),
                                      stop=(b == NBLK - 1 and kc == KC - 1
                                            and tc == 3))
                gate_mm[b] = ch_t.inc(li)
            T(on_gt)

            def on_sp(sy, b=b, q_tok=q_tok, k_tok=k_tok, v_tok=v_tok):
                wait(sy, q_tok)
                spq[b] = SPQ.dma(sy, sp_view(qsp, b), sb_kc(QSB))
                wait(sy, k_tok)
                spk[b] = SPK.dma(sy, sp_view(ksp, b), sb_kc(KSB))
                wait(sy, v_tok)
                spv[b] = SPV.dma(sy, vsp[:, b * 4096:(b + 1) * 4096], VSB[:, :])
            SY(on_sp)
            prev_guards[0] = [ln1f_tok, spq[b], spk[b], spv[b], gate_mm[b]]

            if b == 0:
                tap("ln1", LN1T[:, :], ln_tok)
                tap("kfm", KSB[:, :], k_tok)
                tap("vtm", VSB[:, :], v_tok)
                tap("qfm", QSB[:, :], q_tok)

        last_1a_pe = ch_t.tok()

        def on_gf1(s):
            wait(s, gate_mm[NBLK - 1])
            s.activation(colsum_bf[:, :], PS[4][:, 0:8], AF.Identity)
            ch_a.inc(s.drain(fusable=False))
        A(on_gf1)
        colsum_tok = ch_a.tok()
        bset(4, colsum_tok)

        def on_gf2(t):
            wait(t, colsum_tok)
            wait(t, wgt_tok)
            bwait(t, 5)
            for kc in range(KC):
                li = t.matmul(PS[5][0:1, 0:3], colsum_bf[:, kc:kc + 1],
                              wgt[:, kc * 4:kc * 4 + 3],
                              start=(kc == 0), stop=(kc == KC - 1))
            ch_t.inc(li)
        T(on_gf2)
        gmv_tok = ch_t.tok()

        def on_gf3(v):
            wait(v, gmv_tok)
            v.tensor_copy(g3[0:1, 0:3], PS[5][0:1, 0:3])
            ch_v.inc(v.drain(fusable=False))
        V(on_gf3)
        gate_red[NBLK - 1] = ch_v.tok()
        bset(5, gate_red[NBLK - 1])

        def finalize_partial():
            def fin_p(g):
                for q in [WQ, TRQ, XQ, SPQ, SPK, SPV, KVQ[0], KVQ[1], Q2Q, RSQ, LNQ,
                          CCS, CCL, GBQ, EXQ[0], EXQ[1], LSQ[0], LSQ[1], ACQ[0],
                          ACQ[1], P4Q[0], P4Q[1], STQ[0], STQ[1], TAPQ]:
                    if q.n:
                        g.wait_ge(q.h, q.n)
            G(fin_p)

        if phases < 2:
            finalize_partial()
            return nc

        def gz(v):
            v.memset(G0[:, :], 0.0)
            v.memset(G1[:, :], 0.0)
            ch_v.inc(v.drain(fusable=False))
        V(gz)
        gzero_tok = ch_v.tok()

        def on_woutt(g):
            wait(g, last_1a_pe)
            cast_toks["woutT"] = CCL.dma(
                g, WOUTT[:, :].rearrange("p (c e) -> p c e", c=KC),
                w_d["woutt"].rearrange("(c p) e -> p c e", p=128))
        G(on_woutt)

        kld_t = [None] * NBLK
        vld_t = [None] * NBLK

        def kv_load(b, guard):
            def f(sy):
                if guard is not None:
                    wait(sy, guard)
                wait(sy, TAPQ.tok())
                wait(sy, init_tok)
                wait(sy, spq[NBLK - 1])
                wait(sy, spk[NBLK - 1])
                wait(sy, spv[NBLK - 1])
                kld_t[b] = KVQ[b % 2].dma(sy, KLD[b % 2][:, :], sp_view(ksp, b))
                vld_t[b] = KVQ[b % 2].dma(sy, VLD[b % 2][:, :],
                                          vsp[:, b * 4096:(b + 1) * 4096])
            SY(f)

        kv_load(0, None)
        blk_done = [None] * NBLK

        def emit_g(lhs_tm, rhs_tm, acc, lhs_tok, rhs_tok, first_guard):
            mm = {}
            ev = {}
            idx = 0
            for ec in range(KC):
                for on in range(2):
                    bank = PS[2 + idx % 2]

                    def on_t(t, ec=ec, on=on, bank=bank, idx=idx):
                        if idx == 0:
                            wait(t, lhs_tok)
                            wait(t, rhs_tok)
                        bwait(t, 2 + idx % 2)
                        for tc in range(4):
                            li = t.matmul(bank[:, :],
                                          lhs_tm[:, tc * 1024 + ec * 128:
                                                 tc * 1024 + ec * 128 + 128],
                                          rhs_tm[:, tc * 1024 + on * 512:
                                                 tc * 1024 + (on + 1) * 512],
                                          start=(tc == 0), stop=(tc == 3))
                        mm[idx] = ch_t.inc(li)
                    T(on_t)

                    def on_v(v, ec=ec, on=on, bank=bank, idx=idx):
                        wait(v, mm[idx])
                        if idx == 0 and first_guard is not None:
                            wait(v, first_guard)
                        sl = slice(ec * 1024 + on * 512, ec * 1024 + (on + 1) * 512)
                        v.scalar_tensor_tensor(acc[:, sl], bank[:, :], SCALE, acc[:, sl],
                                               ALU.mult, ALU.add)
                        ev[idx] = ch_v.inc(v.drain(fusable=False))
                    V(on_v)
                    bset(2 + idx % 2, ev[idx])
                    idx += 1
            return {"mm_done": mm[idx - 1], "acc_done": ev[idx - 1]}

        for b in range(NBLK):
            kf = KLD[b % 2]
            vt = VLD[b % 2]
            if b + 1 < NBLK:
                kv_load(b + 1, blk_done[b - 1] if b >= 1 else None)

            pre_mm = [None] * 8
            pre_ev = [None] * 8
            for i, (tc, h) in enumerate([(tc, h) for tc in range(4) for h in range(2)]):
                bank = PS[i % 2]

                def on_t(t, i=i, tc=tc, h=h, bank=bank, b=b):
                    if i == 0:
                        wait(t, kld_t[b])
                        if b == 0:
                            wait(t, cast_toks["w0T"])
                    bwait(t, i % 2)
                    for kc in range(KC):
                        li = t.matmul(bank[:, :],
                                      kf[:, kc * 512 + tc * 128:kc * 512 + tc * 128 + 128],
                                      WT["w0"][:, kc * 1024 + h * 512:
                                               kc * 1024 + (h + 1) * 512],
                                      start=(kc == 0), stop=(kc == KC - 1))
                    pre_mm[i] = ch_t.inc(li)
                T(on_t)

                def on_s(v, i=i, tc=tc, h=h, bank=bank):
                    wait(v, pre_mm[i])
                    sl = slice(tc * 1024 + h * 512, tc * 1024 + (h + 1) * 512)
                    pre_ev[i] = ch_v.inc(v.tensor_copy(PRC[:, sl], bank[:, :]))
                V(on_s)
                bset(i % 2, pre_ev[i])
            pre_tok = pre_ev[-1]

            def on_sg(sa):
                wait(sa, pre_tok)
                ch_a.inc(sa.activation(SGT[:, :], PRC[:, :], AF.Sigmoid))
            A(on_sg)
            sg_tok = ch_a.tok()

            def on_a1(v):
                wait(v, sg_tok)
                v.tensor_tensor(A1T[:, :], PRC[:, :], SGT[:, :], ALU.mult)
                ch_v.inc(v.drain(fusable=False))
            V(on_a1)
            a1_tok = ch_v.tok()
            if b == 0:
                tap("a1tm", A1T[:, :], a1_tok)

            a1f_tok = transpose_block(A1T, AF1, True, a1_tok)

            prd_mm = [None] * 8
            prd_ev = [None] * 8
            for i, (tc, h) in enumerate([(tc, h) for tc in range(4) for h in range(2)]):
                bank = PS[i % 2]

                def on_t(t, i=i, tc=tc, h=h, bank=bank, b=b):
                    if i == 0:
                        wait(t, a1f_tok)
                        if b == 0:
                            wait(t, cast_toks["w1T"])
                    bwait(t, i % 2)
                    for kc in range(KC):
                        li = t.matmul(bank[:, :],
                                      AF1[:, kc * 512 + tc * 128:kc * 512 + tc * 128 + 128],
                                      WT["w1"][:, kc * 1024 + h * 512:
                                               kc * 1024 + (h + 1) * 512],
                                      start=(kc == 0), stop=(kc == KC - 1))
                    prd_mm[i] = ch_t.inc(li)
                T(on_t)

                def on_v(v, i=i, tc=tc, h=h, bank=bank, b=b):
                    wait(v, prd_mm[i])
                    if i == 0:
                        wait(v, vld_t[b])
                    sl = slice(tc * 1024 + h * 512, tc * 1024 + (h + 1) * 512)
                    v.tensor_tensor(ETM[:, sl], bank[:, :], vt[:, sl], ALU.subtract)
                    tk = ch_v.inc(v.drain(fusable=False)); wait(v, tk)
                    v.tensor_scalar(ETM[:, sl], ETM[:, sl], ERR_CLIP, -ERR_CLIP,
                                    ALU.min, ALU.max)
                    prd_ev[i] = ch_v.inc(v.drain(fusable=False))
                V(on_v)
                bset(i % 2, prd_ev[i])
            e_tok = prd_ev[-1]
            if b == 0:
                tap("etm", ETM[:, :], e_tok)

            def on_sd(v):
                wait(v, a1_tok)
                v.tensor_scalar(PRC[:, :], SGT[:, :], -1.0, 1.0, ALU.mult, ALU.add)
                tk = ch_v.inc(v.drain(fusable=False)); wait(v, tk)
                v.tensor_tensor(PRC[:, :], A1T[:, :], PRC[:, :], ALU.mult)
                tk = ch_v.inc(v.drain(fusable=False)); wait(v, tk)
                v.tensor_tensor(SGT[:, :], SGT[:, :], PRC[:, :], ALU.add)
                ch_v.inc(v.drain(fusable=False))
            V(on_sd)
            sd_tok = ch_v.tok()

            ef_tok = transpose_block(ETM, AF1, True, e_tok)

            d1_mm = [None] * 8
            d1_ev = [None] * 8
            for i, (tc, h) in enumerate([(tc, h) for tc in range(4) for h in range(2)]):
                bank = PS[i % 2]

                def on_t(t, i=i, tc=tc, h=h, bank=bank, b=b):
                    if i == 0:
                        wait(t, ef_tok)
                        if b == 0:
                            wait(t, cast_toks["w1p"])
                    bwait(t, i % 2)
                    for kc in range(KC):
                        li = t.matmul(bank[:, :],
                                      AF1[:, kc * 512 + tc * 128:kc * 512 + tc * 128 + 128],
                                      WT["w1p"][:, kc * 1024 + h * 512:
                                               kc * 1024 + (h + 1) * 512],
                                      start=(kc == 0), stop=(kc == KC - 1))
                    d1_mm[i] = ch_t.inc(li)
                T(on_t)

                def on_v(v, i=i, tc=tc, h=h, bank=bank):
                    wait(v, d1_mm[i])
                    if i == 0:
                        wait(v, sd_tok)
                    sl = slice(tc * 1024 + h * 512, tc * 1024 + (h + 1) * 512)
                    v.tensor_tensor(PRC[:, sl], bank[:, :], SGT[:, sl], ALU.mult)
                    d1_ev[i] = ch_v.inc(v.drain(fusable=False))
                V(on_v)
                bset(i % 2, d1_ev[i])
            d1_tok = d1_ev[-1]
            if b == 0:
                tap("d1tm", PRC[:, :], d1_tok)

            def ktm_guard(t):
                wait(t, TAPQ.tok())
            T(ktm_guard)
            ktm_tok = transpose_block(kf, SGT, False, d1_tok)

            g0r = emit_g(SGT, PRC, G0, ktm_tok, d1_tok, gzero_tok if b == 0 else None)

            g1r = emit_g(A1T, ETM, G1, a1_tok, e_tok, None)
            blk_done[b] = g1r["acc_done"]

        g0_fin = g0r["acc_done"]
        g1_fin = g1r["acc_done"]
        gates_fin = gate_red[NBLK - 1]
        cc_tok = {}

        def r_stage0(sy):
            wait(sy, gates_fin)
            cc_tok["ga"] = CCS.dma(sy, ccg_in[0:1, 0:4], g3[0:1, 0:4])
        SY(r_stage0)

        def r_cc0(g):
            wait(g, cc_tok["ga"])
            g.collective_compute(
                "AllReduce", ALU.add,
                replica_groups=[[0, 1, 2, 3, 4, 5, 6, 7]],
                ins=[ccg_in[0:1, :]], outs=[ccg_out[0:1, :]],
            ).then_inc(ch_cc.h, 1)
            ch_cc.n += 1
            g.wait_ge(ch_cc.h, ch_cc.n)
            cc_tok["gar"] = CCL.dma(g, g3[0:1, 0:3], ccg_out[0:1, 0:3])
        G(r_cc0)
        tap("g0", G0[:, :], g0_fin)
        tap("g1", G1[:, :], g1_fin)
        tap("gacc", g3[0:1, 0:4], gates_fin)

        if phases < 3:
            finalize_partial()
            return nc

        def r_stage(sy):
            wait(sy, g0_fin)
            cc_tok["g0"] = CCS.dma(sy, ccin[:, :], G0[:, :])
        SY(r_stage)

        def r_stage2(sy):
            wait(sy, g1_fin)
            cc_tok["g1"] = LNQ.dma(sy, ccin2[:, :], G1[:, :])
        SY(r_stage2)

        def r_cc(g):
            wait(g, cc_tok["g0"])
            g.collective_compute(
                "AllReduce", ALU.add,
                replica_groups=[[0, 1, 2, 3, 4, 5, 6, 7]],
                ins=[ccin[:, :]], outs=[ccout[:, :]],
            ).then_inc(ch_cc.h, 1)
            ch_cc.n += 1
            wait(g, cc_tok["g1"])
            g.collective_compute(
                "AllReduce", ALU.add,
                replica_groups=[[0, 1, 2, 3, 4, 5, 6, 7]],
                ins=[ccin2[:, :]], outs=[ccout2[:, :]],
            ).then_inc(ch_cc.h, 1)
            ch_cc.n += 1
            g.wait_ge(ch_cc.h, ch_cc.n - 1)
            for c in range(4):
                cc_tok["g0r%d" % c] = CCL.dma(
                    g, G0[:, c * 2048:(c + 1) * 2048], ccout[:, c * 2048:(c + 1) * 2048])
            g.wait_ge(ch_cc.h, ch_cc.n)
            for c in range(4):
                cc_tok["g1r%d" % c] = CCL.dma(
                    g, G1[:, c * 2048:(c + 1) * 2048], ccout2[:, c * 2048:(c + 1) * 2048])
        G(r_cc)

        def r_gv(v):
            wait(v, cc_tok["gar"])
            v.tensor_scalar(g3[0:1, 8:11], g3[0:1, 0:3], 1.0 / 32768.0, None, ALU.mult)
            tk = ch_v.inc(v.drain(fusable=False)); wait(v, tk)
            v.tensor_tensor(g3[0:1, 8:11], g3[0:1, 8:11], g3[0:1, 4:7], ALU.add)
            ch_v.inc(v.drain(fusable=False))
        V(r_gv)
        gate_m_tok = ch_v.tok()

        def r_gs(s):
            wait(s, gate_m_tok)
            s.activation(g3[0:1, 0:3], g3[0:1, 8:11], AF.Sigmoid)
            ch_a.inc(s.drain(fusable=False))
        A(r_gs)
        gate_sig_tok = ch_a.tok()
        tap("gsig", g3[0:1, 0:4], gate_sig_tok)

        def r_gv2(v):
            wait(v, gate_sig_tok)
            v.tensor_scalar(g3[0:1, 12:13], g3[0:1, 0:1], -1.0, 1.0, ALU.mult, ALU.add)
            tk = ch_v.inc(v.drain(fusable=False)); wait(v, tk)
            v.tensor_scalar(g3[0:1, 13:14], g3[0:1, 1:2], -1.0, None, ALU.mult)
            ch_v.inc(v.drain(fusable=False))
        V(r_gv2)
        gate_c_tok = ch_v.tok()

        def r_bc(t):
            wait(t, gate_c_tok)
            wait(t, gz_tok)
            bwait(t, 5)
            li = t.matmul(PS[5][:, 0:2], ones1[:, :], g3[0:1, 12:14],
                          start=True, stop=True)
            ch_t.inc(li)
        T(r_bc)
        bc_tok = ch_t.tok()

        def r_bce(v):
            wait(v, bc_tok)
            v.tensor_copy(small[:, 48:50], PS[5][:, 0:2])
            ch_v.inc(v.drain(fusable=False))
        V(r_bce)
        coef_tok = ch_v.tok()
        bset(5, coef_tok)
        C_OMA = small[:, 48:49]
        C_NLR = small[:, 49:50]

        upd_toks = []
        for gsrc, wsrc, wdst, grt, sfin in [
                (G0, WT["w0"], W0N, "g0r", [blk_done[NBLK - 1], spq[NBLK - 1],
                                            spk[NBLK - 1]]),
                (G1, WT["w1"], W1N, "g1r", [blk_done[NBLK - 1], spv[NBLK - 1]])]:
            def on_us(s, wsrc=wsrc, wdst=wdst, sfin=sfin):
                wait(s, coef_tok)
                for tk in sfin:
                    wait(s, tk)
                s.activation(wdst[:, 0:4096], wsrc[:, 0:4096], AF.Identity, scale=C_OMA)
                s.activation(wdst[:, 4096:8192], wsrc[:, 4096:8192], AF.Identity, scale=C_OMA)
                ch_a.inc(s.drain(fusable=False))
            A(on_us)
            sc_tok = ch_a.tok()

            def on_uv(v, gsrc=gsrc, wdst=wdst, sc_tok=sc_tok, grt=grt):
                wait(v, sc_tok)
                for c in range(4):
                    wait(v, cc_tok[grt + str(c)])
                    sl = slice(c * 2048, (c + 1) * 2048)
                    v.tensor_scalar(gsrc[:, sl], gsrc[:, sl], GRAD_CLIP, -GRAD_CLIP,
                                    ALU.min, ALU.max)
                    tk = ch_v.inc(v.drain(fusable=False)); wait(v, tk)
                    v.scalar_tensor_tensor(wdst[:, sl], gsrc[:, sl], C_NLR, wdst[:, sl],
                                           ALU.mult, ALU.add)
                    tk = ch_v.inc(v.drain(fusable=False)); wait(v, tk)
                upd_toks.append(tk)
            V(on_uv)
        w0n_tok, w1n_tok = upd_toks
        tap("w0n", W0N[:, :], w0n_tok)

        if phases < 4:
            finalize_partial()
            return nc

        q2t = [None] * NBLK
        x2t = [None] * NBLK
        res_sp = [None] * NBLK
        ln2_sp = [None] * NBLK
        logit_t = [None] * NBLK
        u_last = [None] * NBLK
        res_last = [None] * NBLK
        ln2_toks = [None] * NBLK
        ln2tm_sp = [None] * NBLK

        HUGE = 1.0e6
        SCQ = DmaQ(ctx, nc, "SCQ")
        bnd_regs = {}

        def setup_bnd(g):
            r1 = g.alloc_register("bnd_idx")
            g.reg_mov(r1, NEXP * 512 - 1)
            r2 = g.alloc_register("bnd_tok")
            g.reg_mov(r2, TC - 1)
            bnd_regs["idx"] = r1
            bnd_regs["tok"] = r2
        G(setup_bnd)

        QLQ = [Q2Q, KVQ[0], KVQ[1], XQ]
        X2Q = [GBQ, DmaQ(ctx, nc, "X2Q1"), DmaQ(ctx, nc, "X2Q2"),
               DmaQ(ctx, nc, "X2Q3")]
        JF2 = f32a(24, 1024)
        MKB = bfa(28, 256)
        routeA_fin = [None] * NBLK
        routeB_fin = [None]
        rk_base_tok = [None, None]
        sc_last = [None]
        scq_slot = [None] * 4
        PAIRJ = [640, 656, 672, 688]

        def v3(ap):
            return ap.rearrange("p (c e) -> p c e", e=32)

        def bc3(ap):
            return ap.unsqueeze(2).broadcast_to([128, 4, 32])

        def route_soft(b):
            sA = small[:, 40:44]
            sR = small[:, 44:48]
            sM2 = small[:, 52:56]
            E = JF2[:, 0:128]
            M1 = JF2[:, 128:256]
            T = JF2[:, 256:384]
            M2 = JF2[:, 384:512]
            W = WMA[:, b * 128:(b + 1) * 128]
            L = LOGI[:, b * 128:(b + 1) * 128]
            neg_tok = [None]
            exp_tok = [None]

            def on_a(v, b=b):
                wait(v, logit_t[b])
                if b == 0:
                    wait(v, cast_toks["iota32"])
                    wait(v, cast_toks["iotac"])
                    wait(v, cast_toks["tokc4"])
                else:
                    wait(v, routeB_fin[0])
                v.tensor_reduce(sA, v3(L), AX, ALU.max)
                tk = ch_v.inc(v.drain(fusable=False)); wait(v, tk)
                v.tensor_scalar_mul(sA, sA, -1.0)
                tk = ch_v.inc(v.drain(fusable=False)); wait(v, tk)
                v.tensor_scalar(small[:, 60:64], tokc4[:, :], 1.0, float(b * 512),
                                ALU.mult, ALU.add)
                neg_tok[0] = ch_v.inc(v.drain(fusable=False))
            V(on_a)

            def on_exp(sa, b=b):
                wait(sa, neg_tok[0])
                for c in range(4):
                    sa.activation(JF2[:, c * 32:(c + 1) * 32],
                                  LOGI[:, (b * 4 + c) * 32:(b * 4 + c + 1) * 32],
                                  AF.Exp, bias=small[:, 40 + c:41 + c], scale=1.0)
                exp_tok[0] = ch_a.inc(sa.drain(fusable=False))
            A(on_exp)

            def on_sm(v, b=b):
                wait(v, exp_tok[0])
                v.tensor_reduce(sR, v3(E), AX, ALU.add)
                tk = ch_v.inc(v.drain(fusable=False)); wait(v, tk)
                v.reciprocal(sR, sR)
                tk = ch_v.inc(v.drain(fusable=False)); wait(v, tk)
                v.tensor_tensor(v3(W), v3(E), bc3(sR), ALU.mult)
                tk = ch_v.inc(v.drain(fusable=False)); wait(v, tk)
                v.tensor_reduce(sA, v3(W), AX, ALU.max)
                tk = ch_v.inc(v.drain(fusable=False)); wait(v, tk)
                v.tensor_tensor(v3(M1), v3(W), bc3(sA), ALU.is_ge)
                tk = ch_v.inc(v.drain(fusable=False)); wait(v, tk)
                v.tensor_tensor(v3(T), v3(W), bc3(sA), ALU.is_lt)
                tk = ch_v.inc(v.drain(fusable=False)); wait(v, tk)
                v.tensor_tensor(T, T, W, ALU.mult)
                tk = ch_v.inc(v.drain(fusable=False)); wait(v, tk)
                v.tensor_reduce(sM2, v3(T), AX, ALU.max)
                tk = ch_v.inc(v.drain(fusable=False)); wait(v, tk)
                v.tensor_tensor(v3(T), v3(W), bc3(sM2), ALU.is_ge)
                tk = ch_v.inc(v.drain(fusable=False)); wait(v, tk)
                v.tensor_tensor(M2, T, M1, ALU.subtract)
                tk = ch_v.inc(v.drain(fusable=False)); wait(v, tk)
                v.tensor_tensor(W, W, T, ALU.mult)
                routeA_fin[b] = ch_v.inc(v.drain(fusable=False))
            V(on_sm)

        def route_ranks(b):
            for r in range(2):
                mcol = JF2[:, 128:256] if r == 0 else JF2[:, 384:512]
                mk = MKB[:, r * 128:(r + 1) * 128]
                mb_tok = [None]
                cn_tok = [None]
                cv_tok = [None]
                br_tok = [None]
                rk_tok = [None]
                sl_tok = [None]

                def on_mb(v, b=b, r=r, mcol=mcol, mk=mk):
                    wait(v, routeA_fin[b])
                    wait(v, rk_base_tok[r])
                    mb_tok[0] = ch_v.inc(v.tensor_copy(mk, mcol))
                V(on_mb)

                def on_mm1(t, b=b, r=r, mk=mk):
                    wait(t, mb_tok[0])
                    if b == 0 and r == 0:
                        wait(t, cast_toks["lt"])
                        wait(t, gz_tok)
                    bwait(t, 4 + r)
                    cn_tok[0] = ch_t.inc(t.matmul(
                        PS[4 + r][0:1, 0:128], ones_bf[:, :], mk,
                        start=True, stop=True))
                    bwait(t, 2 + r)
                    t.matmul(PS[2 + r][:, 0:128], lt128[:, :], mk,
                             start=True, stop=False)
                T(on_mm1)

                def on_cv(s, r=r):
                    wait(s, cn_tok[0])
                    s.copy(JF2[0:1, 712:840], PS[4 + r][0:1, 0:128])
                    cv_tok[0] = ch_a.inc(s.drain(fusable=False))
                A(on_cv)
                bset(4 + r, cv_tok[0])

                def on_br(v, r=r):
                    wait(v, cv_tok[0])
                    CNT = JF2[0:1, 712:840]
                    BR4 = JF2[0:1, 840:968]
                    bro = baserow[0:1, r * 32:(r + 1) * 32]
                    v.tensor_copy(BR4[0:1, 0:32], bro)
                    tk = ch_v.inc(v.drain(fusable=False)); wait(v, tk)
                    v.tensor_tensor(BR4[0:1, 32:64], BR4[0:1, 0:32],
                                    CNT[0:1, 0:32], ALU.add)
                    tk = ch_v.inc(v.drain(fusable=False)); wait(v, tk)
                    v.tensor_tensor(BR4[0:1, 64:96], BR4[0:1, 32:64],
                                    CNT[0:1, 32:64], ALU.add)
                    tk = ch_v.inc(v.drain(fusable=False)); wait(v, tk)
                    v.tensor_tensor(BR4[0:1, 96:128], BR4[0:1, 64:96],
                                    CNT[0:1, 64:96], ALU.add)
                    tk = ch_v.inc(v.drain(fusable=False)); wait(v, tk)
                    v.tensor_tensor(bro, BR4[0:1, 96:128],
                                    CNT[0:1, 96:128], ALU.add)
                    tk = ch_v.inc(v.drain(fusable=False)); wait(v, tk)
                    v.tensor_copy(br4b[0:1, r * 128:(r + 1) * 128], BR4[0:1, :])
                    br_tok[0] = ch_v.inc(v.drain(fusable=False))
                V(on_br)

                def on_mm2(t, r=r):
                    wait(t, br_tok[0])
                    rk_tok[0] = ch_t.inc(t.matmul(
                        PS[2 + r][:, 0:128], ones1b[:, :],
                        br4b[0:1, r * 128:(r + 1) * 128], start=False, stop=True))
                    rk_base_tok[r] = rk_tok[0]
                T(on_mm2)

                def on_sl(v, b=b, r=r, mcol=mcol):
                    wait(v, rk_tok[0])
                    SL = JF2[:, 512:640]
                    T3 = JF2[:, 256:384]
                    v.tensor_copy(SL, PS[2 + r][:, 0:128])
                    tk = ch_v.inc(v.drain(fusable=False)); wait(v, tk)
                    v.tensor_scalar(T3, SL, 255.5, None, ALU.is_gt)
                    tk = ch_v.inc(v.drain(fusable=False)); wait(v, tk)
                    v.scalar_tensor_tensor(SL, T3, HUGE, SL, ALU.mult, ALU.add)
                    tk = ch_v.inc(v.drain(fusable=False)); wait(v, tk)
                    v.tensor_scalar(T3, mcol, -1.0, 1.0, ALU.mult, ALU.add)
                    tk = ch_v.inc(v.drain(fusable=False)); wait(v, tk)
                    v.scalar_tensor_tensor(SL, T3, HUGE, SL, ALU.mult, ALU.add)
                    tk = ch_v.inc(v.drain(fusable=False)); wait(v, tk)
                    v.tensor_reduce(small[:, 52:56], v3(SL), AX, ALU.min)
                    tk = ch_v.inc(v.drain(fusable=False)); wait(v, tk)
                    v.tensor_tensor(v3(T3), v3(mcol),
                                    iota32[:, :].unsqueeze(1).broadcast_to(
                                        [128, 4, 32]), ALU.mult)
                    tk = ch_v.inc(v.drain(fusable=False)); wait(v, tk)
                    v.tensor_reduce(small[:, 56:60], v3(T3), AX, ALU.add)
                    tk = ch_v.inc(v.drain(fusable=False)); wait(v, tk)
                    v.tensor_tensor(T3, WMA[:, b * 128:(b + 1) * 128], mcol,
                                    ALU.mult)
                    tk = ch_v.inc(v.drain(fusable=False)); wait(v, tk)
                    v.tensor_reduce(small[:, 48:52], v3(T3), AX, ALU.add)
                    tk = ch_v.inc(v.drain(fusable=False)); wait(v, tk)
                    SLG = JF2[:, 968:972]
                    v.scalar_tensor_tensor(SLG, small[:, 56:60], 512.0,
                                           small[:, 52:56], ALU.mult, ALU.add)
                    tk = ch_v.inc(v.drain(fusable=False)); wait(v, tk)
                    if r:
                        v.tensor_scalar_add(SLG, SLG, 256.0)
                        tk = ch_v.inc(v.drain(fusable=False)); wait(v, tk)
                    sidx = (b * 2 + r) % 4
                    pc = PAIRJ[sidx]
                    wait(v, scq_slot[sidx])
                    P3 = JF2[:, pc:pc + 8].rearrange("p (c two) -> p c two", two=2)
                    v.tensor_copy(P3[:, :, 0:1], small[:, 60:64].unsqueeze(2))
                    tk = ch_v.inc(v.drain(fusable=False)); wait(v, tk)
                    v.tensor_copy(P3[:, :, 1:2], small[:, 48:52].unsqueeze(2))
                    tk = ch_v.inc(v.drain(fusable=False)); wait(v, tk)
                    v.tensor_copy(JF2[:, pc + 8:pc + 12].bitcast(mybir.dt.uint32),
                                  SLG)
                    sl_tok[0] = ch_v.inc(v.drain(fusable=False))
                    routeB_fin[0] = sl_tok[0]
                V(on_sl)
                bset(2 + r, sl_tok[0])

                def on_sc(g, b=b, r=r):
                    wait(g, sl_tok[0])
                    sidx = (b * 2 + r) % 4
                    pc = PAIRJ[sidx]
                    for c in range(4):
                        sc_last[0] = SCQ.dma_indirect_scatter(
                            g, idxstage,
                            JF2[:, pc + 8 + c:pc + 9 + c].bitcast(mybir.dt.uint32),
                            JF2[:, pc + c * 2:pc + c * 2 + 2], bnd_regs["idx"])
                    scq_slot[sidx] = sc_last[0]
                G(on_sc)

        for b in range(NBLK):
            def on_q2(sy, b=b):
                if b == 0:
                    wait(sy, blk_done[NBLK - 1])
                else:
                    wait(sy, u_last[b - 1])
                wait(sy, spq[b])
                wait(sy, TAPQ.tok())
                tks = []
                for j in range(4):
                    tks.append(QLQ[j].dma(
                        sy, Q2[b % 2][:, j * 1024:(j + 1) * 1024].rearrange(
                            "p (c t) -> p c t", c=2),
                        sp_view(qsp, b)[:, 2 * j:2 * j + 2, :]))
                q2t[b] = _multi(tks)
            SY(on_q2)

            u_mm = [None] * KC
            u_ev = [None] * KC
            for oc in range(KC):
                bank = PS[oc % 2]

                def on_t(t, oc=oc, bank=bank, b=b):
                    if oc == 0:
                        wait(t, q2t[b])
                        if b == 0:
                            wait(t, w0n_tok)
                    bwait(t, oc % 2)
                    for kc in range(KC):
                        li = t.matmul(bank[:, :],
                                      W0N[:, kc * 1024 + oc * 128:kc * 1024 + oc * 128 + 128],
                                      Q2[b % 2][:, kc * 512:(kc + 1) * 512],
                                      start=(kc == 0), stop=(kc == KC - 1))
                    u_mm[oc] = ch_t.inc(li)
                T(on_t)

                def on_s(v, oc=oc, bank=bank):
                    wait(v, u_mm[oc])
                    u_ev[oc] = ch_v.inc(v.tensor_copy(
                        U2[:, oc * 512:(oc + 1) * 512], bank[:, :]))
                V(on_s)
                bset(oc % 2, u_ev[oc])
            u_last[b] = u_mm[-1]

            def on_sg2(sa, dep=u_ev[-1]):
                wait(sa, dep)
                ch_a.inc(sa.activation(SG2[:, :], U2[:, :], AF.Sigmoid))
            A(on_sg2)
            sg2_tok = ch_a.tok()

            def on_u(v):
                wait(v, sg2_tok)
                v.tensor_tensor(U2[:, :], U2[:, :], SG2[:, :], ALU.mult)
                ch_v.inc(v.drain(fusable=False))
            V(on_u)
            u_tok = ch_v.tok()

            if phases >= 5 and b > 0:
                route_ranks(b - 1)

            def on_x2(sa, b=b):
                if b == 0:
                    wait(sa, w0n_tok)
                else:
                    wait(sa, res_last[b - 1])
                tks = []
                for tc in range(4):
                    tks.append(X2Q[tc].dma(
                        sa, X2[:, tc * 1024:(tc + 1) * 1024],
                        x_d[b * TB + tc * 128:b * TB + (tc + 1) * 128, :]))
                x2t[b] = _multi(tks)
            A(on_x2)

            h2_tok, _ = proj_fm(W1N, U2, H2, u_tok, w1n_tok if b == 0 else None)

            at_mm = [None] * 8
            at_ev = [None] * 8
            for i, (tc, dn) in enumerate([(tc, dn) for tc in range(4) for dn in range(2)]):
                bank = PS[2 + i % 2]

                def on_t(t, i=i, tc=tc, dn=dn, bank=bank, b=b):
                    if i == 0:
                        wait(t, h2_tok)
                        if b == 0:
                            wait(t, cast_toks["woutT"])
                    bwait(t, 2 + i % 2)
                    for kc in range(KC):
                        li = t.matmul(bank[:, :],
                                      H2[:, kc * 512 + tc * 128:kc * 512 + tc * 128 + 128],
                                      WOUTT[:, kc * 1024 + dn * 512:
                                            kc * 1024 + (dn + 1) * 512],
                                      start=(kc == 0), stop=(kc == KC - 1))
                    at_mm[i] = ch_t.inc(li)
                T(on_t)

                def on_v(v, i=i, tc=tc, dn=dn, bank=bank, b=b):
                    wait(v, at_mm[i])
                    if i == 0:
                        wait(v, x2t[b])
                        if b > 0:
                            wait(v, ln2_toks[b - 1])
                            wait(v, res_sp[b - 1])
                    sl = slice(tc * 1024 + dn * 512, tc * 1024 + (dn + 1) * 512)
                    v.tensor_tensor(RES[:, sl], bank[:, :], X2[:, sl], ALU.add)
                    at_ev[i] = ch_v.inc(v.drain(fusable=False))
                V(on_v)
                bset(2 + i % 2, at_ev[i])
            res_tok = at_ev[-1]
            res_last[b] = res_tok

            def on_rsp(sy, b=b, res_tok=res_tok):
                wait(sy, res_tok)
                res_sp[b] = RSQ.dma(sy, residsp[:, b * 4096:(b + 1) * 4096], RES[:, :])
            SY(on_rsp)

            ln2_guard = [res_tok]
            if b > 0:
                ln2_guard.append(logit_t[b - 1])
                ln2_guard.append(ln2_sp[b - 1])
            ln2_tok = emit_ln(RES, [HIT, LN2F32], JNK2, _multi(ln2_guard))
            ln2_toks[b] = ln2_tok

            def on_lo(v, ln2_tok=ln2_tok):
                wait(v, ln2_tok)
                v.tensor_tensor(LOT[:, :], LN2F32[:, :], HIT[:, :], ALU.subtract)
                ch_v.inc(v.drain(fusable=False))
            V(on_lo)
            lo_tok = ch_v.tok()

            hif_tok = transpose_block(HIT, HIF, True, ln2_tok)
            lof_tok = transpose_block(LOT, LOF, True, lo_tok)

            def on_lsp(sy, b=b, hif_tok=hif_tok, ln2_tok=ln2_tok):
                wait(sy, ln2_tok)
                ln2tm_sp[b] = LNQ.dma(
                    sy, ln2tmsp[b * TB:(b + 1) * TB, :].rearrange("(tc p) d -> p tc d", p=128),
                    HIT[:, :].rearrange("p (tc d) -> p tc d", tc=4))
                ln2_sp[b] = ln2tm_sp[b]
            SY(on_lsp)

            lg_mm = [None] * 4
            lg_ev = [None] * 4
            for tc in range(4):
                bank = PS[4 + tc % 2]

                def on_t(t, tc=tc, bank=bank, b=b):
                    if tc == 0:
                        wait(t, hif_tok)
                        wait(t, lof_tok)
                        if b == 0:
                            wait(t, rt_tok)
                    bwait(t, 4 + tc % 2)
                    for kc in range(KC):
                        sl = slice(kc * 512 + tc * 128, kc * 512 + tc * 128 + 128)
                        t.matmul(bank[:, 0:32], HIF[:, sl], rthi[:, kc * 32:(kc + 1) * 32],
                                 start=(kc == 0), stop=False)
                        t.matmul(bank[:, 32:64], HIF[:, sl], rtlo[:, kc * 32:(kc + 1) * 32],
                                 start=False, stop=False)
                        li = t.matmul(bank[:, 0:32], LOF[:, sl],
                                      rthi[:, kc * 32:(kc + 1) * 32],
                                      start=False, stop=(kc == KC - 1))
                    lg_mm[tc] = ch_t.inc(li)
                T(on_t)

                lg_cp = [None]

                def on_sc(s, tc=tc, bank=bank, b=b):
                    wait(s, lg_mm[tc])
                    cc = b * 4 + tc
                    s.copy(LOGI[:, cc * 32:(cc + 1) * 32], bank[:, 0:32])
                    lg_cp[0] = ch_a.inc(s.drain(fusable=False))
                A(on_sc)

                def on_v(v, tc=tc, bank=bank, b=b):
                    wait(v, lg_cp[0])
                    cc = b * 4 + tc
                    v.tensor_tensor(LOGI[:, cc * 32:(cc + 1) * 32],
                                    LOGI[:, cc * 32:(cc + 1) * 32],
                                    bank[:, 32:64], ALU.add)
                    lg_ev[tc] = ch_v.inc(v.drain(fusable=False))
                V(on_v)
                bset(4 + tc % 2, lg_ev[tc])
            logit_t[b] = lg_ev[-1]

            if phases >= 5:
                route_soft(b)

            if b == 0:
                tap("u2", U2[:, :], u_tok)
                tap("h2", H2[:, :], h2_tok)
                tap("resid", RES[:, :], res_tok)
                tap("hit", HIT[:, :], ln2_tok)

        tap("logits", LOGI[:, :], logit_t[NBLK - 1])

        if phases < 5:
            finalize_partial()
            return nc

        route_ranks(NBLK - 1)
        tap("wma", WMA[:, :], routeA_fin[NBLK - 1])
        bookkeeping_fin = sc_last[0]

        if phases < 6:
            finalize_partial()
            return nc

        EXS = [bfa(0, KC * D), bfa(16, KC * D), bfa(32, KC * D), bfa(144, KC * D)]
        EXC = [EXQ[0], EXQ[1], WQ, CCL]
        ex_t = [None] * NEXP
        IQ = [LSQ[0], LSQ[1]]
        GQ = [ACQ[0], ACQ[1]]

        def exs_load(g, ei, guard=None):
            if guard is not None:
                wait(g, guard)
            ex_t[ei] = EXC[ei % 4].dma(
                g, EXS[ei % 4][:, :].rearrange("p (c o) -> p c o", c=KC),
                experts_d[ei * D:(ei + 1) * D, :].rearrange("(c p) o -> p c o", p=128))

        ROT = 4
        IDXS4 = [f32a(96, 256), f32a(97, 256), f32a(98, 256), f32a(99, 256)]
        IDXU4 = [f32a(100, 256), f32a(101, 256), f32a(102, 256), f32a(103, 256)]
        GXT4 = [bfa(104, 4096), bfa(112, 4096), bfa(120, 4096), bfa(128, 4096)]
        GXF2 = [bfa(64, 4096), bfa(72, 4096)]
        YSB2 = [bfa(80, 4096), bfa(88, 4096)]
        SQC = [DmaQ(ctx, nc, "SQC0"), DmaQ(ctx, nc, "SQC1"),
               DmaQ(ctx, nc, "SQC2"), DmaQ(ctx, nc, "SQC3")]
        GQ4 = [ACQ[0], ACQ[1], DmaQ(ctx, nc, "GQ2"), DmaQ(ctx, nc, "GQ3")]
        idx_t = [None] * ROT
        idxu_t = [None] * ROT
        gxt_t = [None] * ROT
        tr_fin = {}
        mm_fin = {}
        ev_fin = {}
        sc_fin = {}

        def S0(e):
            r4 = e % ROT

            def on_idx(sy, e=e, r4=r4):
                if e < 2:
                    wait(sy, bookkeeping_fin)
                    wait(sy, SCQ.tok())
                    for b in range(NBLK):
                        wait(sy, ln2tm_sp[b])
                    wait(sy, res_sp[NBLK - 1])
                if e - ROT >= 0:
                    wait(sy, sc_fin[e - ROT])
                idx_t[r4] = IQ[e % 2].dma(
                    sy, IDXS4[r4][:, 0:8].rearrange("p (c two) -> p c two", two=2),
                    idxstage[e * 512:(e + 1) * 512, :].rearrange(
                        "(c p) two -> p c two", p=128))
            SY(on_idx)

            def on_cast(v, e=e, r4=r4):
                wait(v, idx_t[r4])
                idxu_t[r4] = ch_v.inc(v.tensor_copy(
                    IDXU4[r4][:, 0:8].bitcast(mybir.dt.uint32), IDXS4[r4][:, 0:8]))
            V(on_cast)

        def S2(e):
            r4 = e % ROT

            def on_gather(g, e=e, r4=r4):
                wait(g, idxu_t[r4])
                if e - ROT + 1 > 0 and (e - ROT) in tr_fin:
                    wait(g, tr_fin[e - ROT])
                tks = []
                for c in range(4):
                    tks.append(GQ4[c].dma_indirect_gather(
                        g, GXT4[r4][:, c * 1024:(c + 1) * 1024], ln2tmsp,
                        IDXU4[r4][:, 0:8].bitcast(mybir.dt.uint32).rearrange(
                            "p (c two) -> p c two", two=2)[:, c:c + 1, 0:1],
                        bnd_regs["tok"]))
                gxt_t[r4] = _multi(tks)
            G(on_gather)

        def S345(e):
            r4 = e % ROT
            s2 = e % 2
            tr_toks = [None] * 8
            for i in range(8):
                bank = PT[i % 2]

                def on_tr(t, i=i, r4=r4, s2=s2, e=e, bank=bank):
                    if i == 0:
                        wait(t, gxt_t[r4])
                        if (e - 2) in mm_fin:
                            wait(t, mm_fin[e - 2])
                    bwait(t, 6 + i % 2)
                    for tc in range(4):
                        li = t.transpose(bank[:, tc * 128:tc * 128 + 128],
                                         GXT4[r4][:, tc * 1024 + i * 128:
                                                  tc * 1024 + i * 128 + 128],
                                         ident[:, :])
                    tr_toks[i] = ch_t.inc(li)
                T(on_tr)

                def on_tre(v, i=i, s2=s2, e=e, bank=bank):
                    wait(v, tr_toks[i])
                    tr_fin[e] = ch_v.inc(v.tensor_copy(
                        GXF2[s2][:, i * 512:(i + 1) * 512], bank[:, 0:512]))
                V(on_tre)
                bset(6 + i % 2, tr_fin[e])

            mm_toks = [None] * 8
            ev_toks = [None] * 8
            for gi, (ts, on) in enumerate([(ts, on) for ts in range(4) for on in range(2)]):
                bank = PS[gi % 6]

                def on_t(t, gi=gi, ts=ts, on=on, bank=bank, s2=s2, e=e):
                    if gi == 0:
                        wait(t, tr_fin[e])
                        wait(t, ex_t[e])
                        if (e - 2) in ev_fin:
                            wait(t, ev_fin[e - 2])
                    bwait(t, gi % 6)
                    for kc in range(KC):
                        li = t.matmul(bank[:, :],
                                      GXF2[s2][:, kc * 512 + ts * 128:
                                               kc * 512 + ts * 128 + 128],
                                      EXS[e % 4][:, kc * 1024 + on * 512:
                                                 kc * 1024 + (on + 1) * 512],
                                      start=(kc == 0), stop=(kc == KC - 1))
                    mm_toks[gi] = ch_t.inc(li)
                T(on_t)

                def on_ev(v, gi=gi, ts=ts, on=on, bank=bank, s2=s2, e=e, r4=r4):
                    wait(v, mm_toks[gi])
                    if gi == 0 and (e - 2) in sc_fin:
                        wait(v, sc_fin[e - 2])
                    wcol = IDXS4[r4][:, 0:8].rearrange("p (c two) -> p c two", two=2)[
                        :, ts:ts + 1, 1:2]
                    ev_toks[gi] = ch_v.inc(v.tensor_scalar(
                        YSB2[s2][:, ts * 1024 + on * 512:ts * 1024 + (on + 1) * 512],
                        bank[:, :], wcol, None, ALU.mult))
                V(on_ev)
                bset(gi % 6, ev_toks[gi])
            mm_fin[e] = mm_toks[-1]
            ev_fin[e] = ev_toks[-1]

            def on_scat(g, e=e, s2=s2, r4=r4):
                wait(g, ev_fin[e])
                tks = []
                for c in range(4):
                    dstbuf = bufA if c < 2 else bufB
                    tks.append(SQC[c].dma_indirect_scatter(
                        g, dstbuf,
                        IDXU4[r4][:, 0:8].bitcast(mybir.dt.uint32).rearrange(
                            "p (c two) -> p c two", two=2)[:, c:c + 1, 0:1],
                        YSB2[s2][:, c * 1024:(c + 1) * 1024], bnd_regs["tok"]))
                sc_fin[e] = _multi(tks)
            G(on_scat)

        def ph3_pre(g):
            wait(g, logit_t[NBLK - 1])
            wait(g, bookkeeping_fin)
            wait(g, SCQ.tok())
            for ei in range(4):
                exs_load(g, ei)
        G(ph3_pre)

        S0(0)
        S0(1)
        S2(0)
        for e in range(NEXP):
            if e + 2 < NEXP:
                S0(e + 2)
            if e + 1 < NEXP:
                S2(e + 1)
            S345(e)
            if e + 4 < NEXP:
                def on_pf2(g, e=e):
                    wait(g, mm_fin[e])
                    exs_load(g, e + 4)
                G(on_pf2)

        moe_fin = [sc_fin[NEXP - 2], sc_fin[NEXP - 1]]

        if phases < 7:
            finalize_partial()
            return nc

        st_toks = [None] * NBLK
        p4_guard = [None, None]
        AB4 = [bfa(100, 4096), bfa(108, 4096), bfa(116, 4096), bfa(124, 4096)]
        for b in range(NBLK):
            sl = b % 2
            ld_tok = [None]

            def on_ld(sy, b=b, sl=sl):
                if b == 0:
                    wait(sy, moe_fin[0])
                    wait(sy, moe_fin[1])
                    wait(sy, TAPQ.tok())
                if p4_guard[sl] is not None:
                    wait(sy, p4_guard[sl])
                P4Q[sl].dma(sy, R4[sl][:, :], residsp[:, b * 4096:(b + 1) * 4096])
                P4Q[sl].dma(sy, AB4[sl][:, :].rearrange("p (tc d) -> p tc d", tc=4),
                            bufA[b * TB:(b + 1) * TB, :].rearrange(
                                "(tc p) d -> p tc d", p=128))
                ld_tok[0] = P4Q[sl].dma(
                    sy, AB4[2 + sl][:, :].rearrange("p (tc d) -> p tc d", tc=4),
                    bufB[b * TB:(b + 1) * TB, :].rearrange("(tc p) d -> p tc d", p=128))
            SY(on_ld)

            add_tok = [None]

            def on_add(v, b=b, sl=sl):
                wait(v, ld_tok[0])
                wait(v, STQ[sl].tok())
                v.tensor_tensor(O4[sl][:, :], R4[sl][:, :], AB4[sl][:, :], ALU.add)
                tk = ch_v.inc(v.drain(fusable=False)); wait(v, tk)
                v.tensor_tensor(O4[sl][:, :], O4[sl][:, :], AB4[2 + sl][:, :], ALU.add)
                add_tok[0] = ch_v.inc(v.drain(fusable=False))
                p4_guard[sl] = add_tok[0]
            V(on_add)

            def on_st(sa, b=b, sl=sl):
                wait(sa, add_tok[0])
                st_toks[b] = STQ[sl].dma(
                    sa, out_d[b * TB:(b + 1) * TB, :].rearrange("(tc p) d -> p tc d", p=128),
                    O4[sl][:, :].rearrange("p (tc d) -> p tc d", tc=4))
            A(on_st)

        def fin(s):
            wait(s, STQ[0].tok())
            wait(s, STQ[1].tok())
            wait(s, TAPQ.tok())
        A(fin)

    return nc


def kernel(L0_attn_x, L0_attn_Wk, L0_attn_Wv, L0_attn_Wq, L0_attn_Wout,
           L0_attn_W0, L0_attn_W1, L0_attn_Wg_decay, L0_attn_bg_decay,
           L0_attn_Wg_lr, L0_attn_bg_lr, L0_attn_Wg_mom, L0_attn_bg_mom,
           L0_ffn_router_w, L0_ffn_expert_weights):
    B, S, DD = L0_attn_x.shape
    xf = np.ascontiguousarray(np.asarray(L0_attn_x, dtype=np.float32).reshape(B * S, DD))
    wg3 = np.ascontiguousarray(np.stack([
        np.asarray(L0_attn_Wg_decay)[0], np.asarray(L0_attn_Wg_lr)[0],
        np.asarray(L0_attn_Wg_mom)[0]], axis=0).astype(np.float32))
    bg3 = np.zeros((1, 4), dtype=np.float32)
    bg3[0, 0] = np.asarray(L0_attn_bg_decay).reshape(())
    bg3[0, 1] = np.asarray(L0_attn_bg_lr).reshape(())
    bg3[0, 2] = np.asarray(L0_attn_bg_mom).reshape(())
    ident = np.eye(128, dtype=BFNP)
    oh = np.zeros((NEXP, NEXP, 128), dtype=BFNP)
    for e_ in range(NEXP):
        oh[e_, e_, :] = 1
    oh = oh.reshape(NEXP, NEXP * 128)
    lt = np.triu(np.ones((128, 128), dtype=np.float32), k=1).astype(BFNP)
    iota32 = np.broadcast_to(np.arange(NEXP, dtype=np.float32), (128, NEXP)).copy()
    iotac = np.arange(128, dtype=np.float32).reshape(128, 1)
    tokc4 = (np.arange(128, dtype=np.float32).reshape(128, 1)
             + 128.0 * np.arange(4, dtype=np.float32).reshape(1, 4))
    experts = np.ascontiguousarray(
        np.asarray(L0_ffn_expert_weights, dtype=np.float32).reshape(NEXP * D, D))

    nc = build(taps=tuple(DEBUG_TAPS))
    base = {
        "wkt": np.ascontiguousarray(np.asarray(L0_attn_Wk, dtype=np.float32).T),
        "wvt": np.ascontiguousarray(np.asarray(L0_attn_Wv, dtype=np.float32).T),
        "wqt": np.ascontiguousarray(np.asarray(L0_attn_Wq, dtype=np.float32).T),
        "woutt": np.ascontiguousarray(np.asarray(L0_attn_Wout, dtype=np.float32).T),
        "w0t": np.ascontiguousarray(np.asarray(L0_attn_W0, dtype=np.float32).T),
        "w1t": np.ascontiguousarray(np.asarray(L0_attn_W1, dtype=np.float32).T),
        "w1": np.ascontiguousarray(np.asarray(L0_attn_W1, dtype=np.float32)),
        "wg3": wg3, "bg3": bg3,
        "router": np.ascontiguousarray(np.asarray(L0_ffn_router_w, dtype=np.float32)),
        "experts": experts, "identin": ident, "oh": oh,
        "lt128d": lt, "iota32d": iota32, "iotacd": iotac, "tokc4d": tokc4,
    }
    in_maps = [dict(base, x=xf[c * TC:(c + 1) * TC]) for c in range(8)]
    res = run_bass_kernel_spmd(nc, in_maps, core_ids=list(range(8)))
    out = np.concatenate([res.results[c]["out"] for c in range(8)], axis=0)
    kernel.last_results = res
    return out.reshape(B, S, DD)


# version 20
# speedup vs baseline: 1.0982x; 1.0114x over previous
import numpy as np
import ml_dtypes
from contextlib import ExitStack
import concourse.bass as bass
import concourse.mybir as mybir
from concourse.bass_utils import run_bass_kernel_spmd
from concourse.bass import IndirectOffsetOnAxis as IOA

F32 = mybir.dt.float32
BF16 = mybir.dt.bfloat16
I32 = mybir.dt.int32
AF = mybir.ActivationFunctionType
ALU = mybir.AluOpType
AX = mybir.AxisListType.X
BFNP = ml_dtypes.bfloat16

D = 1024
TC = 4096
NBLK = 8
TB = 512
KC = 8
NEXP = 32
EGRP = 2
NGRP = NEXP // EGRP
SCALE = 2.0 / 4096.0
ERR_CLIP = 5.0
GRAD_CLIP = 1.0
LN_EPS = 1e-5

DEBUG_TAPS = []

TAP_SHAPES = {
    "ln1": ([128, 4096], BF16), "kfm": ([128, 4096], BF16), "vtm": ([128, 4096], BF16),
    "qfm": ([128, 4096], BF16), "a1tm": ([128, 4096], BF16), "etm": ([128, 4096], BF16),
    "d1tm": ([128, 4096], BF16), "g0": ([128, 8192], F32), "g1": ([128, 8192], F32),
    "w0n": ([128, 8192], BF16), "resid": ([128, 4096], F32), "hit": ([128, 4096], BF16),
    "logits": ([128, 1024], F32), "wma": ([128, 1024], F32), "gacc": ([1, 4], F32),
    "gsig": ([1, 4], F32), "u2": ([128, 4096], BF16), "h2": ([128, 4096], BF16),
}


class Chan:
    def __init__(self, ctx, nc, name):
        self.h = ctx.enter_context(nc.semaphore(name))
        self.n = 0

    def inc(self, inst, k=1):
        inst.then_inc(self.h, k)
        self.n += k
        return (self.h, self.n)

    def tok(self):
        return (self.h, self.n)


def wait(eng, tok):
    if tok is None:
        return
    if isinstance(tok, _Multi):
        for t in tok.toks:
            wait(eng, t)
        return
    eng.wait_ge(tok[0], tok[1])


class _Multi:
    def __init__(self, toks):
        self.toks = [t for t in toks if t is not None]


def _multi(toks):
    return _Multi(toks)


class DmaQ:

    def __init__(self, ctx, nc, name):
        self.h = ctx.enter_context(nc.semaphore(name))
        self.n = 0

    def dma(self, eng, out, in_, **kw):
        if self.n:
            eng.wait_ge(self.h, self.n)
        inst = eng.dma_start(out, in_, **kw)
        inst.then_inc(self.h, 16)
        self.n += 16
        return (self.h, self.n)

    def dma_indirect_gather(self, eng, out, in_dram, idx_ap, bound):
        if self.n:
            eng.wait_ge(self.h, self.n)
        inst = eng.indirect_dma_start(
            out=out, out_offset=None, in_=in_dram,
            in_offset=IOA(axis=0, ap=idx_ap),
            bounds_check=bound, oob_is_err=False)
        inst.then_inc(self.h, 16)
        self.n += 16
        return (self.h, self.n)

    def dma_indirect_scatter(self, eng, out_dram, idx_ap, in_sb, bound):
        if self.n:
            eng.wait_ge(self.h, self.n)
        inst = eng.indirect_dma_start(
            out=out_dram, out_offset=IOA(axis=0, ap=idx_ap),
            in_=in_sb, in_offset=None,
            bounds_check=bound, oob_is_err=False)
        inst.then_inc(self.h, 16)
        self.n += 16
        return (self.h, self.n)

    def dma_tr(self, eng, out, in_):
        if self.n:
            eng.wait_ge(self.h, self.n)
        inst = eng.dma_start_transpose(out, in_)
        inst.then_inc(self.h, 16)
        self.n += 16
        return (self.h, self.n)

    def tok(self):
        return (self.h, self.n) if self.n else None


def build(taps=(), phases=7):
    nc = bass.Bass("TRN2", target_bir_lowering=False, debug=False)

    x_d = nc.dram_tensor("x", [TC, D], F32, kind="ExternalInput").ap()
    w_d = {w: nc.dram_tensor(w, [D, D], F32, kind="ExternalInput").ap()
           for w in ["w1", "wkt", "wvt", "wqt", "w0t", "w1t", "woutt"]}
    wg3_d = nc.dram_tensor("wg3", [3, D], F32, kind="ExternalInput").ap()
    bg3_d = nc.dram_tensor("bg3", [1, 4], F32, kind="ExternalInput").ap()
    router_d = nc.dram_tensor("router", [NEXP, D], F32, kind="ExternalInput").ap()
    experts_d = nc.dram_tensor("experts", [NEXP * D, D], F32, kind="ExternalInput").ap()
    ident_d = nc.dram_tensor("identin", [128, 128], BF16, kind="ExternalInput").ap()
    oh_d = nc.dram_tensor("oh", [NEXP, NEXP * 128], BF16, kind="ExternalInput").ap()
    lt_d = nc.dram_tensor("lt128d", [128, 128], BF16, kind="ExternalInput").ap()
    iota32_d = nc.dram_tensor("iota32d", [128, NEXP], F32, kind="ExternalInput").ap()
    iotac_d = nc.dram_tensor("iotacd", [128, 1], F32, kind="ExternalInput").ap()
    tokc4_d = nc.dram_tensor("tokc4d", [128, 4], F32, kind="ExternalInput").ap()
    out_d = nc.dram_tensor("out", [TC, D], F32, kind="ExternalOutput").ap()

    ksp = nc.dram_tensor("ksp", [128, KC * TC], BF16, kind="Internal").ap()
    qsp = nc.dram_tensor("qsp", [128, KC * TC], BF16, kind="Internal").ap()
    vsp = nc.dram_tensor("vsp", [128, (TC // 128) * D], BF16, kind="Internal").ap()
    ln2sp = nc.dram_tensor("ln2sp", [128, KC * TC], BF16, kind="Internal").ap()
    residsp = nc.dram_tensor("residsp", [128, (TC // 128) * D], F32, kind="Internal").ap()
    moeacc = nc.dram_tensor("moeacc", [TC, D], F32, kind="Internal").ap()
    ln2tmsp = nc.dram_tensor("ln2tmsp", [TC, D], BF16, kind="Internal").ap()
    idxstage = nc.dram_tensor("idxstage", [NEXP * 512, 2], F32, kind="Internal").ap()
    bufA = nc.dram_tensor("bufA", [TC, D], BF16, kind="Internal").ap()
    bufB = nc.dram_tensor("bufB", [TC, D], BF16, kind="Internal").ap()
    ccin = nc.dram_tensor("ccin", [128, 8192], F32, kind="Internal").ap()
    ccout = nc.dram_tensor("ccout", [128, 8192], F32, kind="Internal").ap()
    ccg_in = nc.dram_tensor("ccg_in", [1, 4], F32, kind="Internal").ap()
    ccg_out = nc.dram_tensor("ccg_out", [1, 4], F32, kind="Internal").ap()
    ccin2 = nc.dram_tensor("ccin2", [128, 8192], F32, kind="Internal").ap()
    ccout2 = nc.dram_tensor("ccout2", [128, 8192], F32, kind="Internal").ap()
    gbd = nc.dram_tensor("gbd", [1, 4], F32, kind="Internal").ap()

    tap_d = {t: nc.dram_tensor("tap_" + t, TAP_SHAPES[t][0], TAP_SHAPES[t][1],
                               kind="ExternalOutput").ap() for t in taps}

    with ExitStack() as ctx:
        en = ctx.enter_context
        ART = en(nc.sbuf_tensor("arena", [128, 102400], BF16))
        ARF = ART.bitcast(F32)
        ident = en(nc.sbuf_tensor("ident", [128, 128], BF16))
        rthi = en(nc.sbuf_tensor("rthi", [128, KC * NEXP], BF16))
        rtlo = en(nc.sbuf_tensor("rtlo", [128, KC * NEXP], BF16))
        wgt = en(nc.sbuf_tensor("wgt", [128, KC * 4], BF16))
        small = en(nc.sbuf_tensor("small", [128, 64], F32))
        stats = en(nc.sbuf_tensor("stats", [128, 32], F32))
        g3 = en(nc.sbuf_tensor("g3", [1, 16], F32))
        ones_bf = en(nc.sbuf_tensor("ones_bf", [128, 1], BF16))
        colsum_bf = en(nc.sbuf_tensor("colsum_bf", [128, 8], BF16))
        ones1 = en(nc.sbuf_tensor("ones1", [1, 128], F32))
        ones1b = en(nc.sbuf_tensor("ones1b", [1, 128], BF16))
        lt128 = en(nc.sbuf_tensor("lt128", [128, 128], BF16))
        iota32 = en(nc.sbuf_tensor("iota32", [128, NEXP], F32))
        iotac = en(nc.sbuf_tensor("iotac", [128, 1], F32))
        tokc4 = en(nc.sbuf_tensor("tokc4", [128, 4], F32))
        baserow = en(nc.sbuf_tensor("baserow", [1, 2 * NEXP], F32))
        baserow_bf = en(nc.sbuf_tensor("baserow_bf", [1, 2 * NEXP], BF16))
        br4b = en(nc.sbuf_tensor("br4b", [1, 256], BF16))
        wgsb = en(nc.sbuf_tensor("wgsb", [4, D], BF16))

        PS = [en(nc.psum_tensor(f"ps{i}", [128, 512], F32)) for i in range(6)]
        PT = [en(nc.psum_tensor(f"pt{i}", [128, 1024], BF16)) for i in range(2)]
        PTF = [p.bitcast(F32) for p in PT]
        PSF = [None] * 8

        def bwait(eng, bi):
            wait(eng, PSF[bi])

        def bset(bi, tok):
            PSF[bi] = tok

        def bfa(off_kb, cols):
            off = int(off_kb * 1024) // 2
            return ART[:, off:off + cols]

        def f32a(off_kb, cols):
            off = int(off_kb * 1024) // 4
            return ARF[:, off:off + cols]

        W_OFF = {"wk": 0, "wv": 16, "wq": 32, "w0": 48, "w1": 64, "w1p": 80}
        WT = {k: bfa(v, KC * D) for k, v in W_OFF.items()}
        X1 = f32a(96, 4096)
        LN1T = bfa(112, 4096)
        LN1F = bfa(120, 4096)
        QSB = bfa(128, 4096)
        KSB = bfa(136, 4096)
        VSB = bfa(144, 4096)
        RSBF = f32a(168, 1024)
        RHI = bfa(172, 1024)
        RLO = bfa(174, 1024)
        JNK1A = bfa(184, 4096)
        G0 = f32a(0, 8192)
        WOUTT = bfa(32, KC * D)
        G1 = f32a(96, 8192)
        KLD = [bfa(128, 4096), bfa(136, 4096)]
        VLD = [bfa(144, 4096), bfa(152, 4096)]
        SGT = bfa(160, 4096)
        PRC = bfa(168, 4096)
        A1T = bfa(176, 4096)
        AF1 = bfa(184, 4096)
        ETM = bfa(192, 4096)
        W0N = bfa(128, KC * D)
        W1N = bfa(144, KC * D)
        X2 = f32a(0, 4096)
        JNK2 = bfa(16, 4096)
        JNKF2 = f32a(18, 512)
        RES = f32a(48, 4096)
        LN2F32 = f32a(64, 4096)
        HIT = bfa(80, 4096)
        LOT = bfa(88, 4096)
        HIF = bfa(96, 4096)
        LOF = bfa(104, 4096)
        WMT = bfa(112, 4096)
        LOGI = f32a(120, 1024)
        WMA = f32a(124, 1024)
        Q2 = [bfa(160, 4096), bfa(168, 4096)]
        SG2 = bfa(176, 4096)
        U2 = bfa(184, 4096)
        H2 = bfa(192, 4096)
        EX = [bfa(0, KC * D), bfa(16, KC * D), bfa(32, KC * D), bfa(48, KC * D)]
        OHSB = bfa(64, NEXP * 128)
        WBC = bfa(72, 1024)
        LSTR = [bfa(74, 4096), bfa(82, 4096)]
        ZSB = [[bfa(90, 4096), bfa(106, 4096)], [bfa(98, 4096), bfa(136, 4096)]]
        STG = [bfa(144, 4096), bfa(152, 4096)]
        R4 = [f32a(0, 4096), f32a(16, 4096)]
        M4 = [f32a(32, 4096), f32a(48, 4096)]
        O4 = [f32a(64, 4096), f32a(80, 4096)]

        ch_t = Chan(ctx, nc, "ch_t")
        ch_a = Chan(ctx, nc, "ch_a")
        ch_v = Chan(ctx, nc, "ch_v")
        ch_cc = Chan(ctx, nc, "ch_cc")
        WQ = DmaQ(ctx, nc, "WQ")
        TRQ = DmaQ(ctx, nc, "TRQ")
        XQ = DmaQ(ctx, nc, "XQ")
        SPQ = DmaQ(ctx, nc, "SPQ")
        SPK = DmaQ(ctx, nc, "SPK")
        SPV = DmaQ(ctx, nc, "SPV")
        KVQ = [DmaQ(ctx, nc, "KVQ0"), DmaQ(ctx, nc, "KVQ1")]
        Q2Q = DmaQ(ctx, nc, "Q2Q")
        RSQ = DmaQ(ctx, nc, "RSQ")
        LNQ = DmaQ(ctx, nc, "LNQ")
        CCS = DmaQ(ctx, nc, "CCS")
        CCL = DmaQ(ctx, nc, "CCL")
        GBQ = DmaQ(ctx, nc, "GBQ")
        EXQ = [DmaQ(ctx, nc, "EXQ0"), DmaQ(ctx, nc, "EXQ1")]
        LSQ = [DmaQ(ctx, nc, "LSQ0"), DmaQ(ctx, nc, "LSQ1")]
        ACQ = [DmaQ(ctx, nc, "ACQ0"), DmaQ(ctx, nc, "ACQ1")]
        P4Q = [DmaQ(ctx, nc, "P4Q0"), DmaQ(ctx, nc, "P4Q1")]
        STQ = [DmaQ(ctx, nc, "STQ0"), DmaQ(ctx, nc, "STQ1")]
        TAPQ = DmaQ(ctx, nc, "TAPQ")

        block = en(nc.Block())
        T, A, V, SY, G = block.tensor, block.scalar, block.vector, block.sync, block.gpsimd

        def tap(name, src, tok):
            if name in tap_d:
                def f(g):
                    wait(g, tok)
                    TAPQ.dma(g, tap_d[name], src)
                G(f)

        cast_toks = {}

        def stw_g(g):
            for w, q in [("wq", WQ), ("wk", EXQ[0]), ("wv", EXQ[1]),
                         ("w0", ACQ[0]), ("w1", ACQ[1])]:
                cast_toks[w + "T"] = q.dma(
                    g, WT[w][:, :].rearrange("p (c e) -> p c e", c=KC),
                    w_d[w + "t"].rearrange("(c p) e -> p c e", p=128))
            cast_toks["router"] = WQ.dma(g, RSBF[0:32, 0:D], router_d)
            cast_toks["bg3"] = WQ.dma(g, g3[0:1, 4:8], bg3_d)
            cast_toks["wg3"] = WQ.dma(g, wgsb[0:3, :], wg3_d)
            cast_toks["lt"] = WQ.dma(g, lt128[:, :], lt_d)
            cast_toks["iota32"] = WQ.dma(g, iota32[:, :], iota32_d)
            cast_toks["iotac"] = WQ.dma(g, iotac[:, :], iotac_d)
            cast_toks["tokc4"] = WQ.dma(g, tokc4[:, :], tokc4_d)
            cast_toks["w1p"] = WQ.dma(
                g, WT["w1p"].rearrange("p (c e) -> p c e", c=KC),
                w_d["w1"].rearrange("(c p) e -> p c e", p=128))
        G(stw_g)

        def stw_sy(sy):
            cast_toks["ident"] = TRQ.dma(sy, ident[:, :], ident_d)
        SY(stw_sy)

        def stw_wg_t(t):
            wait(t, cast_toks["wg3"])
            wait(t, cast_toks["ident"])
            bwait(t, 6)
            for kc in range(KC):
                li = t.transpose(PT[0][0:128, kc * 128:kc * 128 + 3],
                                 wgsb[0:3, kc * 128:(kc + 1) * 128], ident[0:3, 0:3])
            ch_t.inc(li)
        T(stw_wg_t)
        wg_ps_tok = ch_t.tok()

        def stw_wg_v(v):
            wait(v, wg_ps_tok)
            for kc in range(KC):
                v.tensor_copy(wgt[:, kc * 4:kc * 4 + 3], PT[0][:, kc * 128:kc * 128 + 3])
            ch_v.inc(v.drain(fusable=False))
        V(stw_wg_v)
        wgt_tok = ch_v.tok()
        bset(6, wgt_tok)

        def stw_r_v(v):
            wait(v, cast_toks["router"])
            v.tensor_copy(RHI[0:32, 0:D], RSBF[0:32, 0:D])
            t1 = ch_v.inc(v.drain(fusable=False))
            wait(v, t1)
            v.tensor_tensor(RLO[0:32, 0:D], RSBF[0:32, 0:D], RHI[0:32, 0:D], ALU.subtract)
            ch_v.inc(v.drain(fusable=False))
        V(stw_r_v)
        rhilo_tok = ch_v.tok()

        rt_ev = [None] * KC
        rt_pe = [None] * KC
        for kc in range(KC):
            def rt_t(t, kc=kc):
                if kc == 0:
                    wait(t, rhilo_tok)
                    bwait(t, 7)
                if kc >= 2:
                    wait(t, rt_ev[kc - 2])
                half = (kc % 2) * 512
                t.transpose(PT[1][0:128, half:half + 32],
                            RHI[0:32, kc * 128:(kc + 1) * 128], ident[0:32, 0:32])
                li = t.transpose(PT[1][0:128, half + 128:half + 160],
                                 RLO[0:32, kc * 128:(kc + 1) * 128], ident[0:32, 0:32])
                rt_pe[kc] = ch_t.inc(li)
            T(rt_t)

            def rt_e(s, kc=kc):
                wait(s, rt_pe[kc])
                half = (kc % 2) * 512
                s.copy(rthi[:, kc * NEXP:(kc + 1) * NEXP], PT[1][:, half:half + 32])
                s.copy(rtlo[:, kc * NEXP:(kc + 1) * NEXP], PT[1][:, half + 128:half + 160])
                rt_ev[kc] = ch_a.inc(s.drain(fusable=False))
            A(rt_e)
        rt_tok = rt_ev[-1]
        bset(7, rt_tok)

        def emit_ln(x_ap, outs, junk, xin_tok, so=0):
            sums = stats[:, so * 8:so * 8 + 4]
            sq = stats[:, so * 8 + 4:so * 8 + 8]
            rstd = stats[:, 16 + so * 8:20 + so * 8]
            nb = stats[:, 20 + so * 8:24 + so * 8]
            t1 = small[:, 32 + so * 8:36 + so * 8]
            t2 = small[:, 36 + so * 8:40 + so * 8]

            def on_sq(s):
                wait(s, xin_tok)
                for tc in range(4):
                    s.activation(junk[:, tc * 1024:(tc + 1) * 1024],
                                 x_ap[:, tc * 1024:(tc + 1) * 1024],
                                 AF.Square, accum_out=sq[:, tc:tc + 1])
                ch_a.inc(s.drain(fusable=False))
            A(on_sq)
            sq_tok = ch_a.tok()

            def on_v(v):
                wait(v, xin_tok)
                for tc in range(4):
                    v.reduce_sum(sums[:, tc:tc + 1], x_ap[:, tc * 1024:(tc + 1) * 1024], axis=AX)
                tk = ch_v.inc(v.drain(fusable=False)); wait(v, tk)
                wait(v, sq_tok)
                v.tensor_scalar_mul(nb[:, :], sums[:, :], 1.0 / D)
                tk = ch_v.inc(v.drain(fusable=False)); wait(v, tk)
                v.tensor_tensor(t1[:, :], nb[:, :], nb[:, :], ALU.mult)
                v.tensor_scalar(t2[:, :], sq[:, :], 1.0 / D, LN_EPS, ALU.mult, ALU.add)
                tk = ch_v.inc(v.drain(fusable=False)); wait(v, tk)
                v.tensor_tensor(t2[:, :], t2[:, :], t1[:, :], ALU.subtract)
                tk = ch_v.inc(v.drain(fusable=False)); wait(v, tk)
                ti = t1[:, :].bitcast(I32)
                ri = rstd[:, :].bitcast(I32)
                v.tensor_single_scalar(ti, t2[:, :].bitcast(I32), 1, ALU.logical_shift_right)
                tk = ch_v.inc(v.drain(fusable=False)); wait(v, tk)
                v.tensor_scalar(ri, ti, -1, 0x5F3759DF, ALU.mult, ALU.add)
                tk = ch_v.inc(v.drain(fusable=False)); wait(v, tk)
                for _ in range(3):
                    v.tensor_tensor(t1[:, :], rstd[:, :], rstd[:, :], ALU.mult)
                    tk = ch_v.inc(v.drain(fusable=False)); wait(v, tk)
                    v.tensor_tensor(t1[:, :], t1[:, :], t2[:, :], ALU.mult)
                    tk = ch_v.inc(v.drain(fusable=False)); wait(v, tk)
                    v.tensor_scalar(t1[:, :], t1[:, :], -0.5, 1.5, ALU.mult, ALU.add)
                    tk = ch_v.inc(v.drain(fusable=False)); wait(v, tk)
                    v.tensor_tensor(rstd[:, :], rstd[:, :], t1[:, :], ALU.mult)
                    tk = ch_v.inc(v.drain(fusable=False)); wait(v, tk)
                v.tensor_tensor(nb[:, :], nb[:, :], rstd[:, :], ALU.mult)
                tk = ch_v.inc(v.drain(fusable=False)); wait(v, tk)
                v.tensor_scalar_mul(nb[:, :], nb[:, :], -1.0)
                ch_v.inc(v.drain(fusable=False))
            V(on_v)
            stat_tok = ch_v.tok()

            def on_s(s):
                wait(s, stat_tok)
                for ap in outs:
                    for tc in range(4):
                        s.activation(ap[:, tc * 1024:(tc + 1) * 1024],
                                     x_ap[:, tc * 1024:(tc + 1) * 1024], AF.Identity,
                                     bias=nb[:, tc:tc + 1], scale=rstd[:, tc:tc + 1])
                ch_a.inc(s.drain(fusable=False))
            A(on_s)
            return ch_a.tok()

        def transpose_block(src, dst, to_fm, pe_wait):
            n_out = KC if to_fm else 4
            pe_toks = [None] * n_out
            ev_toks = [None] * n_out
            for i in range(n_out):
                bank = PT[i % 2]

                def on_t(t, i=i, bank=bank):
                    if i == 0 and pe_wait is not None:
                        wait(t, pe_wait)
                    bwait(t, 6 + i % 2)
                    if to_fm:
                        for tc in range(4):
                            li = t.transpose(bank[:, tc * 128:tc * 128 + 128],
                                             src[:, tc * 1024 + i * 128:
                                                 tc * 1024 + i * 128 + 128],
                                             ident[:, :])
                    else:
                        for kc in range(KC):
                            li = t.transpose(bank[:, kc * 128:kc * 128 + 128],
                                             src[:, kc * 512 + i * 128:
                                                 kc * 512 + i * 128 + 128],
                                             ident[:, :])
                    pe_toks[i] = ch_t.inc(li)
                T(on_t)

                def on_e(v, i=i, bank=bank):
                    wait(v, pe_toks[i])
                    if to_fm:
                        inst = v.tensor_copy(dst[:, i * 512:(i + 1) * 512], bank[:, 0:512])
                    else:
                        inst = v.tensor_copy(dst[:, i * 1024:(i + 1) * 1024], bank[:, :])
                    ev_toks[i] = ch_v.inc(inst)
                V(on_e)
                bset(6 + i % 2, ev_toks[i])
            return ev_toks[-1]

        def proj_fm(wt_ap, rhs_fm, dst, rhs_tok, w_tok, dst2=None, fn=AF.Identity,
                    fn2=AF.Identity, ev_guard=None):
            mm = [None] * KC
            ev = [None] * KC
            for oc in range(KC):
                bank = PS[oc % 2]

                def on_t(t, oc=oc, bank=bank):
                    if oc == 0:
                        wait(t, rhs_tok)
                        wait(t, w_tok)
                    bwait(t, oc % 2)
                    for kc in range(KC):
                        li = t.matmul(bank[:, :],
                                      wt_ap[:, kc * 1024 + oc * 128:
                                            kc * 1024 + oc * 128 + 128],
                                      rhs_fm[:, kc * 512:(kc + 1) * 512],
                                      start=(kc == 0), stop=(kc == KC - 1))
                    mm[oc] = ch_t.inc(li)
                T(on_t)

                def on_s(v, oc=oc, bank=bank):
                    wait(v, mm[oc])
                    if oc == 0 and ev_guard is not None:
                        wait(v, ev_guard)
                    li = v.tensor_copy(dst[:, oc * 512:(oc + 1) * 512], bank[:, :])
                    if dst2 is not None:
                        li = v.tensor_copy(dst2[:, oc * 512:(oc + 1) * 512], bank[:, :])
                    ev[oc] = ch_v.inc(li)
                V(on_s)
                bset(oc % 2, ev[oc])
            return ev[-1], mm[-1]

        def x_view(b):
            return x_d[b * TB:(b + 1) * TB, :].rearrange("(tc p) d -> p tc d", p=128)

        def sp_view(sp, b):
            return sp.rearrange("p (kc t) -> p kc t", kc=KC)[:, :, b * TB:(b + 1) * TB]

        def sb_kc(sb):
            return sb.rearrange("p (kc t) -> p kc t", kc=KC)

        gate_red = [None] * NBLK
        gate_mm = [None] * NBLK
        spq = [None] * NBLK
        spk = [None] * NBLK
        spv = [None] * NBLK
        xtok = [None] * NBLK

        def z_gates(v):
            v.memset(ones1[:, :], 1.0)
            v.memset(ones_bf[:, :], 1.0)
            v.memset(ones1b[:, :], 1.0)
            v.memset(g3[0:1, 3:4], 0.0)
            v.memset(baserow[:, :], 0.0)
            v.memset(baserow_bf[:, :], 0.0)
            ch_v.inc(v.drain(fusable=False))
        V(z_gates)
        gz_tok = ch_v.tok()

        ZTILE = VLD[1]
        PADP = f32a(160, 256)
        INITQ = GBQ

        def init_v(v):
            v.memset(ZTILE[:, :], 0.0)
            pv = PADP.rearrange("p (c two) -> p c two", two=2)
            v.memset(pv[:, :, 0:1], 100000.0)
            tk = ch_v.inc(v.drain(fusable=False)); wait(v, tk)
            v.memset(pv[:, :, 1:2], 0.0)
            ch_v.inc(v.drain(fusable=False))
        V(init_v)
        initv_tok = ch_v.tok()

        def init_dma(sa):
            wait(sa, initv_tok)
            for i in range(NBLK):
                INITQ.dma(sa, bufA[i * TB:(i + 1) * TB, :].rearrange(
                    "(tc p) d -> p tc d", p=128),
                    ZTILE[:, :].rearrange("p (tc d) -> p tc d", tc=4))
            for i in range(NBLK):
                INITQ.dma(sa, bufB[i * TB:(i + 1) * TB, :].rearrange(
                    "(tc p) d -> p tc d", p=128),
                    ZTILE[:, :].rearrange("p (tc d) -> p tc d", tc=4))
            INITQ.dma(sa, idxstage.rearrange("(c p) two -> p c two", p=128),
                      PADP.rearrange("p (c two) -> p c two", two=2))
        A(init_dma)
        init_tok = INITQ.tok()


        XLQ = [XQ, TRQ, CCS, RSQ]
        XP = [X1, f32a(160, 4096)]
        LNTP = [LN1T, bfa(176, 4096)]
        JNKP = [JNK1A, bfa(192, 4096)]
        pg = [None, None]
        for b in range(NBLK):
            p = b % 2

            def on_x(sy, b=b, p=p, gl=pg[p]):
                if gl is not None:
                    for g in gl:
                        wait(sy, g)
                if b == 1:
                    wait(sy, init_tok)
                    wait(sy, rt_tok)
                wait(sy, TAPQ.tok())
                tks = []
                for tc in range(4):
                    tks.append(XLQ[tc].dma(
                        sy, XP[p][:, tc * 1024:(tc + 1) * 1024],
                        x_d[b * TB + tc * 128:b * TB + (tc + 1) * 128, :]))
                xtok[b] = _multi(tks)
            SY(on_x)
            ln_tok = emit_ln(XP[p], [LNTP[p]], JNKP[p], xtok[b], so=p)
            ln1f_tok = transpose_block(LNTP[p], LN1F, True, ln_tok)

            q_tok, q_mm = proj_fm(WT["wq"], LN1F, QSB, ln1f_tok,
                                  cast_toks["wqT"] if b == 0 else None,
                                  ev_guard=spq[b - 1] if b else None)
            k_tok, _ = proj_fm(WT["wk"], LN1F, KSB, ln1f_tok,
                               cast_toks["wkT"] if b == 0 else None,
                               ev_guard=spk[b - 1] if b else None)

            v_mm = [None] * 8
            v_ev = [None] * 8
            for i, (tc, h) in enumerate([(tc, h) for tc in range(4) for h in range(2)]):
                bank = PS[2 + i % 2]

                def on_t(t, i=i, tc=tc, h=h, bank=bank, b=b):
                    if i == 0:
                        wait(t, ln1f_tok)
                        if b == 0:
                            wait(t, cast_toks["wvT"])
                    bwait(t, 2 + i % 2)
                    for kc in range(KC):
                        li = t.matmul(bank[:, :],
                                      LN1F[:, kc * 512 + tc * 128:kc * 512 + tc * 128 + 128],
                                      WT["wv"][:, kc * 1024 + h * 512:
                                               kc * 1024 + (h + 1) * 512],
                                      start=(kc == 0), stop=(kc == KC - 1))
                    v_mm[i] = ch_t.inc(li)
                T(on_t)

                def on_s(v, i=i, tc=tc, h=h, bank=bank, b=b):
                    wait(v, v_mm[i])
                    if i == 0 and b > 0:
                        wait(v, spv[b - 1])
                    v_ev[i] = ch_v.inc(v.tensor_copy(
                        VSB[:, tc * 1024 + h * 512:tc * 1024 + (h + 1) * 512], bank[:, :]))
                V(on_s)
                bset(2 + i % 2, v_ev[i])
            v_tok = v_ev[-1]

            def on_gt(t, b=b, p=p):
                wait(t, ln_tok)
                if b == 0:
                    wait(t, gz_tok)
                    bwait(t, 4)
                for kc in range(KC):
                    for tc in range(4):
                        li = t.matmul(PS[4][:, kc:kc + 1],
                                      LNTP[p][:, tc * 1024 + kc * 128:
                                           tc * 1024 + kc * 128 + 128],
                                      ones_bf[:, :],
                                      start=(b == 0 and kc == 0 and tc == 0),
                                      stop=(b == NBLK - 1 and kc == KC - 1
                                            and tc == 3))
                gate_mm[b] = ch_t.inc(li)
            T(on_gt)

            def on_sp(sy, b=b, q_tok=q_tok, k_tok=k_tok, v_tok=v_tok):
                wait(sy, q_tok)
                spq[b] = SPQ.dma(sy, sp_view(qsp, b), sb_kc(QSB))
                wait(sy, k_tok)
                spk[b] = SPK.dma(sy, sp_view(ksp, b), sb_kc(KSB))
                wait(sy, v_tok)
                spv[b] = SPV.dma(sy, vsp[:, b * 4096:(b + 1) * 4096], VSB[:, :])
            SY(on_sp)
            pg[p] = [ln_tok, ln1f_tok, gate_mm[b]]

            if b == 0:
                tap("ln1", LN1T[:, :], ln_tok)
                tap("kfm", KSB[:, :], k_tok)
                tap("vtm", VSB[:, :], v_tok)
                tap("qfm", QSB[:, :], q_tok)

        last_1a_pe = ch_t.tok()

        def on_gf1(s):
            wait(s, gate_mm[NBLK - 1])
            s.activation(colsum_bf[:, :], PS[4][:, 0:8], AF.Identity)
            ch_a.inc(s.drain(fusable=False))
        A(on_gf1)
        colsum_tok = ch_a.tok()
        bset(4, colsum_tok)

        def on_gf2(t):
            wait(t, colsum_tok)
            wait(t, wgt_tok)
            bwait(t, 5)
            for kc in range(KC):
                li = t.matmul(PS[5][0:1, 0:3], colsum_bf[:, kc:kc + 1],
                              wgt[:, kc * 4:kc * 4 + 3],
                              start=(kc == 0), stop=(kc == KC - 1))
            ch_t.inc(li)
        T(on_gf2)
        gmv_tok = ch_t.tok()

        def on_gf3(v):
            wait(v, gmv_tok)
            v.tensor_copy(g3[0:1, 0:3], PS[5][0:1, 0:3])
            ch_v.inc(v.drain(fusable=False))
        V(on_gf3)
        gate_red[NBLK - 1] = ch_v.tok()
        bset(5, gate_red[NBLK - 1])

        def finalize_partial():
            def fin_p(g):
                for q in [WQ, TRQ, XQ, SPQ, SPK, SPV, KVQ[0], KVQ[1], Q2Q, RSQ, LNQ,
                          CCS, CCL, GBQ, EXQ[0], EXQ[1], LSQ[0], LSQ[1], ACQ[0],
                          ACQ[1], P4Q[0], P4Q[1], STQ[0], STQ[1], TAPQ]:
                    if q.n:
                        g.wait_ge(q.h, q.n)
            G(fin_p)

        if phases < 2:
            finalize_partial()
            return nc

        def gz(v):
            v.memset(G0[:, :], 0.0)
            v.memset(G1[:, :], 0.0)
            ch_v.inc(v.drain(fusable=False))
        V(gz)
        gzero_tok = ch_v.tok()

        def on_woutt(g):
            wait(g, last_1a_pe)
            cast_toks["woutT"] = CCL.dma(
                g, WOUTT[:, :].rearrange("p (c e) -> p c e", c=KC),
                w_d["woutt"].rearrange("(c p) e -> p c e", p=128))
        G(on_woutt)

        kld_t = [None] * NBLK
        vld_t = [None] * NBLK

        def kv_load(b, guard):
            def f(sy):
                if guard is not None:
                    wait(sy, guard)
                wait(sy, TAPQ.tok())
                wait(sy, init_tok)
                wait(sy, spq[NBLK - 1])
                wait(sy, spk[NBLK - 1])
                wait(sy, spv[NBLK - 1])
                kld_t[b] = KVQ[b % 2].dma(sy, KLD[b % 2][:, :], sp_view(ksp, b))
                vld_t[b] = KVQ[b % 2].dma(sy, VLD[b % 2][:, :],
                                          vsp[:, b * 4096:(b + 1) * 4096])
            SY(f)

        kv_load(0, None)
        blk_done = [None] * NBLK

        def emit_g(lhs_tm, rhs_tm, acc, lhs_tok, rhs_tok, first_guard):
            mm = {}
            ev = {}
            idx = 0
            for ec in range(KC):
                for on in range(2):
                    bank = PS[2 + idx % 2]

                    def on_t(t, ec=ec, on=on, bank=bank, idx=idx):
                        if idx == 0:
                            wait(t, lhs_tok)
                            wait(t, rhs_tok)
                        bwait(t, 2 + idx % 2)
                        for tc in range(4):
                            li = t.matmul(bank[:, :],
                                          lhs_tm[:, tc * 1024 + ec * 128:
                                                 tc * 1024 + ec * 128 + 128],
                                          rhs_tm[:, tc * 1024 + on * 512:
                                                 tc * 1024 + (on + 1) * 512],
                                          start=(tc == 0), stop=(tc == 3))
                        mm[idx] = ch_t.inc(li)
                    T(on_t)

                    def on_v(v, ec=ec, on=on, bank=bank, idx=idx):
                        wait(v, mm[idx])
                        if idx == 0 and first_guard is not None:
                            wait(v, first_guard)
                        sl = slice(ec * 1024 + on * 512, ec * 1024 + (on + 1) * 512)
                        v.scalar_tensor_tensor(acc[:, sl], bank[:, :], SCALE, acc[:, sl],
                                               ALU.mult, ALU.add)
                        ev[idx] = ch_v.inc(v.drain(fusable=False))
                    V(on_v)
                    bset(2 + idx % 2, ev[idx])
                    idx += 1
            return {"mm_done": mm[idx - 1], "acc_done": ev[idx - 1]}

        for b in range(NBLK):
            kf = KLD[b % 2]
            vt = VLD[b % 2]
            if b + 1 < NBLK:
                kv_load(b + 1, blk_done[b - 1] if b >= 1 else None)

            pre_mm = [None] * 8
            pre_ev = [None] * 8
            for i, (tc, h) in enumerate([(tc, h) for tc in range(4) for h in range(2)]):
                bank = PS[i % 2]

                def on_t(t, i=i, tc=tc, h=h, bank=bank, b=b):
                    if i == 0:
                        wait(t, kld_t[b])
                        if b == 0:
                            wait(t, cast_toks["w0T"])
                    bwait(t, i % 2)
                    for kc in range(KC):
                        li = t.matmul(bank[:, :],
                                      kf[:, kc * 512 + tc * 128:kc * 512 + tc * 128 + 128],
                                      WT["w0"][:, kc * 1024 + h * 512:
                                               kc * 1024 + (h + 1) * 512],
                                      start=(kc == 0), stop=(kc == KC - 1))
                    pre_mm[i] = ch_t.inc(li)
                T(on_t)

                def on_s(v, i=i, tc=tc, h=h, bank=bank):
                    wait(v, pre_mm[i])
                    sl = slice(tc * 1024 + h * 512, tc * 1024 + (h + 1) * 512)
                    pre_ev[i] = ch_v.inc(v.tensor_copy(PRC[:, sl], bank[:, :]))
                V(on_s)
                bset(i % 2, pre_ev[i])
            pre_tok = pre_ev[-1]

            def on_sg(sa):
                wait(sa, pre_tok)
                ch_a.inc(sa.activation(SGT[:, :], PRC[:, :], AF.Sigmoid))
            A(on_sg)
            sg_tok = ch_a.tok()

            def on_a1(v):
                wait(v, sg_tok)
                v.tensor_tensor(A1T[:, :], PRC[:, :], SGT[:, :], ALU.mult)
                ch_v.inc(v.drain(fusable=False))
            V(on_a1)
            a1_tok = ch_v.tok()
            if b == 0:
                tap("a1tm", A1T[:, :], a1_tok)

            a1f_tok = transpose_block(A1T, AF1, True, a1_tok)

            prd_mm = [None] * 8
            prd_ev = [None] * 8
            for i, (tc, h) in enumerate([(tc, h) for tc in range(4) for h in range(2)]):
                bank = PS[i % 2]

                def on_t(t, i=i, tc=tc, h=h, bank=bank, b=b):
                    if i == 0:
                        wait(t, a1f_tok)
                        if b == 0:
                            wait(t, cast_toks["w1T"])
                    bwait(t, i % 2)
                    for kc in range(KC):
                        li = t.matmul(bank[:, :],
                                      AF1[:, kc * 512 + tc * 128:kc * 512 + tc * 128 + 128],
                                      WT["w1"][:, kc * 1024 + h * 512:
                                               kc * 1024 + (h + 1) * 512],
                                      start=(kc == 0), stop=(kc == KC - 1))
                    prd_mm[i] = ch_t.inc(li)
                T(on_t)

                def on_v(v, i=i, tc=tc, h=h, bank=bank, b=b):
                    wait(v, prd_mm[i])
                    if i == 0:
                        wait(v, vld_t[b])
                    sl = slice(tc * 1024 + h * 512, tc * 1024 + (h + 1) * 512)
                    v.tensor_tensor(ETM[:, sl], bank[:, :], vt[:, sl], ALU.subtract)
                    tk = ch_v.inc(v.drain(fusable=False)); wait(v, tk)
                    v.tensor_scalar(ETM[:, sl], ETM[:, sl], ERR_CLIP, -ERR_CLIP,
                                    ALU.min, ALU.max)
                    prd_ev[i] = ch_v.inc(v.drain(fusable=False))
                V(on_v)
                bset(i % 2, prd_ev[i])
            e_tok = prd_ev[-1]
            if b == 0:
                tap("etm", ETM[:, :], e_tok)

            def on_sd(v):
                wait(v, a1_tok)
                v.tensor_scalar(PRC[:, :], SGT[:, :], -1.0, 1.0, ALU.mult, ALU.add)
                tk = ch_v.inc(v.drain(fusable=False)); wait(v, tk)
                v.tensor_tensor(PRC[:, :], A1T[:, :], PRC[:, :], ALU.mult)
                tk = ch_v.inc(v.drain(fusable=False)); wait(v, tk)
                v.tensor_tensor(SGT[:, :], SGT[:, :], PRC[:, :], ALU.add)
                ch_v.inc(v.drain(fusable=False))
            V(on_sd)
            sd_tok = ch_v.tok()

            ef_tok = transpose_block(ETM, AF1, True, e_tok)

            d1_mm = [None] * 8
            d1_ev = [None] * 8
            for i, (tc, h) in enumerate([(tc, h) for tc in range(4) for h in range(2)]):
                bank = PS[i % 2]

                def on_t(t, i=i, tc=tc, h=h, bank=bank, b=b):
                    if i == 0:
                        wait(t, ef_tok)
                        if b == 0:
                            wait(t, cast_toks["w1p"])
                    bwait(t, i % 2)
                    for kc in range(KC):
                        li = t.matmul(bank[:, :],
                                      AF1[:, kc * 512 + tc * 128:kc * 512 + tc * 128 + 128],
                                      WT["w1p"][:, kc * 1024 + h * 512:
                                               kc * 1024 + (h + 1) * 512],
                                      start=(kc == 0), stop=(kc == KC - 1))
                    d1_mm[i] = ch_t.inc(li)
                T(on_t)

                def on_v(v, i=i, tc=tc, h=h, bank=bank):
                    wait(v, d1_mm[i])
                    if i == 0:
                        wait(v, sd_tok)
                    sl = slice(tc * 1024 + h * 512, tc * 1024 + (h + 1) * 512)
                    v.tensor_tensor(PRC[:, sl], bank[:, :], SGT[:, sl], ALU.mult)
                    d1_ev[i] = ch_v.inc(v.drain(fusable=False))
                V(on_v)
                bset(i % 2, d1_ev[i])
            d1_tok = d1_ev[-1]
            if b == 0:
                tap("d1tm", PRC[:, :], d1_tok)

            def ktm_guard(t):
                wait(t, TAPQ.tok())
            T(ktm_guard)
            ktm_tok = transpose_block(kf, SGT, False, d1_tok)

            g0r = emit_g(SGT, PRC, G0, ktm_tok, d1_tok, gzero_tok if b == 0 else None)

            g1r = emit_g(A1T, ETM, G1, a1_tok, e_tok, None)
            blk_done[b] = g1r["acc_done"]

        g0_fin = g0r["acc_done"]
        g1_fin = g1r["acc_done"]
        gates_fin = gate_red[NBLK - 1]
        cc_tok = {}

        def r_stage0(sy):
            wait(sy, gates_fin)
            cc_tok["ga"] = CCS.dma(sy, ccg_in[0:1, 0:4], g3[0:1, 0:4])
        SY(r_stage0)

        def r_cc0(g):
            wait(g, cc_tok["ga"])
            g.collective_compute(
                "AllReduce", ALU.add,
                replica_groups=[[0, 1, 2, 3, 4, 5, 6, 7]],
                ins=[ccg_in[0:1, :]], outs=[ccg_out[0:1, :]],
            ).then_inc(ch_cc.h, 1)
            ch_cc.n += 1
            g.wait_ge(ch_cc.h, ch_cc.n)
            cc_tok["gar"] = CCL.dma(g, g3[0:1, 0:3], ccg_out[0:1, 0:3])
        G(r_cc0)
        tap("g0", G0[:, :], g0_fin)
        tap("g1", G1[:, :], g1_fin)
        tap("gacc", g3[0:1, 0:4], gates_fin)

        if phases < 3:
            finalize_partial()
            return nc

        def r_stage(sy):
            wait(sy, g0_fin)
            cc_tok["g0"] = CCS.dma(sy, ccin[:, :], G0[:, :])
        SY(r_stage)

        def r_stage2(sy):
            wait(sy, g1_fin)
            cc_tok["g1"] = LNQ.dma(sy, ccin2[:, :], G1[:, :])
        SY(r_stage2)

        def r_cc(g):
            wait(g, cc_tok["g0"])
            g.collective_compute(
                "AllReduce", ALU.add,
                replica_groups=[[0, 1, 2, 3, 4, 5, 6, 7]],
                ins=[ccin[:, :]], outs=[ccout[:, :]],
            ).then_inc(ch_cc.h, 1)
            ch_cc.n += 1
            wait(g, cc_tok["g1"])
            g.collective_compute(
                "AllReduce", ALU.add,
                replica_groups=[[0, 1, 2, 3, 4, 5, 6, 7]],
                ins=[ccin2[:, :]], outs=[ccout2[:, :]],
            ).then_inc(ch_cc.h, 1)
            ch_cc.n += 1
            g.wait_ge(ch_cc.h, ch_cc.n - 1)
            for c in range(4):
                cc_tok["g0r%d" % c] = CCL.dma(
                    g, G0[:, c * 2048:(c + 1) * 2048], ccout[:, c * 2048:(c + 1) * 2048])
            g.wait_ge(ch_cc.h, ch_cc.n)
            for c in range(4):
                cc_tok["g1r%d" % c] = CCL.dma(
                    g, G1[:, c * 2048:(c + 1) * 2048], ccout2[:, c * 2048:(c + 1) * 2048])
        G(r_cc)

        def r_gv(v):
            wait(v, cc_tok["gar"])
            v.tensor_scalar(g3[0:1, 8:11], g3[0:1, 0:3], 1.0 / 32768.0, None, ALU.mult)
            tk = ch_v.inc(v.drain(fusable=False)); wait(v, tk)
            v.tensor_tensor(g3[0:1, 8:11], g3[0:1, 8:11], g3[0:1, 4:7], ALU.add)
            ch_v.inc(v.drain(fusable=False))
        V(r_gv)
        gate_m_tok = ch_v.tok()

        def r_gs(s):
            wait(s, gate_m_tok)
            s.activation(g3[0:1, 0:3], g3[0:1, 8:11], AF.Sigmoid)
            ch_a.inc(s.drain(fusable=False))
        A(r_gs)
        gate_sig_tok = ch_a.tok()
        tap("gsig", g3[0:1, 0:4], gate_sig_tok)

        def r_gv2(v):
            wait(v, gate_sig_tok)
            v.tensor_scalar(g3[0:1, 12:13], g3[0:1, 0:1], -1.0, 1.0, ALU.mult, ALU.add)
            tk = ch_v.inc(v.drain(fusable=False)); wait(v, tk)
            v.tensor_scalar(g3[0:1, 13:14], g3[0:1, 1:2], -1.0, None, ALU.mult)
            ch_v.inc(v.drain(fusable=False))
        V(r_gv2)
        gate_c_tok = ch_v.tok()

        def r_bc(t):
            wait(t, gate_c_tok)
            wait(t, gz_tok)
            bwait(t, 5)
            li = t.matmul(PS[5][:, 0:2], ones1[:, :], g3[0:1, 12:14],
                          start=True, stop=True)
            ch_t.inc(li)
        T(r_bc)
        bc_tok = ch_t.tok()

        def r_bce(v):
            wait(v, bc_tok)
            v.tensor_copy(small[:, 48:50], PS[5][:, 0:2])
            ch_v.inc(v.drain(fusable=False))
        V(r_bce)
        coef_tok = ch_v.tok()
        bset(5, coef_tok)
        C_OMA = small[:, 48:49]
        C_NLR = small[:, 49:50]

        upd_toks = []
        for gsrc, wsrc, wdst, grt, sfin in [
                (G0, WT["w0"], W0N, "g0r", [blk_done[NBLK - 1], spq[NBLK - 1],
                                            spk[NBLK - 1]]),
                (G1, WT["w1"], W1N, "g1r", [blk_done[NBLK - 1], spv[NBLK - 1]])]:
            def on_us(s, wsrc=wsrc, wdst=wdst, sfin=sfin):
                wait(s, coef_tok)
                for tk in sfin:
                    wait(s, tk)
                s.activation(wdst[:, 0:4096], wsrc[:, 0:4096], AF.Identity, scale=C_OMA)
                s.activation(wdst[:, 4096:8192], wsrc[:, 4096:8192], AF.Identity, scale=C_OMA)
                ch_a.inc(s.drain(fusable=False))
            A(on_us)
            sc_tok = ch_a.tok()

            def on_uv(v, gsrc=gsrc, wdst=wdst, sc_tok=sc_tok, grt=grt):
                wait(v, sc_tok)
                for c in range(4):
                    wait(v, cc_tok[grt + str(c)])
                    sl = slice(c * 2048, (c + 1) * 2048)
                    v.tensor_scalar(gsrc[:, sl], gsrc[:, sl], GRAD_CLIP, -GRAD_CLIP,
                                    ALU.min, ALU.max)
                    tk = ch_v.inc(v.drain(fusable=False)); wait(v, tk)
                    v.scalar_tensor_tensor(wdst[:, sl], gsrc[:, sl], C_NLR, wdst[:, sl],
                                           ALU.mult, ALU.add)
                    tk = ch_v.inc(v.drain(fusable=False)); wait(v, tk)
                upd_toks.append(tk)
            V(on_uv)
        w0n_tok, w1n_tok = upd_toks
        tap("w0n", W0N[:, :], w0n_tok)

        if phases < 4:
            finalize_partial()
            return nc

        q2t = [None] * NBLK
        x2t = [None] * NBLK
        res_sp = [None] * NBLK
        ln2_sp = [None] * NBLK
        logit_t = [None] * NBLK
        u_last = [None] * NBLK
        res_last = [None] * NBLK
        ln2_toks = [None] * NBLK
        ln2tm_sp = [None] * NBLK

        HUGE = 1.0e6
        SCQ = DmaQ(ctx, nc, "SCQ")
        bnd_regs = {}

        def setup_bnd(g):
            r1 = g.alloc_register("bnd_idx")
            g.reg_mov(r1, NEXP * 512 - 1)
            r2 = g.alloc_register("bnd_tok")
            g.reg_mov(r2, TC - 1)
            bnd_regs["idx"] = r1
            bnd_regs["tok"] = r2
        G(setup_bnd)

        QLQ = [Q2Q, KVQ[0], KVQ[1], XQ]
        X2Q = [GBQ, DmaQ(ctx, nc, "X2Q1"), DmaQ(ctx, nc, "X2Q2"),
               DmaQ(ctx, nc, "X2Q3")]
        JF2 = f32a(24, 1024)
        MKB = bfa(28, 256)
        routeA_fin = [None] * NBLK
        routeB_fin = [None]
        rk_base_tok = [None, None]
        sc_last = [None]
        scq_slot = [None] * 4
        PAIRJ = [640, 656, 672, 688]

        def v3(ap):
            return ap.rearrange("p (c e) -> p c e", e=32)

        def bc3(ap):
            return ap.unsqueeze(2).broadcast_to([128, 4, 32])

        def route_soft(b):
            sA = small[:, 40:44]
            sR = small[:, 44:48]
            sM2 = small[:, 52:56]
            E = JF2[:, 0:128]
            M1 = JF2[:, 128:256]
            T = JF2[:, 256:384]
            M2 = JF2[:, 384:512]
            W = WMA[:, b * 128:(b + 1) * 128]
            L = LOGI[:, b * 128:(b + 1) * 128]
            neg_tok = [None]
            exp_tok = [None]

            def on_a(v, b=b):
                wait(v, logit_t[b])
                if b == 0:
                    wait(v, cast_toks["iota32"])
                    wait(v, cast_toks["iotac"])
                    wait(v, cast_toks["tokc4"])
                else:
                    wait(v, routeB_fin[0])
                v.tensor_reduce(sA, v3(L), AX, ALU.max)
                tk = ch_v.inc(v.drain(fusable=False)); wait(v, tk)
                v.tensor_scalar_mul(sA, sA, -1.0)
                tk = ch_v.inc(v.drain(fusable=False)); wait(v, tk)
                v.tensor_scalar(small[:, 60:64], tokc4[:, :], 1.0, float(b * 512),
                                ALU.mult, ALU.add)
                neg_tok[0] = ch_v.inc(v.drain(fusable=False))
            V(on_a)

            def on_exp(sa, b=b):
                wait(sa, neg_tok[0])
                for c in range(4):
                    sa.activation(JF2[:, c * 32:(c + 1) * 32],
                                  LOGI[:, (b * 4 + c) * 32:(b * 4 + c + 1) * 32],
                                  AF.Exp, bias=small[:, 40 + c:41 + c], scale=1.0)
                exp_tok[0] = ch_a.inc(sa.drain(fusable=False))
            A(on_exp)

            def on_sm(v, b=b):
                wait(v, exp_tok[0])
                v.tensor_reduce(sR, v3(E), AX, ALU.add)
                tk = ch_v.inc(v.drain(fusable=False)); wait(v, tk)
                v.reciprocal(sR, sR)
                tk = ch_v.inc(v.drain(fusable=False)); wait(v, tk)
                v.tensor_tensor(v3(W), v3(E), bc3(sR), ALU.mult)
                tk = ch_v.inc(v.drain(fusable=False)); wait(v, tk)
                v.tensor_reduce(sA, v3(W), AX, ALU.max)
                tk = ch_v.inc(v.drain(fusable=False)); wait(v, tk)
                v.tensor_tensor(v3(M1), v3(W), bc3(sA), ALU.is_ge)
                tk = ch_v.inc(v.drain(fusable=False)); wait(v, tk)
                v.tensor_tensor(v3(T), v3(W), bc3(sA), ALU.is_lt)
                tk = ch_v.inc(v.drain(fusable=False)); wait(v, tk)
                v.tensor_tensor(T, T, W, ALU.mult)
                tk = ch_v.inc(v.drain(fusable=False)); wait(v, tk)
                v.tensor_reduce(sM2, v3(T), AX, ALU.max)
                tk = ch_v.inc(v.drain(fusable=False)); wait(v, tk)
                v.tensor_tensor(v3(T), v3(W), bc3(sM2), ALU.is_ge)
                tk = ch_v.inc(v.drain(fusable=False)); wait(v, tk)
                v.tensor_tensor(M2, T, M1, ALU.subtract)
                tk = ch_v.inc(v.drain(fusable=False)); wait(v, tk)
                v.tensor_tensor(W, W, T, ALU.mult)
                routeA_fin[b] = ch_v.inc(v.drain(fusable=False))
            V(on_sm)

        def route_ranks(b):
            for r in range(2):
                mcol = JF2[:, 128:256] if r == 0 else JF2[:, 384:512]
                mk = MKB[:, r * 128:(r + 1) * 128]
                mb_tok = [None]
                cn_tok = [None]
                cv_tok = [None]
                br_tok = [None]
                rk_tok = [None]
                sl_tok = [None]

                def on_mb(v, b=b, r=r, mcol=mcol, mk=mk):
                    wait(v, routeA_fin[b])
                    wait(v, rk_base_tok[r])
                    mb_tok[0] = ch_v.inc(v.tensor_copy(mk, mcol))
                V(on_mb)

                def on_mm1(t, b=b, r=r, mk=mk):
                    wait(t, mb_tok[0])
                    if b == 0 and r == 0:
                        wait(t, cast_toks["lt"])
                        wait(t, gz_tok)
                    bwait(t, 4 + r)
                    cn_tok[0] = ch_t.inc(t.matmul(
                        PS[4 + r][0:1, 0:128], ones_bf[:, :], mk,
                        start=True, stop=True))
                    bwait(t, 2 + r)
                    t.matmul(PS[2 + r][:, 0:128], lt128[:, :], mk,
                             start=True, stop=False)
                T(on_mm1)

                def on_cv(s, r=r):
                    wait(s, cn_tok[0])
                    s.copy(JF2[0:1, 712:840], PS[4 + r][0:1, 0:128])
                    cv_tok[0] = ch_a.inc(s.drain(fusable=False))
                A(on_cv)
                bset(4 + r, cv_tok[0])

                def on_br(v, r=r):
                    wait(v, cv_tok[0])
                    CNT = JF2[0:1, 712:840]
                    BR4 = JF2[0:1, 840:968]
                    bro = baserow[0:1, r * 32:(r + 1) * 32]
                    v.tensor_copy(BR4[0:1, 0:32], bro)
                    tk = ch_v.inc(v.drain(fusable=False)); wait(v, tk)
                    v.tensor_tensor(BR4[0:1, 32:64], BR4[0:1, 0:32],
                                    CNT[0:1, 0:32], ALU.add)
                    tk = ch_v.inc(v.drain(fusable=False)); wait(v, tk)
                    v.tensor_tensor(BR4[0:1, 64:96], BR4[0:1, 32:64],
                                    CNT[0:1, 32:64], ALU.add)
                    tk = ch_v.inc(v.drain(fusable=False)); wait(v, tk)
                    v.tensor_tensor(BR4[0:1, 96:128], BR4[0:1, 64:96],
                                    CNT[0:1, 64:96], ALU.add)
                    tk = ch_v.inc(v.drain(fusable=False)); wait(v, tk)
                    v.tensor_tensor(bro, BR4[0:1, 96:128],
                                    CNT[0:1, 96:128], ALU.add)
                    tk = ch_v.inc(v.drain(fusable=False)); wait(v, tk)
                    v.tensor_copy(br4b[0:1, r * 128:(r + 1) * 128], BR4[0:1, :])
                    br_tok[0] = ch_v.inc(v.drain(fusable=False))
                V(on_br)

                def on_mm2(t, r=r):
                    wait(t, br_tok[0])
                    rk_tok[0] = ch_t.inc(t.matmul(
                        PS[2 + r][:, 0:128], ones1b[:, :],
                        br4b[0:1, r * 128:(r + 1) * 128], start=False, stop=True))
                    rk_base_tok[r] = rk_tok[0]
                T(on_mm2)

                def on_sl(v, b=b, r=r, mcol=mcol):
                    wait(v, rk_tok[0])
                    SL = JF2[:, 512:640]
                    T3 = JF2[:, 256:384]
                    v.tensor_copy(SL, PS[2 + r][:, 0:128])
                    tk = ch_v.inc(v.drain(fusable=False)); wait(v, tk)
                    v.tensor_scalar(T3, SL, 255.5, None, ALU.is_gt)
                    tk = ch_v.inc(v.drain(fusable=False)); wait(v, tk)
                    v.scalar_tensor_tensor(SL, T3, HUGE, SL, ALU.mult, ALU.add)
                    tk = ch_v.inc(v.drain(fusable=False)); wait(v, tk)
                    v.tensor_scalar(T3, mcol, -1.0, 1.0, ALU.mult, ALU.add)
                    tk = ch_v.inc(v.drain(fusable=False)); wait(v, tk)
                    v.scalar_tensor_tensor(SL, T3, HUGE, SL, ALU.mult, ALU.add)
                    tk = ch_v.inc(v.drain(fusable=False)); wait(v, tk)
                    v.tensor_reduce(small[:, 52:56], v3(SL), AX, ALU.min)
                    tk = ch_v.inc(v.drain(fusable=False)); wait(v, tk)
                    v.tensor_tensor(v3(T3), v3(mcol),
                                    iota32[:, :].unsqueeze(1).broadcast_to(
                                        [128, 4, 32]), ALU.mult)
                    tk = ch_v.inc(v.drain(fusable=False)); wait(v, tk)
                    v.tensor_reduce(small[:, 56:60], v3(T3), AX, ALU.add)
                    tk = ch_v.inc(v.drain(fusable=False)); wait(v, tk)
                    v.tensor_tensor(T3, WMA[:, b * 128:(b + 1) * 128], mcol,
                                    ALU.mult)
                    tk = ch_v.inc(v.drain(fusable=False)); wait(v, tk)
                    v.tensor_reduce(small[:, 48:52], v3(T3), AX, ALU.add)
                    tk = ch_v.inc(v.drain(fusable=False)); wait(v, tk)
                    SLG = JF2[:, 968:972]
                    v.scalar_tensor_tensor(SLG, small[:, 56:60], 512.0,
                                           small[:, 52:56], ALU.mult, ALU.add)
                    tk = ch_v.inc(v.drain(fusable=False)); wait(v, tk)
                    if r:
                        v.tensor_scalar_add(SLG, SLG, 256.0)
                        tk = ch_v.inc(v.drain(fusable=False)); wait(v, tk)
                    sidx = (b * 2 + r) % 4
                    pc = PAIRJ[sidx]
                    wait(v, scq_slot[sidx])
                    P3 = JF2[:, pc:pc + 8].rearrange("p (c two) -> p c two", two=2)
                    v.tensor_copy(P3[:, :, 0:1], small[:, 60:64].unsqueeze(2))
                    tk = ch_v.inc(v.drain(fusable=False)); wait(v, tk)
                    v.tensor_copy(P3[:, :, 1:2], small[:, 48:52].unsqueeze(2))
                    tk = ch_v.inc(v.drain(fusable=False)); wait(v, tk)
                    v.tensor_copy(JF2[:, pc + 8:pc + 12].bitcast(mybir.dt.uint32),
                                  SLG)
                    sl_tok[0] = ch_v.inc(v.drain(fusable=False))
                    routeB_fin[0] = sl_tok[0]
                V(on_sl)
                bset(2 + r, sl_tok[0])

                def on_sc(g, b=b, r=r):
                    wait(g, sl_tok[0])
                    sidx = (b * 2 + r) % 4
                    pc = PAIRJ[sidx]
                    for c in range(4):
                        sc_last[0] = SCQ.dma_indirect_scatter(
                            g, idxstage,
                            JF2[:, pc + 8 + c:pc + 9 + c].bitcast(mybir.dt.uint32),
                            JF2[:, pc + c * 2:pc + c * 2 + 2], bnd_regs["idx"])
                    scq_slot[sidx] = sc_last[0]
                G(on_sc)

        for b in range(NBLK):
            def on_q2(sy, b=b):
                if b == 0:
                    wait(sy, blk_done[NBLK - 1])
                else:
                    wait(sy, u_last[b - 1])
                wait(sy, spq[b])
                wait(sy, TAPQ.tok())
                tks = []
                for j in range(4):
                    tks.append(QLQ[j].dma(
                        sy, Q2[b % 2][:, j * 1024:(j + 1) * 1024].rearrange(
                            "p (c t) -> p c t", c=2),
                        sp_view(qsp, b)[:, 2 * j:2 * j + 2, :]))
                q2t[b] = _multi(tks)
            SY(on_q2)

            u_mm = [None] * KC
            u_ev = [None] * KC
            for oc in range(KC):
                bank = PS[oc % 2]

                def on_t(t, oc=oc, bank=bank, b=b):
                    if oc == 0:
                        wait(t, q2t[b])
                        if b == 0:
                            wait(t, w0n_tok)
                    bwait(t, oc % 2)
                    for kc in range(KC):
                        li = t.matmul(bank[:, :],
                                      W0N[:, kc * 1024 + oc * 128:kc * 1024 + oc * 128 + 128],
                                      Q2[b % 2][:, kc * 512:(kc + 1) * 512],
                                      start=(kc == 0), stop=(kc == KC - 1))
                    u_mm[oc] = ch_t.inc(li)
                T(on_t)

                def on_s(v, oc=oc, bank=bank):
                    wait(v, u_mm[oc])
                    u_ev[oc] = ch_v.inc(v.tensor_copy(
                        U2[:, oc * 512:(oc + 1) * 512], bank[:, :]))
                V(on_s)
                bset(oc % 2, u_ev[oc])
            u_last[b] = u_mm[-1]

            def on_sg2(sa, dep=u_ev[-1]):
                wait(sa, dep)
                ch_a.inc(sa.activation(SG2[:, :], U2[:, :], AF.Sigmoid))
            A(on_sg2)
            sg2_tok = ch_a.tok()

            def on_u(v):
                wait(v, sg2_tok)
                v.tensor_tensor(U2[:, :], U2[:, :], SG2[:, :], ALU.mult)
                ch_v.inc(v.drain(fusable=False))
            V(on_u)
            u_tok = ch_v.tok()

            if phases >= 5 and b > 0:
                route_ranks(b - 1)

            def on_x2(sa, b=b):
                if b == 0:
                    wait(sa, w0n_tok)
                else:
                    wait(sa, res_last[b - 1])
                tks = []
                for tc in range(4):
                    tks.append(X2Q[tc].dma(
                        sa, X2[:, tc * 1024:(tc + 1) * 1024],
                        x_d[b * TB + tc * 128:b * TB + (tc + 1) * 128, :]))
                x2t[b] = _multi(tks)
            A(on_x2)

            h2_tok, _ = proj_fm(W1N, U2, H2, u_tok, w1n_tok if b == 0 else None)

            at_mm = [None] * 8
            at_ev = [None] * 8
            for i, (tc, dn) in enumerate([(tc, dn) for tc in range(4) for dn in range(2)]):
                bank = PS[2 + i % 2]

                def on_t(t, i=i, tc=tc, dn=dn, bank=bank, b=b):
                    if i == 0:
                        wait(t, h2_tok)
                        if b == 0:
                            wait(t, cast_toks["woutT"])
                    bwait(t, 2 + i % 2)
                    for kc in range(KC):
                        li = t.matmul(bank[:, :],
                                      H2[:, kc * 512 + tc * 128:kc * 512 + tc * 128 + 128],
                                      WOUTT[:, kc * 1024 + dn * 512:
                                            kc * 1024 + (dn + 1) * 512],
                                      start=(kc == 0), stop=(kc == KC - 1))
                    at_mm[i] = ch_t.inc(li)
                T(on_t)

                def on_v(v, i=i, tc=tc, dn=dn, bank=bank, b=b):
                    wait(v, at_mm[i])
                    if i == 0:
                        wait(v, x2t[b])
                        if b > 0:
                            wait(v, ln2_toks[b - 1])
                            wait(v, res_sp[b - 1])
                    sl = slice(tc * 1024 + dn * 512, tc * 1024 + (dn + 1) * 512)
                    v.tensor_tensor(RES[:, sl], bank[:, :], X2[:, sl], ALU.add)
                    at_ev[i] = ch_v.inc(v.drain(fusable=False))
                V(on_v)
                bset(2 + i % 2, at_ev[i])
            res_tok = at_ev[-1]
            res_last[b] = res_tok

            def on_rsp(sy, b=b, res_tok=res_tok):
                wait(sy, res_tok)
                res_sp[b] = RSQ.dma(sy, residsp[:, b * 4096:(b + 1) * 4096], RES[:, :])
            SY(on_rsp)

            ln2_guard = [res_tok]
            if b > 0:
                ln2_guard.append(logit_t[b - 1])
                ln2_guard.append(ln2_sp[b - 1])
            ln2_tok = emit_ln(RES, [HIT, LN2F32], JNK2, _multi(ln2_guard))
            ln2_toks[b] = ln2_tok

            def on_lo(v, ln2_tok=ln2_tok):
                wait(v, ln2_tok)
                v.tensor_tensor(LOT[:, :], LN2F32[:, :], HIT[:, :], ALU.subtract)
                ch_v.inc(v.drain(fusable=False))
            V(on_lo)
            lo_tok = ch_v.tok()

            hif_tok = transpose_block(HIT, HIF, True, ln2_tok)
            lof_tok = transpose_block(LOT, LOF, True, lo_tok)

            def on_lsp(sy, b=b, hif_tok=hif_tok, ln2_tok=ln2_tok):
                wait(sy, ln2_tok)
                ln2tm_sp[b] = LNQ.dma(
                    sy, ln2tmsp[b * TB:(b + 1) * TB, :].rearrange("(tc p) d -> p tc d", p=128),
                    HIT[:, :].rearrange("p (tc d) -> p tc d", tc=4))
                ln2_sp[b] = ln2tm_sp[b]
            SY(on_lsp)

            lg_mm = [None] * 4
            lg_ev = [None] * 4
            for tc in range(4):
                bank = PS[4 + tc % 2]

                def on_t(t, tc=tc, bank=bank, b=b):
                    if tc == 0:
                        wait(t, hif_tok)
                        wait(t, lof_tok)
                        if b == 0:
                            wait(t, rt_tok)
                    bwait(t, 4 + tc % 2)
                    for kc in range(KC):
                        sl = slice(kc * 512 + tc * 128, kc * 512 + tc * 128 + 128)
                        t.matmul(bank[:, 0:32], HIF[:, sl], rthi[:, kc * 32:(kc + 1) * 32],
                                 start=(kc == 0), stop=False)
                        t.matmul(bank[:, 32:64], HIF[:, sl], rtlo[:, kc * 32:(kc + 1) * 32],
                                 start=False, stop=False)
                        li = t.matmul(bank[:, 0:32], LOF[:, sl],
                                      rthi[:, kc * 32:(kc + 1) * 32],
                                      start=False, stop=(kc == KC - 1))
                    lg_mm[tc] = ch_t.inc(li)
                T(on_t)

                lg_cp = [None]

                def on_sc(s, tc=tc, bank=bank, b=b):
                    wait(s, lg_mm[tc])
                    cc = b * 4 + tc
                    s.copy(LOGI[:, cc * 32:(cc + 1) * 32], bank[:, 0:32])
                    lg_cp[0] = ch_a.inc(s.drain(fusable=False))
                A(on_sc)

                def on_v(v, tc=tc, bank=bank, b=b):
                    wait(v, lg_cp[0])
                    cc = b * 4 + tc
                    v.tensor_tensor(LOGI[:, cc * 32:(cc + 1) * 32],
                                    LOGI[:, cc * 32:(cc + 1) * 32],
                                    bank[:, 32:64], ALU.add)
                    lg_ev[tc] = ch_v.inc(v.drain(fusable=False))
                V(on_v)
                bset(4 + tc % 2, lg_ev[tc])
            logit_t[b] = lg_ev[-1]

            if phases >= 5:
                route_soft(b)

            if b == 0:
                tap("u2", U2[:, :], u_tok)
                tap("h2", H2[:, :], h2_tok)
                tap("resid", RES[:, :], res_tok)
                tap("hit", HIT[:, :], ln2_tok)

        tap("logits", LOGI[:, :], logit_t[NBLK - 1])

        if phases < 5:
            finalize_partial()
            return nc

        route_ranks(NBLK - 1)
        tap("wma", WMA[:, :], routeA_fin[NBLK - 1])
        bookkeeping_fin = sc_last[0]

        if phases < 6:
            finalize_partial()
            return nc

        EXS = [bfa(0, KC * D), bfa(16, KC * D), bfa(32, KC * D), bfa(144, KC * D)]
        EXC = [EXQ[0], EXQ[1], WQ, CCL]
        ex_t = [None] * NEXP
        IQ = [LSQ[0], LSQ[1]]
        GQ = [ACQ[0], ACQ[1]]

        def exs_load(g, ei, guard=None):
            if guard is not None:
                wait(g, guard)
            ex_t[ei] = EXC[ei % 4].dma(
                g, EXS[ei % 4][:, :].rearrange("p (c o) -> p c o", c=KC),
                experts_d[ei * D:(ei + 1) * D, :].rearrange("(c p) o -> p c o", p=128))

        ROT = 4
        IDXS4 = [f32a(96, 256), f32a(97, 256), f32a(98, 256), f32a(99, 256)]
        IDXU4 = [f32a(100, 256), f32a(101, 256), f32a(102, 256), f32a(103, 256)]
        GXT4 = [bfa(104, 4096), bfa(112, 4096), bfa(120, 4096), bfa(128, 4096)]
        GXF2 = [bfa(64, 4096), bfa(72, 4096)]
        YSB2 = [bfa(80, 4096), bfa(88, 4096)]
        SQC = [DmaQ(ctx, nc, "SQC0"), DmaQ(ctx, nc, "SQC1"),
               DmaQ(ctx, nc, "SQC2"), DmaQ(ctx, nc, "SQC3")]
        GQ4 = [ACQ[0], ACQ[1], DmaQ(ctx, nc, "GQ2"), DmaQ(ctx, nc, "GQ3")]
        idx_t = [None] * ROT
        idxu_t = [None] * ROT
        gxt_t = [None] * ROT
        tr_fin = {}
        mm_fin = {}
        ev_fin = {}
        sc_fin = {}

        def S0(e):
            r4 = e % ROT

            def on_idx(sy, e=e, r4=r4):
                if e < 2:
                    wait(sy, bookkeeping_fin)
                    wait(sy, SCQ.tok())
                    for b in range(NBLK):
                        wait(sy, ln2tm_sp[b])
                    wait(sy, res_sp[NBLK - 1])
                if e - ROT >= 0:
                    wait(sy, sc_fin[e - ROT])
                idx_t[r4] = IQ[e % 2].dma(
                    sy, IDXS4[r4][:, 0:8].rearrange("p (c two) -> p c two", two=2),
                    idxstage[e * 512:(e + 1) * 512, :].rearrange(
                        "(c p) two -> p c two", p=128))
            SY(on_idx)

            def on_cast(v, e=e, r4=r4):
                wait(v, idx_t[r4])
                idxu_t[r4] = ch_v.inc(v.tensor_copy(
                    IDXU4[r4][:, 0:8].bitcast(mybir.dt.uint32), IDXS4[r4][:, 0:8]))
            V(on_cast)

        def S2(e):
            r4 = e % ROT

            def on_gather(g, e=e, r4=r4):
                wait(g, idxu_t[r4])
                if e - ROT + 1 > 0 and (e - ROT) in tr_fin:
                    wait(g, tr_fin[e - ROT])
                tks = []
                for c in range(4):
                    tks.append(GQ4[c].dma_indirect_gather(
                        g, GXT4[r4][:, c * 1024:(c + 1) * 1024], ln2tmsp,
                        IDXU4[r4][:, 0:8].bitcast(mybir.dt.uint32).rearrange(
                            "p (c two) -> p c two", two=2)[:, c:c + 1, 0:1],
                        bnd_regs["tok"]))
                gxt_t[r4] = _multi(tks)
            G(on_gather)

        def S345(e):
            r4 = e % ROT
            s2 = e % 2
            tr_toks = [None] * 8
            for i in range(8):
                bank = PT[i % 2]

                def on_tr(t, i=i, r4=r4, s2=s2, e=e, bank=bank):
                    if i == 0:
                        wait(t, gxt_t[r4])
                        if (e - 2) in mm_fin:
                            wait(t, mm_fin[e - 2])
                    bwait(t, 6 + i % 2)
                    for tc in range(4):
                        li = t.transpose(bank[:, tc * 128:tc * 128 + 128],
                                         GXT4[r4][:, tc * 1024 + i * 128:
                                                  tc * 1024 + i * 128 + 128],
                                         ident[:, :])
                    tr_toks[i] = ch_t.inc(li)
                T(on_tr)

                def on_tre(v, i=i, s2=s2, e=e, bank=bank):
                    wait(v, tr_toks[i])
                    tr_fin[e] = ch_v.inc(v.tensor_copy(
                        GXF2[s2][:, i * 512:(i + 1) * 512], bank[:, 0:512]))
                V(on_tre)
                bset(6 + i % 2, tr_fin[e])

            mm_toks = [None] * 8
            ev_toks = [None] * 8
            for gi, (ts, on) in enumerate([(ts, on) for ts in range(4) for on in range(2)]):
                bank = PS[gi % 6]

                def on_t(t, gi=gi, ts=ts, on=on, bank=bank, s2=s2, e=e):
                    if gi == 0:
                        wait(t, tr_fin[e])
                        wait(t, ex_t[e])
                        if (e - 2) in ev_fin:
                            wait(t, ev_fin[e - 2])
                    bwait(t, gi % 6)
                    for kc in range(KC):
                        li = t.matmul(bank[:, :],
                                      GXF2[s2][:, kc * 512 + ts * 128:
                                               kc * 512 + ts * 128 + 128],
                                      EXS[e % 4][:, kc * 1024 + on * 512:
                                                 kc * 1024 + (on + 1) * 512],
                                      start=(kc == 0), stop=(kc == KC - 1))
                    mm_toks[gi] = ch_t.inc(li)
                T(on_t)

                def on_ev(v, gi=gi, ts=ts, on=on, bank=bank, s2=s2, e=e, r4=r4):
                    wait(v, mm_toks[gi])
                    if gi == 0 and (e - 2) in sc_fin:
                        wait(v, sc_fin[e - 2])
                    wcol = IDXS4[r4][:, 0:8].rearrange("p (c two) -> p c two", two=2)[
                        :, ts:ts + 1, 1:2]
                    ev_toks[gi] = ch_v.inc(v.tensor_scalar(
                        YSB2[s2][:, ts * 1024 + on * 512:ts * 1024 + (on + 1) * 512],
                        bank[:, :], wcol, None, ALU.mult))
                V(on_ev)
                bset(gi % 6, ev_toks[gi])
            mm_fin[e] = mm_toks[-1]
            ev_fin[e] = ev_toks[-1]

            def on_scat(g, e=e, s2=s2, r4=r4):
                wait(g, ev_fin[e])
                tks = []
                for c in range(4):
                    dstbuf = bufA if c < 2 else bufB
                    tks.append(SQC[c].dma_indirect_scatter(
                        g, dstbuf,
                        IDXU4[r4][:, 0:8].bitcast(mybir.dt.uint32).rearrange(
                            "p (c two) -> p c two", two=2)[:, c:c + 1, 0:1],
                        YSB2[s2][:, c * 1024:(c + 1) * 1024], bnd_regs["tok"]))
                sc_fin[e] = _multi(tks)
            G(on_scat)

        def ph3_pre(g):
            wait(g, logit_t[NBLK - 1])
            wait(g, bookkeeping_fin)
            wait(g, SCQ.tok())
            for ei in range(4):
                exs_load(g, ei)
        G(ph3_pre)

        S0(0)
        S0(1)
        S2(0)
        for e in range(NEXP):
            if e + 2 < NEXP:
                S0(e + 2)
            if e + 1 < NEXP:
                S2(e + 1)
            S345(e)
            if e + 4 < NEXP:
                def on_pf2(g, e=e):
                    wait(g, mm_fin[e])
                    exs_load(g, e + 4)
                G(on_pf2)

        moe_fin = [sc_fin[NEXP - 2], sc_fin[NEXP - 1]]

        if phases < 7:
            finalize_partial()
            return nc

        st_toks = [None] * NBLK
        p4_guard = [None, None]
        AB4 = [bfa(100, 4096), bfa(108, 4096), bfa(116, 4096), bfa(124, 4096)]
        for b in range(NBLK):
            sl = b % 2
            ld_tok = [None]

            def on_ld(sy, b=b, sl=sl):
                if b == 0:
                    wait(sy, moe_fin[0])
                    wait(sy, moe_fin[1])
                    wait(sy, TAPQ.tok())
                if p4_guard[sl] is not None:
                    wait(sy, p4_guard[sl])
                P4Q[sl].dma(sy, R4[sl][:, :], residsp[:, b * 4096:(b + 1) * 4096])
                P4Q[sl].dma(sy, AB4[sl][:, :].rearrange("p (tc d) -> p tc d", tc=4),
                            bufA[b * TB:(b + 1) * TB, :].rearrange(
                                "(tc p) d -> p tc d", p=128))
                ld_tok[0] = P4Q[sl].dma(
                    sy, AB4[2 + sl][:, :].rearrange("p (tc d) -> p tc d", tc=4),
                    bufB[b * TB:(b + 1) * TB, :].rearrange("(tc p) d -> p tc d", p=128))
            SY(on_ld)

            add_tok = [None]

            def on_add(v, b=b, sl=sl):
                wait(v, ld_tok[0])
                wait(v, STQ[sl].tok())
                v.tensor_tensor(O4[sl][:, :], R4[sl][:, :], AB4[sl][:, :], ALU.add)
                tk = ch_v.inc(v.drain(fusable=False)); wait(v, tk)
                v.tensor_tensor(O4[sl][:, :], O4[sl][:, :], AB4[2 + sl][:, :], ALU.add)
                add_tok[0] = ch_v.inc(v.drain(fusable=False))
                p4_guard[sl] = add_tok[0]
            V(on_add)

            def on_st(sa, b=b, sl=sl):
                wait(sa, add_tok[0])
                st_toks[b] = STQ[sl].dma(
                    sa, out_d[b * TB:(b + 1) * TB, :].rearrange("(tc p) d -> p tc d", p=128),
                    O4[sl][:, :].rearrange("p (tc d) -> p tc d", tc=4))
            A(on_st)

        def fin(s):
            wait(s, STQ[0].tok())
            wait(s, STQ[1].tok())
            wait(s, TAPQ.tok())
        A(fin)

    return nc


def kernel(L0_attn_x, L0_attn_Wk, L0_attn_Wv, L0_attn_Wq, L0_attn_Wout,
           L0_attn_W0, L0_attn_W1, L0_attn_Wg_decay, L0_attn_bg_decay,
           L0_attn_Wg_lr, L0_attn_bg_lr, L0_attn_Wg_mom, L0_attn_bg_mom,
           L0_ffn_router_w, L0_ffn_expert_weights):
    B, S, DD = L0_attn_x.shape
    xf = np.ascontiguousarray(np.asarray(L0_attn_x, dtype=np.float32).reshape(B * S, DD))
    wg3 = np.ascontiguousarray(np.stack([
        np.asarray(L0_attn_Wg_decay)[0], np.asarray(L0_attn_Wg_lr)[0],
        np.asarray(L0_attn_Wg_mom)[0]], axis=0).astype(np.float32))
    bg3 = np.zeros((1, 4), dtype=np.float32)
    bg3[0, 0] = np.asarray(L0_attn_bg_decay).reshape(())
    bg3[0, 1] = np.asarray(L0_attn_bg_lr).reshape(())
    bg3[0, 2] = np.asarray(L0_attn_bg_mom).reshape(())
    ident = np.eye(128, dtype=BFNP)
    oh = np.zeros((NEXP, NEXP, 128), dtype=BFNP)
    for e_ in range(NEXP):
        oh[e_, e_, :] = 1
    oh = oh.reshape(NEXP, NEXP * 128)
    lt = np.triu(np.ones((128, 128), dtype=np.float32), k=1).astype(BFNP)
    iota32 = np.broadcast_to(np.arange(NEXP, dtype=np.float32), (128, NEXP)).copy()
    iotac = np.arange(128, dtype=np.float32).reshape(128, 1)
    tokc4 = (np.arange(128, dtype=np.float32).reshape(128, 1)
             + 128.0 * np.arange(4, dtype=np.float32).reshape(1, 4))
    experts = np.ascontiguousarray(
        np.asarray(L0_ffn_expert_weights, dtype=np.float32).reshape(NEXP * D, D))

    nc = build(taps=tuple(DEBUG_TAPS))
    base = {
        "wkt": np.ascontiguousarray(np.asarray(L0_attn_Wk, dtype=np.float32).T),
        "wvt": np.ascontiguousarray(np.asarray(L0_attn_Wv, dtype=np.float32).T),
        "wqt": np.ascontiguousarray(np.asarray(L0_attn_Wq, dtype=np.float32).T),
        "woutt": np.ascontiguousarray(np.asarray(L0_attn_Wout, dtype=np.float32).T),
        "w0t": np.ascontiguousarray(np.asarray(L0_attn_W0, dtype=np.float32).T),
        "w1t": np.ascontiguousarray(np.asarray(L0_attn_W1, dtype=np.float32).T),
        "w1": np.ascontiguousarray(np.asarray(L0_attn_W1, dtype=np.float32)),
        "wg3": wg3, "bg3": bg3,
        "router": np.ascontiguousarray(np.asarray(L0_ffn_router_w, dtype=np.float32)),
        "experts": experts, "identin": ident, "oh": oh,
        "lt128d": lt, "iota32d": iota32, "iotacd": iotac, "tokc4d": tokc4,
    }
    in_maps = [dict(base, x=xf[c * TC:(c + 1) * TC]) for c in range(8)]
    res = run_bass_kernel_spmd(nc, in_maps, core_ids=list(range(8)))
    out = np.concatenate([res.results[c]["out"] for c in range(8)], axis=0)
    kernel.last_results = res
    return out.reshape(B, S, DD)


# version 22 (final)
# speedup vs baseline: 1.0997x; 1.0014x over previous
import numpy as np
import ml_dtypes
from contextlib import ExitStack
import concourse.bass as bass
import concourse.mybir as mybir
from concourse.bass_utils import run_bass_kernel_spmd
from concourse.bass import IndirectOffsetOnAxis as IOA

F32 = mybir.dt.float32
BF16 = mybir.dt.bfloat16
I32 = mybir.dt.int32
AF = mybir.ActivationFunctionType
ALU = mybir.AluOpType
AX = mybir.AxisListType.X
BFNP = ml_dtypes.bfloat16

D = 1024
TC = 4096
NBLK = 8
TB = 512
KC = 8
NEXP = 32
EGRP = 2
NGRP = NEXP // EGRP
SCALE = 2.0 / 4096.0
ERR_CLIP = 5.0
GRAD_CLIP = 1.0
LN_EPS = 1e-5

DEBUG_TAPS = []

TAP_SHAPES = {
    "ln1": ([128, 4096], BF16), "kfm": ([128, 4096], BF16), "vtm": ([128, 4096], BF16),
    "qfm": ([128, 4096], BF16), "a1tm": ([128, 4096], BF16), "etm": ([128, 4096], BF16),
    "d1tm": ([128, 4096], BF16), "g0": ([128, 8192], F32), "g1": ([128, 8192], F32),
    "w0n": ([128, 8192], BF16), "resid": ([128, 4096], F32), "hit": ([128, 4096], BF16),
    "logits": ([128, 1024], F32), "wma": ([128, 1024], F32), "gacc": ([1, 4], F32),
    "gsig": ([1, 4], F32), "u2": ([128, 4096], BF16), "h2": ([128, 4096], BF16),
}


class Chan:
    def __init__(self, ctx, nc, name):
        self.h = ctx.enter_context(nc.semaphore(name))
        self.n = 0

    def inc(self, inst, k=1):
        inst.then_inc(self.h, k)
        self.n += k
        return (self.h, self.n)

    def tok(self):
        return (self.h, self.n)


def wait(eng, tok):
    if tok is None:
        return
    if isinstance(tok, _Multi):
        for t in tok.toks:
            wait(eng, t)
        return
    eng.wait_ge(tok[0], tok[1])


class _Multi:
    def __init__(self, toks):
        self.toks = [t for t in toks if t is not None]


def _multi(toks):
    return _Multi(toks)


class DmaQ:

    def __init__(self, ctx, nc, name):
        self.h = ctx.enter_context(nc.semaphore(name))
        self.n = 0

    def dma(self, eng, out, in_, **kw):
        if self.n:
            eng.wait_ge(self.h, self.n)
        inst = eng.dma_start(out, in_, **kw)
        inst.then_inc(self.h, 16)
        self.n += 16
        return (self.h, self.n)

    def dma_indirect_gather(self, eng, out, in_dram, idx_ap, bound):
        if self.n:
            eng.wait_ge(self.h, self.n)
        inst = eng.indirect_dma_start(
            out=out, out_offset=None, in_=in_dram,
            in_offset=IOA(axis=0, ap=idx_ap),
            bounds_check=bound, oob_is_err=False)
        inst.then_inc(self.h, 16)
        self.n += 16
        return (self.h, self.n)

    def dma_indirect_scatter(self, eng, out_dram, idx_ap, in_sb, bound):
        if self.n:
            eng.wait_ge(self.h, self.n)
        inst = eng.indirect_dma_start(
            out=out_dram, out_offset=IOA(axis=0, ap=idx_ap),
            in_=in_sb, in_offset=None,
            bounds_check=bound, oob_is_err=False)
        inst.then_inc(self.h, 16)
        self.n += 16
        return (self.h, self.n)

    def dma_tr(self, eng, out, in_):
        if self.n:
            eng.wait_ge(self.h, self.n)
        inst = eng.dma_start_transpose(out, in_)
        inst.then_inc(self.h, 16)
        self.n += 16
        return (self.h, self.n)

    def tok(self):
        return (self.h, self.n) if self.n else None


def build(taps=(), phases=7):
    nc = bass.Bass("TRN2", target_bir_lowering=False, debug=False)

    x_d = nc.dram_tensor("x", [TC, D], F32, kind="ExternalInput").ap()
    w_d = {w: nc.dram_tensor(w, [D, D], F32, kind="ExternalInput").ap()
           for w in ["w1", "wkt", "wvt", "wqt", "w0t", "w1t", "woutt"]}
    wg3_d = nc.dram_tensor("wg3", [3, D], F32, kind="ExternalInput").ap()
    bg3_d = nc.dram_tensor("bg3", [1, 4], F32, kind="ExternalInput").ap()
    router_d = nc.dram_tensor("router", [NEXP, D], F32, kind="ExternalInput").ap()
    experts_d = nc.dram_tensor("experts", [NEXP * D, D], F32, kind="ExternalInput").ap()
    ident_d = nc.dram_tensor("identin", [128, 128], BF16, kind="ExternalInput").ap()
    oh_d = nc.dram_tensor("oh", [NEXP, NEXP * 128], BF16, kind="ExternalInput").ap()
    lt_d = nc.dram_tensor("lt128d", [128, 128], BF16, kind="ExternalInput").ap()
    iota32_d = nc.dram_tensor("iota32d", [128, NEXP], F32, kind="ExternalInput").ap()
    iotac_d = nc.dram_tensor("iotacd", [128, 1], F32, kind="ExternalInput").ap()
    tokc4_d = nc.dram_tensor("tokc4d", [128, 4], F32, kind="ExternalInput").ap()
    out_d = nc.dram_tensor("out", [TC, D], F32, kind="ExternalOutput").ap()

    ksp = nc.dram_tensor("ksp", [128, KC * TC], BF16, kind="Internal").ap()
    qsp = nc.dram_tensor("qsp", [128, KC * TC], BF16, kind="Internal").ap()
    vsp = nc.dram_tensor("vsp", [128, (TC // 128) * D], BF16, kind="Internal").ap()
    ln2sp = nc.dram_tensor("ln2sp", [128, KC * TC], BF16, kind="Internal").ap()
    residsp = nc.dram_tensor("residsp", [128, (TC // 128) * D], F32, kind="Internal").ap()
    moeacc = nc.dram_tensor("moeacc", [TC, D], F32, kind="Internal").ap()
    ln2tmsp = nc.dram_tensor("ln2tmsp", [TC, D], BF16, kind="Internal").ap()
    idxstage = nc.dram_tensor("idxstage", [NEXP * 512, 2], F32, kind="Internal").ap()
    bufA = nc.dram_tensor("bufA", [TC, D], BF16, kind="Internal").ap()
    bufB = nc.dram_tensor("bufB", [TC, D], BF16, kind="Internal").ap()
    ccin = nc.dram_tensor("ccin", [128, 8192], F32, kind="Internal").ap()
    ccout = nc.dram_tensor("ccout", [128, 8192], F32, kind="Internal").ap()
    ccg_in = nc.dram_tensor("ccg_in", [1, 4], F32, kind="Internal").ap()
    ccg_out = nc.dram_tensor("ccg_out", [1, 4], F32, kind="Internal").ap()
    ccin2 = nc.dram_tensor("ccin2", [128, 8192], F32, kind="Internal").ap()
    ccout2 = nc.dram_tensor("ccout2", [128, 8192], F32, kind="Internal").ap()
    gbd = nc.dram_tensor("gbd", [1, 4], F32, kind="Internal").ap()

    tap_d = {t: nc.dram_tensor("tap_" + t, TAP_SHAPES[t][0], TAP_SHAPES[t][1],
                               kind="ExternalOutput").ap() for t in taps}

    with ExitStack() as ctx:
        en = ctx.enter_context
        ART = en(nc.sbuf_tensor("arena", [128, 102400], BF16))
        ARF = ART.bitcast(F32)
        ident = en(nc.sbuf_tensor("ident", [128, 128], BF16))
        rthi = en(nc.sbuf_tensor("rthi", [128, KC * NEXP], BF16))
        rtlo = en(nc.sbuf_tensor("rtlo", [128, KC * NEXP], BF16))
        wgt = en(nc.sbuf_tensor("wgt", [128, KC * 4], BF16))
        small = en(nc.sbuf_tensor("small", [128, 64], F32))
        stats = en(nc.sbuf_tensor("stats", [128, 32], F32))
        g3 = en(nc.sbuf_tensor("g3", [1, 16], F32))
        ones_bf = en(nc.sbuf_tensor("ones_bf", [128, 1], BF16))
        colsum_bf = en(nc.sbuf_tensor("colsum_bf", [128, 8], BF16))
        ones1 = en(nc.sbuf_tensor("ones1", [1, 128], F32))
        ones1b = en(nc.sbuf_tensor("ones1b", [1, 128], BF16))
        lt128 = en(nc.sbuf_tensor("lt128", [128, 128], BF16))
        iota32 = en(nc.sbuf_tensor("iota32", [128, NEXP], F32))
        iotac = en(nc.sbuf_tensor("iotac", [128, 1], F32))
        tokc4 = en(nc.sbuf_tensor("tokc4", [128, 4], F32))
        baserow = en(nc.sbuf_tensor("baserow", [1, 2 * NEXP], F32))
        baserow_bf = en(nc.sbuf_tensor("baserow_bf", [1, 2 * NEXP], BF16))
        br4b = en(nc.sbuf_tensor("br4b", [1, 256], BF16))
        wgsb = en(nc.sbuf_tensor("wgsb", [4, D], BF16))

        PS = [en(nc.psum_tensor(f"ps{i}", [128, 512], F32)) for i in range(6)]
        PT = [en(nc.psum_tensor(f"pt{i}", [128, 1024], BF16)) for i in range(2)]
        PTF = [p.bitcast(F32) for p in PT]
        PSF = [None] * 8

        def bwait(eng, bi):
            wait(eng, PSF[bi])

        def bset(bi, tok):
            PSF[bi] = tok

        def bfa(off_kb, cols):
            off = int(off_kb * 1024) // 2
            return ART[:, off:off + cols]

        def f32a(off_kb, cols):
            off = int(off_kb * 1024) // 4
            return ARF[:, off:off + cols]

        W_OFF = {"wk": 0, "wv": 16, "wq": 32, "w0": 48, "w1": 64, "w1p": 80}
        WT = {k: bfa(v, KC * D) for k, v in W_OFF.items()}
        X1 = f32a(96, 4096)
        LN1T = bfa(112, 4096)
        LN1F = bfa(120, 4096)
        QSB = bfa(128, 4096)
        KSB = bfa(136, 4096)
        VSB = bfa(144, 4096)
        RSBF = f32a(168, 1024)
        RHI = bfa(172, 1024)
        RLO = bfa(174, 1024)
        JNK1A = bfa(184, 4096)
        G0 = f32a(0, 8192)
        WOUTT = bfa(32, KC * D)
        G1 = f32a(96, 8192)
        KLD = [bfa(128, 4096), bfa(136, 4096)]
        VLD = [bfa(144, 4096), bfa(152, 4096)]
        SGT = bfa(160, 4096)
        PRC = bfa(168, 4096)
        A1T = bfa(176, 4096)
        AF1 = bfa(184, 4096)
        ETM = bfa(192, 4096)
        W0N = bfa(128, KC * D)
        W1N = bfa(144, KC * D)
        X2 = f32a(0, 4096)
        JNK2 = bfa(16, 4096)
        JNKF2 = f32a(18, 512)
        RES = f32a(48, 4096)
        LN2F32 = f32a(64, 4096)
        HIT = bfa(80, 4096)
        LOT = bfa(88, 4096)
        HIF = bfa(96, 4096)
        LOF = bfa(104, 4096)
        WMT = bfa(112, 4096)
        LOGI = f32a(120, 1024)
        WMA = f32a(124, 1024)
        Q2 = [bfa(160, 4096), bfa(168, 4096)]
        SG2 = bfa(176, 4096)
        U2 = bfa(184, 4096)
        H2 = bfa(192, 4096)
        EX = [bfa(0, KC * D), bfa(16, KC * D), bfa(32, KC * D), bfa(48, KC * D)]
        OHSB = bfa(64, NEXP * 128)
        WBC = bfa(72, 1024)
        LSTR = [bfa(74, 4096), bfa(82, 4096)]
        ZSB = [[bfa(90, 4096), bfa(106, 4096)], [bfa(98, 4096), bfa(136, 4096)]]
        STG = [bfa(144, 4096), bfa(152, 4096)]
        R4 = [f32a(0, 4096), f32a(16, 4096)]
        M4 = [f32a(32, 4096), f32a(48, 4096)]
        O4 = [f32a(64, 4096), f32a(80, 4096)]

        ch_t = Chan(ctx, nc, "ch_t")
        ch_a = Chan(ctx, nc, "ch_a")
        ch_v = Chan(ctx, nc, "ch_v")
        ch_cc = Chan(ctx, nc, "ch_cc")
        WQ = DmaQ(ctx, nc, "WQ")
        TRQ = DmaQ(ctx, nc, "TRQ")
        XQ = DmaQ(ctx, nc, "XQ")
        SPQ = DmaQ(ctx, nc, "SPQ")
        SPK = DmaQ(ctx, nc, "SPK")
        SPV = DmaQ(ctx, nc, "SPV")
        KVQ = [DmaQ(ctx, nc, "KVQ0"), DmaQ(ctx, nc, "KVQ1")]
        Q2Q = DmaQ(ctx, nc, "Q2Q")
        RSQ = DmaQ(ctx, nc, "RSQ")
        LNQ = DmaQ(ctx, nc, "LNQ")
        CCS = DmaQ(ctx, nc, "CCS")
        CCL = DmaQ(ctx, nc, "CCL")
        GBQ = DmaQ(ctx, nc, "GBQ")
        EXQ = [DmaQ(ctx, nc, "EXQ0"), DmaQ(ctx, nc, "EXQ1")]
        LSQ = [DmaQ(ctx, nc, "LSQ0"), DmaQ(ctx, nc, "LSQ1")]
        ACQ = [DmaQ(ctx, nc, "ACQ0"), DmaQ(ctx, nc, "ACQ1")]
        P4Q = [DmaQ(ctx, nc, "P4Q0"), DmaQ(ctx, nc, "P4Q1")]
        STQ = [DmaQ(ctx, nc, "STQ0"), DmaQ(ctx, nc, "STQ1")]
        TAPQ = DmaQ(ctx, nc, "TAPQ")

        block = en(nc.Block())
        T, A, V, SY, G = block.tensor, block.scalar, block.vector, block.sync, block.gpsimd

        def tap(name, src, tok):
            if name in tap_d:
                def f(g):
                    wait(g, tok)
                    TAPQ.dma(g, tap_d[name], src)
                G(f)

        cast_toks = {}

        def stw_g(g):
            for w, q in [("wq", WQ), ("wk", EXQ[0]), ("wv", EXQ[1]),
                         ("w0", ACQ[0]), ("w1", ACQ[1])]:
                cast_toks[w + "T"] = q.dma(
                    g, WT[w][:, :].rearrange("p (c e) -> p c e", c=KC),
                    w_d[w + "t"].rearrange("(c p) e -> p c e", p=128))
            cast_toks["router"] = WQ.dma(g, RSBF[0:32, 0:D], router_d)
            cast_toks["bg3"] = WQ.dma(g, g3[0:1, 4:8], bg3_d)
            cast_toks["wg3"] = WQ.dma(g, wgsb[0:3, :], wg3_d)
            cast_toks["lt"] = WQ.dma(g, lt128[:, :], lt_d)
            cast_toks["iota32"] = WQ.dma(g, iota32[:, :], iota32_d)
            cast_toks["iotac"] = WQ.dma(g, iotac[:, :], iotac_d)
            cast_toks["tokc4"] = WQ.dma(g, tokc4[:, :], tokc4_d)
            cast_toks["w1p"] = WQ.dma(
                g, WT["w1p"].rearrange("p (c e) -> p c e", c=KC),
                w_d["w1"].rearrange("(c p) e -> p c e", p=128))
        G(stw_g)

        def stw_sy(sy):
            cast_toks["ident"] = TRQ.dma(sy, ident[:, :], ident_d)
        SY(stw_sy)

        def stw_wg_t(t):
            wait(t, cast_toks["wg3"])
            wait(t, cast_toks["ident"])
            bwait(t, 6)
            for kc in range(KC):
                li = t.transpose(PT[0][0:128, kc * 128:kc * 128 + 3],
                                 wgsb[0:3, kc * 128:(kc + 1) * 128], ident[0:3, 0:3])
            ch_t.inc(li)
        T(stw_wg_t)
        wg_ps_tok = ch_t.tok()

        def stw_wg_v(v):
            wait(v, wg_ps_tok)
            for kc in range(KC):
                v.tensor_copy(wgt[:, kc * 4:kc * 4 + 3], PT[0][:, kc * 128:kc * 128 + 3])
            ch_v.inc(v.drain(fusable=False))
        V(stw_wg_v)
        wgt_tok = ch_v.tok()
        bset(6, wgt_tok)

        def stw_r_v(v):
            wait(v, cast_toks["router"])
            v.tensor_copy(RHI[0:32, 0:D], RSBF[0:32, 0:D])
            t1 = ch_v.inc(v.drain(fusable=False))
            wait(v, t1)
            v.tensor_tensor(RLO[0:32, 0:D], RSBF[0:32, 0:D], RHI[0:32, 0:D], ALU.subtract)
            ch_v.inc(v.drain(fusable=False))
        V(stw_r_v)
        rhilo_tok = ch_v.tok()

        rt_ev = [None] * KC
        rt_pe = [None] * KC
        for kc in range(KC):
            def rt_t(t, kc=kc):
                if kc == 0:
                    wait(t, rhilo_tok)
                    bwait(t, 7)
                if kc >= 2:
                    wait(t, rt_ev[kc - 2])
                half = (kc % 2) * 512
                t.transpose(PT[1][0:128, half:half + 32],
                            RHI[0:32, kc * 128:(kc + 1) * 128], ident[0:32, 0:32])
                li = t.transpose(PT[1][0:128, half + 128:half + 160],
                                 RLO[0:32, kc * 128:(kc + 1) * 128], ident[0:32, 0:32])
                rt_pe[kc] = ch_t.inc(li)
            T(rt_t)

            def rt_e(s, kc=kc):
                wait(s, rt_pe[kc])
                half = (kc % 2) * 512
                s.copy(rthi[:, kc * NEXP:(kc + 1) * NEXP], PT[1][:, half:half + 32])
                s.copy(rtlo[:, kc * NEXP:(kc + 1) * NEXP], PT[1][:, half + 128:half + 160])
                rt_ev[kc] = ch_a.inc(s.drain(fusable=False))
            A(rt_e)
        rt_tok = rt_ev[-1]
        bset(7, rt_tok)

        def emit_ln(x_ap, outs, junk, xin_tok, so=0):
            sums = stats[:, so * 8:so * 8 + 4]
            sq = stats[:, so * 8 + 4:so * 8 + 8]
            rstd = stats[:, 16 + so * 8:20 + so * 8]
            nb = stats[:, 20 + so * 8:24 + so * 8]
            t1 = small[:, 32 + so * 8:36 + so * 8]
            t2 = small[:, 36 + so * 8:40 + so * 8]

            def on_sq(s):
                wait(s, xin_tok)
                for tc in range(4):
                    s.activation(junk[:, tc * 1024:(tc + 1) * 1024],
                                 x_ap[:, tc * 1024:(tc + 1) * 1024],
                                 AF.Square, accum_out=sq[:, tc:tc + 1])
                ch_a.inc(s.drain(fusable=False))
            A(on_sq)
            sq_tok = ch_a.tok()

            def on_v(v):
                wait(v, xin_tok)
                for tc in range(4):
                    v.reduce_sum(sums[:, tc:tc + 1], x_ap[:, tc * 1024:(tc + 1) * 1024], axis=AX)
                tk = ch_v.inc(v.drain(fusable=False)); wait(v, tk)
                wait(v, sq_tok)
                v.tensor_scalar_mul(nb[:, :], sums[:, :], 1.0 / D)
                tk = ch_v.inc(v.drain(fusable=False)); wait(v, tk)
                v.tensor_tensor(t1[:, :], nb[:, :], nb[:, :], ALU.mult)
                v.tensor_scalar(t2[:, :], sq[:, :], 1.0 / D, LN_EPS, ALU.mult, ALU.add)
                tk = ch_v.inc(v.drain(fusable=False)); wait(v, tk)
                v.tensor_tensor(t2[:, :], t2[:, :], t1[:, :], ALU.subtract)
                ch_v.inc(v.drain(fusable=False))
            V(on_v)
            var_tok = ch_v.tok()

            def on_rsq(sa):
                wait(sa, var_tok)
                sa.activation(t1[:, :], t2[:, :], AF.Sqrt)
                ch_a.inc(sa.drain(fusable=False))
            A(on_rsq)
            rsq_tok = ch_a.tok()

            def on_v2(v):
                wait(v, rsq_tok)
                v.reciprocal(rstd[:, :], t1[:, :])
                tk = ch_v.inc(v.drain(fusable=False)); wait(v, tk)
                v.tensor_tensor(nb[:, :], nb[:, :], rstd[:, :], ALU.mult)
                tk = ch_v.inc(v.drain(fusable=False)); wait(v, tk)
                v.tensor_scalar_mul(nb[:, :], nb[:, :], -1.0)
                ch_v.inc(v.drain(fusable=False))
            V(on_v2)
            stat_tok = ch_v.tok()

            def on_s(s):
                wait(s, stat_tok)
                for ap in outs:
                    for tc in range(4):
                        s.activation(ap[:, tc * 1024:(tc + 1) * 1024],
                                     x_ap[:, tc * 1024:(tc + 1) * 1024], AF.Identity,
                                     bias=nb[:, tc:tc + 1], scale=rstd[:, tc:tc + 1])
                ch_a.inc(s.drain(fusable=False))
            A(on_s)
            return ch_a.tok()

        def transpose_block(src, dst, to_fm, pe_wait):
            n_out = KC if to_fm else 4
            pe_toks = [None] * n_out
            ev_toks = [None] * n_out
            for i in range(n_out):
                bank = PT[i % 2]

                def on_t(t, i=i, bank=bank):
                    if i == 0 and pe_wait is not None:
                        wait(t, pe_wait)
                    bwait(t, 6 + i % 2)
                    if to_fm:
                        for tc in range(4):
                            li = t.transpose(bank[:, tc * 128:tc * 128 + 128],
                                             src[:, tc * 1024 + i * 128:
                                                 tc * 1024 + i * 128 + 128],
                                             ident[:, :])
                    else:
                        for kc in range(KC):
                            li = t.transpose(bank[:, kc * 128:kc * 128 + 128],
                                             src[:, kc * 512 + i * 128:
                                                 kc * 512 + i * 128 + 128],
                                             ident[:, :])
                    pe_toks[i] = ch_t.inc(li)
                T(on_t)

                def on_e(v, i=i, bank=bank):
                    wait(v, pe_toks[i])
                    if to_fm:
                        inst = v.tensor_copy(dst[:, i * 512:(i + 1) * 512], bank[:, 0:512])
                    else:
                        inst = v.tensor_copy(dst[:, i * 1024:(i + 1) * 1024], bank[:, :])
                    ev_toks[i] = ch_v.inc(inst)
                V(on_e)
                bset(6 + i % 2, ev_toks[i])
            return ev_toks[-1]

        def proj_fm(wt_ap, rhs_fm, dst, rhs_tok, w_tok, dst2=None, fn=AF.Identity,
                    fn2=AF.Identity, ev_guard=None):
            mm = [None] * KC
            ev = [None] * KC
            for oc in range(KC):
                bank = PS[oc % 2]

                def on_t(t, oc=oc, bank=bank):
                    if oc == 0:
                        wait(t, rhs_tok)
                        wait(t, w_tok)
                    bwait(t, oc % 2)
                    for kc in range(KC):
                        li = t.matmul(bank[:, :],
                                      wt_ap[:, kc * 1024 + oc * 128:
                                            kc * 1024 + oc * 128 + 128],
                                      rhs_fm[:, kc * 512:(kc + 1) * 512],
                                      start=(kc == 0), stop=(kc == KC - 1))
                    mm[oc] = ch_t.inc(li)
                T(on_t)

                def on_s(v, oc=oc, bank=bank):
                    wait(v, mm[oc])
                    if oc == 0 and ev_guard is not None:
                        wait(v, ev_guard)
                    li = v.tensor_copy(dst[:, oc * 512:(oc + 1) * 512], bank[:, :])
                    if dst2 is not None:
                        li = v.tensor_copy(dst2[:, oc * 512:(oc + 1) * 512], bank[:, :])
                    ev[oc] = ch_v.inc(li)
                V(on_s)
                bset(oc % 2, ev[oc])
            return ev[-1], mm[-1]

        def x_view(b):
            return x_d[b * TB:(b + 1) * TB, :].rearrange("(tc p) d -> p tc d", p=128)

        def sp_view(sp, b):
            return sp.rearrange("p (kc t) -> p kc t", kc=KC)[:, :, b * TB:(b + 1) * TB]

        def sb_kc(sb):
            return sb.rearrange("p (kc t) -> p kc t", kc=KC)

        gate_red = [None] * NBLK
        gate_mm = [None] * NBLK
        spq = [None] * NBLK
        spk = [None] * NBLK
        spv = [None] * NBLK
        xtok = [None] * NBLK

        def z_gates(v):
            v.memset(ones1[:, :], 1.0)
            v.memset(ones_bf[:, :], 1.0)
            v.memset(ones1b[:, :], 1.0)
            v.memset(g3[0:1, 3:4], 0.0)
            v.memset(baserow[:, :], 0.0)
            v.memset(baserow_bf[:, :], 0.0)
            ch_v.inc(v.drain(fusable=False))
        V(z_gates)
        gz_tok = ch_v.tok()

        ZTILE = VLD[1]
        PADP = f32a(160, 256)
        INITQ = GBQ

        def init_v(v):
            v.memset(ZTILE[:, :], 0.0)
            pv = PADP.rearrange("p (c two) -> p c two", two=2)
            v.memset(pv[:, :, 0:1], 100000.0)
            tk = ch_v.inc(v.drain(fusable=False)); wait(v, tk)
            v.memset(pv[:, :, 1:2], 0.0)
            ch_v.inc(v.drain(fusable=False))
        V(init_v)
        initv_tok = ch_v.tok()

        def init_dma(sa):
            wait(sa, initv_tok)
            for i in range(NBLK):
                INITQ.dma(sa, bufA[i * TB:(i + 1) * TB, :].rearrange(
                    "(tc p) d -> p tc d", p=128),
                    ZTILE[:, :].rearrange("p (tc d) -> p tc d", tc=4))
            for i in range(NBLK):
                INITQ.dma(sa, bufB[i * TB:(i + 1) * TB, :].rearrange(
                    "(tc p) d -> p tc d", p=128),
                    ZTILE[:, :].rearrange("p (tc d) -> p tc d", tc=4))
            INITQ.dma(sa, idxstage.rearrange("(c p) two -> p c two", p=128),
                      PADP.rearrange("p (c two) -> p c two", two=2))
        A(init_dma)
        init_tok = INITQ.tok()


        XLQ = [XQ, TRQ, CCS, RSQ]
        XP = [X1, f32a(160, 4096)]
        LNTP = [LN1T, bfa(176, 4096)]
        JNKP = [JNK1A, bfa(192, 4096)]
        pg = [None, None]
        for b in range(NBLK):
            p = b % 2

            def on_x(sy, b=b, p=p, gl=pg[p]):
                if gl is not None:
                    for g in gl:
                        wait(sy, g)
                if b == 1:
                    wait(sy, init_tok)
                    wait(sy, rt_tok)
                wait(sy, TAPQ.tok())
                tks = []
                for tc in range(4):
                    tks.append(XLQ[tc].dma(
                        sy, XP[p][:, tc * 1024:(tc + 1) * 1024],
                        x_d[b * TB + tc * 128:b * TB + (tc + 1) * 128, :]))
                xtok[b] = _multi(tks)
            SY(on_x)
            ln_tok = emit_ln(XP[p], [LNTP[p]], JNKP[p], xtok[b], so=p)
            ln1f_tok = transpose_block(LNTP[p], LN1F, True, ln_tok)

            q_tok, q_mm = proj_fm(WT["wq"], LN1F, QSB, ln1f_tok,
                                  cast_toks["wqT"] if b == 0 else None,
                                  ev_guard=spq[b - 1] if b else None)
            k_tok, _ = proj_fm(WT["wk"], LN1F, KSB, ln1f_tok,
                               cast_toks["wkT"] if b == 0 else None,
                               ev_guard=spk[b - 1] if b else None)

            v_mm = [None] * 8
            v_ev = [None] * 8
            for i, (tc, h) in enumerate([(tc, h) for tc in range(4) for h in range(2)]):
                bank = PS[2 + i % 2]

                def on_t(t, i=i, tc=tc, h=h, bank=bank, b=b):
                    if i == 0:
                        wait(t, ln1f_tok)
                        if b == 0:
                            wait(t, cast_toks["wvT"])
                    bwait(t, 2 + i % 2)
                    for kc in range(KC):
                        li = t.matmul(bank[:, :],
                                      LN1F[:, kc * 512 + tc * 128:kc * 512 + tc * 128 + 128],
                                      WT["wv"][:, kc * 1024 + h * 512:
                                               kc * 1024 + (h + 1) * 512],
                                      start=(kc == 0), stop=(kc == KC - 1))
                    v_mm[i] = ch_t.inc(li)
                T(on_t)

                def on_s(v, i=i, tc=tc, h=h, bank=bank, b=b):
                    wait(v, v_mm[i])
                    if i == 0 and b > 0:
                        wait(v, spv[b - 1])
                    v_ev[i] = ch_v.inc(v.tensor_copy(
                        VSB[:, tc * 1024 + h * 512:tc * 1024 + (h + 1) * 512], bank[:, :]))
                V(on_s)
                bset(2 + i % 2, v_ev[i])
            v_tok = v_ev[-1]

            def on_gt(t, b=b, p=p):
                wait(t, ln_tok)
                if b == 0:
                    wait(t, gz_tok)
                    bwait(t, 4)
                for kc in range(KC):
                    for tc in range(4):
                        li = t.matmul(PS[4][:, kc:kc + 1],
                                      LNTP[p][:, tc * 1024 + kc * 128:
                                           tc * 1024 + kc * 128 + 128],
                                      ones_bf[:, :],
                                      start=(b == 0 and kc == 0 and tc == 0),
                                      stop=(b == NBLK - 1 and kc == KC - 1
                                            and tc == 3))
                gate_mm[b] = ch_t.inc(li)
            T(on_gt)

            def on_sp(sy, b=b, q_tok=q_tok, k_tok=k_tok, v_tok=v_tok):
                wait(sy, q_tok)
                spq[b] = SPQ.dma(sy, sp_view(qsp, b), sb_kc(QSB))
                wait(sy, k_tok)
                spk[b] = SPK.dma(sy, sp_view(ksp, b), sb_kc(KSB))
                wait(sy, v_tok)
                spv[b] = SPV.dma(sy, vsp[:, b * 4096:(b + 1) * 4096], VSB[:, :])
            SY(on_sp)
            pg[p] = [ln_tok, ln1f_tok, gate_mm[b]]

            if b == 0:
                tap("ln1", LN1T[:, :], ln_tok)
                tap("kfm", KSB[:, :], k_tok)
                tap("vtm", VSB[:, :], v_tok)
                tap("qfm", QSB[:, :], q_tok)

        last_1a_pe = ch_t.tok()

        def on_gf1(s):
            wait(s, gate_mm[NBLK - 1])
            s.activation(colsum_bf[:, :], PS[4][:, 0:8], AF.Identity)
            ch_a.inc(s.drain(fusable=False))
        A(on_gf1)
        colsum_tok = ch_a.tok()
        bset(4, colsum_tok)

        def on_gf2(t):
            wait(t, colsum_tok)
            wait(t, wgt_tok)
            bwait(t, 5)
            for kc in range(KC):
                li = t.matmul(PS[5][0:1, 0:3], colsum_bf[:, kc:kc + 1],
                              wgt[:, kc * 4:kc * 4 + 3],
                              start=(kc == 0), stop=(kc == KC - 1))
            ch_t.inc(li)
        T(on_gf2)
        gmv_tok = ch_t.tok()

        def on_gf3(v):
            wait(v, gmv_tok)
            v.tensor_copy(g3[0:1, 0:3], PS[5][0:1, 0:3])
            ch_v.inc(v.drain(fusable=False))
        V(on_gf3)
        gate_red[NBLK - 1] = ch_v.tok()
        bset(5, gate_red[NBLK - 1])

        def finalize_partial():
            def fin_p(g):
                for q in [WQ, TRQ, XQ, SPQ, SPK, SPV, KVQ[0], KVQ[1], Q2Q, RSQ, LNQ,
                          CCS, CCL, GBQ, EXQ[0], EXQ[1], LSQ[0], LSQ[1], ACQ[0],
                          ACQ[1], P4Q[0], P4Q[1], STQ[0], STQ[1], TAPQ]:
                    if q.n:
                        g.wait_ge(q.h, q.n)
            G(fin_p)

        if phases < 2:
            finalize_partial()
            return nc

        def gz(v):
            v.memset(G0[:, :], 0.0)
            v.memset(G1[:, :], 0.0)
            ch_v.inc(v.drain(fusable=False))
        V(gz)
        gzero_tok = ch_v.tok()

        def on_woutt(g):
            wait(g, last_1a_pe)
            cast_toks["woutT"] = CCL.dma(
                g, WOUTT[:, :].rearrange("p (c e) -> p c e", c=KC),
                w_d["woutt"].rearrange("(c p) e -> p c e", p=128))
        G(on_woutt)

        kld_t = [None] * NBLK
        vld_t = [None] * NBLK

        def kv_load(b, guard):
            def f(sy):
                if guard is not None:
                    wait(sy, guard)
                wait(sy, TAPQ.tok())
                wait(sy, init_tok)
                wait(sy, spq[NBLK - 1])
                wait(sy, spk[NBLK - 1])
                wait(sy, spv[NBLK - 1])
                kld_t[b] = KVQ[b % 2].dma(sy, KLD[b % 2][:, :], sp_view(ksp, b))
                vld_t[b] = KVQ[b % 2].dma(sy, VLD[b % 2][:, :],
                                          vsp[:, b * 4096:(b + 1) * 4096])
            SY(f)

        kv_load(0, None)
        blk_done = [None] * NBLK

        def emit_g(lhs_tm, rhs_tm, acc, lhs_tok, rhs_tok, first_guard):
            mm = {}
            ev = {}
            idx = 0
            for ec in range(KC):
                for on in range(2):
                    bank = PS[2 + idx % 2]

                    def on_t(t, ec=ec, on=on, bank=bank, idx=idx):
                        if idx == 0:
                            wait(t, lhs_tok)
                            wait(t, rhs_tok)
                        bwait(t, 2 + idx % 2)
                        for tc in range(4):
                            li = t.matmul(bank[:, :],
                                          lhs_tm[:, tc * 1024 + ec * 128:
                                                 tc * 1024 + ec * 128 + 128],
                                          rhs_tm[:, tc * 1024 + on * 512:
                                                 tc * 1024 + (on + 1) * 512],
                                          start=(tc == 0), stop=(tc == 3))
                        mm[idx] = ch_t.inc(li)
                    T(on_t)

                    def on_v(v, ec=ec, on=on, bank=bank, idx=idx):
                        wait(v, mm[idx])
                        if idx == 0 and first_guard is not None:
                            wait(v, first_guard)
                        sl = slice(ec * 1024 + on * 512, ec * 1024 + (on + 1) * 512)
                        v.scalar_tensor_tensor(acc[:, sl], bank[:, :], SCALE, acc[:, sl],
                                               ALU.mult, ALU.add)
                        ev[idx] = ch_v.inc(v.drain(fusable=False))
                    V(on_v)
                    bset(2 + idx % 2, ev[idx])
                    idx += 1
            return {"mm_done": mm[idx - 1], "acc_done": ev[idx - 1]}

        for b in range(NBLK):
            kf = KLD[b % 2]
            vt = VLD[b % 2]
            if b + 1 < NBLK:
                kv_load(b + 1, blk_done[b - 1] if b >= 1 else None)

            pre_mm = [None] * 8
            pre_ev = [None] * 8
            for i, (tc, h) in enumerate([(tc, h) for tc in range(4) for h in range(2)]):
                bank = PS[i % 2]

                def on_t(t, i=i, tc=tc, h=h, bank=bank, b=b):
                    if i == 0:
                        wait(t, kld_t[b])
                        if b == 0:
                            wait(t, cast_toks["w0T"])
                    bwait(t, i % 2)
                    for kc in range(KC):
                        li = t.matmul(bank[:, :],
                                      kf[:, kc * 512 + tc * 128:kc * 512 + tc * 128 + 128],
                                      WT["w0"][:, kc * 1024 + h * 512:
                                               kc * 1024 + (h + 1) * 512],
                                      start=(kc == 0), stop=(kc == KC - 1))
                    pre_mm[i] = ch_t.inc(li)
                T(on_t)

                def on_s(v, i=i, tc=tc, h=h, bank=bank):
                    wait(v, pre_mm[i])
                    sl = slice(tc * 1024 + h * 512, tc * 1024 + (h + 1) * 512)
                    pre_ev[i] = ch_v.inc(v.tensor_copy(PRC[:, sl], bank[:, :]))
                V(on_s)
                bset(i % 2, pre_ev[i])
            pre_tok = pre_ev[-1]

            def on_sg(sa):
                wait(sa, pre_tok)
                ch_a.inc(sa.activation(SGT[:, :], PRC[:, :], AF.Sigmoid))
            A(on_sg)
            sg_tok = ch_a.tok()

            def on_a1(v):
                wait(v, sg_tok)
                v.tensor_tensor(A1T[:, :], PRC[:, :], SGT[:, :], ALU.mult)
                ch_v.inc(v.drain(fusable=False))
            V(on_a1)
            a1_tok = ch_v.tok()
            if b == 0:
                tap("a1tm", A1T[:, :], a1_tok)

            a1f_tok = transpose_block(A1T, AF1, True, a1_tok)

            prd_mm = [None] * 8
            prd_ev = [None] * 8
            for i, (tc, h) in enumerate([(tc, h) for tc in range(4) for h in range(2)]):
                bank = PS[i % 2]

                def on_t(t, i=i, tc=tc, h=h, bank=bank, b=b):
                    if i == 0:
                        wait(t, a1f_tok)
                        if b == 0:
                            wait(t, cast_toks["w1T"])
                    bwait(t, i % 2)
                    for kc in range(KC):
                        li = t.matmul(bank[:, :],
                                      AF1[:, kc * 512 + tc * 128:kc * 512 + tc * 128 + 128],
                                      WT["w1"][:, kc * 1024 + h * 512:
                                               kc * 1024 + (h + 1) * 512],
                                      start=(kc == 0), stop=(kc == KC - 1))
                    prd_mm[i] = ch_t.inc(li)
                T(on_t)

                def on_v(v, i=i, tc=tc, h=h, bank=bank, b=b):
                    wait(v, prd_mm[i])
                    if i == 0:
                        wait(v, vld_t[b])
                    sl = slice(tc * 1024 + h * 512, tc * 1024 + (h + 1) * 512)
                    v.tensor_tensor(ETM[:, sl], bank[:, :], vt[:, sl], ALU.subtract)
                    tk = ch_v.inc(v.drain(fusable=False)); wait(v, tk)
                    v.tensor_scalar(ETM[:, sl], ETM[:, sl], ERR_CLIP, -ERR_CLIP,
                                    ALU.min, ALU.max)
                    prd_ev[i] = ch_v.inc(v.drain(fusable=False))
                V(on_v)
                bset(i % 2, prd_ev[i])
            e_tok = prd_ev[-1]
            if b == 0:
                tap("etm", ETM[:, :], e_tok)

            def on_sd(v):
                wait(v, a1_tok)
                v.tensor_scalar(PRC[:, :], SGT[:, :], -1.0, 1.0, ALU.mult, ALU.add)
                tk = ch_v.inc(v.drain(fusable=False)); wait(v, tk)
                v.tensor_tensor(PRC[:, :], A1T[:, :], PRC[:, :], ALU.mult)
                tk = ch_v.inc(v.drain(fusable=False)); wait(v, tk)
                v.tensor_tensor(SGT[:, :], SGT[:, :], PRC[:, :], ALU.add)
                ch_v.inc(v.drain(fusable=False))
            V(on_sd)
            sd_tok = ch_v.tok()

            ef_tok = transpose_block(ETM, AF1, True, e_tok)

            d1_mm = [None] * 8
            d1_ev = [None] * 8
            for i, (tc, h) in enumerate([(tc, h) for tc in range(4) for h in range(2)]):
                bank = PS[i % 2]

                def on_t(t, i=i, tc=tc, h=h, bank=bank, b=b):
                    if i == 0:
                        wait(t, ef_tok)
                        if b == 0:
                            wait(t, cast_toks["w1p"])
                    bwait(t, i % 2)
                    for kc in range(KC):
                        li = t.matmul(bank[:, :],
                                      AF1[:, kc * 512 + tc * 128:kc * 512 + tc * 128 + 128],
                                      WT["w1p"][:, kc * 1024 + h * 512:
                                               kc * 1024 + (h + 1) * 512],
                                      start=(kc == 0), stop=(kc == KC - 1))
                    d1_mm[i] = ch_t.inc(li)
                T(on_t)

                def on_v(v, i=i, tc=tc, h=h, bank=bank):
                    wait(v, d1_mm[i])
                    if i == 0:
                        wait(v, sd_tok)
                    sl = slice(tc * 1024 + h * 512, tc * 1024 + (h + 1) * 512)
                    v.tensor_tensor(PRC[:, sl], bank[:, :], SGT[:, sl], ALU.mult)
                    d1_ev[i] = ch_v.inc(v.drain(fusable=False))
                V(on_v)
                bset(i % 2, d1_ev[i])
            d1_tok = d1_ev[-1]
            if b == 0:
                tap("d1tm", PRC[:, :], d1_tok)

            def ktm_guard(t):
                wait(t, TAPQ.tok())
            T(ktm_guard)
            ktm_tok = transpose_block(kf, SGT, False, d1_tok)

            g0r = emit_g(SGT, PRC, G0, ktm_tok, d1_tok, gzero_tok if b == 0 else None)

            g1r = emit_g(A1T, ETM, G1, a1_tok, e_tok, None)
            blk_done[b] = g1r["acc_done"]

        g0_fin = g0r["acc_done"]
        g1_fin = g1r["acc_done"]
        gates_fin = gate_red[NBLK - 1]
        cc_tok = {}

        def r_stage0(sy):
            wait(sy, gates_fin)
            cc_tok["ga"] = CCS.dma(sy, ccg_in[0:1, 0:4], g3[0:1, 0:4])
        SY(r_stage0)

        def r_cc0(g):
            wait(g, cc_tok["ga"])
            g.collective_compute(
                "AllReduce", ALU.add,
                replica_groups=[[0, 1, 2, 3, 4, 5, 6, 7]],
                ins=[ccg_in[0:1, :]], outs=[ccg_out[0:1, :]],
            ).then_inc(ch_cc.h, 1)
            ch_cc.n += 1
            g.wait_ge(ch_cc.h, ch_cc.n)
            cc_tok["gar"] = CCL.dma(g, g3[0:1, 0:3], ccg_out[0:1, 0:3])
        G(r_cc0)
        tap("g0", G0[:, :], g0_fin)
        tap("g1", G1[:, :], g1_fin)
        tap("gacc", g3[0:1, 0:4], gates_fin)

        if phases < 3:
            finalize_partial()
            return nc

        def r_stage(sy):
            wait(sy, g0_fin)
            cc_tok["g0"] = CCS.dma(sy, ccin[:, :], G0[:, :])
        SY(r_stage)

        def r_stage2(sy):
            wait(sy, g1_fin)
            cc_tok["g1"] = LNQ.dma(sy, ccin2[:, :], G1[:, :])
        SY(r_stage2)

        def r_cc(g):
            wait(g, cc_tok["g0"])
            g.collective_compute(
                "AllReduce", ALU.add,
                replica_groups=[[0, 1, 2, 3, 4, 5, 6, 7]],
                ins=[ccin[:, :]], outs=[ccout[:, :]],
            ).then_inc(ch_cc.h, 1)
            ch_cc.n += 1
            wait(g, cc_tok["g1"])
            g.collective_compute(
                "AllReduce", ALU.add,
                replica_groups=[[0, 1, 2, 3, 4, 5, 6, 7]],
                ins=[ccin2[:, :]], outs=[ccout2[:, :]],
            ).then_inc(ch_cc.h, 1)
            ch_cc.n += 1
            g.wait_ge(ch_cc.h, ch_cc.n - 1)
            for c in range(4):
                cc_tok["g0r%d" % c] = CCL.dma(
                    g, G0[:, c * 2048:(c + 1) * 2048], ccout[:, c * 2048:(c + 1) * 2048])
            g.wait_ge(ch_cc.h, ch_cc.n)
            for c in range(4):
                cc_tok["g1r%d" % c] = CCL.dma(
                    g, G1[:, c * 2048:(c + 1) * 2048], ccout2[:, c * 2048:(c + 1) * 2048])
        G(r_cc)

        def r_gv(v):
            wait(v, cc_tok["gar"])
            v.tensor_scalar(g3[0:1, 8:11], g3[0:1, 0:3], 1.0 / 32768.0, None, ALU.mult)
            tk = ch_v.inc(v.drain(fusable=False)); wait(v, tk)
            v.tensor_tensor(g3[0:1, 8:11], g3[0:1, 8:11], g3[0:1, 4:7], ALU.add)
            ch_v.inc(v.drain(fusable=False))
        V(r_gv)
        gate_m_tok = ch_v.tok()

        def r_gs(s):
            wait(s, gate_m_tok)
            s.activation(g3[0:1, 0:3], g3[0:1, 8:11], AF.Sigmoid)
            ch_a.inc(s.drain(fusable=False))
        A(r_gs)
        gate_sig_tok = ch_a.tok()
        tap("gsig", g3[0:1, 0:4], gate_sig_tok)

        def r_gv2(v):
            wait(v, gate_sig_tok)
            v.tensor_scalar(g3[0:1, 12:13], g3[0:1, 0:1], -1.0, 1.0, ALU.mult, ALU.add)
            tk = ch_v.inc(v.drain(fusable=False)); wait(v, tk)
            v.tensor_scalar(g3[0:1, 13:14], g3[0:1, 1:2], -1.0, None, ALU.mult)
            ch_v.inc(v.drain(fusable=False))
        V(r_gv2)
        gate_c_tok = ch_v.tok()

        def r_bc(t):
            wait(t, gate_c_tok)
            wait(t, gz_tok)
            bwait(t, 5)
            li = t.matmul(PS[5][:, 0:2], ones1[:, :], g3[0:1, 12:14],
                          start=True, stop=True)
            ch_t.inc(li)
        T(r_bc)
        bc_tok = ch_t.tok()

        def r_bce(v):
            wait(v, bc_tok)
            v.tensor_copy(small[:, 48:50], PS[5][:, 0:2])
            ch_v.inc(v.drain(fusable=False))
        V(r_bce)
        coef_tok = ch_v.tok()
        bset(5, coef_tok)
        C_OMA = small[:, 48:49]
        C_NLR = small[:, 49:50]

        upd_toks = []
        for gsrc, wsrc, wdst, grt, sfin in [
                (G0, WT["w0"], W0N, "g0r", [blk_done[NBLK - 1], spq[NBLK - 1],
                                            spk[NBLK - 1]]),
                (G1, WT["w1"], W1N, "g1r", [blk_done[NBLK - 1], spv[NBLK - 1]])]:
            def on_us(s, wsrc=wsrc, wdst=wdst, sfin=sfin):
                wait(s, coef_tok)
                for tk in sfin:
                    wait(s, tk)
                s.activation(wdst[:, 0:4096], wsrc[:, 0:4096], AF.Identity, scale=C_OMA)
                s.activation(wdst[:, 4096:8192], wsrc[:, 4096:8192], AF.Identity, scale=C_OMA)
                ch_a.inc(s.drain(fusable=False))
            A(on_us)
            sc_tok = ch_a.tok()

            def on_uv(v, gsrc=gsrc, wdst=wdst, sc_tok=sc_tok, grt=grt):
                wait(v, sc_tok)
                for c in range(4):
                    wait(v, cc_tok[grt + str(c)])
                    sl = slice(c * 2048, (c + 1) * 2048)
                    v.tensor_scalar(gsrc[:, sl], gsrc[:, sl], GRAD_CLIP, -GRAD_CLIP,
                                    ALU.min, ALU.max)
                    tk = ch_v.inc(v.drain(fusable=False)); wait(v, tk)
                    v.scalar_tensor_tensor(wdst[:, sl], gsrc[:, sl], C_NLR, wdst[:, sl],
                                           ALU.mult, ALU.add)
                    tk = ch_v.inc(v.drain(fusable=False)); wait(v, tk)
                upd_toks.append(tk)
            V(on_uv)
        w0n_tok, w1n_tok = upd_toks
        tap("w0n", W0N[:, :], w0n_tok)

        if phases < 4:
            finalize_partial()
            return nc

        q2t = [None] * NBLK
        x2t = [None] * NBLK
        res_sp = [None] * NBLK
        ln2_sp = [None] * NBLK
        logit_t = [None] * NBLK
        u_last = [None] * NBLK
        res_last = [None] * NBLK
        ln2_toks = [None] * NBLK
        ln2tm_sp = [None] * NBLK

        HUGE = 1.0e6
        SCQ = DmaQ(ctx, nc, "SCQ")
        bnd_regs = {}

        def setup_bnd(g):
            r1 = g.alloc_register("bnd_idx")
            g.reg_mov(r1, NEXP * 512 - 1)
            r2 = g.alloc_register("bnd_tok")
            g.reg_mov(r2, TC - 1)
            bnd_regs["idx"] = r1
            bnd_regs["tok"] = r2
        G(setup_bnd)

        QLQ = [Q2Q, KVQ[0], KVQ[1], XQ]
        X2Q = [GBQ, DmaQ(ctx, nc, "X2Q1"), DmaQ(ctx, nc, "X2Q2"),
               DmaQ(ctx, nc, "X2Q3")]
        JF2 = f32a(24, 1024)
        MKB = bfa(28, 256)
        routeA_fin = [None] * NBLK
        routeB_fin = [None]
        rk_base_tok = [None, None]
        sc_last = [None]
        scq_slot = [None] * 4
        PAIRJ = [640, 656, 672, 688]

        def v3(ap):
            return ap.rearrange("p (c e) -> p c e", e=32)

        def bc3(ap):
            return ap.unsqueeze(2).broadcast_to([128, 4, 32])

        def route_soft(b):
            sA = small[:, 40:44]
            sR = small[:, 44:48]
            sM2 = small[:, 52:56]
            E = JF2[:, 0:128]
            M1 = JF2[:, 128:256]
            T = JF2[:, 256:384]
            M2 = JF2[:, 384:512]
            W = WMA[:, b * 128:(b + 1) * 128]
            L = LOGI[:, b * 128:(b + 1) * 128]
            neg_tok = [None]
            exp_tok = [None]

            def on_a(v, b=b):
                wait(v, logit_t[b])
                if b == 0:
                    wait(v, cast_toks["iota32"])
                    wait(v, cast_toks["iotac"])
                    wait(v, cast_toks["tokc4"])
                else:
                    wait(v, routeB_fin[0])
                v.tensor_reduce(sA, v3(L), AX, ALU.max)
                tk = ch_v.inc(v.drain(fusable=False)); wait(v, tk)
                v.tensor_scalar_mul(sA, sA, -1.0)
                tk = ch_v.inc(v.drain(fusable=False)); wait(v, tk)
                v.tensor_scalar(small[:, 60:64], tokc4[:, :], 1.0, float(b * 512),
                                ALU.mult, ALU.add)
                neg_tok[0] = ch_v.inc(v.drain(fusable=False))
            V(on_a)

            def on_exp(sa, b=b):
                wait(sa, neg_tok[0])
                for c in range(4):
                    sa.activation(JF2[:, c * 32:(c + 1) * 32],
                                  LOGI[:, (b * 4 + c) * 32:(b * 4 + c + 1) * 32],
                                  AF.Exp, bias=small[:, 40 + c:41 + c], scale=1.0)
                exp_tok[0] = ch_a.inc(sa.drain(fusable=False))
            A(on_exp)

            def on_sm(v, b=b):
                wait(v, exp_tok[0])
                v.tensor_reduce(sR, v3(E), AX, ALU.add)
                tk = ch_v.inc(v.drain(fusable=False)); wait(v, tk)
                v.reciprocal(sR, sR)
                tk = ch_v.inc(v.drain(fusable=False)); wait(v, tk)
                v.tensor_tensor(v3(W), v3(E), bc3(sR), ALU.mult)
                tk = ch_v.inc(v.drain(fusable=False)); wait(v, tk)
                v.tensor_reduce(sA, v3(W), AX, ALU.max)
                tk = ch_v.inc(v.drain(fusable=False)); wait(v, tk)
                v.tensor_tensor(v3(M1), v3(W), bc3(sA), ALU.is_ge)
                tk = ch_v.inc(v.drain(fusable=False)); wait(v, tk)
                v.tensor_tensor(v3(T), v3(W), bc3(sA), ALU.is_lt)
                tk = ch_v.inc(v.drain(fusable=False)); wait(v, tk)
                v.tensor_tensor(T, T, W, ALU.mult)
                tk = ch_v.inc(v.drain(fusable=False)); wait(v, tk)
                v.tensor_reduce(sM2, v3(T), AX, ALU.max)
                tk = ch_v.inc(v.drain(fusable=False)); wait(v, tk)
                v.tensor_tensor(v3(T), v3(W), bc3(sM2), ALU.is_ge)
                tk = ch_v.inc(v.drain(fusable=False)); wait(v, tk)
                v.tensor_tensor(M2, T, M1, ALU.subtract)
                tk = ch_v.inc(v.drain(fusable=False)); wait(v, tk)
                v.tensor_tensor(W, W, T, ALU.mult)
                routeA_fin[b] = ch_v.inc(v.drain(fusable=False))
            V(on_sm)

        def route_ranks(b):
            for r in range(2):
                mcol = JF2[:, 128:256] if r == 0 else JF2[:, 384:512]
                mk = MKB[:, r * 128:(r + 1) * 128]
                mb_tok = [None]
                cn_tok = [None]
                cv_tok = [None]
                br_tok = [None]
                rk_tok = [None]
                sl_tok = [None]

                def on_mb(v, b=b, r=r, mcol=mcol, mk=mk):
                    wait(v, routeA_fin[b])
                    wait(v, rk_base_tok[r])
                    mb_tok[0] = ch_v.inc(v.tensor_copy(mk, mcol))
                V(on_mb)

                def on_mm1(t, b=b, r=r, mk=mk):
                    wait(t, mb_tok[0])
                    if b == 0 and r == 0:
                        wait(t, cast_toks["lt"])
                        wait(t, gz_tok)
                    bwait(t, 4 + r)
                    cn_tok[0] = ch_t.inc(t.matmul(
                        PS[4 + r][0:1, 0:128], ones_bf[:, :], mk,
                        start=True, stop=True))
                    bwait(t, 2 + r)
                    t.matmul(PS[2 + r][:, 0:128], lt128[:, :], mk,
                             start=True, stop=False)
                T(on_mm1)

                def on_cv(s, r=r):
                    wait(s, cn_tok[0])
                    s.copy(JF2[0:1, 712:840], PS[4 + r][0:1, 0:128])
                    cv_tok[0] = ch_a.inc(s.drain(fusable=False))
                A(on_cv)
                bset(4 + r, cv_tok[0])

                def on_br(v, r=r):
                    wait(v, cv_tok[0])
                    CNT = JF2[0:1, 712:840]
                    BR4 = JF2[0:1, 840:968]
                    bro = baserow[0:1, r * 32:(r + 1) * 32]
                    v.tensor_copy(BR4[0:1, 0:32], bro)
                    tk = ch_v.inc(v.drain(fusable=False)); wait(v, tk)
                    v.tensor_tensor(BR4[0:1, 32:64], BR4[0:1, 0:32],
                                    CNT[0:1, 0:32], ALU.add)
                    tk = ch_v.inc(v.drain(fusable=False)); wait(v, tk)
                    v.tensor_tensor(BR4[0:1, 64:96], BR4[0:1, 32:64],
                                    CNT[0:1, 32:64], ALU.add)
                    tk = ch_v.inc(v.drain(fusable=False)); wait(v, tk)
                    v.tensor_tensor(BR4[0:1, 96:128], BR4[0:1, 64:96],
                                    CNT[0:1, 64:96], ALU.add)
                    tk = ch_v.inc(v.drain(fusable=False)); wait(v, tk)
                    v.tensor_tensor(bro, BR4[0:1, 96:128],
                                    CNT[0:1, 96:128], ALU.add)
                    tk = ch_v.inc(v.drain(fusable=False)); wait(v, tk)
                    v.tensor_copy(br4b[0:1, r * 128:(r + 1) * 128], BR4[0:1, :])
                    br_tok[0] = ch_v.inc(v.drain(fusable=False))
                V(on_br)

                def on_mm2(t, r=r):
                    wait(t, br_tok[0])
                    rk_tok[0] = ch_t.inc(t.matmul(
                        PS[2 + r][:, 0:128], ones1b[:, :],
                        br4b[0:1, r * 128:(r + 1) * 128], start=False, stop=True))
                    rk_base_tok[r] = rk_tok[0]
                T(on_mm2)

                def on_sl(v, b=b, r=r, mcol=mcol):
                    wait(v, rk_tok[0])
                    SL = JF2[:, 512:640]
                    T3 = JF2[:, 256:384]
                    v.tensor_copy(SL, PS[2 + r][:, 0:128])
                    tk = ch_v.inc(v.drain(fusable=False)); wait(v, tk)
                    v.tensor_scalar(T3, SL, 255.5, None, ALU.is_gt)
                    tk = ch_v.inc(v.drain(fusable=False)); wait(v, tk)
                    v.scalar_tensor_tensor(SL, T3, HUGE, SL, ALU.mult, ALU.add)
                    tk = ch_v.inc(v.drain(fusable=False)); wait(v, tk)
                    v.tensor_scalar(T3, mcol, -1.0, 1.0, ALU.mult, ALU.add)
                    tk = ch_v.inc(v.drain(fusable=False)); wait(v, tk)
                    v.scalar_tensor_tensor(SL, T3, HUGE, SL, ALU.mult, ALU.add)
                    tk = ch_v.inc(v.drain(fusable=False)); wait(v, tk)
                    v.tensor_reduce(small[:, 52:56], v3(SL), AX, ALU.min)
                    tk = ch_v.inc(v.drain(fusable=False)); wait(v, tk)
                    v.tensor_tensor(v3(T3), v3(mcol),
                                    iota32[:, :].unsqueeze(1).broadcast_to(
                                        [128, 4, 32]), ALU.mult)
                    tk = ch_v.inc(v.drain(fusable=False)); wait(v, tk)
                    v.tensor_reduce(small[:, 56:60], v3(T3), AX, ALU.add)
                    tk = ch_v.inc(v.drain(fusable=False)); wait(v, tk)
                    v.tensor_tensor(T3, WMA[:, b * 128:(b + 1) * 128], mcol,
                                    ALU.mult)
                    tk = ch_v.inc(v.drain(fusable=False)); wait(v, tk)
                    v.tensor_reduce(small[:, 48:52], v3(T3), AX, ALU.add)
                    tk = ch_v.inc(v.drain(fusable=False)); wait(v, tk)
                    SLG = JF2[:, 968:972]
                    v.scalar_tensor_tensor(SLG, small[:, 56:60], 512.0,
                                           small[:, 52:56], ALU.mult, ALU.add)
                    tk = ch_v.inc(v.drain(fusable=False)); wait(v, tk)
                    if r:
                        v.tensor_scalar_add(SLG, SLG, 256.0)
                        tk = ch_v.inc(v.drain(fusable=False)); wait(v, tk)
                    sidx = (b * 2 + r) % 4
                    pc = PAIRJ[sidx]
                    wait(v, scq_slot[sidx])
                    P3 = JF2[:, pc:pc + 8].rearrange("p (c two) -> p c two", two=2)
                    v.tensor_copy(P3[:, :, 0:1], small[:, 60:64].unsqueeze(2))
                    tk = ch_v.inc(v.drain(fusable=False)); wait(v, tk)
                    v.tensor_copy(P3[:, :, 1:2], small[:, 48:52].unsqueeze(2))
                    tk = ch_v.inc(v.drain(fusable=False)); wait(v, tk)
                    v.tensor_copy(JF2[:, pc + 8:pc + 12].bitcast(mybir.dt.uint32),
                                  SLG)
                    sl_tok[0] = ch_v.inc(v.drain(fusable=False))
                    routeB_fin[0] = sl_tok[0]
                V(on_sl)
                bset(2 + r, sl_tok[0])

                def on_sc(g, b=b, r=r):
                    wait(g, sl_tok[0])
                    sidx = (b * 2 + r) % 4
                    pc = PAIRJ[sidx]
                    for c in range(4):
                        sc_last[0] = SCQ.dma_indirect_scatter(
                            g, idxstage,
                            JF2[:, pc + 8 + c:pc + 9 + c].bitcast(mybir.dt.uint32),
                            JF2[:, pc + c * 2:pc + c * 2 + 2], bnd_regs["idx"])
                    scq_slot[sidx] = sc_last[0]
                G(on_sc)

        for b in range(NBLK):
            def on_q2(sy, b=b):
                if b < 2:
                    wait(sy, blk_done[NBLK - 1])
                else:
                    wait(sy, u_last[b - 2])
                wait(sy, spq[b])
                wait(sy, TAPQ.tok())
                tks = []
                for j in range(4):
                    tks.append(QLQ[j].dma(
                        sy, Q2[b % 2][:, j * 1024:(j + 1) * 1024].rearrange(
                            "p (c t) -> p c t", c=2),
                        sp_view(qsp, b)[:, 2 * j:2 * j + 2, :]))
                q2t[b] = _multi(tks)
            SY(on_q2)

            u_mm = [None] * KC
            u_ev = [None] * KC
            for oc in range(KC):
                bank = PS[oc % 2]

                def on_t(t, oc=oc, bank=bank, b=b):
                    if oc == 0:
                        wait(t, q2t[b])
                        if b == 0:
                            wait(t, w0n_tok)
                    bwait(t, oc % 2)
                    for kc in range(KC):
                        li = t.matmul(bank[:, :],
                                      W0N[:, kc * 1024 + oc * 128:kc * 1024 + oc * 128 + 128],
                                      Q2[b % 2][:, kc * 512:(kc + 1) * 512],
                                      start=(kc == 0), stop=(kc == KC - 1))
                    u_mm[oc] = ch_t.inc(li)
                T(on_t)

                def on_s(v, oc=oc, bank=bank):
                    wait(v, u_mm[oc])
                    u_ev[oc] = ch_v.inc(v.tensor_copy(
                        U2[:, oc * 512:(oc + 1) * 512], bank[:, :]))
                V(on_s)
                bset(oc % 2, u_ev[oc])
            u_last[b] = u_mm[-1]

            def on_sg2(sa, dep=u_ev[-1]):
                wait(sa, dep)
                ch_a.inc(sa.activation(SG2[:, :], U2[:, :], AF.Sigmoid))
            A(on_sg2)
            sg2_tok = ch_a.tok()

            def on_u(v):
                wait(v, sg2_tok)
                v.tensor_tensor(U2[:, :], U2[:, :], SG2[:, :], ALU.mult)
                ch_v.inc(v.drain(fusable=False))
            V(on_u)
            u_tok = ch_v.tok()

            if phases >= 5 and b > 0:
                route_ranks(b - 1)

            def on_x2(sa, b=b):
                if b == 0:
                    wait(sa, w0n_tok)
                else:
                    wait(sa, res_last[b - 1])
                tks = []
                for tc in range(4):
                    tks.append(X2Q[tc].dma(
                        sa, X2[:, tc * 1024:(tc + 1) * 1024],
                        x_d[b * TB + tc * 128:b * TB + (tc + 1) * 128, :]))
                x2t[b] = _multi(tks)
            A(on_x2)

            h2_tok, _ = proj_fm(W1N, U2, H2, u_tok, w1n_tok if b == 0 else None)

            at_mm = [None] * 8
            at_ev = [None] * 8
            for i, (tc, dn) in enumerate([(tc, dn) for tc in range(4) for dn in range(2)]):
                bank = PS[2 + i % 2]

                def on_t(t, i=i, tc=tc, dn=dn, bank=bank, b=b):
                    if i == 0:
                        wait(t, h2_tok)
                        if b == 0:
                            wait(t, cast_toks["woutT"])
                    bwait(t, 2 + i % 2)
                    for kc in range(KC):
                        li = t.matmul(bank[:, :],
                                      H2[:, kc * 512 + tc * 128:kc * 512 + tc * 128 + 128],
                                      WOUTT[:, kc * 1024 + dn * 512:
                                            kc * 1024 + (dn + 1) * 512],
                                      start=(kc == 0), stop=(kc == KC - 1))
                    at_mm[i] = ch_t.inc(li)
                T(on_t)

                def on_v(v, i=i, tc=tc, dn=dn, bank=bank, b=b):
                    wait(v, at_mm[i])
                    if i == 0:
                        wait(v, x2t[b])
                        if b > 0:
                            wait(v, ln2_toks[b - 1])
                            wait(v, res_sp[b - 1])
                    sl = slice(tc * 1024 + dn * 512, tc * 1024 + (dn + 1) * 512)
                    v.tensor_tensor(RES[:, sl], bank[:, :], X2[:, sl], ALU.add)
                    at_ev[i] = ch_v.inc(v.drain(fusable=False))
                V(on_v)
                bset(2 + i % 2, at_ev[i])
            res_tok = at_ev[-1]
            res_last[b] = res_tok

            def on_rsp(sy, b=b, res_tok=res_tok):
                wait(sy, res_tok)
                res_sp[b] = RSQ.dma(sy, residsp[:, b * 4096:(b + 1) * 4096], RES[:, :])
            SY(on_rsp)

            ln2_guard = [res_tok]
            if b > 0:
                ln2_guard.append(logit_t[b - 1])
                ln2_guard.append(ln2_sp[b - 1])
            ln2_tok = emit_ln(RES, [HIT, LN2F32], JNK2, _multi(ln2_guard))
            ln2_toks[b] = ln2_tok

            def on_lo(v, ln2_tok=ln2_tok):
                wait(v, ln2_tok)
                v.tensor_tensor(LOT[:, :], LN2F32[:, :], HIT[:, :], ALU.subtract)
                ch_v.inc(v.drain(fusable=False))
            V(on_lo)
            lo_tok = ch_v.tok()

            hif_tok = transpose_block(HIT, HIF, True, ln2_tok)
            lof_tok = transpose_block(LOT, LOF, True, lo_tok)

            def on_lsp(sy, b=b, hif_tok=hif_tok, ln2_tok=ln2_tok):
                wait(sy, ln2_tok)
                ln2tm_sp[b] = LNQ.dma(
                    sy, ln2tmsp[b * TB:(b + 1) * TB, :].rearrange("(tc p) d -> p tc d", p=128),
                    HIT[:, :].rearrange("p (tc d) -> p tc d", tc=4))
                ln2_sp[b] = ln2tm_sp[b]
            SY(on_lsp)

            lg_mm = [None] * 4
            lg_ev = [None] * 4
            for tc in range(4):
                bank = PS[4 + tc % 2]

                def on_t(t, tc=tc, bank=bank, b=b):
                    if tc == 0:
                        wait(t, hif_tok)
                        wait(t, lof_tok)
                        if b == 0:
                            wait(t, rt_tok)
                    bwait(t, 4 + tc % 2)
                    for kc in range(KC):
                        sl = slice(kc * 512 + tc * 128, kc * 512 + tc * 128 + 128)
                        t.matmul(bank[:, 0:32], HIF[:, sl], rthi[:, kc * 32:(kc + 1) * 32],
                                 start=(kc == 0), stop=False)
                        t.matmul(bank[:, 32:64], HIF[:, sl], rtlo[:, kc * 32:(kc + 1) * 32],
                                 start=False, stop=False)
                        li = t.matmul(bank[:, 0:32], LOF[:, sl],
                                      rthi[:, kc * 32:(kc + 1) * 32],
                                      start=False, stop=(kc == KC - 1))
                    lg_mm[tc] = ch_t.inc(li)
                T(on_t)

                lg_cp = [None]

                def on_sc(s, tc=tc, bank=bank, b=b):
                    wait(s, lg_mm[tc])
                    cc = b * 4 + tc
                    s.copy(LOGI[:, cc * 32:(cc + 1) * 32], bank[:, 0:32])
                    lg_cp[0] = ch_a.inc(s.drain(fusable=False))
                A(on_sc)

                def on_v(v, tc=tc, bank=bank, b=b):
                    wait(v, lg_cp[0])
                    cc = b * 4 + tc
                    v.tensor_tensor(LOGI[:, cc * 32:(cc + 1) * 32],
                                    LOGI[:, cc * 32:(cc + 1) * 32],
                                    bank[:, 32:64], ALU.add)
                    lg_ev[tc] = ch_v.inc(v.drain(fusable=False))
                V(on_v)
                bset(4 + tc % 2, lg_ev[tc])
            logit_t[b] = lg_ev[-1]

            if phases >= 5:
                route_soft(b)

            if b == 0:
                tap("u2", U2[:, :], u_tok)
                tap("h2", H2[:, :], h2_tok)
                tap("resid", RES[:, :], res_tok)
                tap("hit", HIT[:, :], ln2_tok)

        tap("logits", LOGI[:, :], logit_t[NBLK - 1])

        if phases < 5:
            finalize_partial()
            return nc

        route_ranks(NBLK - 1)
        tap("wma", WMA[:, :], routeA_fin[NBLK - 1])
        bookkeeping_fin = sc_last[0]

        if phases < 6:
            finalize_partial()
            return nc

        EXS = [bfa(0, KC * D), bfa(16, KC * D), bfa(32, KC * D), bfa(144, KC * D)]
        EXC = [EXQ[0], EXQ[1], WQ, CCL]
        ex_t = [None] * NEXP
        IQ = [LSQ[0], LSQ[1]]
        GQ = [ACQ[0], ACQ[1]]

        def exs_load(g, ei, guard=None):
            if guard is not None:
                wait(g, guard)
            ex_t[ei] = EXC[ei % 4].dma(
                g, EXS[ei % 4][:, :].rearrange("p (c o) -> p c o", c=KC),
                experts_d[ei * D:(ei + 1) * D, :].rearrange("(c p) o -> p c o", p=128))

        ROT = 4
        IDXS4 = [f32a(96, 256), f32a(97, 256), f32a(98, 256), f32a(99, 256)]
        IDXU4 = [f32a(100, 256), f32a(101, 256), f32a(102, 256), f32a(103, 256)]
        GXT4 = [bfa(104, 4096), bfa(112, 4096), bfa(120, 4096), bfa(128, 4096)]
        GXF2 = [bfa(64, 4096), bfa(72, 4096)]
        YSB2 = [bfa(80, 4096), bfa(88, 4096)]
        SQC = [DmaQ(ctx, nc, "SQC0"), DmaQ(ctx, nc, "SQC1"),
               DmaQ(ctx, nc, "SQC2"), DmaQ(ctx, nc, "SQC3")]
        GQ4 = [ACQ[0], ACQ[1], DmaQ(ctx, nc, "GQ2"), DmaQ(ctx, nc, "GQ3")]
        idx_t = [None] * ROT
        idxu_t = [None] * ROT
        gxt_t = [None] * ROT
        tr_fin = {}
        mm_fin = {}
        ev_fin = {}
        sc_fin = {}

        def S0(e):
            r4 = e % ROT

            def on_idx(sy, e=e, r4=r4):
                if e < 2:
                    wait(sy, bookkeeping_fin)
                    wait(sy, SCQ.tok())
                    for b in range(NBLK):
                        wait(sy, ln2tm_sp[b])
                    wait(sy, res_sp[NBLK - 1])
                if e - ROT >= 0:
                    wait(sy, sc_fin[e - ROT])
                idx_t[r4] = IQ[e % 2].dma(
                    sy, IDXS4[r4][:, 0:8].rearrange("p (c two) -> p c two", two=2),
                    idxstage[e * 512:(e + 1) * 512, :].rearrange(
                        "(c p) two -> p c two", p=128))
            SY(on_idx)

            def on_cast(v, e=e, r4=r4):
                wait(v, idx_t[r4])
                idxu_t[r4] = ch_v.inc(v.tensor_copy(
                    IDXU4[r4][:, 0:8].bitcast(mybir.dt.uint32), IDXS4[r4][:, 0:8]))
            V(on_cast)

        def S2(e):
            r4 = e % ROT

            def on_gather(g, e=e, r4=r4):
                wait(g, idxu_t[r4])
                if e - ROT + 1 > 0 and (e - ROT) in tr_fin:
                    wait(g, tr_fin[e - ROT])
                tks = []
                for c in range(4):
                    tks.append(GQ4[c].dma_indirect_gather(
                        g, GXT4[r4][:, c * 1024:(c + 1) * 1024], ln2tmsp,
                        IDXU4[r4][:, 0:8].bitcast(mybir.dt.uint32).rearrange(
                            "p (c two) -> p c two", two=2)[:, c:c + 1, 0:1],
                        bnd_regs["tok"]))
                gxt_t[r4] = _multi(tks)
            G(on_gather)

        def S345(e):
            r4 = e % ROT
            s2 = e % 2
            tr_toks = [None] * 8
            for i in range(8):
                bank = PT[i % 2]

                def on_tr(t, i=i, r4=r4, s2=s2, e=e, bank=bank):
                    if i == 0:
                        wait(t, gxt_t[r4])
                        if (e - 2) in mm_fin:
                            wait(t, mm_fin[e - 2])
                    bwait(t, 6 + i % 2)
                    for tc in range(4):
                        li = t.transpose(bank[:, tc * 128:tc * 128 + 128],
                                         GXT4[r4][:, tc * 1024 + i * 128:
                                                  tc * 1024 + i * 128 + 128],
                                         ident[:, :])
                    tr_toks[i] = ch_t.inc(li)
                T(on_tr)

                def on_tre(v, i=i, s2=s2, e=e, bank=bank):
                    wait(v, tr_toks[i])
                    tr_fin[e] = ch_v.inc(v.tensor_copy(
                        GXF2[s2][:, i * 512:(i + 1) * 512], bank[:, 0:512]))
                V(on_tre)
                bset(6 + i % 2, tr_fin[e])

            mm_toks = [None] * 8
            ev_toks = [None] * 8
            for gi, (ts, on) in enumerate([(ts, on) for ts in range(4) for on in range(2)]):
                bank = PS[gi % 6]

                def on_t(t, gi=gi, ts=ts, on=on, bank=bank, s2=s2, e=e):
                    if gi == 0:
                        wait(t, tr_fin[e])
                        wait(t, ex_t[e])
                        if (e - 2) in ev_fin:
                            wait(t, ev_fin[e - 2])
                    bwait(t, gi % 6)
                    for kc in range(KC):
                        li = t.matmul(bank[:, :],
                                      GXF2[s2][:, kc * 512 + ts * 128:
                                               kc * 512 + ts * 128 + 128],
                                      EXS[e % 4][:, kc * 1024 + on * 512:
                                                 kc * 1024 + (on + 1) * 512],
                                      start=(kc == 0), stop=(kc == KC - 1))
                    mm_toks[gi] = ch_t.inc(li)
                T(on_t)

                def on_ev(v, gi=gi, ts=ts, on=on, bank=bank, s2=s2, e=e, r4=r4):
                    wait(v, mm_toks[gi])
                    if gi == 0 and (e - 2) in sc_fin:
                        wait(v, sc_fin[e - 2])
                    wcol = IDXS4[r4][:, 0:8].rearrange("p (c two) -> p c two", two=2)[
                        :, ts:ts + 1, 1:2]
                    ev_toks[gi] = ch_v.inc(v.tensor_scalar(
                        YSB2[s2][:, ts * 1024 + on * 512:ts * 1024 + (on + 1) * 512],
                        bank[:, :], wcol, None, ALU.mult))
                V(on_ev)
                bset(gi % 6, ev_toks[gi])
            mm_fin[e] = mm_toks[-1]
            ev_fin[e] = ev_toks[-1]

            def on_scat(g, e=e, s2=s2, r4=r4):
                wait(g, ev_fin[e])
                tks = []
                for c in range(4):
                    dstbuf = bufA if c < 2 else bufB
                    tks.append(SQC[c].dma_indirect_scatter(
                        g, dstbuf,
                        IDXU4[r4][:, 0:8].bitcast(mybir.dt.uint32).rearrange(
                            "p (c two) -> p c two", two=2)[:, c:c + 1, 0:1],
                        YSB2[s2][:, c * 1024:(c + 1) * 1024], bnd_regs["tok"]))
                sc_fin[e] = _multi(tks)
            G(on_scat)

        def ph3_pre(g):
            wait(g, logit_t[NBLK - 1])
            wait(g, bookkeeping_fin)
            wait(g, SCQ.tok())
            for ei in range(4):
                exs_load(g, ei)
        G(ph3_pre)

        S0(0)
        S0(1)
        S2(0)
        for e in range(NEXP):
            if e + 2 < NEXP:
                S0(e + 2)
            if e + 1 < NEXP:
                S2(e + 1)
            S345(e)
            if e + 4 < NEXP:
                def on_pf2(g, e=e):
                    wait(g, mm_fin[e])
                    exs_load(g, e + 4)
                G(on_pf2)

        moe_fin = [sc_fin[NEXP - 2], sc_fin[NEXP - 1]]

        if phases < 7:
            finalize_partial()
            return nc

        st_toks = [None] * NBLK
        p4_guard = [None, None]
        AB4 = [bfa(100, 4096), bfa(108, 4096), bfa(116, 4096), bfa(124, 4096)]
        for b in range(NBLK):
            sl = b % 2
            ld_tok = [None]

            def on_ld(sy, b=b, sl=sl):
                if b == 0:
                    wait(sy, moe_fin[0])
                    wait(sy, moe_fin[1])
                    wait(sy, TAPQ.tok())
                if p4_guard[sl] is not None:
                    wait(sy, p4_guard[sl])
                P4Q[sl].dma(sy, R4[sl][:, :], residsp[:, b * 4096:(b + 1) * 4096])
                P4Q[sl].dma(sy, AB4[sl][:, :].rearrange("p (tc d) -> p tc d", tc=4),
                            bufA[b * TB:(b + 1) * TB, :].rearrange(
                                "(tc p) d -> p tc d", p=128))
                ld_tok[0] = P4Q[sl].dma(
                    sy, AB4[2 + sl][:, :].rearrange("p (tc d) -> p tc d", tc=4),
                    bufB[b * TB:(b + 1) * TB, :].rearrange("(tc p) d -> p tc d", p=128))
            SY(on_ld)

            add_tok = [None]

            def on_add(v, b=b, sl=sl):
                wait(v, ld_tok[0])
                wait(v, STQ[sl].tok())
                v.tensor_tensor(O4[sl][:, :], R4[sl][:, :], AB4[sl][:, :], ALU.add)
                tk = ch_v.inc(v.drain(fusable=False)); wait(v, tk)
                v.tensor_tensor(O4[sl][:, :], O4[sl][:, :], AB4[2 + sl][:, :], ALU.add)
                add_tok[0] = ch_v.inc(v.drain(fusable=False))
                p4_guard[sl] = add_tok[0]
            V(on_add)

            def on_st(sa, b=b, sl=sl):
                wait(sa, add_tok[0])
                st_toks[b] = STQ[sl].dma(
                    sa, out_d[b * TB:(b + 1) * TB, :].rearrange("(tc p) d -> p tc d", p=128),
                    O4[sl][:, :].rearrange("p (tc d) -> p tc d", tc=4))
            A(on_st)

        def fin(s):
            wait(s, STQ[0].tok())
            wait(s, STQ[1].tok())
            wait(s, TAPQ.tok())
        A(fin)

    return nc


def kernel(L0_attn_x, L0_attn_Wk, L0_attn_Wv, L0_attn_Wq, L0_attn_Wout,
           L0_attn_W0, L0_attn_W1, L0_attn_Wg_decay, L0_attn_bg_decay,
           L0_attn_Wg_lr, L0_attn_bg_lr, L0_attn_Wg_mom, L0_attn_bg_mom,
           L0_ffn_router_w, L0_ffn_expert_weights):
    B, S, DD = L0_attn_x.shape
    xf = np.ascontiguousarray(np.asarray(L0_attn_x, dtype=np.float32).reshape(B * S, DD))
    wg3 = np.ascontiguousarray(np.stack([
        np.asarray(L0_attn_Wg_decay)[0], np.asarray(L0_attn_Wg_lr)[0],
        np.asarray(L0_attn_Wg_mom)[0]], axis=0).astype(np.float32))
    bg3 = np.zeros((1, 4), dtype=np.float32)
    bg3[0, 0] = np.asarray(L0_attn_bg_decay).reshape(())
    bg3[0, 1] = np.asarray(L0_attn_bg_lr).reshape(())
    bg3[0, 2] = np.asarray(L0_attn_bg_mom).reshape(())
    ident = np.eye(128, dtype=BFNP)
    oh = np.zeros((NEXP, NEXP, 128), dtype=BFNP)
    for e_ in range(NEXP):
        oh[e_, e_, :] = 1
    oh = oh.reshape(NEXP, NEXP * 128)
    lt = np.triu(np.ones((128, 128), dtype=np.float32), k=1).astype(BFNP)
    iota32 = np.broadcast_to(np.arange(NEXP, dtype=np.float32), (128, NEXP)).copy()
    iotac = np.arange(128, dtype=np.float32).reshape(128, 1)
    tokc4 = (np.arange(128, dtype=np.float32).reshape(128, 1)
             + 128.0 * np.arange(4, dtype=np.float32).reshape(1, 4))
    experts = np.ascontiguousarray(
        np.asarray(L0_ffn_expert_weights, dtype=np.float32).reshape(NEXP * D, D))

    nc = build(taps=tuple(DEBUG_TAPS))
    base = {
        "wkt": np.ascontiguousarray(np.asarray(L0_attn_Wk, dtype=np.float32).T),
        "wvt": np.ascontiguousarray(np.asarray(L0_attn_Wv, dtype=np.float32).T),
        "wqt": np.ascontiguousarray(np.asarray(L0_attn_Wq, dtype=np.float32).T),
        "woutt": np.ascontiguousarray(np.asarray(L0_attn_Wout, dtype=np.float32).T),
        "w0t": np.ascontiguousarray(np.asarray(L0_attn_W0, dtype=np.float32).T),
        "w1t": np.ascontiguousarray(np.asarray(L0_attn_W1, dtype=np.float32).T),
        "w1": np.ascontiguousarray(np.asarray(L0_attn_W1, dtype=np.float32)),
        "wg3": wg3, "bg3": bg3,
        "router": np.ascontiguousarray(np.asarray(L0_ffn_router_w, dtype=np.float32)),
        "experts": experts, "identin": ident, "oh": oh,
        "lt128d": lt, "iota32d": iota32, "iotacd": iotac, "tokc4d": tokc4,
    }
    in_maps = [dict(base, x=xf[c * TC:(c + 1) * TC]) for c in range(8)]
    res = run_bass_kernel_spmd(nc, in_maps, core_ids=list(range(8)))
    out = np.concatenate([res.results[c]["out"] for c in range(8)], axis=0)
    kernel.last_results = res
    return out.reshape(B, S, DD)
